# version 04
# baseline (speedup 1.0000x reference)
import numpy as np
import ml_dtypes
import concourse.bass as bass
import concourse.mybir as mybir
from concourse.bass_utils import run_bass_kernel_spmd

F32 = mybir.dt.float32
BF16 = mybir.dt.bfloat16
AF = mybir.ActivationFunctionType
ALU = mybir.AluOpType

S = 2048
D = 4096
NB = D // 128
NT = S // 128
H = 32
SCALE_A = float(192.0 ** -0.5)
QL, KV = 12, 4
SCALE_B = 0.5
EPS = 1e-5
AG = 2048


class Op:
    __slots__ = ("engine", "fn", "dma", "deps", "needed", "rank", "waits",
                 "inc_sem", "inc_amt", "tag")

    def __init__(self, engine, fn, dma, tag):
        self.engine = engine
        self.fn = fn
        self.dma = dma
        self.deps = []
        self.needed = False
        self.rank = 0
        self.waits = []
        self.inc_sem = None
        self.inc_amt = 0
        self.tag = tag


class Builder:
    ENGINES = ("tensor", "vector", "scalar", "gpsimd", "sync")

    def __init__(self, nc):
        self.nc = nc
        self.ops = {e: [] for e in self.ENGINES}
        self.last_write = {}
        self.readers = {}

    def emit(self, engine, fn, reads=(), writes=(), dma=False, tag=""):
        op = Op(engine, fn, dma, tag)
        deps = {}
        for k in reads:
            w = self.last_write.get(k)
            if w is not None:
                deps[id(w)] = w
        for k in writes:
            w = self.last_write.get(k)
            if w is not None:
                deps[id(w)] = w
            for r in self.readers.get(k, ()):
                deps[id(r)] = r
        for d in deps.values():
            if d is op:
                continue
            if d.engine == "tensor" and engine == "tensor" and not d.dma and not dma:
                continue
            op.deps.append(d)
        for k in reads:
            self.readers.setdefault(k, []).append(op)
        for k in writes:
            self.last_write[k] = op
            self.readers[k] = []
        self.ops[engine].append(op)
        return op

    LANES = 8

    def finalize_and_emit(self, stack):
        nc = self.nc
        csems = {e: stack.enter_context(nc.semaphore(f"s_{e}"))
                 for e in ("tensor", "vector", "scalar", "sync")}
        lsems = {}
        for q in ("sync", "gpsimd", "scalar"):
            lsems[("d", q)] = [stack.enter_context(nc.semaphore(f"d_{q}{i}"))
                               for i in range(self.LANES)]
        lsems[("c", "gpsimd")] = [stack.enter_context(nc.semaphore(f"g_{i}"))
                                  for i in range(self.LANES)]
        for e in self.ENGINES:
            for op in self.ops[e]:
                for d in op.deps:
                    d.needed = True
        counters = {}
        for e in self.ENGINES:
            for op in self.ops[e]:
                if op.dma:
                    key = ("d", e)
                    i = counters.get(key, 0)
                    counters[key] = i + 1
                    op.rank = i
                    op.inc_sem = lsems[key][i % self.LANES]
                    op.inc_amt = 16
                    if i >= self.LANES:
                        op.waits.append((op.inc_sem, 16 * (i // self.LANES)))
                elif e == "gpsimd":
                    key = ("c", "gpsimd")
                    i = counters.get(key, 0)
                    counters[key] = i + 1
                    op.rank = i
                    op.inc_sem = lsems[key][i % self.LANES]
                    op.inc_amt = 1
                    if i >= self.LANES:
                        op.waits.append((op.inc_sem, i // self.LANES))
                elif op.needed:
                    key = ("c", e)
                    counters[key] = counters.get(key, 0) + 1
                    op.rank = counters[key]
                    op.inc_sem = csems[e]
                    op.inc_amt = 1
        for e in self.ENGINES:
            known = {}
            for op in self.ops[e]:
                for d in op.deps:
                    if d.dma:
                        sem = lsems[("d", d.engine)][d.rank % self.LANES]
                        val = 16 * (d.rank // self.LANES + 1)
                    elif d.engine == "gpsimd":
                        sem = lsems[("c", "gpsimd")][d.rank % self.LANES]
                        val = d.rank // self.LANES + 1
                    else:
                        sem, val = csems[d.engine], d.rank
                    kid = id(sem)
                    if known.get(kid, -1) < val:
                        op.waits.append((sem, val))
                        known[kid] = val
        blk = stack.enter_context(nc.Block())
        for ename in self.ENGINES:
            ops = self.ops[ename]

            def section(eng, ops=ops):
                for op in ops:
                    for (sem, val) in op.waits:
                        eng.wait_ge(sem, val)
                    inst = op.fn(eng)
                    if op.inc_sem is not None:
                        inst.then_inc(op.inc_sem, op.inc_amt)

            getattr(blk, ename)(section)


def build_program(upto="full"):
    nc = bass.Bass("TRN2", target_bir_lowering=False)
    x0 = nc.dram_tensor("x0", [S, D], F32, kind="ExternalInput").ap()
    wq = nc.dram_tensor("wq", [D, 6144], F32, kind="ExternalInput").ap()
    wkv = nc.dram_tensor("wkv", [D, 8256], F32, kind="ExternalInput").ap()
    wo_a = nc.dram_tensor("wo_a", [D, D], F32, kind="ExternalInput").ap()
    Wq_d = nc.dram_tensor("Wq_d", [D, QL], F32, kind="ExternalInput").ap()
    bq_d = nc.dram_tensor("bq_d", [1, QL], F32, kind="ExternalInput").ap()
    W_qk = nc.dram_tensor("W_qk", [QL, 128], F32, kind="ExternalInput").ap()
    b_qk = nc.dram_tensor("b_qk", [1, 128], F32, kind="ExternalInput").ap()
    Wkv_d = nc.dram_tensor("Wkv_d", [D, KV], F32, kind="ExternalInput").ap()
    bkv_d = nc.dram_tensor("bkv_d", [1, KV], F32, kind="ExternalInput").ap()
    Wv_u = nc.dram_tensor("Wv_u", [KV, D], F32, kind="ExternalInput").ap()
    bv_u = nc.dram_tensor("bv_u", [1, D], F32, kind="ExternalInput").ap()
    Wo_b = nc.dram_tensor("Wo_b", [D, D], F32, kind="ExternalInput").ap()
    bo_b = nc.dram_tensor("bo_b", [1, D], F32, kind="ExternalInput").ap()
    lora_A = nc.dram_tensor("lora_A", [128, D], F32, kind="ExternalInput").ap()
    lora_B = nc.dram_tensor("lora_B", [8 * D, 16], F32, kind="ExternalInput").ap()
    ffn_W = nc.dram_tensor("ffn_W", [D, D], F32, kind="ExternalInput").ap()
    ffn_bias = nc.dram_tensor("ffn_bias", [1, D], F32, kind="ExternalInput").ap()
    lora_w = nc.dram_tensor("lora_w", [1, 8], F32, kind="ExternalInput").ap()
    y = nc.dram_tensor("y", [S, D], F32, kind="ExternalOutput").ap()
    x1 = nc.dram_tensor("x1", [S, D], F32).ap()
    x2 = nc.dram_tensor("x2", [S, D], F32).ap()
    x3 = nc.dram_tensor("x3", [S, D], F32).ap()
    Ua = nc.dram_tensor("Ua", [D, S], BF16).ap()
    Abh = nc.dram_tensor("Abh", [D, S], BF16).ap()
    WT = nc.dram_tensor("WTs", [D, D], BF16).ap()
    AcH = nc.dram_tensor("AcH", [128, D], BF16).ap()
    BtH = nc.dram_tensor("BtH", [128, D], BF16).ap()
    cfH = nc.dram_tensor("cfH", [1, 128], F32).ap()
    dbg = nc.dram_tensor("dbg", [128, 32 * AG], BF16, kind="ExternalOutput").ap() \
        if upto in ("ln", "h0", "ua", "wt", "b0") else None

    from contextlib import ExitStack
    stack = ExitStack()
    sb = lambda name, shape, dt: stack.enter_context(nc.sbuf_tensor(name, shape, dt))
    psa = lambda name, shape, dt: stack.enter_context(nc.psum_tensor(name, shape, dt))
    B = Builder(nc)

    BIGA = sb("BIGA", [128, 32 * AG], BF16)
    BIGB = sb("BIGB", [128, 8 * AG], BF16)
    ones_row = sb("ones_row", [1, 512], BF16)
    ones_col = sb("ones_col", [128, 1], BF16)
    ident = sb("ident", [128, 128], BF16)
    masks = sb("masks", [128, 896], BF16)
    lacc = sb("lacc", [128, 512], F32)
    oncf = sb("oncf", [128, 1], F32)
    wqd_sb = sb("wqd_sb", [128, NB * QL], BF16)
    wkvd_sb = sb("wkvd_sb", [128, NB * KV], BF16)
    bqd_row = sb("bqd_row", [1, QL], BF16)
    bkvd_row = sb("bkvd_row", [1, KV], BF16)
    wqk_sb = sb("wqk_sb", [QL, 128], BF16)
    bqk_row = sb("bqk_row", [1, 128], BF16)
    wvu_h = sb("wvu_h", [KV, 128], BF16)
    bvu_h = sb("bvu_h", [1, 128], BF16)
    row_big = sb("row_big", [1, D], BF16)
    lw_sb = sb("lw_sb", [1, 8], F32)
    coef_col = sb("coef_col", [128, 1], F32)
    stat = sb("stat", [128, 16], F32)
    eps_col = sb("eps_col", [128, 1], F32)
    diag_bf = [sb(f"diag{i}", [128, 128], BF16) for i in range(2)]
    bcol_bf = [sb(f"bcolb{i}", [128, 1], BF16) for i in range(2)]
    brow_bf = [sb(f"browb{i}", [1, 128], BF16) for i in range(2)]
    Qt_sb = sb("Qt_sb", [128, S], BF16)
    Kt_sb = sb("Kt_sb", [128, S], BF16)
    expS = [sb(f"expS{i}", [128, 512], BF16) for i in range(4)]
    lhi = sb("lhi", [1, 512], BF16)
    llo = sb("llo", [1, 512], BF16)
    bcast_sb = sb("bcast_sb", [128, 512], F32)
    Uh_sb = sb("Uh_sb", [128, S], BF16)
    rbuf = [sb(f"rbuf{i}", [128, 512], F32) for i in range(2)]
    nbuf = [sb(f"nbuf{i}", [128, 512], F32) for i in range(2)]
    lrow = nbuf[0][0:1, :]
    lrec = nbuf[1][0:1, :]
    hbf = sb("hbf", [128, 1024], BF16)
    Ckvs_sb = sb("Ckvs_sb", [128, NT * 40], BF16)
    u5_sb = lacc
    Cq_sb = Uh_sb
    Ckv_sb = Kt_sb
    ctx_bf = Qt_sb
    bct_sb = Kt_sb

    ps = [psa(f"ps{i}", [128, 512], F32) for i in range(6)]
    psb = psa("psb", [128, 512], BF16)
    PK = [f"ps{i}" for i in range(6)]

    def A(c0, w):
        return BIGA[:, c0:c0 + w]

    def Bv(c0, w):
        return BIGB[:, c0:c0 + w]

    def akeys(c0, w):
        return [f"A{g}" for g in range(c0 // AG, (c0 + w - 1) // AG + 1)]

    def bkeys(c0, w):
        return [f"B{g}" for g in range(c0 // AG, (c0 + w - 1) // AG + 1)]

    XBF = [0, 4096]
    SQS = 8192
    QW, KW, VW2 = 0, 4096, 8192
    VT, VS = 12288, 14336
    ACT_ = 12288
    AH = [12288, 14336]
    UAT = [8192, 12288]
    WCOL = [0, 0]
    WSTG = 4096
    STGB = 0
    STG16 = 4096
    QH = 0

    def dma(queue, out_ap, in_ap, reads, writes, tag=""):
        return B.emit(queue, lambda e: e.dma_start(out_ap, in_ap), reads, writes,
                      dma=True, tag=tag)

    def mm(out_ap, lhsT_ap, rhs_ap, reads, writes, start, stop, tag=""):
        return B.emit("tensor",
                      lambda e: e.matmul(out_ap, lhsT_ap, rhs_ap, start=start, stop=stop),
                      reads, writes, tag=tag)

    def tr(out_ap, in_ap, p, reads, writes, tag=""):
        return B.emit("tensor",
                      lambda e: e.transpose(out_ap, in_ap, ident[0:p, 0:p]),
                      list(reads) + ["ident"], writes, tag=tag)

    def act(out_ap, in_ap, func, reads, writes, bias=0.0, scale=1.0, accum=None, tag=""):
        return B.emit("scalar",
                      lambda e: e.activation(out_ap, in_ap, func, bias=bias, scale=scale,
                                             accum_out=accum),
                      reads, writes, tag=tag)

    def vec(fn, reads, writes, tag=""):
        return B.emit("vector", fn, reads, writes, tag=tag)

    def gp(fn, reads, writes, tag=""):
        return B.emit("gpsimd", fn, reads, writes, tag=tag)

    gp(lambda e: e.memset(ones_row[:, :], 1.0), [], ["ones_row"])
    gp(lambda e: e.memset(ones_col[:, :], 1.0), [], ["ones_col"])
    gp(lambda e: e.memset(eps_col[:, :], EPS), [], ["eps_col"])
    gp(lambda e: e.memset(oncf[:, :], 1.0), [], ["oncf"])
    gp(lambda e: e.memset(masks[:, :], 1.0), ["mkM"], ["mkM"])
    gp(lambda e: e.affine_select(masks[:, :], masks[:, :], [[1, 896]], ALU.is_ge, 0.0,
                                 base=-384, channel_multiplier=-1), ["mkM"], ["masks"])
    gp(lambda e: e.memset(ident[:, :], 1.0), ["identM"], ["identM"])
    gp(lambda e: e.affine_select(ident[:, :], ident[:, :], [[1, 128]], ALU.is_equal, 0.0,
                                 base=0, channel_multiplier=-1), ["identM"], ["ident"])
    dma("gpsimd", wqd_sb[:, :], Wq_d.rearrange("(b p) l -> p b l", p=128), [], ["wqd_sb"])
    dma("gpsimd", wkvd_sb[:, :], Wkv_d.rearrange("(b p) l -> p b l", p=128), [], ["wkvd_sb"])
    dma("gpsimd", bqd_row[:, :], bq_d, [], ["bqd_row"])
    dma("gpsimd", bkvd_row[:, :], bkv_d, [], ["bkvd_row"])
    dma("gpsimd", wqk_sb[:, :], W_qk, [], ["wqk_sb"])
    dma("gpsimd", bqk_row[:, :], b_qk, [], ["bqk_row"])
    dma("sync", lw_sb[:, :], lora_w, [], ["lw_sb"])
    vec(lambda e: e.tensor_scalar_mul(
        lrec[0:1, 0:128], bass.AP(lw_sb, 0, [[8, 1], [1, 8], [0, 16]]), 2.0),
        ["lw_sb", "lrec"], ["coef_row"])
    dma("sync", cfH[:, :], lrec[0:1, 0:128], ["coef_row"], ["cfHk"])
    dma("sync", coef_col[:, :], cfH.rearrange("a (p o) -> p (a o)", p=128),
        ["cfHk"], ["coef_col"])
    dma("gpsimd", Bv(STGB, 4096), lora_A, bkeys(STGB, 4096),
        bkeys(STGB, 4096) + ["stgA"])
    for g4 in range(8):
        for j in range(4):
            db = g4 * 4 + j
            tr(psb[:, j * 128:(j + 1) * 128], Bv(STGB + db * 128, 128), 128,
               ["stgA"] + bkeys(STGB + db * 128, 128), ["psb"])
        B.emit("scalar",
               lambda e, g4=g4: e.copy(Bv(WSTG + g4 * 512, 512), psb[:, :]),
               ["psb"], bkeys(WSTG + g4 * 512, 512) + ["stgAc"])
    dma("sync", AcH[:, :], Bv(WSTG, 4096), ["stgAc"] + bkeys(WSTG, 4096), ["AcHk"])
    for n in range(8):
        dma("gpsimd", Bv(STGB, 512),
            lora_B[n * D:(n + 1) * D, :].rearrange("(bb p) r -> p bb r", p=128),
            ["stgA", "stgBn"] + bkeys(STGB, 512), bkeys(STGB, 512) + ["stgBn"])
        for b4 in range(8):
            for j in range(4):
                bb = b4 * 4 + j
                tr(psb[0:16, j * 128:(j + 1) * 128], Bv(STGB + bb * 16, 16), 128,
                   ["stgBn"] + bkeys(STGB, 512), ["psb"])
            B.emit("vector",
                   lambda e, b4=b4: e.tensor_copy(
                       BIGB[0:16, STG16 + b4 * 512: STG16 + (b4 + 1) * 512],
                       psb[0:16, 0:512]),
                   ["psb"] + bkeys(STG16, 4096), bkeys(STG16, 4096) + ["stg16"])
        dma("sync", BtH[n * 16:(n + 1) * 16, :], BIGB[0:16, STG16:STG16 + 4096],
            ["stg16"] + bkeys(STG16, 4096), [f"BtHk{n}"])
    bt_keys = [f"BtHk{n}" for n in range(8)]

    def layernorm_block(xin_ap, t, dst_base, dst_stride, dst_keys_fn, xin_keys, tag):
        sl = t % 2
        xb = Bv(XBF[sl], 4096)
        kx = bkeys(XBF[sl], 4096)
        dma("gpsimd", xb, xin_ap[t * 128:(t + 1) * 128, :], kx + xin_keys, kx,
            tag=f"{tag}x{t}")
        junk = Bv(SQS, 4096)
        kj = bkeys(SQS, 4096)
        act(junk, xb, AF.Copy, kx + kj, kj + [f"sum{sl}"],
            accum=stat[:, sl * 8:sl * 8 + 1], tag=f"{tag}cp{t}")
        act(junk, xb, AF.Square, kx + kj, kj + [f"sumsq{sl}"],
            accum=stat[:, sl * 8 + 1:sl * 8 + 2], tag=f"{tag}sq{t}")
        st = stat[:, sl * 8:sl * 8 + 8]
        vec(lambda e, st=st: e.tensor_scalar_mul(st[:, 2:3], st[:, 0:1], 1.0 / D),
            [f"sum{sl}"], [f"mean{sl}"])
        vec(lambda e, st=st: e.tensor_scalar_mul(st[:, 3:4], st[:, 1:2], 1.0 / D),
            [f"sumsq{sl}"], [f"msq{sl}"])
        vec(lambda e, st=st: e.tensor_mul(st[:, 4:5], st[:, 2:3], st[:, 2:3]),
            [f"mean{sl}"], [f"m2{sl}"])
        vec(lambda e, st=st: e.tensor_sub(st[:, 3:4], st[:, 3:4], st[:, 4:5]),
            [f"msq{sl}", f"m2{sl}"], [f"var{sl}"])
        act(st[:, 4:5], st[:, 3:4], AF.Ln, [f"var{sl}", f"m2{sl}", "eps_col"],
            [f"lnv{sl}"], bias=eps_col[:, 0:1])
        act(st[:, 5:6], st[:, 4:5], AF.Exp, [f"lnv{sl}"], [f"rstd0{sl}"], scale=-0.5)
        vec(lambda e, st=st: e.tensor_mul(st[:, 6:7], st[:, 5:6], st[:, 5:6]),
            [f"rstd0{sl}"], [f"r2{sl}"])
        vec(lambda e, st=st: e.tensor_scalar_add(st[:, 4:5], st[:, 3:4], EPS),
            [f"var{sl}", f"lnv{sl}"], [f"vpe{sl}"])
        vec(lambda e, st=st: e.tensor_mul(st[:, 6:7], st[:, 6:7], st[:, 4:5]),
            [f"r2{sl}", f"vpe{sl}"], [f"r2{sl}"])
        vec(lambda e, st=st: e.tensor_scalar(st[:, 6:7], st[:, 6:7], -0.5, 1.5,
                                             ALU.mult, ALU.add),
            [f"r2{sl}"], [f"r2{sl}"])
        vec(lambda e, st=st: e.tensor_mul(st[:, 5:6], st[:, 5:6], st[:, 6:7]),
            [f"rstd0{sl}", f"r2{sl}"], [f"rstd{sl}"])
        vec(lambda e, st=st: e.tensor_mul(st[:, 7:8], st[:, 2:3], st[:, 5:6]),
            [f"mean{sl}", f"rstd{sl}"], [f"mr{sl}"])
        vec(lambda e, st=st, sl=sl: e.tensor_scalar_mul(diag_bf[sl][:, :], ident[:, :],
                                                        st[:, 5:6]),
            ["ident", f"rstd{sl}"], [f"diag{sl}"])
        vec(lambda e, st=st, sl=sl: e.tensor_copy(bcol_bf[sl][:, :], st[:, 7:8]),
            [f"mr{sl}"], [f"bcol{sl}"])
        pbr = psb[0:1, sl * 128:sl * 128 + 128]
        tr(pbr, bcol_bf[sl][:, :], 128, [f"bcol{sl}"], ["psb"])
        act(brow_bf[sl][:, :], pbr, AF.Copy, ["psb"], [f"brow{sl}"], scale=-1.0)
        for g4 in range(8):
            bank = ps[g4 % 2]
            bkey = PK[g4 % 2]
            for j in range(4):
                db = g4 * 4 + j
                mm(bank[:, j * 128:(j + 1) * 128], xb[:, db * 128:(db + 1) * 128],
                   diag_bf[sl][:, :], kx + [f"diag{sl}"], [bkey], start=True, stop=False)
                mm(bank[:, j * 128:(j + 1) * 128], ones_row[0:1, 0:128],
                   brow_bf[sl][0:1, :], ["ones_row", f"brow{sl}"], [bkey],
                   start=False, stop=True)
            off = dst_base(g4 * 4) + 0
            dst_ap = bass.AP(BIGA, off, [[32 * AG, 128], [dst_stride, 4], [1, 128]])
            src_ap = bass.AP(bank, 0, [[512, 128], [128, 4], [1, 128]])
            vec(lambda e, dst_ap=dst_ap, src_ap=src_ap: e.tensor_copy(dst_ap, src_ap),
                [bkey], dst_keys_fn(g4), tag=f"{tag}ev{t}_{g4}")

    def wprep_steps():
        steps = []
        for i in range(NB):
            sl = i % 2

            def step(i=i, sl=sl):
                dma("gpsimd", Bv(WCOL[sl], 4096),
                    ffn_W[:, i * 128:(i + 1) * 128].rearrange("(b p) d -> p b d", p=128),
                    bkeys(WCOL[sl], 4096), bkeys(WCOL[sl], 4096), tag=f"wp{i}")
                for b4 in range(8):
                    for j in range(4):
                        b = b4 * 4 + j
                        tr(psb[:, j * 128:(j + 1) * 128], Bv(WCOL[sl] + b * 128, 128),
                           128, bkeys(WCOL[sl], 4096), ["psb"])
                    B.emit("scalar",
                           lambda e, b4=b4: e.copy(Bv(WSTG + b4 * 512, 512), psb[:, :]),
                           ["psb"], bkeys(WSTG + b4 * 512, 512) + ["wstg"])
                dma("sync", WT[i * 128:(i + 1) * 128, :], Bv(WSTG, 4096),
                    ["wstg"], [f"WTr{i}"])

            steps.append(step)
        return steps

    def phase_a():
        for t in range(NT):
            layernorm_block(
                x0, t,
                dst_base=lambda db, t=t: db * AG + t * 128,
                dst_stride=AG,
                dst_keys_fn=lambda g4: [f"A{g4 * 4 + j}" for j in range(4)],
                xin_keys=[], tag="a1")
        for h in range(H):
            qwk = bkeys(QW, 4096)
            kwk = bkeys(KW, 4096)
            vwk = bkeys(VW2, 4096)
            dma("gpsimd", Bv(QW, 4096),
                wq[:, h * 192: h * 192 + 128].rearrange("(b p) e -> p b e", p=128),
                qwk, qwk, tag=f"qw{h}")
            dma("gpsimd", Bv(KW, 4096),
                wkv[:, h * 128: (h + 1) * 128].rearrange("(b p) e -> p b e", p=128),
                kwk, kwk, tag=f"kw{h}")
            dma("gpsimd", Bv(VW2, 4096),
                wkv[:, 4096 + h * 128: 4096 + (h + 1) * 128].rearrange(
                    "(b p) e -> p b e", p=128),
                vwk, vwk, tag=f"vw{h}")
            vtk = bkeys(VT, 2048)
            vsk = bkeys(VS, 2048)
            for c in range(4):
                bv_ = ps[2 + c % 2]
                for db in range(NB):
                    mm(bv_[:, :], Bv(VW2 + db * 128, 128), A(db * AG + c * 512, 512),
                       vwk + [f"A{db}"], [PK[2 + c % 2]], start=(db == 0),
                       stop=(db == NB - 1))
                act(Bv(VT + c * 512, 512), bv_[:, :], AF.Copy,
                    [PK[2 + c % 2]] + vtk, vtk)
            for g4 in range(4):
                for j in range(4):
                    kb = g4 * 4 + j
                    tr(psb[:, j * 128:(j + 1) * 128], Bv(VT + kb * 128, 128), 128,
                       vtk, ["psb"])
                B.emit("scalar",
                       lambda e, g4=g4: e.copy(Bv(VS + g4 * 512, 512), psb[:, :]),
                       ["psb"] + vsk, vsk)
            for c in range(4):
                bq, bk = ps[4], ps[5]
                for db in range(NB):
                    mm(bq[:, :], Bv(QW + db * 128, 128), A(db * AG + c * 512, 512),
                       qwk + [f"A{db}"], [PK[4]], start=(db == 0), stop=(db == NB - 1))
                for db in range(NB):
                    mm(bk[:, :], Bv(KW + db * 128, 128), A(db * AG + c * 512, 512),
                       kwk + [f"A{db}"], [PK[5]], start=(db == 0), stop=(db == NB - 1))
                act(Qt_sb[:, c * 512:(c + 1) * 512], bq[:, :], AF.Copy, [PK[4]], ["Qt"])
                act(Kt_sb[:, c * 512:(c + 1) * 512], bk[:, :], AF.Copy, [PK[5]], ["Kt"])
            deferred = [None]

            def fin_mms(c, ub):
                bB = ps[4]
                mm(bB[:, :], ones_row[0:1, 0:128], lhi[0:1, :], ["ones_row", "lhi"],
                   [PK[4]], start=True, stop=False)
                mm(bB[:, :], ones_row[0:1, 0:128], llo[0:1, :], ["ones_row", "llo"],
                   [PK[4]], start=False, stop=True)
                vec(lambda e: e.tensor_copy(bcast_sb[:, :], ps[4][:, :]), [PK[4]],
                    ["bcast"])
                vec(lambda e, c=c, ub=ub: e.tensor_mul(Uh_sb[:, c * 512:(c + 1) * 512],
                                                       ps[ub][:, :], bcast_sb[:, :]),
                    [PK[ub], "bcast"], ["Uh"], tag=f"Un{h}_{c}")

            for c in range(4):
                nkb = 4 * c + 4
                ub = 2 + c % 2
                for kb in range(nkb):
                    bS = ps[kb % 2]
                    skey = PK[kb % 2]
                    ekey = f"expS{kb % 4}"
                    mm(bS[:, :], Kt_sb[:, kb * 128:(kb + 1) * 128],
                       Qt_sb[:, c * 512:(c + 1) * 512], ["Kt", "Qt"], [skey],
                       start=True, stop=True, tag=f"sc{h}_{c}_{kb}")
                    act(expS[kb % 4][:, :], bS[:, :], AF.Exp, [skey], [ekey],
                        scale=SCALE_A, tag=f"exp{h}_{c}_{kb}")
                    if kb >= 4 * c:
                        m = kb - 4 * c
                        vec(lambda e, kb=kb, m=m: e.tensor_mul(
                            expS[kb % 4][:, :], expS[kb % 4][:, :],
                            masks[:, 384 - 128 * m: 896 - 128 * m]),
                            [ekey, "masks"], [ekey], tag=f"msk{h}_{c}_{kb}")
                    mm(ps[ub][:, :], Bv(VS + kb * 128, 128), expS[kb % 4][:, :],
                       vsk + [ekey], [PK[ub]], start=(kb == 0), stop=(kb == nkb - 1))
                    if kb == 0:
                        vec(lambda e, kb=kb: e.tensor_copy(lacc[:, :], expS[kb % 4][:, :]),
                            [ekey, "lacc"], ["lacc"])
                    else:
                        vec(lambda e, kb=kb: e.tensor_add(lacc[:, :], lacc[:, :],
                                                          expS[kb % 4][:, :]),
                            [ekey, "lacc"], ["lacc"])
                    if kb == 5 and deferred[0] is not None:
                        deferred[0]()
                        deferred[0] = None
                mm(ps[5][0:1, :], oncf[:, :], lacc[:, :], ["oncf", "lacc"], [PK[5]],
                   start=True, stop=True)
                if deferred[0] is not None:
                    deferred[0]()
                    deferred[0] = None
                act(lrow[:, :], ps[5][0:1, :], AF.Ln, [PK[5], "lrow"], ["lrow"])
                act(lrec[:, :], lrow[:, :], AF.Exp, ["lrow", "coef_row"], ["lrec"],
                    scale=-1.0)
                act(lhi[:, :], lrec[:, :], AF.Copy, ["lrec", "lhi"], ["lhi"])
                vec(lambda e: e.tensor_sub(lrow[:, :], lrec[:, :], lhi[:, :]),
                    ["lrec", "lhi", "lrow"], ["llo_f"])
                vec(lambda e: e.tensor_copy(llo[:, :], lrow[:, :]), ["llo_f"], ["llo"])
                deferred[0] = (lambda c=c, ub=ub: fin_mms(c, ub))
            if deferred[0] is not None:
                deferred[0]()
                deferred[0] = None
            dma("sync", Ua[h * 128:(h + 1) * 128, :], Uh_sb[:, :], ["Uh"], [f"UaH{h}"],
                tag=f"spill{h}")
        ua_ready = [f"UaH{h}" for h in range(H)]
        wsteps = wprep_steps()
        wi = 0
        outk = []
        for Q in range(4):
            for e_blk in range(NB):
                wkey = akeys(e_blk * 1024, 1024)
                dma("gpsimd", A(e_blk * 1024, 1024),
                    wo_a[e_blk * 128:(e_blk + 1) * 128, Q * 1024:(Q + 1) * 1024],
                    wkey, wkey, tag=f"wo{Q}_{e_blk}")
            for t in range(NT):
                if wi < len(wsteps):
                    wsteps[wi]()
                    wi += 1
                usl = t % 2
                uk = bkeys(UAT[usl], 4096)
                dma("sync", Bv(UAT[usl], 4096),
                    Ua[:, t * 128:(t + 1) * 128].rearrange("(eb p) q -> p eb q", p=128),
                    ua_ready + uk, uk, tag=f"uat{Q}_{t}")
                for oc in range(2):
                    rsl = oc
                    dma("sync", rbuf[rsl][:, :],
                        x0[t * 128:(t + 1) * 128,
                           Q * 1024 + oc * 512: Q * 1024 + (oc + 1) * 512],
                        [f"rbuf{rsl}"], [f"rbuf{rsl}"])
                    bank = ps[oc]
                    for e_blk in range(NB):
                        mm(bank[:, :], Bv(UAT[usl] + e_blk * 128, 128),
                           A(e_blk * 1024 + oc * 512, 512),
                           uk + akeys(e_blk * 1024 + oc * 512, 512), [PK[oc]],
                           start=(e_blk == 0), stop=(e_blk == NB - 1))
                    vec(lambda e, bank=bank, rsl=rsl: e.tensor_add(
                        nbuf[rsl][:, :], bank[:, :], rbuf[rsl][:, :]),
                        [PK[oc], f"rbuf{rsl}"], [f"nbuf{rsl}"], tag=f"woev{Q}_{t}_{oc}")
                    o8 = Q * 2 + oc
                    outk.append(f"x1r{t}_{o8}")
                    dma("sync", x1[t * 128:(t + 1) * 128, o8 * 512:(o8 + 1) * 512],
                        nbuf[rsl][:, :], [f"nbuf{rsl}"], [f"x1r{t}_{o8}"],
                        tag=f"x1st{t}_{o8}")
        while wi < len(wsteps):
            wsteps[wi]()
            wi += 1
        return outk

    def phase_f(xin_ap, xout_ap, xname, xin_name, inst):
        ack = bkeys(ACT_, 4096)
        dma("sync", Bv(ACT_, 4096), AcH[:, :], ["AcHk"] + ack, ack, tag=f"f{inst}Ac")
        dma("gpsimd", row_big[:, :], ffn_bias, ["row_big"], ["row_big"], tag=f"f{inst}fb")
        wt_keys = [f"WTr{i}" for i in range(NB)]
        outk = []
        for ch in range(2):
            for tt in range(8):
                t = ch * 8 + tt
                layernorm_block(
                    xin_ap, t,
                    dst_base=lambda db, tt=tt: db * 1024 + tt * 128,
                    dst_stride=1024,
                    dst_keys_fn=lambda g4: [f"A{(g4 * 4) // 2}", f"A{(g4 * 4 + 3) // 2}"],
                    xin_keys=[f"{xin_name}r{t}_{o}" for o in range(8)],
                    tag=f"f{inst}ln")
            for half in range(2):
                bank = ps[4 + half]
                for db in range(NB):
                    mm(bank[:, :], Bv(ACT_ + db * 128, 128),
                       A(db * 1024 + half * 512, 512),
                       ack + [f"A{(db * 1024 + half * 512) // AG}"], [PK[4 + half]],
                       start=(db == 0), stop=(db == NB - 1))
                act(hbf[:, half * 512:(half + 1) * 512], bank[:, :], AF.Copy,
                    [PK[4 + half], "coef_col"], ["hbf"], scale=coef_col[:, 0:1])
            for oc in range(8):
                wsl = oc % 2
                wck = akeys(32768 + wsl * 16384, 16384)
                dma("sync", A(32768 + wsl * 16384, 16384),
                    WT[:, oc * 512:(oc + 1) * 512].rearrange("(b p) o -> p b o", p=128),
                    wt_keys + wck, wck, tag=f"f{inst}wt{ch}_{oc}")
                dma("gpsimd", bct_sb[:, 0:512], BtH[:, oc * 512:(oc + 1) * 512],
                    bt_keys + ["Kt"], ["Kt"], tag=f"f{inst}bt{ch}_{oc}")
                for tt in range(8):
                    t = ch * 8 + tt
                    bank = ps[tt % 2]
                    for db in range(NB):
                        mm(bank[:, :], A(db * 1024 + tt * 128, 128),
                           A(32768 + wsl * 16384 + db * 512, 512),
                           [f"A{(db * 1024 + tt * 128) // AG}",
                            f"A{(32768 + wsl * 16384 + db * 512) // AG}"],
                           [PK[tt % 2]], start=(db == 0), stop=False)
                    mm(bank[:, :], hbf[:, tt * 128:(tt + 1) * 128], bct_sb[:, 0:512],
                       ["hbf", "Kt"], [PK[tt % 2]], start=False, stop=False)
                    mm(bank[:, :], ones_row[0:1, 0:128],
                       row_big[0:1, oc * 512:(oc + 1) * 512],
                       ["ones_row", "row_big"], [PK[tt % 2]], start=False, stop=True)
                    rsl = tt % 2
                    dma("scalar", rbuf[rsl][:, :],
                        xin_ap[t * 128:(t + 1) * 128, oc * 512:(oc + 1) * 512],
                        [f"{xin_name}r{t}_{oc}", f"rbuf{rsl}"], [f"rbuf{rsl}"])
                    vec(lambda e, bank=bank, rsl=rsl: e.tensor_add(
                        nbuf[rsl][:, :], bank[:, :], rbuf[rsl][:, :]),
                        [PK[tt % 2], f"rbuf{rsl}"], [f"nbuf{rsl}"])
                    k = f"{xname}r{t}_{oc}"
                    outk.append(k)
                    dma("scalar", xout_ap[t * 128:(t + 1) * 128, oc * 512:(oc + 1) * 512],
                        nbuf[rsl][:, :], [f"nbuf{rsl}"], [k])
        return outk

    def phase_b(xin_ap, xout_ap, xin_name):
        for ch in range(4):
            for tt in range(4):
                t = ch * 4 + tt
                layernorm_block(
                    xin_ap, t,
                    dst_base=lambda db, tt=tt: db * 512 + tt * 128,
                    dst_stride=512,
                    dst_keys_fn=lambda g4: [f"A{g4}"],
                    xin_keys=[f"{xin_name}r{t}_{o}" for o in range(8)],
                    tag="bln")
            bq, bkv = ps[2], ps[3]
            for db in range(NB):
                mm(bq[0:QL, :], wqd_sb[:, db * QL:(db + 1) * QL], A(db * 512, 512),
                   ["wqd_sb", f"A{db // 4}"], [PK[2]], start=(db == 0), stop=False)
            mm(bq[0:QL, :], bqd_row[0:1, :], ones_row[0:1, :], ["bqd_row", "ones_row"],
               [PK[2]], start=False, stop=True)
            for db in range(NB):
                mm(bkv[0:KV, :], wkvd_sb[:, db * KV:(db + 1) * KV], A(db * 512, 512),
                   ["wkvd_sb", f"A{db // 4}"], [PK[3]], start=(db == 0), stop=False)
            mm(bkv[0:KV, :], bkvd_row[0:1, :], ones_row[0:1, :], ["bkvd_row", "ones_row"],
               [PK[3]], start=False, stop=True)
            act(Cq_sb[0:QL, ch * 512:(ch + 1) * 512], bq[0:QL, :], AF.Copy, [PK[2]], ["Uh"])
            act(Ckv_sb[0:KV, ch * 512:(ch + 1) * 512], bkv[0:KV, :], AF.Copy, [PK[3]], ["Kt"])
        vec(lambda e: e.memset(Ckvs_sb[:, :], 0.0), ["Ckvs"], ["CkvsM"])
        for kb in range(NT):
            pkc = psb[0:128, (kb % 4) * 128:(kb % 4) * 128 + 4]
            tr(pkc, Ckv_sb[0:KV, kb * 128:(kb + 1) * 128], KV, ["Kt"], ["psb"])
            vec(lambda e, pkc=pkc, kb=kb: e.tensor_copy(
                Ckvs_sb[:, kb * 40: kb * 40 + 4], pkc),
                ["psb", "CkvsM"], ["CkvsM"])
        vec(lambda e: e.memset(bass.AP(Ckvs_sb, 32, [[NT * 40, 128], [40, NT], [1, 1]]),
                               1.0),
            ["CkvsM"], ["Ckvs"])
        dma("gpsimd", row_big[:, :], bv_u, ["row_big"], ["row_big"], tag="b_bvu")
        for h in range(H):
            sl = h % 2
            dma("gpsimd", wvu_h[:, :], Wv_u[:, h * 128:(h + 1) * 128], ["wvu_h"],
                ["wvu_h"], tag=f"wvu{h}")
            qhk = bkeys(QH, 2048)
            for c in range(4):
                bqh = ps[4]
                mm(bqh[0:KV, :], wqk_sb[:, h * KV:(h + 1) * KV],
                   Cq_sb[0:QL, c * 512:(c + 1) * 512], ["wqk_sb", "Uh"], [PK[4]],
                   start=True, stop=False)
                mm(bqh[0:KV, :], bqk_row[0:1, h * KV:(h + 1) * KV], ones_row[0:1, :],
                   ["bqk_row", "ones_row"], [PK[4]], start=False, stop=True)
                act(BIGB[0:KV, QH + c * 512: QH + (c + 1) * 512], bqh[0:KV, :], AF.Copy,
                    [PK[4]] + qhk, qhk)
            def b_score(kb, c):
                bS = ps[c % 2]
                skey = PK[c % 2]
                ekey = f"expS{c}"
                mm(bS[:, :], Ckv_sb[0:KV, kb * 128:(kb + 1) * 128],
                   BIGB[0:KV, QH + c * 512: QH + (c + 1) * 512], ["Kt"] + qhk,
                   [skey], start=True, stop=True, tag=f"bsc{h}_{c}_{kb}")
                act(expS[c][:, :], bS[:, :], AF.Exp, [skey], [ekey], scale=SCALE_B)

            def b_u(kb, c):
                mm(ps[2 + c][0:33, :], Ckvs_sb[:, kb * 40: kb * 40 + 33],
                   expS[c][:, :],
                   ["Ckvs", f"expS{c}"], [PK[2 + c]], start=(kb == 0),
                   stop=(kb == NT - 1))

            def b_chain(c):
                bU = ps[2 + c]
                act(lrow[:, :], bU[32:33, :], AF.Ln, [PK[2 + c], "lrow"], ["lrow"])
                act(lrec[:, :], lrow[:, :], AF.Exp, ["lrow", "coef_row"], ["lrec"],
                    scale=-1.0)
                act(lhi[:, :], lrec[:, :], AF.Copy, ["lrec", "lhi"], ["lhi"])
                vec(lambda e: e.tensor_sub(lrow[:, :], lrec[:, :], lhi[:, :]),
                    ["lrec", "lhi", "lrow"], ["llo_f"])
                vec(lambda e: e.tensor_copy(llo[:, :], lrow[:, :]), ["llo_f"], ["llo"])

            def b_final(c):
                bU = ps[2 + c]
                vec(lambda e, bU=bU: e.tensor_copy(u5_sb[0:4, :], bU[0:4, :]),
                    [PK[2 + c], "u5"], ["u5"])
                bB = ps[c % 2]
                mm(bB[:, :], ones_row[0:1, 0:128], lhi[0:1, :], ["ones_row", "lhi"],
                   [PK[c % 2]], start=True, stop=False)
                mm(bB[:, :], ones_row[0:1, 0:128], llo[0:1, :], ["ones_row", "llo"],
                   [PK[c % 2]], start=False, stop=True)
                vec(lambda e, bB=bB: e.tensor_copy(bcast_sb[:, :], bB[:, :]),
                    [PK[c % 2], "bcast"], ["bcast"])
                vec(lambda e: e.tensor_mul(ctx_bf[0:KV, 0:512], u5_sb[0:KV, :],
                                           bcast_sb[0:KV, :]),
                    ["u5", "bcast"], ["Qt"])
                bT = ps[(c + 1) % 2]
                mm(bT[:, :], wvu_h[0:KV, :], ctx_bf[0:KV, 0:512],
                   ["wvu_h", "Qt"], [PK[(c + 1) % 2]], start=True, stop=False)
                mm(bT[:, :], row_big[0:1, h * 128:(h + 1) * 128], ones_row[0:1, :],
                   ["row_big", "ones_row"], [PK[(c + 1) % 2]], start=False, stop=True)
                ahk = bkeys(AH[sl], 2048)
                act(Bv(AH[sl] + c * 512, 512), bT[:, :], AF.Copy,
                    [PK[(c + 1) % 2]] + ahk, ahk)

            for kb in range(NT):
                last = kb == NT - 1
                b_score(kb, 0)
                b_score(kb, 1)
                b_u(kb, 0)
                if last:
                    b_chain(0)
                b_score(kb, 2)
                b_u(kb, 1)
                b_score(kb, 3)
                b_u(kb, 2)
                b_u(kb, 3)
            for c in range(4):
                b_final(c)
                if c < 3:
                    b_chain(c + 1)
            dma("sync", Abh[h * 128:(h + 1) * 128, :], Bv(AH[sl], 2048),
                bkeys(AH[sl], 2048), [f"AbH{h}"])
        ab_ready = [f"AbH{h}" for h in range(H)]
        dma("gpsimd", row_big[:, :], bo_b, ["row_big"], ["row_big"], tag="b_bo")
        outk = []
        for Q in range(4):
            for e_blk in range(NB):
                wkey = akeys(e_blk * 1024, 1024)
                dma("gpsimd", A(e_blk * 1024, 1024),
                    Wo_b[e_blk * 128:(e_blk + 1) * 128, Q * 1024:(Q + 1) * 1024],
                    wkey, wkey, tag=f"bwo{Q}_{e_blk}")
            for t in range(NT):
                usl = t % 2
                uk = bkeys(UAT[usl], 4096)
                dma("sync", Bv(UAT[usl], 4096),
                    Abh[:, t * 128:(t + 1) * 128].rearrange("(eb p) q -> p eb q", p=128),
                    ab_ready + uk, uk)
                for oc in range(2):
                    rsl = oc
                    o8 = Q * 2 + oc
                    dma("sync", rbuf[rsl][:, :],
                        xin_ap[t * 128:(t + 1) * 128, o8 * 512:(o8 + 1) * 512],
                        [f"{xin_name}r{t}_{o8}", f"rbuf{rsl}"], [f"rbuf{rsl}"])
                    bank = ps[oc]
                    for e_blk in range(NB):
                        mm(bank[:, :], Bv(UAT[usl] + e_blk * 128, 128),
                           A(e_blk * 1024 + oc * 512, 512),
                           uk + akeys(e_blk * 1024 + oc * 512, 512), [PK[oc]],
                           start=(e_blk == 0), stop=False)
                    mm(bank[:, :], ones_row[0:1, 0:128],
                       row_big[0:1, o8 * 512:(o8 + 1) * 512],
                       ["ones_row", "row_big"], [PK[oc]], start=False, stop=True)
                    vec(lambda e, bank=bank, rsl=rsl: e.tensor_add(
                        nbuf[rsl][:, :], bank[:, :], rbuf[rsl][:, :]),
                        [PK[oc], f"rbuf{rsl}"], [f"nbuf{rsl}"])
                    outk.append(f"x3r{t}_{o8}")
                    dma("sync", xout_ap[t * 128:(t + 1) * 128, o8 * 512:(o8 + 1) * 512],
                        nbuf[rsl][:, :], [f"nbuf{rsl}"], [f"x3r{t}_{o8}"])
        return outk

    if upto == "b0":
        keys = phase_a()
        k2 = phase_f(x1, x2, "x2", "x1", 1)
        k3 = phase_b(x2, x3, "x2")
        final = []
        final = list(k3)
        for i in range(32):
            sl = i % 2
            kk = bkeys(UAT[sl], 4096)
            dma("sync", Bv(UAT[sl], 2048), Abh[i * 128:(i + 1) * 128, :],
                k3 + kk, kk)
            dma("sync", dbg[:, i * 2048:(i + 1) * 2048], Bv(UAT[sl], 2048), kk,
                [f"dg{i}"])
            final.append(f"dg{i}")
        dma("sync", y[0:128, 0:512], rbuf[0][:, :], ["rbuf0"], ["yk"])
        final.append("yk")
    elif upto == "wt":
        wsteps = wprep_steps()
        for st_ in wsteps:
            st_()
        final = []
        for i in range(8):
            sl = i % 2
            kk = bkeys(UAT[sl], 4096)
            dma("sync", Bv(UAT[sl], 4096), WT[i * 128:(i + 1) * 128, :],
                [f"WTr{i}"] + kk, kk)
            dma("sync", dbg[:, i * 4096:(i + 1) * 4096], Bv(UAT[sl], 4096), kk,
                [f"dw{i}"])
            final.append(f"dw{i}")
        ack = bkeys(ACT_, 4096)
        dma("sync", Bv(ACT_, 4096), AcH[:, :], ["AcHk"] + ack, ack)
        dma("sync", dbg[:, 8 * 4096: 8 * 4096 + 4096], Bv(ACT_, 4096), ack, ["dA"])
        final.append("dA")
        bck = bkeys(XBF[0], 4096)
        dma("sync", Bv(XBF[0], 4096), BtH[:, :], bt_keys + bck, bck)
        dma("sync", dbg[:, 9 * 4096: 9 * 4096 + 4096], Bv(XBF[0], 4096), bck, ["dB"])
        final.append("dB")
        vec(lambda e: e.tensor_copy(expS[0][:, 0:1], coef_col[:, 0:1]),
            ["coef_col", "expS0"], ["expS0"])
        vec(lambda e: e.memset(expS[0][:, 1:512], 0.0), ["expS0"], ["expS0"])
        dma("sync", dbg[:, 10 * 4096: 10 * 4096 + 512], expS[0][:, 0:512], ["expS0"], ["dC"])
        final.append("dC")
        dma("sync", y[0:128, 0:512], rbuf[0][:, :], ["rbuf0"], ["yk"])
        final.append("yk")
    elif upto == "ua":
        keys = phase_a()
        final = []
        for i in range(32):
            sl = i % 2
            kk = bkeys(UAT[sl], 4096)
            dma("sync", Bv(UAT[sl], 2048), Ua[i * 128:(i + 1) * 128, :],
                keys + kk, kk)
            dma("sync", dbg[:, i * 2048:(i + 1) * 2048], Bv(UAT[sl], 2048), kk,
                [f"dg{i}"])
            final.append(f"dg{i}")
        dma("sync", y[0:128, 0:512], rbuf[0][:, :], ["rbuf0"], ["yk"])
        final.append("yk")
    elif upto == "h0":
        for t in range(NT):
            layernorm_block(
                x0, t,
                dst_base=lambda db, t=t: db * AG + t * 128,
                dst_stride=AG,
                dst_keys_fn=lambda g4: [f"A{g4 * 4 + j}" for j in range(4)],
                xin_keys=[], tag="a1")
        h = 0
        qwk = bkeys(QW, 4096)
        kwk = bkeys(KW, 4096)
        vwk = bkeys(VW2, 4096)
        dma("gpsimd", Bv(QW, 4096),
            wq[:, h * 192: h * 192 + 128].rearrange("(b p) e -> p b e", p=128),
            qwk, qwk)
        dma("gpsimd", Bv(KW, 4096),
            wkv[:, h * 128: (h + 1) * 128].rearrange("(b p) e -> p b e", p=128),
            kwk, kwk)
        dma("gpsimd", Bv(VW2, 4096),
            wkv[:, 4096 + h * 128: 4096 + (h + 1) * 128].rearrange(
                "(b p) e -> p b e", p=128),
            vwk, vwk)
        vtk = bkeys(VT, 2048)
        vsk = bkeys(VS, 2048)
        for c in range(4):
            bv_ = ps[2 + c % 2]
            for db in range(NB):
                mm(bv_[:, :], Bv(VW2 + db * 128, 128), A(db * AG + c * 512, 512),
                   vwk + [f"A{db}"], [PK[2 + c % 2]], start=(db == 0),
                   stop=(db == NB - 1))
            act(Bv(VT + c * 512, 512), bv_[:, :], AF.Copy, [PK[2 + c % 2]] + vtk, vtk)
        for g4 in range(4):
            for j in range(4):
                kb = g4 * 4 + j
                tr(psb[:, j * 128:(j + 1) * 128], Bv(VT + kb * 128, 128), 128,
                   vtk, ["psb"])
            B.emit("scalar",
                   lambda e, g4=g4: e.copy(Bv(VS + g4 * 512, 512), psb[:, :]),
                   ["psb"] + vsk, vsk)
        for c in range(4):
            bq, bk = ps[4], ps[5]
            for db in range(NB):
                mm(bq[:, :], Bv(QW + db * 128, 128), A(db * AG + c * 512, 512),
                   qwk + [f"A{db}"], [PK[4]], start=(db == 0), stop=(db == NB - 1))
            for db in range(NB):
                mm(bk[:, :], Bv(KW + db * 128, 128), A(db * AG + c * 512, 512),
                   kwk + [f"A{db}"], [PK[5]], start=(db == 0), stop=(db == NB - 1))
            act(Qt_sb[:, c * 512:(c + 1) * 512], bq[:, :], AF.Copy, [PK[4]], ["Qt"])
            act(Kt_sb[:, c * 512:(c + 1) * 512], bk[:, :], AF.Copy, [PK[5]], ["Kt"])
        for c in range(4):
            nkb = 4 * c + 4
            bU, bL = ps[2], ps[3]
            for kb in range(nkb):
                bS = ps[kb % 2]
                skey = PK[kb % 2]
                ekey = f"expS{kb % 2}"
                mm(bS[:, :], Kt_sb[:, kb * 128:(kb + 1) * 128],
                   Qt_sb[:, c * 512:(c + 1) * 512], ["Kt", "Qt"], [skey],
                   start=True, stop=True)
                act(expS[kb % 2][:, :], bS[:, :], AF.Exp, [skey], [ekey],
                    scale=SCALE_A)
                if kb >= 4 * c:
                    m = kb - 4 * c
                    gp(lambda e, kb=kb, m=m: e.affine_select(
                        expS[kb % 2][:, :], expS[kb % 2][:, :], [[1, 512]],
                        ALU.is_ge, 0.0, base=-128 * m, channel_multiplier=-1),
                        [ekey], [ekey])
                mm(bU[:, :], Bv(VS + kb * 128, 128), expS[kb % 2][:, :],
                   vsk + [ekey], [PK[2]], start=(kb == 0), stop=(kb == nkb - 1))
                mm(bL[0:1, :], ones_col[:, :], expS[kb % 2][:, :],
                   ["ones_col", ekey], [PK[3]],
                   start=(kb == 0), stop=(kb == nkb - 1))
            vec(lambda e: e.tensor_copy(lrow[:, :], ps[3][0:1, :]), [PK[3]], ["lrow"])
            vec(lambda e: e.reciprocal(lrec[:, :], lrow[:, :]), ["lrow", "coef_row"],
                ["lrec"])
            vec(lambda e: e.tensor_copy(lhi[:, :], lrec[:, :]), ["lrec"], ["lhi"])
            vec(lambda e: e.tensor_sub(lrow[:, :], lrec[:, :], lhi[:, :]),
                ["lrec", "lhi", "lrow"], ["llo_f"])
            vec(lambda e: e.tensor_copy(llo[:, :], lrow[:, :]), ["llo_f"], ["llo"])
            bB = ps[4]
            mm(bB[:, :], ones_row[0:1, 0:128], lhi[0:1, :], ["ones_row", "lhi"],
               [PK[4]], start=True, stop=False)
            mm(bB[:, :], ones_row[0:1, 0:128], llo[0:1, :], ["ones_row", "llo"],
               [PK[4]], start=False, stop=True)
            vec(lambda e: e.tensor_copy(bcast_sb[:, :], ps[4][:, :]), [PK[4]],
                ["bcast"])
            vec(lambda e, c=c: e.tensor_mul(Uh_sb[:, c * 512:(c + 1) * 512],
                                            ps[2][:, :], bcast_sb[:, :]),
                [PK[2], "bcast"], ["Uh"])
        dma("sync", dbg[:, 0:2048], Qt_sb[:, :], ["Qt"], ["d0"])
        dma("sync", dbg[:, 2048:4096], Kt_sb[:, :], ["Kt"], ["d1"])
        dma("sync", dbg[:, 4096:6144], Bv(VS, 2048), vsk, ["d2"])
        dma("sync", dbg[:, 6144:8192], Uh_sb[:, :], ["Uh"], ["d3"])
        dma("sync", y[0:128, 0:512], rbuf[0][:, :], ["rbuf0"], ["yk"])
        final = ["d0", "d1", "d2", "d3", "yk"]
    elif upto == "ln":
        for t in range(NT):
            layernorm_block(
                x0, t,
                dst_base=lambda db, t=t: db * AG + t * 128,
                dst_stride=AG,
                dst_keys_fn=lambda g4: [f"A{g4 * 4 + j}" for j in range(4)],
                xin_keys=[], tag="a1")
        dma("sync", dbg, BIGA[:, :], [f"A{g}" for g in range(32)], ["dbgk"])
        dma("sync", y[0:128, 0:512], rbuf[0][:, :], ["rbuf0"], ["yk"])
        final = ["dbgk", "yk"]
    elif upto == "a":
        keys = phase_a()
        for t in range(NT):
            for o in range(8):
                sl = o % 2
                dma("sync", rbuf[sl][:, :], x1[t * 128:(t + 1) * 128, o * 512:(o + 1) * 512],
                    keys + [f"rbuf{sl}"], [f"rbuf{sl}"])
                dma("sync", y[t * 128:(t + 1) * 128, o * 512:(o + 1) * 512], rbuf[sl][:, :],
                    [f"rbuf{sl}"], [f"yk{t}_{o}"])
        final = [f"yk{t}_{o}" for t in range(NT) for o in range(8)]
    elif upto == "f1":
        keys = phase_a()
        final = phase_f(x1, y, "y", "x1", 1)
    elif upto == "b":
        keys = phase_a()
        k2 = phase_f(x1, x2, "x2", "x1", 1)
        final = phase_b(x2, y, "x2")
    else:
        keys = phase_a()
        k2 = phase_f(x1, x2, "x2", "x1", 1)
        k3 = phase_b(x2, x3, "x2")
        final = phase_f(x3, y, "y", "x3", 2)
    B.emit("sync", lambda e: e.nop(), reads=final, writes=[])
    B.finalize_and_emit(stack)
    stack.close()
    return nc


_NC_CACHE = {}


def _get_nc(upto="full"):
    if upto not in _NC_CACHE:
        _NC_CACHE[upto] = build_program(upto)
    return _NC_CACHE[upto]


def _in_maps(inputs):
    c = np.ascontiguousarray
    hs = inputs["L0_attn_a_hidden_states"]
    base = {
        "wq": c(inputs["L0_attn_a_wq"]),
        "wkv": c(inputs["L0_attn_a_wkv"]),
        "wo_a": c(inputs["L0_attn_a_wo"]),
        "Wq_d": c(inputs["L0_attn_b_Wq_d"]),
        "bq_d": c(inputs["L0_attn_b_bq_d"]).reshape(1, QL),
        "W_qk": c(inputs["L0_attn_b_W_qk"]),
        "b_qk": c(inputs["L0_attn_b_b_qk"]).reshape(1, 128),
        "Wkv_d": c(inputs["L0_attn_b_Wkv_d"]),
        "bkv_d": c(inputs["L0_attn_b_bkv_d"]).reshape(1, KV),
        "Wv_u": c(inputs["L0_attn_b_Wv_u"]),
        "bv_u": c(inputs["L0_attn_b_bv_u"]).reshape(1, D),
        "Wo_b": c(inputs["L0_attn_b_Wo"]),
        "bo_b": c(inputs["L0_attn_b_bo"]).reshape(1, D),
        "lora_A": c(inputs["L0_ffn_lora_A"]).reshape(128, D),
        "lora_B": c(inputs["L0_ffn_lora_B"]).reshape(8 * D, 16),
        "ffn_W": c(inputs["L0_ffn_W"]),
        "ffn_bias": c(inputs["L0_ffn_bias"]).reshape(1, D),
        "lora_w": c(inputs["L0_ffn_lora_weights"]).reshape(1, 8),
    }
    maps = []
    for b in range(8):
        m = dict(base)
        m["x0"] = c(np.asarray(hs[b]))
        maps.append(m)
    return maps


def kernel(**inputs):
    nc = _get_nc("full")
    maps = _in_maps(inputs)
    res = run_bass_kernel_spmd(nc, maps, core_ids=list(range(8)))
    out = np.stack([np.asarray(r["y"]).astype(np.float32) for r in res.results], axis=0)
    return out


# version 05
# speedup vs baseline: 1.0378x; 1.0378x over previous
import numpy as np
import ml_dtypes
import concourse.bass as bass
import concourse.mybir as mybir
from concourse.bass_utils import run_bass_kernel_spmd

F32 = mybir.dt.float32
BF16 = mybir.dt.bfloat16
AF = mybir.ActivationFunctionType
ALU = mybir.AluOpType

S = 2048
D = 4096
NB = D // 128
NT = S // 128
H = 32
SCALE_A = float(192.0 ** -0.5)
QL, KV = 12, 4
SCALE_B = 0.5
EPS = 1e-5
AG = 2048


class Op:
    __slots__ = ("engine", "fn", "dma", "deps", "needed", "rank", "waits",
                 "inc_sem", "inc_amt", "tag")

    def __init__(self, engine, fn, dma, tag):
        self.engine = engine
        self.fn = fn
        self.dma = dma
        self.deps = []
        self.needed = False
        self.rank = 0
        self.waits = []
        self.inc_sem = None
        self.inc_amt = 0
        self.tag = tag


class Builder:
    ENGINES = ("tensor", "vector", "scalar", "gpsimd", "sync")

    def __init__(self, nc):
        self.nc = nc
        self.ops = {e: [] for e in self.ENGINES}
        self.last_write = {}
        self.readers = {}

    def emit(self, engine, fn, reads=(), writes=(), dma=False, tag=""):
        op = Op(engine, fn, dma, tag)
        deps = {}
        for k in reads:
            w = self.last_write.get(k)
            if w is not None:
                deps[id(w)] = w
        for k in writes:
            w = self.last_write.get(k)
            if w is not None:
                deps[id(w)] = w
            for r in self.readers.get(k, ()):
                deps[id(r)] = r
        for d in deps.values():
            if d is op:
                continue
            if d.engine == "tensor" and engine == "tensor" and not d.dma and not dma:
                continue
            op.deps.append(d)
        for k in reads:
            self.readers.setdefault(k, []).append(op)
        for k in writes:
            self.last_write[k] = op
            self.readers[k] = []
        self.ops[engine].append(op)
        return op

    LANES = 8

    def finalize_and_emit(self, stack):
        nc = self.nc
        csems = {e: stack.enter_context(nc.semaphore(f"s_{e}"))
                 for e in ("tensor", "vector", "scalar", "sync")}
        lsems = {}
        for q in ("sync", "gpsimd", "scalar"):
            lsems[("d", q)] = [stack.enter_context(nc.semaphore(f"d_{q}{i}"))
                               for i in range(self.LANES)]
        lsems[("c", "gpsimd")] = [stack.enter_context(nc.semaphore(f"g_{i}"))
                                  for i in range(self.LANES)]
        for e in self.ENGINES:
            for op in self.ops[e]:
                for d in op.deps:
                    d.needed = True
        counters = {}
        for e in self.ENGINES:
            for op in self.ops[e]:
                if op.dma:
                    key = ("d", e)
                    i = counters.get(key, 0)
                    counters[key] = i + 1
                    op.rank = i
                    op.inc_sem = lsems[key][i % self.LANES]
                    op.inc_amt = 16
                    if i >= self.LANES:
                        op.waits.append((op.inc_sem, 16 * (i // self.LANES)))
                elif e == "gpsimd":
                    key = ("c", "gpsimd")
                    i = counters.get(key, 0)
                    counters[key] = i + 1
                    op.rank = i
                    op.inc_sem = lsems[key][i % self.LANES]
                    op.inc_amt = 1
                    if i >= self.LANES:
                        op.waits.append((op.inc_sem, i // self.LANES))
                elif op.needed:
                    key = ("c", e)
                    counters[key] = counters.get(key, 0) + 1
                    op.rank = counters[key]
                    op.inc_sem = csems[e]
                    op.inc_amt = 1
        for e in self.ENGINES:
            known = {}
            for op in self.ops[e]:
                for d in op.deps:
                    if d.dma:
                        sem = lsems[("d", d.engine)][d.rank % self.LANES]
                        val = 16 * (d.rank // self.LANES + 1)
                    elif d.engine == "gpsimd":
                        sem = lsems[("c", "gpsimd")][d.rank % self.LANES]
                        val = d.rank // self.LANES + 1
                    else:
                        sem, val = csems[d.engine], d.rank
                    kid = id(sem)
                    if known.get(kid, -1) < val:
                        op.waits.append((sem, val))
                        known[kid] = val
        blk = stack.enter_context(nc.Block())
        for ename in self.ENGINES:
            ops = self.ops[ename]

            def section(eng, ops=ops):
                for op in ops:
                    for (sem, val) in op.waits:
                        eng.wait_ge(sem, val)
                    inst = op.fn(eng)
                    if op.inc_sem is not None:
                        inst.then_inc(op.inc_sem, op.inc_amt)

            getattr(blk, ename)(section)


def build_program(upto="full"):
    nc = bass.Bass("TRN2", target_bir_lowering=False)
    x0 = nc.dram_tensor("x0", [S, D], F32, kind="ExternalInput").ap()
    wq = nc.dram_tensor("wq", [D, 6144], F32, kind="ExternalInput").ap()
    wkv = nc.dram_tensor("wkv", [D, 8256], F32, kind="ExternalInput").ap()
    wo_a = nc.dram_tensor("wo_a", [D, D], F32, kind="ExternalInput").ap()
    Wq_d = nc.dram_tensor("Wq_d", [D, QL], F32, kind="ExternalInput").ap()
    bq_d = nc.dram_tensor("bq_d", [1, QL], F32, kind="ExternalInput").ap()
    W_qk = nc.dram_tensor("W_qk", [QL, 128], F32, kind="ExternalInput").ap()
    b_qk = nc.dram_tensor("b_qk", [1, 128], F32, kind="ExternalInput").ap()
    Wkv_d = nc.dram_tensor("Wkv_d", [D, KV], F32, kind="ExternalInput").ap()
    bkv_d = nc.dram_tensor("bkv_d", [1, KV], F32, kind="ExternalInput").ap()
    Wv_u = nc.dram_tensor("Wv_u", [KV, D], F32, kind="ExternalInput").ap()
    bv_u = nc.dram_tensor("bv_u", [1, D], F32, kind="ExternalInput").ap()
    Wo_b = nc.dram_tensor("Wo_b", [D, D], F32, kind="ExternalInput").ap()
    bo_b = nc.dram_tensor("bo_b", [1, D], F32, kind="ExternalInput").ap()
    lora_A = nc.dram_tensor("lora_A", [128, D], F32, kind="ExternalInput").ap()
    lora_B = nc.dram_tensor("lora_B", [8 * D, 16], F32, kind="ExternalInput").ap()
    ffn_W = nc.dram_tensor("ffn_W", [D, D], F32, kind="ExternalInput").ap()
    ffn_bias = nc.dram_tensor("ffn_bias", [1, D], F32, kind="ExternalInput").ap()
    lora_w = nc.dram_tensor("lora_w", [1, 8], F32, kind="ExternalInput").ap()
    y = nc.dram_tensor("y", [S, D], F32, kind="ExternalOutput").ap()
    x1 = nc.dram_tensor("x1", [S, D], F32).ap()
    x2 = nc.dram_tensor("x2", [S, D], F32).ap()
    x3 = nc.dram_tensor("x3", [S, D], F32).ap()
    Ua = nc.dram_tensor("Ua", [D, S], BF16).ap()
    Abh = nc.dram_tensor("Abh", [D, S], BF16).ap()
    WT = nc.dram_tensor("WTs", [D, D], BF16).ap()
    AcH = nc.dram_tensor("AcH", [128, D], BF16).ap()
    BtH = nc.dram_tensor("BtH", [128, D], BF16).ap()
    cfH = nc.dram_tensor("cfH", [1, 128], F32).ap()
    dbg = nc.dram_tensor("dbg", [128, 32 * AG], BF16, kind="ExternalOutput").ap() \
        if upto in ("ln", "h0", "ua", "wt", "b0") else None

    from contextlib import ExitStack
    stack = ExitStack()
    sb = lambda name, shape, dt: stack.enter_context(nc.sbuf_tensor(name, shape, dt))
    psa = lambda name, shape, dt: stack.enter_context(nc.psum_tensor(name, shape, dt))
    B = Builder(nc)

    BIGA = sb("BIGA", [128, 32 * AG], BF16)
    BIGB = sb("BIGB", [128, 8 * AG], BF16)
    ones_row = sb("ones_row", [1, 512], BF16)
    ones_col = sb("ones_col", [128, 1], BF16)
    ident = sb("ident", [128, 128], BF16)
    masks = sb("masks", [128, 896], BF16)
    lacc = sb("lacc", [128, 512], F32)
    oncf = sb("oncf", [128, 1], F32)
    wqd_sb = sb("wqd_sb", [128, NB * QL], BF16)
    wkvd_sb = sb("wkvd_sb", [128, NB * KV], BF16)
    bqd_row = sb("bqd_row", [1, QL], BF16)
    bkvd_row = sb("bkvd_row", [1, KV], BF16)
    wqk_sb = sb("wqk_sb", [QL, 128], BF16)
    bqk_row = sb("bqk_row", [1, 128], BF16)
    wvu_h = sb("wvu_h", [KV, 128], BF16)
    bvu_h = sb("bvu_h", [1, 128], BF16)
    row_big = sb("row_big", [1, D], BF16)
    lw_sb = sb("lw_sb", [1, 8], F32)
    coef_col = sb("coef_col", [128, 1], F32)
    stat = sb("stat", [128, 16], F32)
    eps_col = sb("eps_col", [128, 1], F32)
    diag_bf = [sb(f"diag{i}", [128, 128], BF16) for i in range(2)]
    bcol_bf = [sb(f"bcolb{i}", [128, 1], BF16) for i in range(2)]
    brow_bf = [sb(f"browb{i}", [1, 128], BF16) for i in range(2)]
    Qt_sb = sb("Qt_sb", [128, S], BF16)
    Kt_sb = sb("Kt_sb", [128, S], BF16)
    expS = [sb(f"expS{i}", [128, 512], BF16) for i in range(4)]
    lhi = sb("lhi", [1, 512], BF16)
    llo = sb("llo", [1, 512], BF16)
    bcast_sb = sb("bcast_sb", [128, 512], F32)
    Uh_sb = sb("Uh_sb", [128, S], BF16)
    rbuf = [sb(f"rbuf{i}", [128, 512], F32) for i in range(2)]
    nbuf = [sb(f"nbuf{i}", [128, 512], F32) for i in range(2)]
    lrow = nbuf[0][0:1, :]
    lrec = nbuf[1][0:1, :]
    hbf = sb("hbf", [128, 1024], BF16)
    Ckvs_sb = sb("Ckvs_sb", [128, NT * 40], BF16)
    u5_sb = lacc
    Cq_sb = Uh_sb
    Ckv_sb = Kt_sb
    ctx_bf = Qt_sb
    bct_sb = Kt_sb

    ps = [psa(f"ps{i}", [128, 512], F32) for i in range(6)]
    psb = psa("psb", [128, 512], BF16)
    psb2 = psa("psb2", [128, 512], BF16)
    PK = [f"ps{i}" for i in range(6)]

    def A(c0, w):
        return BIGA[:, c0:c0 + w]

    def Bv(c0, w):
        return BIGB[:, c0:c0 + w]

    def akeys(c0, w):
        return [f"A{g}" for g in range(c0 // AG, (c0 + w - 1) // AG + 1)]

    def bkeys(c0, w):
        return [f"B{g}" for g in range(c0 // AG, (c0 + w - 1) // AG + 1)]

    XBF = [0, 4096]
    SQS = 8192
    QW, KW, VW2 = 0, 4096, 8192
    VT, VS = 12288, 14336
    ACT_ = 12288
    AH = [12288, 14336]
    UAT = [8192, 12288]
    WCOL = [0, 0]
    WSTG = 4096
    STGB = 0
    STG16 = 4096
    QH = 0

    def dma(queue, out_ap, in_ap, reads, writes, tag=""):
        return B.emit(queue, lambda e: e.dma_start(out_ap, in_ap), reads, writes,
                      dma=True, tag=tag)

    def mm(out_ap, lhsT_ap, rhs_ap, reads, writes, start, stop, tag=""):
        return B.emit("tensor",
                      lambda e: e.matmul(out_ap, lhsT_ap, rhs_ap, start=start, stop=stop),
                      reads, writes, tag=tag)

    def tr(out_ap, in_ap, p, reads, writes, tag=""):
        return B.emit("tensor",
                      lambda e: e.transpose(out_ap, in_ap, ident[0:p, 0:p]),
                      list(reads) + ["ident"], writes, tag=tag)

    def act(out_ap, in_ap, func, reads, writes, bias=0.0, scale=1.0, accum=None, tag=""):
        return B.emit("scalar",
                      lambda e: e.activation(out_ap, in_ap, func, bias=bias, scale=scale,
                                             accum_out=accum),
                      reads, writes, tag=tag)

    def vec(fn, reads, writes, tag=""):
        return B.emit("vector", fn, reads, writes, tag=tag)

    def gp(fn, reads, writes, tag=""):
        return B.emit("gpsimd", fn, reads, writes, tag=tag)

    gp(lambda e: e.memset(ones_row[:, :], 1.0), [], ["ones_row"])
    gp(lambda e: e.memset(ones_col[:, :], 1.0), [], ["ones_col"])
    gp(lambda e: e.memset(eps_col[:, :], EPS), [], ["eps_col"])
    gp(lambda e: e.memset(oncf[:, :], 1.0), [], ["oncf"])
    gp(lambda e: e.memset(masks[:, :], 1.0), ["mkM"], ["mkM"])
    gp(lambda e: e.affine_select(masks[:, :], masks[:, :], [[1, 896]], ALU.is_ge, 0.0,
                                 base=-384, channel_multiplier=-1), ["mkM"], ["masks"])
    gp(lambda e: e.memset(ident[:, :], 1.0), ["identM"], ["identM"])
    gp(lambda e: e.affine_select(ident[:, :], ident[:, :], [[1, 128]], ALU.is_equal, 0.0,
                                 base=0, channel_multiplier=-1), ["identM"], ["ident"])
    dma("gpsimd", wqd_sb[:, :], Wq_d.rearrange("(b p) l -> p b l", p=128), [], ["wqd_sb"])
    dma("gpsimd", wkvd_sb[:, :], Wkv_d.rearrange("(b p) l -> p b l", p=128), [], ["wkvd_sb"])
    dma("gpsimd", bqd_row[:, :], bq_d, [], ["bqd_row"])
    dma("gpsimd", bkvd_row[:, :], bkv_d, [], ["bkvd_row"])
    dma("gpsimd", wqk_sb[:, :], W_qk, [], ["wqk_sb"])
    dma("gpsimd", bqk_row[:, :], b_qk, [], ["bqk_row"])
    dma("sync", lw_sb[:, :], lora_w, [], ["lw_sb"])
    vec(lambda e: e.tensor_scalar_mul(
        lrec[0:1, 0:128], bass.AP(lw_sb, 0, [[8, 1], [1, 8], [0, 16]]), 2.0),
        ["lw_sb", "lrec"], ["coef_row"])
    dma("sync", cfH[:, :], lrec[0:1, 0:128], ["coef_row"], ["cfHk"])
    dma("sync", coef_col[:, :], cfH.rearrange("a (p o) -> p (a o)", p=128),
        ["cfHk"], ["coef_col"])
    dma("gpsimd", Bv(STGB, 4096), lora_A, bkeys(STGB, 4096),
        bkeys(STGB, 4096) + ["stgA"])
    for g4 in range(8):
        for j in range(4):
            db = g4 * 4 + j
            tr(psb[:, j * 128:(j + 1) * 128], Bv(STGB + db * 128, 128), 128,
               ["stgA"] + bkeys(STGB + db * 128, 128), ["psb"])
        B.emit("scalar",
               lambda e, g4=g4: e.copy(Bv(WSTG + g4 * 512, 512), psb[:, :]),
               ["psb"], bkeys(WSTG + g4 * 512, 512) + ["stgAc"])
    dma("sync", AcH[:, :], Bv(WSTG, 4096), ["stgAc"] + bkeys(WSTG, 4096), ["AcHk"])
    for n in range(8):
        dma("gpsimd", Bv(STGB, 512),
            lora_B[n * D:(n + 1) * D, :].rearrange("(bb p) r -> p bb r", p=128),
            ["stgA", "stgBn"] + bkeys(STGB, 512), bkeys(STGB, 512) + ["stgBn"])
        for b4 in range(8):
            for j in range(4):
                bb = b4 * 4 + j
                tr(psb[0:16, j * 128:(j + 1) * 128], Bv(STGB + bb * 16, 16), 128,
                   ["stgBn"] + bkeys(STGB, 512), ["psb"])
            B.emit("vector",
                   lambda e, b4=b4: e.tensor_copy(
                       BIGB[0:16, STG16 + b4 * 512: STG16 + (b4 + 1) * 512],
                       psb[0:16, 0:512]),
                   ["psb"] + bkeys(STG16, 4096), bkeys(STG16, 4096) + ["stg16"])
        dma("sync", BtH[n * 16:(n + 1) * 16, :], BIGB[0:16, STG16:STG16 + 4096],
            ["stg16"] + bkeys(STG16, 4096), [f"BtHk{n}"])
    bt_keys = [f"BtHk{n}" for n in range(8)]

    def layernorm_block(xin_ap, t, dst_base, dst_stride, dst_keys_fn, xin_keys, tag):
        sl = t % 2
        xb = Bv(XBF[sl], 4096)
        kx = bkeys(XBF[sl], 4096)
        dma("gpsimd", xb, xin_ap[t * 128:(t + 1) * 128, :], kx + xin_keys, kx,
            tag=f"{tag}x{t}")
        junk = Bv(SQS, 4096)
        kj = bkeys(SQS, 4096)
        act(junk, xb, AF.Copy, kx + kj, kj + [f"sum{sl}"],
            accum=stat[:, sl * 8:sl * 8 + 1], tag=f"{tag}cp{t}")
        act(junk, xb, AF.Square, kx + kj, kj + [f"sumsq{sl}"],
            accum=stat[:, sl * 8 + 1:sl * 8 + 2], tag=f"{tag}sq{t}")
        st = stat[:, sl * 8:sl * 8 + 8]
        vec(lambda e, st=st: e.tensor_scalar_mul(st[:, 2:3], st[:, 0:1], 1.0 / D),
            [f"sum{sl}"], [f"mean{sl}"])
        vec(lambda e, st=st: e.tensor_scalar_mul(st[:, 3:4], st[:, 1:2], 1.0 / D),
            [f"sumsq{sl}"], [f"msq{sl}"])
        vec(lambda e, st=st: e.tensor_mul(st[:, 4:5], st[:, 2:3], st[:, 2:3]),
            [f"mean{sl}"], [f"m2{sl}"])
        vec(lambda e, st=st: e.tensor_sub(st[:, 3:4], st[:, 3:4], st[:, 4:5]),
            [f"msq{sl}", f"m2{sl}"], [f"var{sl}"])
        act(st[:, 4:5], st[:, 3:4], AF.Ln, [f"var{sl}", f"m2{sl}", "eps_col"],
            [f"lnv{sl}"], bias=eps_col[:, 0:1])
        act(st[:, 5:6], st[:, 4:5], AF.Exp, [f"lnv{sl}"], [f"rstd0{sl}"], scale=-0.5)
        vec(lambda e, st=st: e.tensor_mul(st[:, 6:7], st[:, 5:6], st[:, 5:6]),
            [f"rstd0{sl}"], [f"r2{sl}"])
        vec(lambda e, st=st: e.tensor_scalar_add(st[:, 4:5], st[:, 3:4], EPS),
            [f"var{sl}", f"lnv{sl}"], [f"vpe{sl}"])
        vec(lambda e, st=st: e.tensor_mul(st[:, 6:7], st[:, 6:7], st[:, 4:5]),
            [f"r2{sl}", f"vpe{sl}"], [f"r2{sl}"])
        vec(lambda e, st=st: e.tensor_scalar(st[:, 6:7], st[:, 6:7], -0.5, 1.5,
                                             ALU.mult, ALU.add),
            [f"r2{sl}"], [f"r2{sl}"])
        vec(lambda e, st=st: e.tensor_mul(st[:, 5:6], st[:, 5:6], st[:, 6:7]),
            [f"rstd0{sl}", f"r2{sl}"], [f"rstd{sl}"])
        vec(lambda e, st=st: e.tensor_mul(st[:, 7:8], st[:, 2:3], st[:, 5:6]),
            [f"mean{sl}", f"rstd{sl}"], [f"mr{sl}"])
        vec(lambda e, st=st, sl=sl: e.tensor_scalar_mul(diag_bf[sl][:, :], ident[:, :],
                                                        st[:, 5:6]),
            ["ident", f"rstd{sl}"], [f"diag{sl}"])
        vec(lambda e, st=st, sl=sl: e.tensor_copy(bcol_bf[sl][:, :], st[:, 7:8]),
            [f"mr{sl}"], [f"bcol{sl}"])
        pbr = psb[0:1, sl * 128:sl * 128 + 128]
        tr(pbr, bcol_bf[sl][:, :], 128, [f"bcol{sl}"], ["psb"])
        act(brow_bf[sl][:, :], pbr, AF.Copy, ["psb"], [f"brow{sl}"], scale=-1.0)
        for g4 in range(8):
            bank = ps[g4 % 2]
            bkey = PK[g4 % 2]
            for j in range(4):
                db = g4 * 4 + j
                mm(bank[:, j * 128:(j + 1) * 128], xb[:, db * 128:(db + 1) * 128],
                   diag_bf[sl][:, :], kx + [f"diag{sl}"], [bkey], start=True, stop=False)
                mm(bank[:, j * 128:(j + 1) * 128], ones_row[0:1, 0:128],
                   brow_bf[sl][0:1, :], ["ones_row", f"brow{sl}"], [bkey],
                   start=False, stop=True)
            off = dst_base(g4 * 4) + 0
            dst_ap = bass.AP(BIGA, off, [[32 * AG, 128], [dst_stride, 4], [1, 128]])
            src_ap = bass.AP(bank, 0, [[512, 128], [128, 4], [1, 128]])
            vec(lambda e, dst_ap=dst_ap, src_ap=src_ap: e.tensor_copy(dst_ap, src_ap),
                [bkey], dst_keys_fn(g4), tag=f"{tag}ev{t}_{g4}")

    def wprep_steps():
        steps = []
        for i in range(NB):
            sl = i % 2

            def step(i=i, sl=sl):
                dma("gpsimd", Bv(WCOL[sl], 4096),
                    ffn_W[:, i * 128:(i + 1) * 128].rearrange("(b p) d -> p b d", p=128),
                    bkeys(WCOL[sl], 4096), bkeys(WCOL[sl], 4096), tag=f"wp{i}")
                for b4 in range(8):
                    for j in range(4):
                        b = b4 * 4 + j
                        tr(psb[:, j * 128:(j + 1) * 128], Bv(WCOL[sl] + b * 128, 128),
                           128, bkeys(WCOL[sl], 4096), ["psb"])
                    B.emit("scalar",
                           lambda e, b4=b4: e.copy(Bv(WSTG + b4 * 512, 512), psb[:, :]),
                           ["psb"], bkeys(WSTG + b4 * 512, 512) + ["wstg"])
                dma("sync", WT[i * 128:(i + 1) * 128, :], Bv(WSTG, 4096),
                    ["wstg"], [f"WTr{i}"])

            steps.append(step)
        return steps

    def phase_a():
        for t in range(NT):
            layernorm_block(
                x0, t,
                dst_base=lambda db, t=t: db * AG + t * 128,
                dst_stride=AG,
                dst_keys_fn=lambda g4: [f"A{g4 * 4 + j}" for j in range(4)],
                xin_keys=[], tag="a1")
        for h in range(H):
            qwk = bkeys(QW, 4096)
            kwk = bkeys(KW, 4096)
            vwk = bkeys(VW2, 4096)
            dma("gpsimd", Bv(QW, 4096),
                wq[:, h * 192: h * 192 + 128].rearrange("(b p) e -> p b e", p=128),
                qwk, qwk, tag=f"qw{h}")
            dma("gpsimd", Bv(KW, 4096),
                wkv[:, h * 128: (h + 1) * 128].rearrange("(b p) e -> p b e", p=128),
                kwk, kwk, tag=f"kw{h}")
            dma("gpsimd", Bv(VW2, 4096),
                wkv[:, 4096 + h * 128: 4096 + (h + 1) * 128].rearrange(
                    "(b p) e -> p b e", p=128),
                vwk, vwk, tag=f"vw{h}")
            vtk = bkeys(VT, 2048)
            vsk = bkeys(VS, 2048)
            for c in range(4):
                bv_ = ps[2 + c % 2]
                for db in range(NB):
                    mm(bv_[:, :], Bv(VW2 + db * 128, 128), A(db * AG + c * 512, 512),
                       vwk + [f"A{db}"], [PK[2 + c % 2]], start=(db == 0),
                       stop=(db == NB - 1))
                act(Bv(VT + c * 512, 512), bv_[:, :], AF.Copy,
                    [PK[2 + c % 2]] + vtk, vtk)
            for g4 in range(4):
                pb = psb if g4 % 2 == 0 else psb2
                pbk = "psb" if g4 % 2 == 0 else "psb2"
                for j in range(4):
                    kb = g4 * 4 + j
                    tr(pb[:, j * 128:(j + 1) * 128], Bv(VT + kb * 128, 128), 128,
                       vtk, [pbk])
                B.emit("scalar",
                       lambda e, g4=g4, pb=pb: e.copy(Bv(VS + g4 * 512, 512), pb[:, :]),
                       [pbk] + vsk, vsk)
            for c in range(4):
                bq, bk = ps[4], ps[5]
                for db in range(NB):
                    mm(bq[:, :], Bv(QW + db * 128, 128), A(db * AG + c * 512, 512),
                       qwk + [f"A{db}"], [PK[4]], start=(db == 0), stop=(db == NB - 1))
                for db in range(NB):
                    mm(bk[:, :], Bv(KW + db * 128, 128), A(db * AG + c * 512, 512),
                       kwk + [f"A{db}"], [PK[5]], start=(db == 0), stop=(db == NB - 1))
                act(Qt_sb[:, c * 512:(c + 1) * 512], bq[:, :], AF.Copy, [PK[4]], ["Qt"])
                act(Kt_sb[:, c * 512:(c + 1) * 512], bk[:, :], AF.Copy, [PK[5]], ["Kt"])
            deferred = [None]

            def fin_mms(c, ub):
                bB = ps[4]
                mm(bB[:, :], ones_row[0:1, 0:128], lhi[0:1, :], ["ones_row", "lhi"],
                   [PK[4]], start=True, stop=False)
                mm(bB[:, :], ones_row[0:1, 0:128], llo[0:1, :], ["ones_row", "llo"],
                   [PK[4]], start=False, stop=True)
                vec(lambda e: e.tensor_copy(bcast_sb[:, :], ps[4][:, :]), [PK[4]],
                    ["bcast"])
                vec(lambda e, c=c, ub=ub: e.tensor_mul(Uh_sb[:, c * 512:(c + 1) * 512],
                                                       ps[ub][:, :], bcast_sb[:, :]),
                    [PK[ub], "bcast"], ["Uh"], tag=f"Un{h}_{c}")

            for c in range(4):
                nkb = 4 * c + 4
                ub = 2 + c % 2
                for kb in range(nkb):
                    bS = ps[kb % 2]
                    skey = PK[kb % 2]
                    ekey = f"expS{kb % 4}"
                    mm(bS[:, :], Kt_sb[:, kb * 128:(kb + 1) * 128],
                       Qt_sb[:, c * 512:(c + 1) * 512], ["Kt", "Qt"], [skey],
                       start=True, stop=True, tag=f"sc{h}_{c}_{kb}")
                    act(expS[kb % 4][:, :], bS[:, :], AF.Exp, [skey], [ekey],
                        scale=SCALE_A, tag=f"exp{h}_{c}_{kb}")
                    if kb >= 4 * c:
                        m = kb - 4 * c
                        vec(lambda e, kb=kb, m=m: e.tensor_mul(
                            expS[kb % 4][:, :], expS[kb % 4][:, :],
                            masks[:, 384 - 128 * m: 896 - 128 * m]),
                            [ekey, "masks"], [ekey], tag=f"msk{h}_{c}_{kb}")
                    mm(ps[ub][:, :], Bv(VS + kb * 128, 128), expS[kb % 4][:, :],
                       vsk + [ekey], [PK[ub]], start=(kb == 0), stop=(kb == nkb - 1))
                    if kb == 0:
                        vec(lambda e, kb=kb: e.tensor_copy(lacc[:, :], expS[kb % 4][:, :]),
                            [ekey, "lacc"], ["lacc"])
                    else:
                        vec(lambda e, kb=kb: e.tensor_add(lacc[:, :], lacc[:, :],
                                                          expS[kb % 4][:, :]),
                            [ekey, "lacc"], ["lacc"])
                    if kb == 5 and deferred[0] is not None:
                        deferred[0]()
                        deferred[0] = None
                mm(ps[5][0:1, :], oncf[:, :], lacc[:, :], ["oncf", "lacc"], [PK[5]],
                   start=True, stop=True)
                if deferred[0] is not None:
                    deferred[0]()
                    deferred[0] = None
                act(lrow[:, :], ps[5][0:1, :], AF.Ln, [PK[5], "lrow"], ["lrow"])
                act(lrec[:, :], lrow[:, :], AF.Exp, ["lrow", "coef_row"], ["lrec"],
                    scale=-1.0)
                act(lhi[:, :], lrec[:, :], AF.Copy, ["lrec", "lhi"], ["lhi"])
                vec(lambda e: e.tensor_sub(lrow[:, :], lrec[:, :], lhi[:, :]),
                    ["lrec", "lhi", "lrow"], ["llo_f"])
                vec(lambda e: e.tensor_copy(llo[:, :], lrow[:, :]), ["llo_f"], ["llo"])
                deferred[0] = (lambda c=c, ub=ub: fin_mms(c, ub))
            if deferred[0] is not None:
                deferred[0]()
                deferred[0] = None
            dma("sync", Ua[h * 128:(h + 1) * 128, :], Uh_sb[:, :], ["Uh"], [f"UaH{h}"],
                tag=f"spill{h}")
        ua_ready = [f"UaH{h}" for h in range(H)]
        wsteps = wprep_steps()
        wi = 0
        outk = []
        for Q in range(4):
            for e_blk in range(NB):
                wkey = akeys(e_blk * 1024, 1024)
                dma("gpsimd", A(e_blk * 1024, 1024),
                    wo_a[e_blk * 128:(e_blk + 1) * 128, Q * 1024:(Q + 1) * 1024],
                    wkey, wkey, tag=f"wo{Q}_{e_blk}")
            for tp in range(NT // 2):
                if wi < len(wsteps):
                    wsteps[wi]()
                    wi += 1
                if wi < len(wsteps):
                    wsteps[wi]()
                    wi += 1
                usl = tp % 2
                ubase = 32768 + usl * 8192
                uk = akeys(ubase, 8192)
                dma("sync", A(ubase, 8192),
                    Ua[:, tp * 256:(tp + 1) * 256].rearrange("(eb p) q -> p eb q", p=128),
                    ua_ready + uk, uk, tag=f"uat{Q}_{tp}")
                for tj in range(2):
                    t = tp * 2 + tj
                    for oc in range(2):
                        rsl = oc
                        dma("sync", rbuf[rsl][:, :],
                            x0[t * 128:(t + 1) * 128,
                               Q * 1024 + oc * 512: Q * 1024 + (oc + 1) * 512],
                            [f"rbuf{rsl}"], [f"rbuf{rsl}"])
                        bank = ps[oc]
                        for e_blk in range(NB):
                            mm(bank[:, :], A(ubase + e_blk * 256 + tj * 128, 128),
                               A(e_blk * 1024 + oc * 512, 512),
                               uk + akeys(e_blk * 1024 + oc * 512, 512), [PK[oc]],
                               start=(e_blk == 0), stop=(e_blk == NB - 1))
                        vec(lambda e, bank=bank, rsl=rsl: e.tensor_add(
                            nbuf[rsl][:, :], bank[:, :], rbuf[rsl][:, :]),
                            [PK[oc], f"rbuf{rsl}"], [f"nbuf{rsl}"], tag=f"woev{Q}_{t}_{oc}")
                        o8 = Q * 2 + oc
                        outk.append(f"x1r{t}_{o8}")
                        dma("sync", x1[t * 128:(t + 1) * 128, o8 * 512:(o8 + 1) * 512],
                            nbuf[rsl][:, :], [f"nbuf{rsl}"], [f"x1r{t}_{o8}"],
                            tag=f"x1st{t}_{o8}")
        while wi < len(wsteps):
            wsteps[wi]()
            wi += 1
        return outk

    def phase_f(xin_ap, xout_ap, xname, xin_name, inst):
        ack = bkeys(ACT_, 4096)
        dma("sync", Bv(ACT_, 4096), AcH[:, :], ["AcHk"] + ack, ack, tag=f"f{inst}Ac")
        dma("gpsimd", row_big[:, :], ffn_bias, ["row_big"], ["row_big"], tag=f"f{inst}fb")
        wt_keys = [f"WTr{i}" for i in range(NB)]
        outk = []
        for ch in range(2):
            for tt in range(8):
                t = ch * 8 + tt
                layernorm_block(
                    xin_ap, t,
                    dst_base=lambda db, tt=tt: db * 1024 + tt * 128,
                    dst_stride=1024,
                    dst_keys_fn=lambda g4: [f"A{(g4 * 4) // 2}", f"A{(g4 * 4 + 3) // 2}"],
                    xin_keys=[f"{xin_name}r{t}_{o}" for o in range(8)],
                    tag=f"f{inst}ln")
            for half in range(2):
                bank = ps[4 + half]
                for db in range(NB):
                    mm(bank[:, :], Bv(ACT_ + db * 128, 128),
                       A(db * 1024 + half * 512, 512),
                       ack + [f"A{(db * 1024 + half * 512) // AG}"], [PK[4 + half]],
                       start=(db == 0), stop=(db == NB - 1))
                act(hbf[:, half * 512:(half + 1) * 512], bank[:, :], AF.Copy,
                    [PK[4 + half], "coef_col"], ["hbf"], scale=coef_col[:, 0:1])
            for oc in range(8):
                wsl = oc % 2
                wck = akeys(32768 + wsl * 16384, 16384)
                dma("sync", A(32768 + wsl * 16384, 16384),
                    WT[:, oc * 512:(oc + 1) * 512].rearrange("(b p) o -> p b o", p=128),
                    wt_keys + wck, wck, tag=f"f{inst}wt{ch}_{oc}")
                dma("gpsimd", bct_sb[:, 0:512], BtH[:, oc * 512:(oc + 1) * 512],
                    bt_keys + ["Kt"], ["Kt"], tag=f"f{inst}bt{ch}_{oc}")
                for tt in range(8):
                    t = ch * 8 + tt
                    bank = ps[tt % 2]
                    for db in range(NB):
                        mm(bank[:, :], A(db * 1024 + tt * 128, 128),
                           A(32768 + wsl * 16384 + db * 512, 512),
                           [f"A{(db * 1024 + tt * 128) // AG}",
                            f"A{(32768 + wsl * 16384 + db * 512) // AG}"],
                           [PK[tt % 2]], start=(db == 0), stop=False)
                    mm(bank[:, :], hbf[:, tt * 128:(tt + 1) * 128], bct_sb[:, 0:512],
                       ["hbf", "Kt"], [PK[tt % 2]], start=False, stop=False)
                    mm(bank[:, :], ones_row[0:1, 0:128],
                       row_big[0:1, oc * 512:(oc + 1) * 512],
                       ["ones_row", "row_big"], [PK[tt % 2]], start=False, stop=True)
                    rsl = tt % 2
                    dma("scalar", rbuf[rsl][:, :],
                        xin_ap[t * 128:(t + 1) * 128, oc * 512:(oc + 1) * 512],
                        [f"{xin_name}r{t}_{oc}", f"rbuf{rsl}"], [f"rbuf{rsl}"])
                    vec(lambda e, bank=bank, rsl=rsl: e.tensor_add(
                        nbuf[rsl][:, :], bank[:, :], rbuf[rsl][:, :]),
                        [PK[tt % 2], f"rbuf{rsl}"], [f"nbuf{rsl}"])
                    k = f"{xname}r{t}_{oc}"
                    outk.append(k)
                    dma("scalar", xout_ap[t * 128:(t + 1) * 128, oc * 512:(oc + 1) * 512],
                        nbuf[rsl][:, :], [f"nbuf{rsl}"], [k])
        return outk

    def phase_b(xin_ap, xout_ap, xin_name):
        for ch in range(4):
            for tt in range(4):
                t = ch * 4 + tt
                layernorm_block(
                    xin_ap, t,
                    dst_base=lambda db, tt=tt: db * 512 + tt * 128,
                    dst_stride=512,
                    dst_keys_fn=lambda g4: [f"A{g4}"],
                    xin_keys=[f"{xin_name}r{t}_{o}" for o in range(8)],
                    tag="bln")
            bq, bkv = ps[2], ps[3]
            for db in range(NB):
                mm(bq[0:QL, :], wqd_sb[:, db * QL:(db + 1) * QL], A(db * 512, 512),
                   ["wqd_sb", f"A{db // 4}"], [PK[2]], start=(db == 0), stop=False)
            mm(bq[0:QL, :], bqd_row[0:1, :], ones_row[0:1, :], ["bqd_row", "ones_row"],
               [PK[2]], start=False, stop=True)
            for db in range(NB):
                mm(bkv[0:KV, :], wkvd_sb[:, db * KV:(db + 1) * KV], A(db * 512, 512),
                   ["wkvd_sb", f"A{db // 4}"], [PK[3]], start=(db == 0), stop=False)
            mm(bkv[0:KV, :], bkvd_row[0:1, :], ones_row[0:1, :], ["bkvd_row", "ones_row"],
               [PK[3]], start=False, stop=True)
            act(Cq_sb[0:QL, ch * 512:(ch + 1) * 512], bq[0:QL, :], AF.Copy, [PK[2]], ["Uh"])
            act(Ckv_sb[0:KV, ch * 512:(ch + 1) * 512], bkv[0:KV, :], AF.Copy, [PK[3]], ["Kt"])
        vec(lambda e: e.memset(Ckvs_sb[:, :], 0.0), ["Ckvs"], ["CkvsM"])
        for kb in range(NT):
            pkc = psb[0:128, (kb % 4) * 128:(kb % 4) * 128 + 4]
            tr(pkc, Ckv_sb[0:KV, kb * 128:(kb + 1) * 128], KV, ["Kt"], ["psb"])
            vec(lambda e, pkc=pkc, kb=kb: e.tensor_copy(
                Ckvs_sb[:, kb * 40: kb * 40 + 4], pkc),
                ["psb", "CkvsM"], ["CkvsM"])
        vec(lambda e: e.memset(bass.AP(Ckvs_sb, 32, [[NT * 40, 128], [40, NT], [1, 1]]),
                               1.0),
            ["CkvsM"], ["Ckvs"])
        dma("gpsimd", row_big[:, :], bv_u, ["row_big"], ["row_big"], tag="b_bvu")
        for h in range(H):
            sl = h % 2
            dma("gpsimd", wvu_h[:, :], Wv_u[:, h * 128:(h + 1) * 128], ["wvu_h"],
                ["wvu_h"], tag=f"wvu{h}")
            qhk = bkeys(QH, 2048)
            for c in range(4):
                bqh = ps[4]
                mm(bqh[0:KV, :], wqk_sb[:, h * KV:(h + 1) * KV],
                   Cq_sb[0:QL, c * 512:(c + 1) * 512], ["wqk_sb", "Uh"], [PK[4]],
                   start=True, stop=False)
                mm(bqh[0:KV, :], bqk_row[0:1, h * KV:(h + 1) * KV], ones_row[0:1, :],
                   ["bqk_row", "ones_row"], [PK[4]], start=False, stop=True)
                act(BIGB[0:KV, QH + c * 512: QH + (c + 1) * 512], bqh[0:KV, :], AF.Copy,
                    [PK[4]] + qhk, qhk)
            def b_score(kb, c):
                bS = ps[c % 2]
                skey = PK[c % 2]
                ekey = f"expS{c}"
                mm(bS[:, :], Ckv_sb[0:KV, kb * 128:(kb + 1) * 128],
                   BIGB[0:KV, QH + c * 512: QH + (c + 1) * 512], ["Kt"] + qhk,
                   [skey], start=True, stop=True, tag=f"bsc{h}_{c}_{kb}")
                act(expS[c][:, :], bS[:, :], AF.Exp, [skey], [ekey], scale=SCALE_B)

            def b_u(kb, c):
                mm(ps[2 + c][0:33, :], Ckvs_sb[:, kb * 40: kb * 40 + 33],
                   expS[c][:, :],
                   ["Ckvs", f"expS{c}"], [PK[2 + c]], start=(kb == 0),
                   stop=(kb == NT - 1))

            def b_chain(c):
                bU = ps[2 + c]
                act(lrow[:, :], bU[32:33, :], AF.Ln, [PK[2 + c], "lrow"], ["lrow"])
                act(lrec[:, :], lrow[:, :], AF.Exp, ["lrow", "coef_row"], ["lrec"],
                    scale=-1.0)
                act(lhi[:, :], lrec[:, :], AF.Copy, ["lrec", "lhi"], ["lhi"])
                vec(lambda e: e.tensor_sub(lrow[:, :], lrec[:, :], lhi[:, :]),
                    ["lrec", "lhi", "lrow"], ["llo_f"])
                vec(lambda e: e.tensor_copy(llo[:, :], lrow[:, :]), ["llo_f"], ["llo"])

            def b_final(c):
                bU = ps[2 + c]
                vec(lambda e, bU=bU: e.tensor_copy(u5_sb[0:4, :], bU[0:4, :]),
                    [PK[2 + c], "u5"], ["u5"])
                bB = ps[c % 2]
                mm(bB[:, :], ones_row[0:1, 0:128], lhi[0:1, :], ["ones_row", "lhi"],
                   [PK[c % 2]], start=True, stop=False)
                mm(bB[:, :], ones_row[0:1, 0:128], llo[0:1, :], ["ones_row", "llo"],
                   [PK[c % 2]], start=False, stop=True)
                vec(lambda e, bB=bB: e.tensor_copy(bcast_sb[:, :], bB[:, :]),
                    [PK[c % 2], "bcast"], ["bcast"])
                vec(lambda e: e.tensor_mul(ctx_bf[0:KV, 0:512], u5_sb[0:KV, :],
                                           bcast_sb[0:KV, :]),
                    ["u5", "bcast"], ["Qt"])
                bT = ps[(c + 1) % 2]
                mm(bT[:, :], wvu_h[0:KV, :], ctx_bf[0:KV, 0:512],
                   ["wvu_h", "Qt"], [PK[(c + 1) % 2]], start=True, stop=False)
                mm(bT[:, :], row_big[0:1, h * 128:(h + 1) * 128], ones_row[0:1, :],
                   ["row_big", "ones_row"], [PK[(c + 1) % 2]], start=False, stop=True)
                ahk = bkeys(AH[sl], 2048)
                act(Bv(AH[sl] + c * 512, 512), bT[:, :], AF.Copy,
                    [PK[(c + 1) % 2]] + ahk, ahk)

            for kb in range(NT):
                last = kb == NT - 1
                b_score(kb, 0)
                b_score(kb, 1)
                b_u(kb, 0)
                if last:
                    b_chain(0)
                b_score(kb, 2)
                b_u(kb, 1)
                b_score(kb, 3)
                b_u(kb, 2)
                b_u(kb, 3)
            for c in range(4):
                b_final(c)
                if c < 3:
                    b_chain(c + 1)
            dma("sync", Abh[h * 128:(h + 1) * 128, :], Bv(AH[sl], 2048),
                bkeys(AH[sl], 2048), [f"AbH{h}"])
        ab_ready = [f"AbH{h}" for h in range(H)]
        dma("gpsimd", row_big[:, :], bo_b, ["row_big"], ["row_big"], tag="b_bo")
        outk = []
        for Q in range(4):
            for e_blk in range(NB):
                wkey = akeys(e_blk * 1024, 1024)
                dma("gpsimd", A(e_blk * 1024, 1024),
                    Wo_b[e_blk * 128:(e_blk + 1) * 128, Q * 1024:(Q + 1) * 1024],
                    wkey, wkey, tag=f"bwo{Q}_{e_blk}")
            for tp in range(NT // 2):
                usl = tp % 2
                ubase = 32768 + usl * 8192
                uk = akeys(ubase, 8192)
                dma("sync", A(ubase, 8192),
                    Abh[:, tp * 256:(tp + 1) * 256].rearrange("(eb p) q -> p eb q", p=128),
                    ab_ready + uk, uk)
                for tj in range(2):
                    t = tp * 2 + tj
                    for oc in range(2):
                        rsl = oc
                        o8 = Q * 2 + oc
                        dma("sync", rbuf[rsl][:, :],
                            xin_ap[t * 128:(t + 1) * 128, o8 * 512:(o8 + 1) * 512],
                            [f"{xin_name}r{t}_{o8}", f"rbuf{rsl}"], [f"rbuf{rsl}"])
                        bank = ps[oc]
                        for e_blk in range(NB):
                            mm(bank[:, :], A(ubase + e_blk * 256 + tj * 128, 128),
                               A(e_blk * 1024 + oc * 512, 512),
                               uk + akeys(e_blk * 1024 + oc * 512, 512), [PK[oc]],
                               start=(e_blk == 0), stop=False)
                        mm(bank[:, :], ones_row[0:1, 0:128],
                           row_big[0:1, o8 * 512:(o8 + 1) * 512],
                           ["ones_row", "row_big"], [PK[oc]], start=False, stop=True)
                        vec(lambda e, bank=bank, rsl=rsl: e.tensor_add(
                            nbuf[rsl][:, :], bank[:, :], rbuf[rsl][:, :]),
                            [PK[oc], f"rbuf{rsl}"], [f"nbuf{rsl}"])
                        outk.append(f"x3r{t}_{o8}")
                        dma("sync", xout_ap[t * 128:(t + 1) * 128, o8 * 512:(o8 + 1) * 512],
                            nbuf[rsl][:, :], [f"nbuf{rsl}"], [f"x3r{t}_{o8}"])
        return outk

    if upto == "b0":
        keys = phase_a()
        k2 = phase_f(x1, x2, "x2", "x1", 1)
        k3 = phase_b(x2, x3, "x2")
        final = []
        final = list(k3)
        for i in range(32):
            sl = i % 2
            kk = bkeys(UAT[sl], 4096)
            dma("sync", Bv(UAT[sl], 2048), Abh[i * 128:(i + 1) * 128, :],
                k3 + kk, kk)
            dma("sync", dbg[:, i * 2048:(i + 1) * 2048], Bv(UAT[sl], 2048), kk,
                [f"dg{i}"])
            final.append(f"dg{i}")
        dma("sync", y[0:128, 0:512], rbuf[0][:, :], ["rbuf0"], ["yk"])
        final.append("yk")
    elif upto == "wt":
        wsteps = wprep_steps()
        for st_ in wsteps:
            st_()
        final = []
        for i in range(8):
            sl = i % 2
            kk = bkeys(UAT[sl], 4096)
            dma("sync", Bv(UAT[sl], 4096), WT[i * 128:(i + 1) * 128, :],
                [f"WTr{i}"] + kk, kk)
            dma("sync", dbg[:, i * 4096:(i + 1) * 4096], Bv(UAT[sl], 4096), kk,
                [f"dw{i}"])
            final.append(f"dw{i}")
        ack = bkeys(ACT_, 4096)
        dma("sync", Bv(ACT_, 4096), AcH[:, :], ["AcHk"] + ack, ack)
        dma("sync", dbg[:, 8 * 4096: 8 * 4096 + 4096], Bv(ACT_, 4096), ack, ["dA"])
        final.append("dA")
        bck = bkeys(XBF[0], 4096)
        dma("sync", Bv(XBF[0], 4096), BtH[:, :], bt_keys + bck, bck)
        dma("sync", dbg[:, 9 * 4096: 9 * 4096 + 4096], Bv(XBF[0], 4096), bck, ["dB"])
        final.append("dB")
        vec(lambda e: e.tensor_copy(expS[0][:, 0:1], coef_col[:, 0:1]),
            ["coef_col", "expS0"], ["expS0"])
        vec(lambda e: e.memset(expS[0][:, 1:512], 0.0), ["expS0"], ["expS0"])
        dma("sync", dbg[:, 10 * 4096: 10 * 4096 + 512], expS[0][:, 0:512], ["expS0"], ["dC"])
        final.append("dC")
        dma("sync", y[0:128, 0:512], rbuf[0][:, :], ["rbuf0"], ["yk"])
        final.append("yk")
    elif upto == "ua":
        keys = phase_a()
        final = []
        for i in range(32):
            sl = i % 2
            kk = bkeys(UAT[sl], 4096)
            dma("sync", Bv(UAT[sl], 2048), Ua[i * 128:(i + 1) * 128, :],
                keys + kk, kk)
            dma("sync", dbg[:, i * 2048:(i + 1) * 2048], Bv(UAT[sl], 2048), kk,
                [f"dg{i}"])
            final.append(f"dg{i}")
        dma("sync", y[0:128, 0:512], rbuf[0][:, :], ["rbuf0"], ["yk"])
        final.append("yk")
    elif upto == "h0":
        for t in range(NT):
            layernorm_block(
                x0, t,
                dst_base=lambda db, t=t: db * AG + t * 128,
                dst_stride=AG,
                dst_keys_fn=lambda g4: [f"A{g4 * 4 + j}" for j in range(4)],
                xin_keys=[], tag="a1")
        h = 0
        qwk = bkeys(QW, 4096)
        kwk = bkeys(KW, 4096)
        vwk = bkeys(VW2, 4096)
        dma("gpsimd", Bv(QW, 4096),
            wq[:, h * 192: h * 192 + 128].rearrange("(b p) e -> p b e", p=128),
            qwk, qwk)
        dma("gpsimd", Bv(KW, 4096),
            wkv[:, h * 128: (h + 1) * 128].rearrange("(b p) e -> p b e", p=128),
            kwk, kwk)
        dma("gpsimd", Bv(VW2, 4096),
            wkv[:, 4096 + h * 128: 4096 + (h + 1) * 128].rearrange(
                "(b p) e -> p b e", p=128),
            vwk, vwk)
        vtk = bkeys(VT, 2048)
        vsk = bkeys(VS, 2048)
        for c in range(4):
            bv_ = ps[2 + c % 2]
            for db in range(NB):
                mm(bv_[:, :], Bv(VW2 + db * 128, 128), A(db * AG + c * 512, 512),
                   vwk + [f"A{db}"], [PK[2 + c % 2]], start=(db == 0),
                   stop=(db == NB - 1))
            act(Bv(VT + c * 512, 512), bv_[:, :], AF.Copy, [PK[2 + c % 2]] + vtk, vtk)
        for g4 in range(4):
            for j in range(4):
                kb = g4 * 4 + j
                tr(psb[:, j * 128:(j + 1) * 128], Bv(VT + kb * 128, 128), 128,
                   vtk, ["psb"])
            B.emit("scalar",
                   lambda e, g4=g4: e.copy(Bv(VS + g4 * 512, 512), psb[:, :]),
                   ["psb"] + vsk, vsk)
        for c in range(4):
            bq, bk = ps[4], ps[5]
            for db in range(NB):
                mm(bq[:, :], Bv(QW + db * 128, 128), A(db * AG + c * 512, 512),
                   qwk + [f"A{db}"], [PK[4]], start=(db == 0), stop=(db == NB - 1))
            for db in range(NB):
                mm(bk[:, :], Bv(KW + db * 128, 128), A(db * AG + c * 512, 512),
                   kwk + [f"A{db}"], [PK[5]], start=(db == 0), stop=(db == NB - 1))
            act(Qt_sb[:, c * 512:(c + 1) * 512], bq[:, :], AF.Copy, [PK[4]], ["Qt"])
            act(Kt_sb[:, c * 512:(c + 1) * 512], bk[:, :], AF.Copy, [PK[5]], ["Kt"])
        for c in range(4):
            nkb = 4 * c + 4
            bU, bL = ps[2], ps[3]
            for kb in range(nkb):
                bS = ps[kb % 2]
                skey = PK[kb % 2]
                ekey = f"expS{kb % 2}"
                mm(bS[:, :], Kt_sb[:, kb * 128:(kb + 1) * 128],
                   Qt_sb[:, c * 512:(c + 1) * 512], ["Kt", "Qt"], [skey],
                   start=True, stop=True)
                act(expS[kb % 2][:, :], bS[:, :], AF.Exp, [skey], [ekey],
                    scale=SCALE_A)
                if kb >= 4 * c:
                    m = kb - 4 * c
                    gp(lambda e, kb=kb, m=m: e.affine_select(
                        expS[kb % 2][:, :], expS[kb % 2][:, :], [[1, 512]],
                        ALU.is_ge, 0.0, base=-128 * m, channel_multiplier=-1),
                        [ekey], [ekey])
                mm(bU[:, :], Bv(VS + kb * 128, 128), expS[kb % 2][:, :],
                   vsk + [ekey], [PK[2]], start=(kb == 0), stop=(kb == nkb - 1))
                mm(bL[0:1, :], ones_col[:, :], expS[kb % 2][:, :],
                   ["ones_col", ekey], [PK[3]],
                   start=(kb == 0), stop=(kb == nkb - 1))
            vec(lambda e: e.tensor_copy(lrow[:, :], ps[3][0:1, :]), [PK[3]], ["lrow"])
            vec(lambda e: e.reciprocal(lrec[:, :], lrow[:, :]), ["lrow", "coef_row"],
                ["lrec"])
            vec(lambda e: e.tensor_copy(lhi[:, :], lrec[:, :]), ["lrec"], ["lhi"])
            vec(lambda e: e.tensor_sub(lrow[:, :], lrec[:, :], lhi[:, :]),
                ["lrec", "lhi", "lrow"], ["llo_f"])
            vec(lambda e: e.tensor_copy(llo[:, :], lrow[:, :]), ["llo_f"], ["llo"])
            bB = ps[4]
            mm(bB[:, :], ones_row[0:1, 0:128], lhi[0:1, :], ["ones_row", "lhi"],
               [PK[4]], start=True, stop=False)
            mm(bB[:, :], ones_row[0:1, 0:128], llo[0:1, :], ["ones_row", "llo"],
               [PK[4]], start=False, stop=True)
            vec(lambda e: e.tensor_copy(bcast_sb[:, :], ps[4][:, :]), [PK[4]],
                ["bcast"])
            vec(lambda e, c=c: e.tensor_mul(Uh_sb[:, c * 512:(c + 1) * 512],
                                            ps[2][:, :], bcast_sb[:, :]),
                [PK[2], "bcast"], ["Uh"])
        dma("sync", dbg[:, 0:2048], Qt_sb[:, :], ["Qt"], ["d0"])
        dma("sync", dbg[:, 2048:4096], Kt_sb[:, :], ["Kt"], ["d1"])
        dma("sync", dbg[:, 4096:6144], Bv(VS, 2048), vsk, ["d2"])
        dma("sync", dbg[:, 6144:8192], Uh_sb[:, :], ["Uh"], ["d3"])
        dma("sync", y[0:128, 0:512], rbuf[0][:, :], ["rbuf0"], ["yk"])
        final = ["d0", "d1", "d2", "d3", "yk"]
    elif upto == "ln":
        for t in range(NT):
            layernorm_block(
                x0, t,
                dst_base=lambda db, t=t: db * AG + t * 128,
                dst_stride=AG,
                dst_keys_fn=lambda g4: [f"A{g4 * 4 + j}" for j in range(4)],
                xin_keys=[], tag="a1")
        dma("sync", dbg, BIGA[:, :], [f"A{g}" for g in range(32)], ["dbgk"])
        dma("sync", y[0:128, 0:512], rbuf[0][:, :], ["rbuf0"], ["yk"])
        final = ["dbgk", "yk"]
    elif upto == "a":
        keys = phase_a()
        for t in range(NT):
            for o in range(8):
                sl = o % 2
                dma("sync", rbuf[sl][:, :], x1[t * 128:(t + 1) * 128, o * 512:(o + 1) * 512],
                    keys + [f"rbuf{sl}"], [f"rbuf{sl}"])
                dma("sync", y[t * 128:(t + 1) * 128, o * 512:(o + 1) * 512], rbuf[sl][:, :],
                    [f"rbuf{sl}"], [f"yk{t}_{o}"])
        final = [f"yk{t}_{o}" for t in range(NT) for o in range(8)]
    elif upto == "f1":
        keys = phase_a()
        final = phase_f(x1, y, "y", "x1", 1)
    elif upto == "b":
        keys = phase_a()
        k2 = phase_f(x1, x2, "x2", "x1", 1)
        final = phase_b(x2, y, "x2")
    else:
        keys = phase_a()
        k2 = phase_f(x1, x2, "x2", "x1", 1)
        k3 = phase_b(x2, x3, "x2")
        final = phase_f(x3, y, "y", "x3", 2)
    B.emit("sync", lambda e: e.nop(), reads=final, writes=[])
    B.finalize_and_emit(stack)
    stack.close()
    return nc


_NC_CACHE = {}


def _get_nc(upto="full"):
    if upto not in _NC_CACHE:
        _NC_CACHE[upto] = build_program(upto)
    return _NC_CACHE[upto]


def _in_maps(inputs):
    c = np.ascontiguousarray
    hs = inputs["L0_attn_a_hidden_states"]
    base = {
        "wq": c(inputs["L0_attn_a_wq"]),
        "wkv": c(inputs["L0_attn_a_wkv"]),
        "wo_a": c(inputs["L0_attn_a_wo"]),
        "Wq_d": c(inputs["L0_attn_b_Wq_d"]),
        "bq_d": c(inputs["L0_attn_b_bq_d"]).reshape(1, QL),
        "W_qk": c(inputs["L0_attn_b_W_qk"]),
        "b_qk": c(inputs["L0_attn_b_b_qk"]).reshape(1, 128),
        "Wkv_d": c(inputs["L0_attn_b_Wkv_d"]),
        "bkv_d": c(inputs["L0_attn_b_bkv_d"]).reshape(1, KV),
        "Wv_u": c(inputs["L0_attn_b_Wv_u"]),
        "bv_u": c(inputs["L0_attn_b_bv_u"]).reshape(1, D),
        "Wo_b": c(inputs["L0_attn_b_Wo"]),
        "bo_b": c(inputs["L0_attn_b_bo"]).reshape(1, D),
        "lora_A": c(inputs["L0_ffn_lora_A"]).reshape(128, D),
        "lora_B": c(inputs["L0_ffn_lora_B"]).reshape(8 * D, 16),
        "ffn_W": c(inputs["L0_ffn_W"]),
        "ffn_bias": c(inputs["L0_ffn_bias"]).reshape(1, D),
        "lora_w": c(inputs["L0_ffn_lora_weights"]).reshape(1, 8),
    }
    maps = []
    for b in range(8):
        m = dict(base)
        m["x0"] = c(np.asarray(hs[b]))
        maps.append(m)
    return maps


def kernel(**inputs):
    nc = _get_nc("full")
    maps = _in_maps(inputs)
    res = run_bass_kernel_spmd(nc, maps, core_ids=list(range(8)))
    out = np.stack([np.asarray(r["y"]).astype(np.float32) for r in res.results], axis=0)
    return out


# version 07
# speedup vs baseline: 1.0616x; 1.0229x over previous
import numpy as np
import ml_dtypes
import concourse.bass as bass
import concourse.mybir as mybir
from concourse.bass_utils import run_bass_kernel_spmd

F32 = mybir.dt.float32
BF16 = mybir.dt.bfloat16
AF = mybir.ActivationFunctionType
ALU = mybir.AluOpType

S = 2048
D = 4096
NB = D // 128
NT = S // 128
H = 32
SCALE_A = float(192.0 ** -0.5)
QL, KV = 12, 4
SCALE_B = 0.5
EPS = 1e-5
AG = 2048


class Op:
    __slots__ = ("engine", "fn", "dma", "deps", "needed", "rank", "waits",
                 "inc_sem", "inc_amt", "tag")

    def __init__(self, engine, fn, dma, tag):
        self.engine = engine
        self.fn = fn
        self.dma = dma
        self.deps = []
        self.needed = False
        self.rank = 0
        self.waits = []
        self.inc_sem = None
        self.inc_amt = 0
        self.tag = tag


class Builder:
    ENGINES = ("tensor", "vector", "scalar", "gpsimd", "sync")

    def __init__(self, nc):
        self.nc = nc
        self.ops = {e: [] for e in self.ENGINES}
        self.last_write = {}
        self.readers = {}

    def emit(self, engine, fn, reads=(), writes=(), dma=False, tag=""):
        op = Op(engine, fn, dma, tag)
        deps = {}
        for k in reads:
            w = self.last_write.get(k)
            if w is not None:
                deps[id(w)] = w
        for k in writes:
            w = self.last_write.get(k)
            if w is not None:
                deps[id(w)] = w
            for r in self.readers.get(k, ()):
                deps[id(r)] = r
        for d in deps.values():
            if d is op:
                continue
            if d.engine == "tensor" and engine == "tensor" and not d.dma and not dma:
                continue
            op.deps.append(d)
        for k in reads:
            self.readers.setdefault(k, []).append(op)
        for k in writes:
            self.last_write[k] = op
            self.readers[k] = []
        self.ops[engine].append(op)
        return op

    LANES = 8

    def finalize_and_emit(self, stack):
        nc = self.nc
        csems = {e: stack.enter_context(nc.semaphore(f"s_{e}"))
                 for e in ("tensor", "vector", "scalar", "sync")}
        lsems = {}
        for q in ("sync", "gpsimd", "scalar"):
            lsems[("d", q)] = [stack.enter_context(nc.semaphore(f"d_{q}{i}"))
                               for i in range(self.LANES)]
        lsems[("c", "gpsimd")] = [stack.enter_context(nc.semaphore(f"g_{i}"))
                                  for i in range(self.LANES)]
        for e in self.ENGINES:
            for op in self.ops[e]:
                for d in op.deps:
                    d.needed = True
        counters = {}
        for e in self.ENGINES:
            for op in self.ops[e]:
                if op.dma:
                    key = ("d", e)
                    i = counters.get(key, 0)
                    counters[key] = i + 1
                    op.rank = i
                    op.inc_sem = lsems[key][i % self.LANES]
                    op.inc_amt = 16
                    if i >= self.LANES:
                        op.waits.append((op.inc_sem, 16 * (i // self.LANES)))
                elif e == "gpsimd":
                    key = ("c", "gpsimd")
                    i = counters.get(key, 0)
                    counters[key] = i + 1
                    op.rank = i
                    op.inc_sem = lsems[key][i % self.LANES]
                    op.inc_amt = 1
                    if i >= self.LANES:
                        op.waits.append((op.inc_sem, i // self.LANES))
                elif op.needed:
                    key = ("c", e)
                    counters[key] = counters.get(key, 0) + 1
                    op.rank = counters[key]
                    op.inc_sem = csems[e]
                    op.inc_amt = 1
        for e in self.ENGINES:
            known = {}
            for op in self.ops[e]:
                for d in op.deps:
                    if d.dma:
                        sem = lsems[("d", d.engine)][d.rank % self.LANES]
                        val = 16 * (d.rank // self.LANES + 1)
                    elif d.engine == "gpsimd":
                        sem = lsems[("c", "gpsimd")][d.rank % self.LANES]
                        val = d.rank // self.LANES + 1
                    else:
                        sem, val = csems[d.engine], d.rank
                    kid = id(sem)
                    if known.get(kid, -1) < val:
                        op.waits.append((sem, val))
                        known[kid] = val
        blk = stack.enter_context(nc.Block())
        for ename in self.ENGINES:
            ops = self.ops[ename]

            def section(eng, ops=ops):
                for op in ops:
                    for (sem, val) in op.waits:
                        eng.wait_ge(sem, val)
                    inst = op.fn(eng)
                    if op.inc_sem is not None:
                        inst.then_inc(op.inc_sem, op.inc_amt)

            getattr(blk, ename)(section)


def build_program(upto="full"):
    nc = bass.Bass("TRN2", target_bir_lowering=False)
    x0 = nc.dram_tensor("x0", [S, D], F32, kind="ExternalInput").ap()
    wq = nc.dram_tensor("wq", [D, 6144], F32, kind="ExternalInput").ap()
    wkv = nc.dram_tensor("wkv", [D, 8256], F32, kind="ExternalInput").ap()
    wo_a = nc.dram_tensor("wo_a", [D, D], F32, kind="ExternalInput").ap()
    Wq_d = nc.dram_tensor("Wq_d", [D, QL], F32, kind="ExternalInput").ap()
    bq_d = nc.dram_tensor("bq_d", [1, QL], F32, kind="ExternalInput").ap()
    W_qk = nc.dram_tensor("W_qk", [QL, 128], F32, kind="ExternalInput").ap()
    b_qk = nc.dram_tensor("b_qk", [1, 128], F32, kind="ExternalInput").ap()
    Wkv_d = nc.dram_tensor("Wkv_d", [D, KV], F32, kind="ExternalInput").ap()
    bkv_d = nc.dram_tensor("bkv_d", [1, KV], F32, kind="ExternalInput").ap()
    Wv_u = nc.dram_tensor("Wv_u", [KV, D], F32, kind="ExternalInput").ap()
    bv_u = nc.dram_tensor("bv_u", [1, D], F32, kind="ExternalInput").ap()
    Wo_b = nc.dram_tensor("Wo_b", [D, D], F32, kind="ExternalInput").ap()
    bo_b = nc.dram_tensor("bo_b", [1, D], F32, kind="ExternalInput").ap()
    lora_A = nc.dram_tensor("lora_A", [128, D], F32, kind="ExternalInput").ap()
    lora_B = nc.dram_tensor("lora_B", [8 * D, 16], F32, kind="ExternalInput").ap()
    ffn_W = nc.dram_tensor("ffn_W", [D, D], F32, kind="ExternalInput").ap()
    ffn_bias = nc.dram_tensor("ffn_bias", [1, D], F32, kind="ExternalInput").ap()
    lora_w = nc.dram_tensor("lora_w", [1, 8], F32, kind="ExternalInput").ap()
    y = nc.dram_tensor("y", [S, D], F32, kind="ExternalOutput").ap()
    x1 = nc.dram_tensor("x1", [S, D], F32).ap()
    x2 = nc.dram_tensor("x2", [S, D], F32).ap()
    x3 = nc.dram_tensor("x3", [S, D], F32).ap()
    Ua = nc.dram_tensor("Ua", [D, S], BF16).ap()
    Abh = nc.dram_tensor("Abh", [D, S], BF16).ap()
    WT = nc.dram_tensor("WTs", [D, D], BF16).ap()
    AcH = nc.dram_tensor("AcH", [128, D], BF16).ap()
    BtH = nc.dram_tensor("BtH", [128, D], BF16).ap()
    cfH = nc.dram_tensor("cfH", [1, 128], F32).ap()
    dbg = nc.dram_tensor("dbg", [128, 32 * AG], BF16, kind="ExternalOutput").ap() \
        if upto in ("ln", "h0", "ua", "wt", "b0") else None

    from contextlib import ExitStack
    stack = ExitStack()
    sb = lambda name, shape, dt: stack.enter_context(nc.sbuf_tensor(name, shape, dt))
    psa = lambda name, shape, dt: stack.enter_context(nc.psum_tensor(name, shape, dt))
    B = Builder(nc)

    BIGA = sb("BIGA", [128, 32 * AG], BF16)
    BIGB = sb("BIGB", [128, 8 * AG], BF16)
    ones_row = sb("ones_row", [1, 512], BF16)
    ones_col = sb("ones_col", [128, 1], BF16)
    ident = sb("ident", [128, 128], BF16)
    masks = sb("masks", [128, 896], BF16)
    lacc = sb("lacc", [128, 512], F32)
    oncf = sb("oncf", [128, 1], F32)
    wqd_sb = sb("wqd_sb", [128, NB * QL], BF16)
    wkvd_sb = sb("wkvd_sb", [128, NB * KV], BF16)
    bqd_row = sb("bqd_row", [1, QL], BF16)
    bkvd_row = sb("bkvd_row", [1, KV], BF16)
    wqk_sb = sb("wqk_sb", [QL, 128], BF16)
    bqk_row = sb("bqk_row", [1, 128], BF16)
    wvu_h = sb("wvu_h", [KV, 128], BF16)
    bvu_h = sb("bvu_h", [1, 128], BF16)
    row_big = sb("row_big", [1, D], BF16)
    lw_sb = sb("lw_sb", [1, 8], F32)
    coef_col = sb("coef_col", [128, 1], F32)
    stat = sb("stat", [128, 16], F32)
    eps_col = sb("eps_col", [128, 1], F32)
    diag_bf = [sb(f"diag{i}", [128, 128], BF16) for i in range(2)]
    bcol_bf = [sb(f"bcolb{i}", [128, 1], BF16) for i in range(2)]
    brow_bf = [sb(f"browb{i}", [1, 128], BF16) for i in range(2)]
    Qt_sb = sb("Qt_sb", [128, S], BF16)
    Kt_sb = sb("Kt_sb", [128, S], BF16)
    expS = [sb(f"expS{i}", [128, 512], BF16) for i in range(4)]
    lhi = sb("lhi", [1, 512], BF16)
    llo = sb("llo", [1, 512], BF16)
    bcast_sb = sb("bcast_sb", [128, 512], F32)
    Uh_sb = sb("Uh_sb", [128, S], BF16)
    rbuf = [sb(f"rbuf{i}", [128, 512], F32) for i in range(2)]
    nbuf = [sb(f"nbuf{i}", [128, 512], F32) for i in range(2)]
    lrow = nbuf[0][0:1, :]
    lrec = nbuf[1][0:1, :]
    hbf = sb("hbf", [128, 1024], BF16)
    Ckvs_sb = sb("Ckvs_sb", [128, NT * 40], BF16)
    u5_sb = lacc
    Cq_sb = Uh_sb
    Ckv_sb = Kt_sb
    ctx_bf = Qt_sb
    bct_sb = Kt_sb

    ps = [psa(f"ps{i}", [128, 512], F32) for i in range(6)]
    psb = psa("psb", [128, 512], BF16)
    psb2 = psa("psb2", [128, 512], BF16)
    PK = [f"ps{i}" for i in range(6)]

    def A(c0, w):
        return BIGA[:, c0:c0 + w]

    def Bv(c0, w):
        return BIGB[:, c0:c0 + w]

    def akeys(c0, w):
        return [f"A{g}" for g in range(c0 // AG, (c0 + w - 1) // AG + 1)]

    def bkeys(c0, w):
        return [f"B{g}" for g in range(c0 // AG, (c0 + w - 1) // AG + 1)]

    XBF = [0, 4096]
    SQS = 8192
    QW, KW, VW2 = 0, 4096, 8192
    VT, VS = 12288, 14336
    ACT_ = 12288
    AH = [12288, 14336]
    UAT = [8192, 12288]
    WCOL = [0, 0]
    WSTG = 4096
    STGB = 0
    STG16 = 4096
    QH = 0

    def dma(queue, out_ap, in_ap, reads, writes, tag=""):
        return B.emit(queue, lambda e: e.dma_start(out_ap, in_ap), reads, writes,
                      dma=True, tag=tag)

    def mm(out_ap, lhsT_ap, rhs_ap, reads, writes, start, stop, tag=""):
        return B.emit("tensor",
                      lambda e: e.matmul(out_ap, lhsT_ap, rhs_ap, start=start, stop=stop),
                      reads, writes, tag=tag)

    def tr(out_ap, in_ap, p, reads, writes, tag=""):
        return B.emit("tensor",
                      lambda e: e.transpose(out_ap, in_ap, ident[0:p, 0:p]),
                      list(reads) + ["ident"], writes, tag=tag)

    def act(out_ap, in_ap, func, reads, writes, bias=0.0, scale=1.0, accum=None, tag=""):
        return B.emit("scalar",
                      lambda e: e.activation(out_ap, in_ap, func, bias=bias, scale=scale,
                                             accum_out=accum),
                      reads, writes, tag=tag)

    def vec(fn, reads, writes, tag=""):
        return B.emit("vector", fn, reads, writes, tag=tag)

    def gp(fn, reads, writes, tag=""):
        return B.emit("gpsimd", fn, reads, writes, tag=tag)

    gp(lambda e: e.memset(ones_row[:, :], 1.0), [], ["ones_row"])
    gp(lambda e: e.memset(ones_col[:, :], 1.0), [], ["ones_col"])
    gp(lambda e: e.memset(eps_col[:, :], EPS), [], ["eps_col"])
    gp(lambda e: e.memset(oncf[:, :], 1.0), [], ["oncf"])
    gp(lambda e: e.memset(masks[:, :], 1.0), ["mkM"], ["mkM"])
    gp(lambda e: e.affine_select(masks[:, :], masks[:, :], [[1, 896]], ALU.is_ge, 0.0,
                                 base=-384, channel_multiplier=-1), ["mkM"], ["masks"])
    gp(lambda e: e.memset(ident[:, :], 1.0), ["identM"], ["identM"])
    gp(lambda e: e.affine_select(ident[:, :], ident[:, :], [[1, 128]], ALU.is_equal, 0.0,
                                 base=0, channel_multiplier=-1), ["identM"], ["ident"])
    dma("gpsimd", wqd_sb[:, :], Wq_d.rearrange("(b p) l -> p b l", p=128), [], ["wqd_sb"])
    dma("gpsimd", wkvd_sb[:, :], Wkv_d.rearrange("(b p) l -> p b l", p=128), [], ["wkvd_sb"])
    dma("gpsimd", bqd_row[:, :], bq_d, [], ["bqd_row"])
    dma("gpsimd", bkvd_row[:, :], bkv_d, [], ["bkvd_row"])
    dma("gpsimd", wqk_sb[:, :], W_qk, [], ["wqk_sb"])
    dma("gpsimd", bqk_row[:, :], b_qk, [], ["bqk_row"])
    dma("sync", lw_sb[:, :], lora_w, [], ["lw_sb"])
    vec(lambda e: e.tensor_scalar_mul(
        lrec[0:1, 0:128], bass.AP(lw_sb, 0, [[8, 1], [1, 8], [0, 16]]), 2.0),
        ["lw_sb", "lrec"], ["coef_row"])
    dma("sync", cfH[:, :], lrec[0:1, 0:128], ["coef_row"], ["cfHk"])
    dma("sync", coef_col[:, :], cfH.rearrange("a (p o) -> p (a o)", p=128),
        ["cfHk"], ["coef_col"])
    dma("gpsimd", Bv(STGB, 4096), lora_A, bkeys(STGB, 4096),
        bkeys(STGB, 4096) + ["stgA"])
    for g4 in range(8):
        for j in range(4):
            db = g4 * 4 + j
            tr(psb[:, j * 128:(j + 1) * 128], Bv(STGB + db * 128, 128), 128,
               ["stgA"] + bkeys(STGB + db * 128, 128), ["psb"])
        B.emit("scalar",
               lambda e, g4=g4: e.copy(Bv(WSTG + g4 * 512, 512), psb[:, :]),
               ["psb"], bkeys(WSTG + g4 * 512, 512) + ["stgAc"])
    dma("sync", AcH[:, :], Bv(WSTG, 4096), ["stgAc"] + bkeys(WSTG, 4096), ["AcHk"])
    for n in range(8):
        dma("gpsimd", Bv(STGB, 512),
            lora_B[n * D:(n + 1) * D, :].rearrange("(bb p) r -> p bb r", p=128),
            ["stgA", "stgBn"] + bkeys(STGB, 512), bkeys(STGB, 512) + ["stgBn"])
        for b4 in range(8):
            for j in range(4):
                bb = b4 * 4 + j
                tr(psb[0:16, j * 128:(j + 1) * 128], Bv(STGB + bb * 16, 16), 128,
                   ["stgBn"] + bkeys(STGB, 512), ["psb"])
            B.emit("vector",
                   lambda e, b4=b4: e.tensor_copy(
                       BIGB[0:16, STG16 + b4 * 512: STG16 + (b4 + 1) * 512],
                       psb[0:16, 0:512]),
                   ["psb"] + bkeys(STG16, 4096), bkeys(STG16, 4096) + ["stg16"])
        dma("sync", BtH[n * 16:(n + 1) * 16, :], BIGB[0:16, STG16:STG16 + 4096],
            ["stg16"] + bkeys(STG16, 4096), [f"BtHk{n}"])
    bt_keys = [f"BtHk{n}" for n in range(8)]

    def layernorm_block(xin_ap, t, dst_base, dst_stride, dst_keys_fn, xin_keys, tag):
        sl = t % 2
        xb = Bv(XBF[sl], 4096)
        kx = bkeys(XBF[sl], 4096)
        dma("gpsimd", xb, xin_ap[t * 128:(t + 1) * 128, :], kx + xin_keys, kx,
            tag=f"{tag}x{t}")
        junk = Bv(SQS, 4096)
        kj = bkeys(SQS, 4096)
        act(junk, xb, AF.Copy, kx + kj, kj + [f"sum{sl}"],
            accum=stat[:, sl * 8:sl * 8 + 1], tag=f"{tag}cp{t}")
        act(junk, xb, AF.Square, kx + kj, kj + [f"sumsq{sl}"],
            accum=stat[:, sl * 8 + 1:sl * 8 + 2], tag=f"{tag}sq{t}")
        st = stat[:, sl * 8:sl * 8 + 8]
        vec(lambda e, st=st: e.tensor_scalar_mul(st[:, 2:3], st[:, 0:1], 1.0 / D),
            [f"sum{sl}"], [f"mean{sl}"])
        vec(lambda e, st=st: e.tensor_scalar_mul(st[:, 3:4], st[:, 1:2], 1.0 / D),
            [f"sumsq{sl}"], [f"msq{sl}"])
        vec(lambda e, st=st: e.tensor_mul(st[:, 4:5], st[:, 2:3], st[:, 2:3]),
            [f"mean{sl}"], [f"m2{sl}"])
        vec(lambda e, st=st: e.tensor_sub(st[:, 3:4], st[:, 3:4], st[:, 4:5]),
            [f"msq{sl}", f"m2{sl}"], [f"var{sl}"])
        act(st[:, 4:5], st[:, 3:4], AF.Ln, [f"var{sl}", f"m2{sl}", "eps_col"],
            [f"lnv{sl}"], bias=eps_col[:, 0:1])
        act(st[:, 5:6], st[:, 4:5], AF.Exp, [f"lnv{sl}"], [f"rstd0{sl}"], scale=-0.5)
        vec(lambda e, st=st: e.tensor_mul(st[:, 6:7], st[:, 5:6], st[:, 5:6]),
            [f"rstd0{sl}"], [f"r2{sl}"])
        vec(lambda e, st=st: e.tensor_scalar_add(st[:, 4:5], st[:, 3:4], EPS),
            [f"var{sl}", f"lnv{sl}"], [f"vpe{sl}"])
        vec(lambda e, st=st: e.tensor_mul(st[:, 6:7], st[:, 6:7], st[:, 4:5]),
            [f"r2{sl}", f"vpe{sl}"], [f"r2{sl}"])
        vec(lambda e, st=st: e.tensor_scalar(st[:, 6:7], st[:, 6:7], -0.5, 1.5,
                                             ALU.mult, ALU.add),
            [f"r2{sl}"], [f"r2{sl}"])
        vec(lambda e, st=st: e.tensor_mul(st[:, 5:6], st[:, 5:6], st[:, 6:7]),
            [f"rstd0{sl}", f"r2{sl}"], [f"rstd{sl}"])
        vec(lambda e, st=st: e.tensor_mul(st[:, 7:8], st[:, 2:3], st[:, 5:6]),
            [f"mean{sl}", f"rstd{sl}"], [f"mr{sl}"])
        vec(lambda e, st=st, sl=sl: e.tensor_scalar_mul(diag_bf[sl][:, :], ident[:, :],
                                                        st[:, 5:6]),
            ["ident", f"rstd{sl}"], [f"diag{sl}"])
        vec(lambda e, st=st, sl=sl: e.tensor_copy(bcol_bf[sl][:, :], st[:, 7:8]),
            [f"mr{sl}"], [f"bcol{sl}"])
        pbr = psb[0:1, sl * 128:sl * 128 + 128]
        tr(pbr, bcol_bf[sl][:, :], 128, [f"bcol{sl}"], ["psb"])
        act(brow_bf[sl][:, :], pbr, AF.Copy, ["psb"], [f"brow{sl}"], scale=-1.0)
        for g4 in range(8):
            bank = ps[g4 % 2]
            bkey = PK[g4 % 2]
            for j in range(4):
                db = g4 * 4 + j
                mm(bank[:, j * 128:(j + 1) * 128], xb[:, db * 128:(db + 1) * 128],
                   diag_bf[sl][:, :], kx + [f"diag{sl}"], [bkey], start=True, stop=False)
                mm(bank[:, j * 128:(j + 1) * 128], ones_row[0:1, 0:128],
                   brow_bf[sl][0:1, :], ["ones_row", f"brow{sl}"], [bkey],
                   start=False, stop=True)
            off = dst_base(g4 * 4) + 0
            dst_ap = bass.AP(BIGA, off, [[32 * AG, 128], [dst_stride, 4], [1, 128]])
            src_ap = bass.AP(bank, 0, [[512, 128], [128, 4], [1, 128]])
            vec(lambda e, dst_ap=dst_ap, src_ap=src_ap: e.tensor_copy(dst_ap, src_ap),
                [bkey], dst_keys_fn(g4), tag=f"{tag}ev{t}_{g4}")

    def wprep_steps():
        steps = []
        for i in range(NB):
            sl = i % 2

            def step(i=i, sl=sl):
                dma("gpsimd", Bv(WCOL[sl], 4096),
                    ffn_W[:, i * 128:(i + 1) * 128].rearrange("(b p) d -> p b d", p=128),
                    bkeys(WCOL[sl], 4096), bkeys(WCOL[sl], 4096), tag=f"wp{i}")
                for b4 in range(8):
                    for j in range(4):
                        b = b4 * 4 + j
                        tr(psb[:, j * 128:(j + 1) * 128], Bv(WCOL[sl] + b * 128, 128),
                           128, bkeys(WCOL[sl], 4096), ["psb"])
                    B.emit("scalar",
                           lambda e, b4=b4: e.copy(Bv(WSTG + b4 * 512, 512), psb[:, :]),
                           ["psb"], bkeys(WSTG + b4 * 512, 512) + ["wstg"])
                dma("sync", WT[i * 128:(i + 1) * 128, :], Bv(WSTG, 4096),
                    ["wstg"], [f"WTr{i}"])

            steps.append(step)
        return steps

    def phase_a():
        for t in range(NT):
            layernorm_block(
                x0, t,
                dst_base=lambda db, t=t: db * AG + t * 128,
                dst_stride=AG,
                dst_keys_fn=lambda g4: [f"A{g4 * 4 + j}" for j in range(4)],
                xin_keys=[], tag="a1")
        for h in range(H):
            qwk = bkeys(QW, 4096)
            kwk = bkeys(KW, 4096)
            vwk = bkeys(VW2, 4096)
            dma("gpsimd", Bv(QW, 4096),
                wq[:, h * 192: h * 192 + 128].rearrange("(b p) e -> p b e", p=128),
                qwk, qwk, tag=f"qw{h}")
            dma("gpsimd", Bv(KW, 4096),
                wkv[:, h * 128: (h + 1) * 128].rearrange("(b p) e -> p b e", p=128),
                kwk, kwk, tag=f"kw{h}")
            dma("gpsimd", Bv(VW2, 4096),
                wkv[:, 4096 + h * 128: 4096 + (h + 1) * 128].rearrange(
                    "(b p) e -> p b e", p=128),
                vwk, vwk, tag=f"vw{h}")
            vtk = bkeys(VT, 2048)
            vsk = bkeys(VS, 2048)
            for c in range(4):
                bv_ = ps[2 + c % 2]
                for db in range(NB):
                    mm(bv_[:, :], Bv(VW2 + db * 128, 128), A(db * AG + c * 512, 512),
                       vwk + [f"A{db}"], [PK[2 + c % 2]], start=(db == 0),
                       stop=(db == NB - 1))
                act(Bv(VT + c * 512, 512), bv_[:, :], AF.Copy,
                    [PK[2 + c % 2]] + vtk, vtk)
            for g4 in range(4):
                pb = psb if g4 % 2 == 0 else psb2
                pbk = "psb" if g4 % 2 == 0 else "psb2"
                for j in range(4):
                    kb = g4 * 4 + j
                    tr(pb[:, j * 128:(j + 1) * 128], Bv(VT + kb * 128, 128), 128,
                       vtk, [pbk])
                B.emit("scalar",
                       lambda e, g4=g4, pb=pb: e.copy(Bv(VS + g4 * 512, 512), pb[:, :]),
                       [pbk] + vsk, vsk)
            for c in range(4):
                bq, bk = ps[4], ps[5]
                for db in range(NB):
                    mm(bq[:, :], Bv(QW + db * 128, 128), A(db * AG + c * 512, 512),
                       qwk + [f"A{db}"], [PK[4]], start=(db == 0), stop=(db == NB - 1))
                for db in range(NB):
                    mm(bk[:, :], Bv(KW + db * 128, 128), A(db * AG + c * 512, 512),
                       kwk + [f"A{db}"], [PK[5]], start=(db == 0), stop=(db == NB - 1))
                act(Qt_sb[:, c * 512:(c + 1) * 512], bq[:, :], AF.Copy, [PK[4]], ["Qt"])
                act(Kt_sb[:, c * 512:(c + 1) * 512], bk[:, :], AF.Copy, [PK[5]], ["Kt"])
            deferred = [None]

            def fin_mms(c, ub):
                bb_i = (0, 1, 4, 5)[(4 * c + 5) % 4]
                bB = ps[bb_i]
                mm(bB[:, :], ones_row[0:1, 0:128], lhi[0:1, :], ["ones_row", "lhi"],
                   [PK[bb_i]], start=True, stop=False)
                mm(bB[:, :], ones_row[0:1, 0:128], llo[0:1, :], ["ones_row", "llo"],
                   [PK[bb_i]], start=False, stop=True)
                vec(lambda e, bB=bB: e.tensor_copy(bcast_sb[:, :], bB[:, :]), [PK[bb_i]],
                    ["bcast"])
                vec(lambda e, c=c, ub=ub: e.tensor_mul(Uh_sb[:, c * 512:(c + 1) * 512],
                                                       ps[ub][:, :], bcast_sb[:, :]),
                    [PK[ub], "bcast"], ["Uh"], tag=f"Un{h}_{c}")

            for c in range(4):
                nkb = 4 * c + 4
                ub = 2 + c % 2
                ROT = (0, 1, 4, 5)
                for kb in range(nkb):
                    bS = ps[ROT[kb % 4]]
                    skey = PK[ROT[kb % 4]]
                    ekey = f"expS{kb % 4}"
                    mm(bS[:, :], Kt_sb[:, kb * 128:(kb + 1) * 128],
                       Qt_sb[:, c * 512:(c + 1) * 512], ["Kt", "Qt"], [skey],
                       start=True, stop=True, tag=f"sc{h}_{c}_{kb}")
                    act(expS[kb % 4][:, :], bS[:, :], AF.Exp, [skey], [ekey],
                        scale=SCALE_A, tag=f"exp{h}_{c}_{kb}")
                    if kb >= 4 * c:
                        m = kb - 4 * c
                        vec(lambda e, kb=kb, m=m: e.tensor_mul(
                            expS[kb % 4][:, :], expS[kb % 4][:, :],
                            masks[:, 384 - 128 * m: 896 - 128 * m]),
                            [ekey, "masks"], [ekey], tag=f"msk{h}_{c}_{kb}")
                    mm(ps[ub][:, :], Bv(VS + kb * 128, 128), expS[kb % 4][:, :],
                       vsk + [ekey], [PK[ub]], start=(kb == 0), stop=(kb == nkb - 1))
                    if kb == 0:
                        vec(lambda e, kb=kb: e.tensor_copy(lacc[:, :], expS[kb % 4][:, :]),
                            [ekey, "lacc"], ["lacc"])
                    else:
                        vec(lambda e, kb=kb: e.tensor_add(lacc[:, :], lacc[:, :],
                                                          expS[kb % 4][:, :]),
                            [ekey, "lacc"], ["lacc"])
                    if kb == 5 and deferred[0] is not None:
                        deferred[0]()
                        deferred[0] = None
                lb = ROT[nkb % 4]
                mm(ps[lb][0:1, :], oncf[:, :], lacc[:, :], ["oncf", "lacc"], [PK[lb]],
                   start=True, stop=True)
                if deferred[0] is not None:
                    deferred[0]()
                    deferred[0] = None
                act(lrow[:, :], ps[lb][0:1, :], AF.Ln, [PK[lb], "lrow"], ["lrow"])
                act(lrec[:, :], lrow[:, :], AF.Exp, ["lrow", "coef_row"], ["lrec"],
                    scale=-1.0)
                act(lhi[:, :], lrec[:, :], AF.Copy, ["lrec", "lhi"], ["lhi"])
                vec(lambda e: e.tensor_sub(lrow[:, :], lrec[:, :], lhi[:, :]),
                    ["lrec", "lhi", "lrow"], ["llo_f"])
                vec(lambda e: e.tensor_copy(llo[:, :], lrow[:, :]), ["llo_f"], ["llo"])
                deferred[0] = (lambda c=c, ub=ub: fin_mms(c, ub))
            if deferred[0] is not None:
                deferred[0]()
                deferred[0] = None
            dma("sync", Ua[h * 128:(h + 1) * 128, :], Uh_sb[:, :], ["Uh"], [f"UaH{h}"],
                tag=f"spill{h}")
        ua_ready = [f"UaH{h}" for h in range(H)]
        wsteps = wprep_steps()
        wi = 0
        outk = []
        for Q in range(4):
            for e_blk in range(NB):
                wkey = akeys(e_blk * 1024, 1024)
                dma("gpsimd", A(e_blk * 1024, 1024),
                    wo_a[e_blk * 128:(e_blk + 1) * 128, Q * 1024:(Q + 1) * 1024],
                    wkey, wkey, tag=f"wo{Q}_{e_blk}")
            for tp in range(NT // 2):
                if wi < len(wsteps):
                    wsteps[wi]()
                    wi += 1
                if wi < len(wsteps):
                    wsteps[wi]()
                    wi += 1
                usl = tp % 2
                ubase = 32768 + usl * 8192
                uk = akeys(ubase, 8192)
                dma("sync", A(ubase, 8192),
                    Ua[:, tp * 256:(tp + 1) * 256].rearrange("(eb p) q -> p eb q", p=128),
                    ua_ready + uk, uk, tag=f"uat{Q}_{tp}")
                for tj in range(2):
                    t = tp * 2 + tj
                    for oc in range(2):
                        rsl = oc
                        dma("sync", rbuf[rsl][:, :],
                            x0[t * 128:(t + 1) * 128,
                               Q * 1024 + oc * 512: Q * 1024 + (oc + 1) * 512],
                            [f"rbuf{rsl}"], [f"rbuf{rsl}"])
                        bank = ps[oc]
                        for e_blk in range(NB):
                            mm(bank[:, :], A(ubase + e_blk * 256 + tj * 128, 128),
                               A(e_blk * 1024 + oc * 512, 512),
                               uk + akeys(e_blk * 1024 + oc * 512, 512), [PK[oc]],
                               start=(e_blk == 0), stop=(e_blk == NB - 1))
                        vec(lambda e, bank=bank, rsl=rsl: e.tensor_add(
                            nbuf[rsl][:, :], bank[:, :], rbuf[rsl][:, :]),
                            [PK[oc], f"rbuf{rsl}"], [f"nbuf{rsl}"], tag=f"woev{Q}_{t}_{oc}")
                        o8 = Q * 2 + oc
                        outk.append(f"x1r{t}_{o8}")
                        dma("sync", x1[t * 128:(t + 1) * 128, o8 * 512:(o8 + 1) * 512],
                            nbuf[rsl][:, :], [f"nbuf{rsl}"], [f"x1r{t}_{o8}"],
                            tag=f"x1st{t}_{o8}")
        while wi < len(wsteps):
            wsteps[wi]()
            wi += 1
        return outk

    def phase_f(xin_ap, xout_ap, xname, xin_name, inst):
        ack = bkeys(ACT_, 4096)
        dma("sync", Bv(ACT_, 4096), AcH[:, :], ["AcHk"] + ack, ack, tag=f"f{inst}Ac")
        dma("gpsimd", row_big[:, :], ffn_bias, ["row_big"], ["row_big"], tag=f"f{inst}fb")
        wt_keys = [f"WTr{i}" for i in range(NB)]
        outk = []
        for ch in range(2):
            for tt in range(8):
                t = ch * 8 + tt
                layernorm_block(
                    xin_ap, t,
                    dst_base=lambda db, tt=tt: db * 1024 + tt * 128,
                    dst_stride=1024,
                    dst_keys_fn=lambda g4: [f"A{(g4 * 4) // 2}", f"A{(g4 * 4 + 3) // 2}"],
                    xin_keys=[f"{xin_name}r{t}_{o}" for o in range(8)],
                    tag=f"f{inst}ln")
            for half in range(2):
                bank = ps[4 + half]
                for db in range(NB):
                    mm(bank[:, :], Bv(ACT_ + db * 128, 128),
                       A(db * 1024 + half * 512, 512),
                       ack + [f"A{(db * 1024 + half * 512) // AG}"], [PK[4 + half]],
                       start=(db == 0), stop=(db == NB - 1))
                act(hbf[:, half * 512:(half + 1) * 512], bank[:, :], AF.Copy,
                    [PK[4 + half], "coef_col"], ["hbf"], scale=coef_col[:, 0:1])
            for oc in range(8):
                wsl = oc % 2
                wck = akeys(32768 + wsl * 16384, 16384)
                dma("sync", A(32768 + wsl * 16384, 16384),
                    WT[:, oc * 512:(oc + 1) * 512].rearrange("(b p) o -> p b o", p=128),
                    wt_keys + wck, wck, tag=f"f{inst}wt{ch}_{oc}")
                dma("gpsimd", bct_sb[:, 0:512], BtH[:, oc * 512:(oc + 1) * 512],
                    bt_keys + ["Kt"], ["Kt"], tag=f"f{inst}bt{ch}_{oc}")
                for tt in range(8):
                    t = ch * 8 + tt
                    bank = ps[tt % 2]
                    for db in range(NB):
                        mm(bank[:, :], A(db * 1024 + tt * 128, 128),
                           A(32768 + wsl * 16384 + db * 512, 512),
                           [f"A{(db * 1024 + tt * 128) // AG}",
                            f"A{(32768 + wsl * 16384 + db * 512) // AG}"],
                           [PK[tt % 2]], start=(db == 0), stop=False)
                    mm(bank[:, :], hbf[:, tt * 128:(tt + 1) * 128], bct_sb[:, 0:512],
                       ["hbf", "Kt"], [PK[tt % 2]], start=False, stop=False)
                    mm(bank[:, :], ones_row[0:1, 0:128],
                       row_big[0:1, oc * 512:(oc + 1) * 512],
                       ["ones_row", "row_big"], [PK[tt % 2]], start=False, stop=True)
                    rsl = tt % 2
                    dma("scalar", rbuf[rsl][:, :],
                        xin_ap[t * 128:(t + 1) * 128, oc * 512:(oc + 1) * 512],
                        [f"{xin_name}r{t}_{oc}", f"rbuf{rsl}"], [f"rbuf{rsl}"])
                    vec(lambda e, bank=bank, rsl=rsl: e.tensor_add(
                        nbuf[rsl][:, :], bank[:, :], rbuf[rsl][:, :]),
                        [PK[tt % 2], f"rbuf{rsl}"], [f"nbuf{rsl}"])
                    k = f"{xname}r{t}_{oc}"
                    outk.append(k)
                    dma("scalar", xout_ap[t * 128:(t + 1) * 128, oc * 512:(oc + 1) * 512],
                        nbuf[rsl][:, :], [f"nbuf{rsl}"], [k])
        return outk

    def phase_b(xin_ap, xout_ap, xin_name):
        for ch in range(4):
            for tt in range(4):
                t = ch * 4 + tt
                layernorm_block(
                    xin_ap, t,
                    dst_base=lambda db, tt=tt: db * 512 + tt * 128,
                    dst_stride=512,
                    dst_keys_fn=lambda g4: [f"A{g4}"],
                    xin_keys=[f"{xin_name}r{t}_{o}" for o in range(8)],
                    tag="bln")
            bq, bkv = ps[2], ps[3]
            for db in range(NB):
                mm(bq[0:QL, :], wqd_sb[:, db * QL:(db + 1) * QL], A(db * 512, 512),
                   ["wqd_sb", f"A{db // 4}"], [PK[2]], start=(db == 0), stop=False)
            mm(bq[0:QL, :], bqd_row[0:1, :], ones_row[0:1, :], ["bqd_row", "ones_row"],
               [PK[2]], start=False, stop=True)
            for db in range(NB):
                mm(bkv[0:KV, :], wkvd_sb[:, db * KV:(db + 1) * KV], A(db * 512, 512),
                   ["wkvd_sb", f"A{db // 4}"], [PK[3]], start=(db == 0), stop=False)
            mm(bkv[0:KV, :], bkvd_row[0:1, :], ones_row[0:1, :], ["bkvd_row", "ones_row"],
               [PK[3]], start=False, stop=True)
            act(Cq_sb[0:QL, ch * 512:(ch + 1) * 512], bq[0:QL, :], AF.Copy, [PK[2]], ["Uh"])
            act(Ckv_sb[0:KV, ch * 512:(ch + 1) * 512], bkv[0:KV, :], AF.Copy, [PK[3]], ["Kt"])
        vec(lambda e: e.memset(Ckvs_sb[:, :], 0.0), ["Ckvs"], ["CkvsM"])
        for kb in range(NT):
            pkc = psb[0:128, (kb % 4) * 128:(kb % 4) * 128 + 4]
            tr(pkc, Ckv_sb[0:KV, kb * 128:(kb + 1) * 128], KV, ["Kt"], ["psb"])
            vec(lambda e, pkc=pkc, kb=kb: e.tensor_copy(
                Ckvs_sb[:, kb * 40: kb * 40 + 4], pkc),
                ["psb", "CkvsM"], ["CkvsM"])
        vec(lambda e: e.memset(bass.AP(Ckvs_sb, 32, [[NT * 40, 128], [40, NT], [1, 1]]),
                               1.0),
            ["CkvsM"], ["Ckvs"])
        dma("gpsimd", row_big[:, :], bv_u, ["row_big"], ["row_big"], tag="b_bvu")
        for h in range(H):
            sl = h % 2
            dma("gpsimd", wvu_h[:, :], Wv_u[:, h * 128:(h + 1) * 128], ["wvu_h"],
                ["wvu_h"], tag=f"wvu{h}")
            qhk = bkeys(QH, 2048)
            for c in range(4):
                bqh = ps[4]
                mm(bqh[0:KV, :], wqk_sb[:, h * KV:(h + 1) * KV],
                   Cq_sb[0:QL, c * 512:(c + 1) * 512], ["wqk_sb", "Uh"], [PK[4]],
                   start=True, stop=False)
                mm(bqh[0:KV, :], bqk_row[0:1, h * KV:(h + 1) * KV], ones_row[0:1, :],
                   ["bqk_row", "ones_row"], [PK[4]], start=False, stop=True)
                act(BIGB[0:KV, QH + c * 512: QH + (c + 1) * 512], bqh[0:KV, :], AF.Copy,
                    [PK[4]] + qhk, qhk)
            BROT = (0, 1, 4, 5)

            def b_score(kb, c, i):
                bS = ps[BROT[i % 4]]
                skey = PK[BROT[i % 4]]
                ekey = f"expS{i % 4}"
                mm(bS[:, :], Ckv_sb[0:KV, kb * 128:(kb + 1) * 128],
                   BIGB[0:KV, QH + c * 512: QH + (c + 1) * 512], ["Kt"] + qhk,
                   [skey], start=True, stop=True, tag=f"bsc{h}_{c}_{kb}")
                act(expS[i % 4][:, :], bS[:, :], AF.Exp, [skey], [ekey], scale=SCALE_B)

            def b_u(kb, c, i):
                mm(ps[2 + c % 2][0:33, :], Ckvs_sb[:, kb * 40: kb * 40 + 33],
                   expS[i % 4][:, :],
                   ["Ckvs", f"expS{i % 4}"], [PK[2 + c % 2]], start=(kb == 0),
                   stop=(kb == NT - 1))

            def b_chain(c):
                bU = ps[2 + c % 2]
                act(lrow[:, :], bU[32:33, :], AF.Ln, [PK[2 + c % 2], "lrow"], ["lrow"])
                act(lrec[:, :], lrow[:, :], AF.Exp, ["lrow", "coef_row"], ["lrec"],
                    scale=-1.0)
                act(lhi[:, :], lrec[:, :], AF.Copy, ["lrec", "lhi"], ["lhi"])
                vec(lambda e: e.tensor_sub(lrow[:, :], lrec[:, :], lhi[:, :]),
                    ["lrec", "lhi", "lrow"], ["llo_f"])
                vec(lambda e: e.tensor_copy(llo[:, :], lrow[:, :]), ["llo_f"], ["llo"])

            def b_final(c):
                bU = ps[2 + c % 2]
                vec(lambda e, bU=bU: e.tensor_copy(u5_sb[0:4, :], bU[0:4, :]),
                    [PK[2 + c % 2], "u5"], ["u5"])
                bb_i = BROT[c % 4]
                bB = ps[bb_i]
                mm(bB[:, :], ones_row[0:1, 0:128], lhi[0:1, :], ["ones_row", "lhi"],
                   [PK[bb_i]], start=True, stop=False)
                mm(bB[:, :], ones_row[0:1, 0:128], llo[0:1, :], ["ones_row", "llo"],
                   [PK[bb_i]], start=False, stop=True)
                vec(lambda e, bB=bB: e.tensor_copy(bcast_sb[:, :], bB[:, :]),
                    [PK[bb_i], "bcast"], ["bcast"])
                vec(lambda e: e.tensor_mul(ctx_bf[0:KV, 0:512], u5_sb[0:KV, :],
                                           bcast_sb[0:KV, :]),
                    ["u5", "bcast"], ["Qt"])
                bt_i = BROT[(c + 1) % 4]
                bT = ps[bt_i]
                mm(bT[:, :], wvu_h[0:KV, :], ctx_bf[0:KV, 0:512],
                   ["wvu_h", "Qt"], [PK[bt_i]], start=True, stop=False)
                mm(bT[:, :], row_big[0:1, h * 128:(h + 1) * 128], ones_row[0:1, :],
                   ["row_big", "ones_row"], [PK[bt_i]], start=False, stop=True)
                ahk = bkeys(AH[sl], 2048)
                act(Bv(AH[sl] + c * 512, 512), bT[:, :], AF.Copy,
                    [PK[bt_i]] + ahk, ahk)

            for cpass in range(2):
                cA, cB = 2 * cpass, 2 * cpass + 1
                flat = [(kb, (cA, cB)[j], 2 * kb + j) for kb in range(NT)
                        for j in range(2)]
                for idx, (kb, c, i) in enumerate(flat):
                    if idx >= 4:
                        b_u(*flat[idx - 4])
                    b_score(kb, c, i)
                for idx in range(len(flat) - 4, len(flat)):
                    b_u(*flat[idx])
                    if idx == len(flat) - 2:
                        b_chain(cA)
                b_final(cA)
                b_chain(cB)
                b_final(cB)
            dma("sync", Abh[h * 128:(h + 1) * 128, :], Bv(AH[sl], 2048),
                bkeys(AH[sl], 2048), [f"AbH{h}"])
        ab_ready = [f"AbH{h}" for h in range(H)]
        dma("gpsimd", row_big[:, :], bo_b, ["row_big"], ["row_big"], tag="b_bo")
        outk = []
        for Q in range(4):
            for e_blk in range(NB):
                wkey = akeys(e_blk * 1024, 1024)
                dma("gpsimd", A(e_blk * 1024, 1024),
                    Wo_b[e_blk * 128:(e_blk + 1) * 128, Q * 1024:(Q + 1) * 1024],
                    wkey, wkey, tag=f"bwo{Q}_{e_blk}")
            for tp in range(NT // 2):
                usl = tp % 2
                ubase = 32768 + usl * 8192
                uk = akeys(ubase, 8192)
                dma("sync", A(ubase, 8192),
                    Abh[:, tp * 256:(tp + 1) * 256].rearrange("(eb p) q -> p eb q", p=128),
                    ab_ready + uk, uk)
                for tj in range(2):
                    t = tp * 2 + tj
                    for oc in range(2):
                        rsl = oc
                        o8 = Q * 2 + oc
                        dma("sync", rbuf[rsl][:, :],
                            xin_ap[t * 128:(t + 1) * 128, o8 * 512:(o8 + 1) * 512],
                            [f"{xin_name}r{t}_{o8}", f"rbuf{rsl}"], [f"rbuf{rsl}"])
                        bank = ps[oc]
                        for e_blk in range(NB):
                            mm(bank[:, :], A(ubase + e_blk * 256 + tj * 128, 128),
                               A(e_blk * 1024 + oc * 512, 512),
                               uk + akeys(e_blk * 1024 + oc * 512, 512), [PK[oc]],
                               start=(e_blk == 0), stop=False)
                        mm(bank[:, :], ones_row[0:1, 0:128],
                           row_big[0:1, o8 * 512:(o8 + 1) * 512],
                           ["ones_row", "row_big"], [PK[oc]], start=False, stop=True)
                        vec(lambda e, bank=bank, rsl=rsl: e.tensor_add(
                            nbuf[rsl][:, :], bank[:, :], rbuf[rsl][:, :]),
                            [PK[oc], f"rbuf{rsl}"], [f"nbuf{rsl}"])
                        outk.append(f"x3r{t}_{o8}")
                        dma("sync", xout_ap[t * 128:(t + 1) * 128, o8 * 512:(o8 + 1) * 512],
                            nbuf[rsl][:, :], [f"nbuf{rsl}"], [f"x3r{t}_{o8}"])
        return outk

    if upto == "b0":
        keys = phase_a()
        k2 = phase_f(x1, x2, "x2", "x1", 1)
        k3 = phase_b(x2, x3, "x2")
        final = []
        final = list(k3)
        for i in range(32):
            sl = i % 2
            kk = bkeys(UAT[sl], 4096)
            dma("sync", Bv(UAT[sl], 2048), Abh[i * 128:(i + 1) * 128, :],
                k3 + kk, kk)
            dma("sync", dbg[:, i * 2048:(i + 1) * 2048], Bv(UAT[sl], 2048), kk,
                [f"dg{i}"])
            final.append(f"dg{i}")
        dma("sync", y[0:128, 0:512], rbuf[0][:, :], ["rbuf0"], ["yk"])
        final.append("yk")
    elif upto == "wt":
        wsteps = wprep_steps()
        for st_ in wsteps:
            st_()
        final = []
        for i in range(8):
            sl = i % 2
            kk = bkeys(UAT[sl], 4096)
            dma("sync", Bv(UAT[sl], 4096), WT[i * 128:(i + 1) * 128, :],
                [f"WTr{i}"] + kk, kk)
            dma("sync", dbg[:, i * 4096:(i + 1) * 4096], Bv(UAT[sl], 4096), kk,
                [f"dw{i}"])
            final.append(f"dw{i}")
        ack = bkeys(ACT_, 4096)
        dma("sync", Bv(ACT_, 4096), AcH[:, :], ["AcHk"] + ack, ack)
        dma("sync", dbg[:, 8 * 4096: 8 * 4096 + 4096], Bv(ACT_, 4096), ack, ["dA"])
        final.append("dA")
        bck = bkeys(XBF[0], 4096)
        dma("sync", Bv(XBF[0], 4096), BtH[:, :], bt_keys + bck, bck)
        dma("sync", dbg[:, 9 * 4096: 9 * 4096 + 4096], Bv(XBF[0], 4096), bck, ["dB"])
        final.append("dB")
        vec(lambda e: e.tensor_copy(expS[0][:, 0:1], coef_col[:, 0:1]),
            ["coef_col", "expS0"], ["expS0"])
        vec(lambda e: e.memset(expS[0][:, 1:512], 0.0), ["expS0"], ["expS0"])
        dma("sync", dbg[:, 10 * 4096: 10 * 4096 + 512], expS[0][:, 0:512], ["expS0"], ["dC"])
        final.append("dC")
        dma("sync", y[0:128, 0:512], rbuf[0][:, :], ["rbuf0"], ["yk"])
        final.append("yk")
    elif upto == "ua":
        keys = phase_a()
        final = []
        for i in range(32):
            sl = i % 2
            kk = bkeys(UAT[sl], 4096)
            dma("sync", Bv(UAT[sl], 2048), Ua[i * 128:(i + 1) * 128, :],
                keys + kk, kk)
            dma("sync", dbg[:, i * 2048:(i + 1) * 2048], Bv(UAT[sl], 2048), kk,
                [f"dg{i}"])
            final.append(f"dg{i}")
        dma("sync", y[0:128, 0:512], rbuf[0][:, :], ["rbuf0"], ["yk"])
        final.append("yk")
    elif upto == "h0":
        for t in range(NT):
            layernorm_block(
                x0, t,
                dst_base=lambda db, t=t: db * AG + t * 128,
                dst_stride=AG,
                dst_keys_fn=lambda g4: [f"A{g4 * 4 + j}" for j in range(4)],
                xin_keys=[], tag="a1")
        h = 0
        qwk = bkeys(QW, 4096)
        kwk = bkeys(KW, 4096)
        vwk = bkeys(VW2, 4096)
        dma("gpsimd", Bv(QW, 4096),
            wq[:, h * 192: h * 192 + 128].rearrange("(b p) e -> p b e", p=128),
            qwk, qwk)
        dma("gpsimd", Bv(KW, 4096),
            wkv[:, h * 128: (h + 1) * 128].rearrange("(b p) e -> p b e", p=128),
            kwk, kwk)
        dma("gpsimd", Bv(VW2, 4096),
            wkv[:, 4096 + h * 128: 4096 + (h + 1) * 128].rearrange(
                "(b p) e -> p b e", p=128),
            vwk, vwk)
        vtk = bkeys(VT, 2048)
        vsk = bkeys(VS, 2048)
        for c in range(4):
            bv_ = ps[2 + c % 2]
            for db in range(NB):
                mm(bv_[:, :], Bv(VW2 + db * 128, 128), A(db * AG + c * 512, 512),
                   vwk + [f"A{db}"], [PK[2 + c % 2]], start=(db == 0),
                   stop=(db == NB - 1))
            act(Bv(VT + c * 512, 512), bv_[:, :], AF.Copy, [PK[2 + c % 2]] + vtk, vtk)
        for g4 in range(4):
            for j in range(4):
                kb = g4 * 4 + j
                tr(psb[:, j * 128:(j + 1) * 128], Bv(VT + kb * 128, 128), 128,
                   vtk, ["psb"])
            B.emit("scalar",
                   lambda e, g4=g4: e.copy(Bv(VS + g4 * 512, 512), psb[:, :]),
                   ["psb"] + vsk, vsk)
        for c in range(4):
            bq, bk = ps[4], ps[5]
            for db in range(NB):
                mm(bq[:, :], Bv(QW + db * 128, 128), A(db * AG + c * 512, 512),
                   qwk + [f"A{db}"], [PK[4]], start=(db == 0), stop=(db == NB - 1))
            for db in range(NB):
                mm(bk[:, :], Bv(KW + db * 128, 128), A(db * AG + c * 512, 512),
                   kwk + [f"A{db}"], [PK[5]], start=(db == 0), stop=(db == NB - 1))
            act(Qt_sb[:, c * 512:(c + 1) * 512], bq[:, :], AF.Copy, [PK[4]], ["Qt"])
            act(Kt_sb[:, c * 512:(c + 1) * 512], bk[:, :], AF.Copy, [PK[5]], ["Kt"])
        for c in range(4):
            nkb = 4 * c + 4
            bU, bL = ps[2], ps[3]
            for kb in range(nkb):
                bS = ps[kb % 2]
                skey = PK[kb % 2]
                ekey = f"expS{kb % 2}"
                mm(bS[:, :], Kt_sb[:, kb * 128:(kb + 1) * 128],
                   Qt_sb[:, c * 512:(c + 1) * 512], ["Kt", "Qt"], [skey],
                   start=True, stop=True)
                act(expS[kb % 2][:, :], bS[:, :], AF.Exp, [skey], [ekey],
                    scale=SCALE_A)
                if kb >= 4 * c:
                    m = kb - 4 * c
                    gp(lambda e, kb=kb, m=m: e.affine_select(
                        expS[kb % 2][:, :], expS[kb % 2][:, :], [[1, 512]],
                        ALU.is_ge, 0.0, base=-128 * m, channel_multiplier=-1),
                        [ekey], [ekey])
                mm(bU[:, :], Bv(VS + kb * 128, 128), expS[kb % 2][:, :],
                   vsk + [ekey], [PK[2]], start=(kb == 0), stop=(kb == nkb - 1))
                mm(bL[0:1, :], ones_col[:, :], expS[kb % 2][:, :],
                   ["ones_col", ekey], [PK[3]],
                   start=(kb == 0), stop=(kb == nkb - 1))
            vec(lambda e: e.tensor_copy(lrow[:, :], ps[3][0:1, :]), [PK[3]], ["lrow"])
            vec(lambda e: e.reciprocal(lrec[:, :], lrow[:, :]), ["lrow", "coef_row"],
                ["lrec"])
            vec(lambda e: e.tensor_copy(lhi[:, :], lrec[:, :]), ["lrec"], ["lhi"])
            vec(lambda e: e.tensor_sub(lrow[:, :], lrec[:, :], lhi[:, :]),
                ["lrec", "lhi", "lrow"], ["llo_f"])
            vec(lambda e: e.tensor_copy(llo[:, :], lrow[:, :]), ["llo_f"], ["llo"])
            bB = ps[4]
            mm(bB[:, :], ones_row[0:1, 0:128], lhi[0:1, :], ["ones_row", "lhi"],
               [PK[4]], start=True, stop=False)
            mm(bB[:, :], ones_row[0:1, 0:128], llo[0:1, :], ["ones_row", "llo"],
               [PK[4]], start=False, stop=True)
            vec(lambda e: e.tensor_copy(bcast_sb[:, :], ps[4][:, :]), [PK[4]],
                ["bcast"])
            vec(lambda e, c=c: e.tensor_mul(Uh_sb[:, c * 512:(c + 1) * 512],
                                            ps[2][:, :], bcast_sb[:, :]),
                [PK[2], "bcast"], ["Uh"])
        dma("sync", dbg[:, 0:2048], Qt_sb[:, :], ["Qt"], ["d0"])
        dma("sync", dbg[:, 2048:4096], Kt_sb[:, :], ["Kt"], ["d1"])
        dma("sync", dbg[:, 4096:6144], Bv(VS, 2048), vsk, ["d2"])
        dma("sync", dbg[:, 6144:8192], Uh_sb[:, :], ["Uh"], ["d3"])
        dma("sync", y[0:128, 0:512], rbuf[0][:, :], ["rbuf0"], ["yk"])
        final = ["d0", "d1", "d2", "d3", "yk"]
    elif upto == "ln":
        for t in range(NT):
            layernorm_block(
                x0, t,
                dst_base=lambda db, t=t: db * AG + t * 128,
                dst_stride=AG,
                dst_keys_fn=lambda g4: [f"A{g4 * 4 + j}" for j in range(4)],
                xin_keys=[], tag="a1")
        dma("sync", dbg, BIGA[:, :], [f"A{g}" for g in range(32)], ["dbgk"])
        dma("sync", y[0:128, 0:512], rbuf[0][:, :], ["rbuf0"], ["yk"])
        final = ["dbgk", "yk"]
    elif upto == "a":
        keys = phase_a()
        for t in range(NT):
            for o in range(8):
                sl = o % 2
                dma("sync", rbuf[sl][:, :], x1[t * 128:(t + 1) * 128, o * 512:(o + 1) * 512],
                    keys + [f"rbuf{sl}"], [f"rbuf{sl}"])
                dma("sync", y[t * 128:(t + 1) * 128, o * 512:(o + 1) * 512], rbuf[sl][:, :],
                    [f"rbuf{sl}"], [f"yk{t}_{o}"])
        final = [f"yk{t}_{o}" for t in range(NT) for o in range(8)]
    elif upto == "f1":
        keys = phase_a()
        final = phase_f(x1, y, "y", "x1", 1)
    elif upto == "b":
        keys = phase_a()
        k2 = phase_f(x1, x2, "x2", "x1", 1)
        final = phase_b(x2, y, "x2")
    else:
        keys = phase_a()
        k2 = phase_f(x1, x2, "x2", "x1", 1)
        k3 = phase_b(x2, x3, "x2")
        final = phase_f(x3, y, "y", "x3", 2)
    B.emit("sync", lambda e: e.nop(), reads=final, writes=[])
    B.finalize_and_emit(stack)
    stack.close()
    return nc


_NC_CACHE = {}


def _get_nc(upto="full"):
    if upto not in _NC_CACHE:
        _NC_CACHE[upto] = build_program(upto)
    return _NC_CACHE[upto]


def _in_maps(inputs):
    c = np.ascontiguousarray
    hs = inputs["L0_attn_a_hidden_states"]
    base = {
        "wq": c(inputs["L0_attn_a_wq"]),
        "wkv": c(inputs["L0_attn_a_wkv"]),
        "wo_a": c(inputs["L0_attn_a_wo"]),
        "Wq_d": c(inputs["L0_attn_b_Wq_d"]),
        "bq_d": c(inputs["L0_attn_b_bq_d"]).reshape(1, QL),
        "W_qk": c(inputs["L0_attn_b_W_qk"]),
        "b_qk": c(inputs["L0_attn_b_b_qk"]).reshape(1, 128),
        "Wkv_d": c(inputs["L0_attn_b_Wkv_d"]),
        "bkv_d": c(inputs["L0_attn_b_bkv_d"]).reshape(1, KV),
        "Wv_u": c(inputs["L0_attn_b_Wv_u"]),
        "bv_u": c(inputs["L0_attn_b_bv_u"]).reshape(1, D),
        "Wo_b": c(inputs["L0_attn_b_Wo"]),
        "bo_b": c(inputs["L0_attn_b_bo"]).reshape(1, D),
        "lora_A": c(inputs["L0_ffn_lora_A"]).reshape(128, D),
        "lora_B": c(inputs["L0_ffn_lora_B"]).reshape(8 * D, 16),
        "ffn_W": c(inputs["L0_ffn_W"]),
        "ffn_bias": c(inputs["L0_ffn_bias"]).reshape(1, D),
        "lora_w": c(inputs["L0_ffn_lora_weights"]).reshape(1, 8),
    }
    maps = []
    for b in range(8):
        m = dict(base)
        m["x0"] = c(np.asarray(hs[b]))
        maps.append(m)
    return maps


def kernel(**inputs):
    nc = _get_nc("full")
    maps = _in_maps(inputs)
    res = run_bass_kernel_spmd(nc, maps, core_ids=list(range(8)))
    out = np.stack([np.asarray(r["y"]).astype(np.float32) for r in res.results], axis=0)
    return out


# version 08
# speedup vs baseline: 1.1809x; 1.1124x over previous
import numpy as np
import ml_dtypes
import concourse.bass as bass
import concourse.mybir as mybir
from concourse.bass_utils import run_bass_kernel_spmd

F32 = mybir.dt.float32
BF16 = mybir.dt.bfloat16
AF = mybir.ActivationFunctionType
ALU = mybir.AluOpType

S = 2048
D = 4096
NB = D // 128
NT = S // 128
H = 32
SCALE_A = float(192.0 ** -0.5)
QL, KV = 12, 4
SCALE_B = 0.5
EPS = 1e-5
AG = 2048


class Op:
    __slots__ = ("engine", "fn", "dma", "deps", "needed", "rank", "waits",
                 "inc_sem", "inc_amt", "tag")

    def __init__(self, engine, fn, dma, tag):
        self.engine = engine
        self.fn = fn
        self.dma = dma
        self.deps = []
        self.needed = False
        self.rank = 0
        self.waits = []
        self.inc_sem = None
        self.inc_amt = 0
        self.tag = tag


class Builder:
    ENGINES = ("tensor", "vector", "scalar", "gpsimd", "sync")

    def __init__(self, nc):
        self.nc = nc
        self.ops = {e: [] for e in self.ENGINES}
        self.last_write = {}
        self.readers = {}

    def emit(self, engine, fn, reads=(), writes=(), dma=False, tag=""):
        op = Op(engine, fn, dma, tag)
        deps = {}
        for k in reads:
            w = self.last_write.get(k)
            if w is not None:
                deps[id(w)] = w
        for k in writes:
            w = self.last_write.get(k)
            if w is not None:
                deps[id(w)] = w
            for r in self.readers.get(k, ()):
                deps[id(r)] = r
        for d in deps.values():
            if d is op:
                continue
            if d.engine == "tensor" and engine == "tensor" and not d.dma and not dma:
                continue
            op.deps.append(d)
        for k in reads:
            self.readers.setdefault(k, []).append(op)
        for k in writes:
            self.last_write[k] = op
            self.readers[k] = []
        self.ops[engine].append(op)
        return op

    LANES = 8

    def finalize_and_emit(self, stack):
        nc = self.nc
        csems = {e: stack.enter_context(nc.semaphore(f"s_{e}"))
                 for e in ("tensor", "vector", "scalar", "sync")}
        lsems = {}
        for q in ("sync", "gpsimd", "scalar"):
            lsems[("d", q)] = [stack.enter_context(nc.semaphore(f"d_{q}{i}"))
                               for i in range(self.LANES)]
        lsems[("c", "gpsimd")] = [stack.enter_context(nc.semaphore(f"g_{i}"))
                                  for i in range(self.LANES)]
        for e in self.ENGINES:
            for op in self.ops[e]:
                for d in op.deps:
                    d.needed = True
        counters = {}
        for e in self.ENGINES:
            for op in self.ops[e]:
                if op.dma:
                    key = ("d", e)
                    i = counters.get(key, 0)
                    counters[key] = i + 1
                    op.rank = i
                    op.inc_sem = lsems[key][i % self.LANES]
                    op.inc_amt = 16
                    if i >= self.LANES:
                        op.waits.append((op.inc_sem, 16 * (i // self.LANES)))
                elif e == "gpsimd":
                    key = ("c", "gpsimd")
                    i = counters.get(key, 0)
                    counters[key] = i + 1
                    op.rank = i
                    op.inc_sem = lsems[key][i % self.LANES]
                    op.inc_amt = 1
                    if i >= self.LANES:
                        op.waits.append((op.inc_sem, i // self.LANES))
                elif op.needed:
                    key = ("c", e)
                    counters[key] = counters.get(key, 0) + 1
                    op.rank = counters[key]
                    op.inc_sem = csems[e]
                    op.inc_amt = 1
        for e in self.ENGINES:
            known = {}
            for op in self.ops[e]:
                for d in op.deps:
                    if d.dma:
                        sem = lsems[("d", d.engine)][d.rank % self.LANES]
                        val = 16 * (d.rank // self.LANES + 1)
                    elif d.engine == "gpsimd":
                        sem = lsems[("c", "gpsimd")][d.rank % self.LANES]
                        val = d.rank // self.LANES + 1
                    else:
                        sem, val = csems[d.engine], d.rank
                    kid = id(sem)
                    if known.get(kid, -1) < val:
                        op.waits.append((sem, val))
                        known[kid] = val
        blk = stack.enter_context(nc.Block())
        for ename in self.ENGINES:
            ops = self.ops[ename]

            def section(eng, ops=ops):
                for op in ops:
                    for (sem, val) in op.waits:
                        eng.wait_ge(sem, val)
                    inst = op.fn(eng)
                    if op.inc_sem is not None:
                        inst.then_inc(op.inc_sem, op.inc_amt)

            getattr(blk, ename)(section)


def build_program(upto="full"):
    nc = bass.Bass("TRN2", target_bir_lowering=False)
    x0 = nc.dram_tensor("x0", [S, D], F32, kind="ExternalInput").ap()
    wq = nc.dram_tensor("wq", [D, 6144], F32, kind="ExternalInput").ap()
    wkv = nc.dram_tensor("wkv", [D, 8256], F32, kind="ExternalInput").ap()
    wo_a = nc.dram_tensor("wo_a", [D, D], F32, kind="ExternalInput").ap()
    Wq_d = nc.dram_tensor("Wq_d", [D, QL], F32, kind="ExternalInput").ap()
    bq_d = nc.dram_tensor("bq_d", [1, QL], F32, kind="ExternalInput").ap()
    W_qk = nc.dram_tensor("W_qk", [QL, 128], F32, kind="ExternalInput").ap()
    b_qk = nc.dram_tensor("b_qk", [1, 128], F32, kind="ExternalInput").ap()
    Wkv_d = nc.dram_tensor("Wkv_d", [D, KV], F32, kind="ExternalInput").ap()
    bkv_d = nc.dram_tensor("bkv_d", [1, KV], F32, kind="ExternalInput").ap()
    Wv_u = nc.dram_tensor("Wv_u", [KV, D], F32, kind="ExternalInput").ap()
    bv_u = nc.dram_tensor("bv_u", [1, D], F32, kind="ExternalInput").ap()
    Wo_b = nc.dram_tensor("Wo_b", [D, D], F32, kind="ExternalInput").ap()
    bo_b = nc.dram_tensor("bo_b", [1, D], F32, kind="ExternalInput").ap()
    lora_A = nc.dram_tensor("lora_A", [128, D], F32, kind="ExternalInput").ap()
    lora_B = nc.dram_tensor("lora_B", [8 * D, 16], F32, kind="ExternalInput").ap()
    ffn_W = nc.dram_tensor("ffn_W", [D, D], F32, kind="ExternalInput").ap()
    ffn_bias = nc.dram_tensor("ffn_bias", [1, D], F32, kind="ExternalInput").ap()
    lora_w = nc.dram_tensor("lora_w", [1, 8], F32, kind="ExternalInput").ap()
    y = nc.dram_tensor("y", [S, D], F32, kind="ExternalOutput").ap()
    x1 = nc.dram_tensor("x1", [S, D], F32).ap()
    x2 = nc.dram_tensor("x2", [S, D], F32).ap()
    x3 = nc.dram_tensor("x3", [S, D], F32).ap()
    Ua = nc.dram_tensor("Ua", [D, S], BF16).ap()
    Abh = nc.dram_tensor("Abh", [D, S], BF16).ap()
    WT = nc.dram_tensor("WTs", [D, D], BF16).ap()
    AcH = nc.dram_tensor("AcH", [128, D], BF16).ap()
    BtH = nc.dram_tensor("BtH", [128, D], BF16).ap()
    cfH = nc.dram_tensor("cfH", [1, 128], F32).ap()
    dbg = nc.dram_tensor("dbg", [128, 32 * AG], BF16, kind="ExternalOutput").ap() \
        if upto in ("ln", "h0", "ua", "wt", "b0") else None

    from contextlib import ExitStack
    stack = ExitStack()
    sb = lambda name, shape, dt: stack.enter_context(nc.sbuf_tensor(name, shape, dt))
    psa = lambda name, shape, dt: stack.enter_context(nc.psum_tensor(name, shape, dt))
    B = Builder(nc)

    BIGA = sb("BIGA", [128, 32 * AG], BF16)
    BIGB = sb("BIGB", [128, 8 * AG], BF16)
    ones_row = sb("ones_row", [1, 512], BF16)
    ones_col = sb("ones_col", [128, 1], BF16)
    ident = sb("ident", [128, 128], BF16)
    masks = sb("masks", [128, 896], BF16)
    lacc = sb("lacc", [128, 512], F32)
    oncf = sb("oncf", [128, 1], F32)
    wqd_sb = sb("wqd_sb", [128, NB * QL], BF16)
    wkvd_sb = sb("wkvd_sb", [128, NB * KV], BF16)
    bqd_row = sb("bqd_row", [1, QL], BF16)
    bkvd_row = sb("bkvd_row", [1, KV], BF16)
    wqk_sb = sb("wqk_sb", [QL, 128], BF16)
    bqk_row = sb("bqk_row", [1, 128], BF16)
    wvu_h = sb("wvu_h", [KV, 128], BF16)
    bvu_h = sb("bvu_h", [1, 128], BF16)
    row_big = sb("row_big", [1, D], BF16)
    lw_sb = sb("lw_sb", [1, 8], F32)
    coef_col = sb("coef_col", [128, 1], F32)
    stat = sb("stat", [128, 16], F32)
    eps_col = sb("eps_col", [128, 1], F32)
    diag_bf = [sb(f"diag{i}", [128, 128], BF16) for i in range(2)]
    bcol_bf = [sb(f"bcolb{i}", [128, 1], BF16) for i in range(2)]
    brow_bf = [sb(f"browb{i}", [1, 128], BF16) for i in range(2)]
    Qt_sb = sb("Qt_sb", [128, S], BF16)
    Kt_sb = sb("Kt_sb", [128, S], BF16)
    expS = [sb(f"expS{i}", [128, 512], BF16) for i in range(4)]
    lhi = sb("lhi", [1, 512], BF16)
    llo = sb("llo", [1, 512], BF16)
    bcast_sb = sb("bcast_sb", [128, 512], F32)
    Uh_sb = sb("Uh_sb", [128, S], BF16)
    rbuf = [sb(f"rbuf{i}", [128, 512], F32) for i in range(2)]
    nbuf = [sb(f"nbuf{i}", [128, 512], F32) for i in range(2)]
    lrow = nbuf[0][0:1, :]
    lrec = nbuf[1][0:1, :]
    hbf = sb("hbf", [128, 1024], BF16)
    Ckvs_sb = sb("Ckvs_sb", [128, NT * 40], BF16)
    u5_sb = lacc
    Cq_sb = Uh_sb
    Ckv_sb = Kt_sb
    ctx_bf = Qt_sb
    bct_sb = Kt_sb

    ps = [psa(f"ps{i}", [128, 512], F32) for i in range(6)]
    psb = psa("psb", [128, 512], BF16)
    psb2 = psa("psb2", [128, 512], BF16)
    PK = [f"ps{i}" for i in range(6)]

    def A(c0, w):
        return BIGA[:, c0:c0 + w]

    def Bv(c0, w):
        return BIGB[:, c0:c0 + w]

    def akeys(c0, w):
        return [f"A{g}" for g in range(c0 // AG, (c0 + w - 1) // AG + 1)]

    def bkeys(c0, w):
        return [f"B{g}" for g in range(c0 // AG, (c0 + w - 1) // AG + 1)]

    XBF = [0, 4096]
    SQS = 8192
    QW, KW, VW2 = 0, 4096, 8192
    VT, VS = 12288, 14336
    ACT_ = 12288
    AH = [12288, 14336]
    UAT = [8192, 12288]
    WCOL = [0, 0]
    WSTG = 4096
    STGB = 0
    STG16 = 4096
    QH = 0

    def dma(queue, out_ap, in_ap, reads, writes, tag=""):
        return B.emit(queue, lambda e: e.dma_start(out_ap, in_ap), reads, writes,
                      dma=True, tag=tag)

    def mm(out_ap, lhsT_ap, rhs_ap, reads, writes, start, stop, tag=""):
        return B.emit("tensor",
                      lambda e: e.matmul(out_ap, lhsT_ap, rhs_ap, start=start, stop=stop),
                      reads, writes, tag=tag)

    def tr(out_ap, in_ap, p, reads, writes, tag=""):
        return B.emit("tensor",
                      lambda e: e.transpose(out_ap, in_ap, ident[0:p, 0:p]),
                      list(reads) + ["ident"], writes, tag=tag)

    def act(out_ap, in_ap, func, reads, writes, bias=0.0, scale=1.0, accum=None, tag=""):
        return B.emit("scalar",
                      lambda e: e.activation(out_ap, in_ap, func, bias=bias, scale=scale,
                                             accum_out=accum),
                      reads, writes, tag=tag)

    def vec(fn, reads, writes, tag=""):
        return B.emit("vector", fn, reads, writes, tag=tag)

    def gp(fn, reads, writes, tag=""):
        return B.emit("gpsimd", fn, reads, writes, tag=tag)

    gp(lambda e: e.memset(ones_row[:, :], 1.0), [], ["ones_row"])
    gp(lambda e: e.memset(ones_col[:, :], 1.0), [], ["ones_col"])
    gp(lambda e: e.memset(eps_col[:, :], EPS), [], ["eps_col"])
    gp(lambda e: e.memset(oncf[:, :], 1.0), [], ["oncf"])
    gp(lambda e: e.memset(masks[:, :], 1.0), ["mkM"], ["mkM"])
    gp(lambda e: e.affine_select(masks[:, :], masks[:, :], [[1, 896]], ALU.is_ge, 0.0,
                                 base=-384, channel_multiplier=-1), ["mkM"], ["masks"])
    gp(lambda e: e.memset(ident[:, :], 1.0), ["identM"], ["identM"])
    gp(lambda e: e.affine_select(ident[:, :], ident[:, :], [[1, 128]], ALU.is_equal, 0.0,
                                 base=0, channel_multiplier=-1), ["identM"], ["ident"])
    dma("gpsimd", wqd_sb[:, :], Wq_d.rearrange("(b p) l -> p b l", p=128), [], ["wqd_sb"])
    dma("gpsimd", wkvd_sb[:, :], Wkv_d.rearrange("(b p) l -> p b l", p=128), [], ["wkvd_sb"])
    dma("gpsimd", bqd_row[:, :], bq_d, [], ["bqd_row"])
    dma("gpsimd", bkvd_row[:, :], bkv_d, [], ["bkvd_row"])
    dma("gpsimd", wqk_sb[:, :], W_qk, [], ["wqk_sb"])
    dma("gpsimd", bqk_row[:, :], b_qk, [], ["bqk_row"])
    dma("sync", lw_sb[:, :], lora_w, [], ["lw_sb"])
    vec(lambda e: e.tensor_scalar_mul(
        lrec[0:1, 0:128], bass.AP(lw_sb, 0, [[8, 1], [1, 8], [0, 16]]), 2.0),
        ["lw_sb", "lrec"], ["coef_row"])
    dma("sync", cfH[:, :], lrec[0:1, 0:128], ["coef_row"], ["cfHk"])
    dma("sync", coef_col[:, :], cfH.rearrange("a (p o) -> p (a o)", p=128),
        ["cfHk"], ["coef_col"])
    dma("gpsimd", Bv(STGB, 4096), lora_A, bkeys(STGB, 4096),
        bkeys(STGB, 4096) + ["stgA"])
    for g4 in range(8):
        for j in range(4):
            db = g4 * 4 + j
            tr(psb[:, j * 128:(j + 1) * 128], Bv(STGB + db * 128, 128), 128,
               ["stgA"] + bkeys(STGB + db * 128, 128), ["psb"])
        B.emit("scalar",
               lambda e, g4=g4: e.copy(Bv(WSTG + g4 * 512, 512), psb[:, :]),
               ["psb"], bkeys(WSTG + g4 * 512, 512) + ["stgAc"])
    dma("sync", AcH[:, :], Bv(WSTG, 4096), ["stgAc"] + bkeys(WSTG, 4096), ["AcHk"])
    for n in range(8):
        dma("gpsimd", Bv(STGB, 512),
            lora_B[n * D:(n + 1) * D, :].rearrange("(bb p) r -> p bb r", p=128),
            ["stgA", "stgBn"] + bkeys(STGB, 512), bkeys(STGB, 512) + ["stgBn"])
        for b4 in range(8):
            for j in range(4):
                bb = b4 * 4 + j
                tr(psb[0:16, j * 128:(j + 1) * 128], Bv(STGB + bb * 16, 16), 128,
                   ["stgBn"] + bkeys(STGB, 512), ["psb"])
            B.emit("vector",
                   lambda e, b4=b4: e.tensor_copy(
                       BIGB[0:16, STG16 + b4 * 512: STG16 + (b4 + 1) * 512],
                       psb[0:16, 0:512]),
                   ["psb"] + bkeys(STG16, 4096), bkeys(STG16, 4096) + ["stg16"])
        dma("sync", BtH[n * 16:(n + 1) * 16, :], BIGB[0:16, STG16:STG16 + 4096],
            ["stg16"] + bkeys(STG16, 4096), [f"BtHk{n}"])
    bt_keys = [f"BtHk{n}" for n in range(8)]

    def layernorm_block(xin_ap, t, dst_base, dst_stride, dst_keys_fn, xin_keys, tag):
        sl = t % 2
        xb = Bv(XBF[sl], 4096)
        kx = bkeys(XBF[sl], 4096)
        dma("gpsimd", xb, xin_ap[t * 128:(t + 1) * 128, :], kx + xin_keys, kx,
            tag=f"{tag}x{t}")
        junk = Bv(SQS, 4096)
        kj = bkeys(SQS, 4096)
        act(junk, xb, AF.Copy, kx + kj, kj + [f"sum{sl}"],
            accum=stat[:, sl * 8:sl * 8 + 1], tag=f"{tag}cp{t}")
        act(junk, xb, AF.Square, kx + kj, kj + [f"sumsq{sl}"],
            accum=stat[:, sl * 8 + 1:sl * 8 + 2], tag=f"{tag}sq{t}")
        st = stat[:, sl * 8:sl * 8 + 8]
        vec(lambda e, st=st: e.tensor_scalar_mul(st[:, 2:3], st[:, 0:1], 1.0 / D),
            [f"sum{sl}"], [f"mean{sl}"])
        vec(lambda e, st=st: e.tensor_scalar_mul(st[:, 3:4], st[:, 1:2], 1.0 / D),
            [f"sumsq{sl}"], [f"msq{sl}"])
        vec(lambda e, st=st: e.tensor_mul(st[:, 4:5], st[:, 2:3], st[:, 2:3]),
            [f"mean{sl}"], [f"m2{sl}"])
        vec(lambda e, st=st: e.tensor_sub(st[:, 3:4], st[:, 3:4], st[:, 4:5]),
            [f"msq{sl}", f"m2{sl}"], [f"var{sl}"])
        act(st[:, 4:5], st[:, 3:4], AF.Ln, [f"var{sl}", f"m2{sl}", "eps_col"],
            [f"lnv{sl}"], bias=eps_col[:, 0:1])
        act(st[:, 5:6], st[:, 4:5], AF.Exp, [f"lnv{sl}"], [f"rstd0{sl}"], scale=-0.5)
        vec(lambda e, st=st: e.tensor_mul(st[:, 6:7], st[:, 5:6], st[:, 5:6]),
            [f"rstd0{sl}"], [f"r2{sl}"])
        vec(lambda e, st=st: e.tensor_scalar_add(st[:, 4:5], st[:, 3:4], EPS),
            [f"var{sl}", f"lnv{sl}"], [f"vpe{sl}"])
        vec(lambda e, st=st: e.tensor_mul(st[:, 6:7], st[:, 6:7], st[:, 4:5]),
            [f"r2{sl}", f"vpe{sl}"], [f"r2{sl}"])
        vec(lambda e, st=st: e.tensor_scalar(st[:, 6:7], st[:, 6:7], -0.5, 1.5,
                                             ALU.mult, ALU.add),
            [f"r2{sl}"], [f"r2{sl}"])
        vec(lambda e, st=st: e.tensor_mul(st[:, 5:6], st[:, 5:6], st[:, 6:7]),
            [f"rstd0{sl}", f"r2{sl}"], [f"rstd{sl}"])
        vec(lambda e, st=st: e.tensor_mul(st[:, 7:8], st[:, 2:3], st[:, 5:6]),
            [f"mean{sl}", f"rstd{sl}"], [f"mr{sl}"])
        vec(lambda e, st=st, sl=sl: e.tensor_scalar_mul(diag_bf[sl][:, :], ident[:, :],
                                                        st[:, 5:6]),
            ["ident", f"rstd{sl}"], [f"diag{sl}"])
        vec(lambda e, st=st, sl=sl: e.tensor_copy(bcol_bf[sl][:, :], st[:, 7:8]),
            [f"mr{sl}"], [f"bcol{sl}"])
        pbr = psb[0:1, sl * 128:sl * 128 + 128]
        tr(pbr, bcol_bf[sl][:, :], 128, [f"bcol{sl}"], ["psb"])
        act(brow_bf[sl][:, :], pbr, AF.Copy, ["psb"], [f"brow{sl}"], scale=-1.0)
        for g4 in range(8):
            bank = ps[g4 % 2]
            bkey = PK[g4 % 2]
            for j in range(4):
                db = g4 * 4 + j
                mm(bank[:, j * 128:(j + 1) * 128], xb[:, db * 128:(db + 1) * 128],
                   diag_bf[sl][:, :], kx + [f"diag{sl}"], [bkey], start=True, stop=False)
                mm(bank[:, j * 128:(j + 1) * 128], ones_row[0:1, 0:128],
                   brow_bf[sl][0:1, :], ["ones_row", f"brow{sl}"], [bkey],
                   start=False, stop=True)
            off = dst_base(g4 * 4) + 0
            dst_ap = bass.AP(BIGA, off, [[32 * AG, 128], [dst_stride, 4], [1, 128]])
            src_ap = bass.AP(bank, 0, [[512, 128], [128, 4], [1, 128]])
            vec(lambda e, dst_ap=dst_ap, src_ap=src_ap: e.tensor_copy(dst_ap, src_ap),
                [bkey], dst_keys_fn(g4), tag=f"{tag}ev{t}_{g4}")

    def wprep_steps():
        steps = []
        for i in range(NB):
            sl = i % 2

            def step(i=i, sl=sl):
                dma("gpsimd", Bv(WCOL[sl], 4096),
                    ffn_W[:, i * 128:(i + 1) * 128].rearrange("(b p) d -> p b d", p=128),
                    bkeys(WCOL[sl], 4096), bkeys(WCOL[sl], 4096), tag=f"wp{i}")
                for b4 in range(8):
                    for j in range(4):
                        b = b4 * 4 + j
                        tr(psb[:, j * 128:(j + 1) * 128], Bv(WCOL[sl] + b * 128, 128),
                           128, bkeys(WCOL[sl], 4096), ["psb"])
                    B.emit("scalar",
                           lambda e, b4=b4: e.copy(Bv(WSTG + b4 * 512, 512), psb[:, :]),
                           ["psb"], bkeys(WSTG + b4 * 512, 512) + ["wstg"])
                dma("sync", WT[i * 128:(i + 1) * 128, :], Bv(WSTG, 4096),
                    ["wstg"], [f"WTr{i}"])

            steps.append(step)
        return steps

    def phase_a():
        for t in range(NT):
            layernorm_block(
                x0, t,
                dst_base=lambda db, t=t: db * AG + t * 128,
                dst_stride=AG,
                dst_keys_fn=lambda g4: [f"A{g4 * 4 + j}" for j in range(4)],
                xin_keys=[], tag="a1")
        for h in range(H):
            qwk = bkeys(QW, 4096)
            kwk = bkeys(KW, 4096)
            vwk = bkeys(VW2, 4096)
            dma("gpsimd", Bv(QW, 4096),
                wq[:, h * 192: h * 192 + 128].rearrange("(b p) e -> p b e", p=128),
                qwk, qwk, tag=f"qw{h}")
            dma("gpsimd", Bv(KW, 4096),
                wkv[:, h * 128: (h + 1) * 128].rearrange("(b p) e -> p b e", p=128),
                kwk, kwk, tag=f"kw{h}")
            dma("gpsimd", Bv(VW2, 4096),
                wkv[:, 4096 + h * 128: 4096 + (h + 1) * 128].rearrange(
                    "(b p) e -> p b e", p=128),
                vwk, vwk, tag=f"vw{h}")
            vtk = bkeys(VT, 2048)
            vsk = bkeys(VS, 2048)
            for c in range(4):
                bv_ = ps[2 + c % 2]
                for db in range(NB):
                    mm(bv_[:, :], Bv(VW2 + db * 128, 128), A(db * AG + c * 512, 512),
                       vwk + [f"A{db}"], [PK[2 + c % 2]], start=(db == 0),
                       stop=(db == NB - 1))
                vec(lambda e, bv_=bv_, c=c: e.tensor_copy(Bv(VT + c * 512, 512),
                                                          bv_[:, :]),
                    [PK[2 + c % 2]] + vtk, vtk)

            def v_transpose(g4):
                pb = psb if g4 % 2 == 0 else psb2
                pbk = "psb" if g4 % 2 == 0 else "psb2"
                for j in range(4):
                    kb = g4 * 4 + j
                    tr(pb[:, j * 128:(j + 1) * 128], Bv(VT + kb * 128, 128), 128,
                       vtk, [pbk])
                vec(lambda e, g4=g4, pb=pb: e.tensor_copy(Bv(VS + g4 * 512, 512),
                                                          pb[:, :]),
                    [pbk] + vsk, vsk)

            for c in range(4):
                bq, bk = ps[4], ps[5]
                for db in range(NB):
                    mm(bq[:, :], Bv(QW + db * 128, 128), A(db * AG + c * 512, 512),
                       qwk + [f"A{db}"], [PK[4]], start=(db == 0), stop=(db == NB - 1))
                v_transpose(c)
                for db in range(NB):
                    mm(bk[:, :], Bv(KW + db * 128, 128), A(db * AG + c * 512, 512),
                       kwk + [f"A{db}"], [PK[5]], start=(db == 0), stop=(db == NB - 1))
                act(Qt_sb[:, c * 512:(c + 1) * 512], bq[:, :], AF.Copy, [PK[4]], ["Qt"])
                act(Kt_sb[:, c * 512:(c + 1) * 512], bk[:, :], AF.Copy, [PK[5]], ["Kt"])
            deferred = [None]

            def fin_mms(c, ub):
                bb_i = (0, 1, 4, 5)[(4 * c + 5) % 4]
                bB = ps[bb_i]
                mm(bB[:, :], ones_row[0:1, 0:128], lhi[0:1, :], ["ones_row", "lhi"],
                   [PK[bb_i]], start=True, stop=False)
                mm(bB[:, :], ones_row[0:1, 0:128], llo[0:1, :], ["ones_row", "llo"],
                   [PK[bb_i]], start=False, stop=True)
                vec(lambda e, bB=bB: e.tensor_copy(bcast_sb[:, :], bB[:, :]), [PK[bb_i]],
                    ["bcast"])
                vec(lambda e, c=c, ub=ub: e.tensor_mul(Uh_sb[:, c * 512:(c + 1) * 512],
                                                       ps[ub][:, :], bcast_sb[:, :]),
                    [PK[ub], "bcast"], ["Uh"], tag=f"Un{h}_{c}")

            upend = []

            def u_flush(n):
                while upend and len(upend) > n:
                    upend.pop(0)()

            for c in range(4):
                nkb = 4 * c + 4
                ub = 2 + c % 2
                ROT = (0, 1, 4, 5)
                for kb in range(nkb):
                    u_flush(2)
                    bS = ps[ROT[kb % 4]]
                    skey = PK[ROT[kb % 4]]
                    ekey = f"expS{kb % 4}"
                    mm(bS[:, :], Kt_sb[:, kb * 128:(kb + 1) * 128],
                       Qt_sb[:, c * 512:(c + 1) * 512], ["Kt", "Qt"], [skey],
                       start=True, stop=True, tag=f"sc{h}_{c}_{kb}")
                    act(expS[kb % 4][:, :], bS[:, :], AF.Exp, [skey], [ekey],
                        scale=SCALE_A, tag=f"exp{h}_{c}_{kb}")
                    if kb >= 4 * c:
                        m = kb - 4 * c
                        vec(lambda e, kb=kb, m=m: e.tensor_mul(
                            expS[kb % 4][:, :], expS[kb % 4][:, :],
                            masks[:, 384 - 128 * m: 896 - 128 * m]),
                            [ekey, "masks"], [ekey], tag=f"msk{h}_{c}_{kb}")
                    def u_mm(kb=kb, c=c, ub=ub, nkb=nkb, ekey=ekey):
                        mm(ps[ub][:, :], Bv(VS + kb * 128, 128), expS[kb % 4][:, :],
                           vsk + [ekey], [PK[ub]], start=(kb == 0),
                           stop=(kb == nkb - 1))
                    upend.append(u_mm)
                    if kb == 0:
                        vec(lambda e, kb=kb: e.tensor_copy(lacc[:, :], expS[kb % 4][:, :]),
                            [ekey, "lacc"], ["lacc"])
                    else:
                        vec(lambda e, kb=kb: e.tensor_add(lacc[:, :], lacc[:, :],
                                                          expS[kb % 4][:, :]),
                            [ekey, "lacc"], ["lacc"])
                    if kb == 5 and deferred[0] is not None:
                        deferred[0]()
                        deferred[0] = None
                u_flush(0)
                lb = ROT[nkb % 4]
                mm(ps[lb][0:1, :], oncf[:, :], lacc[:, :], ["oncf", "lacc"], [PK[lb]],
                   start=True, stop=True)
                if deferred[0] is not None:
                    deferred[0]()
                    deferred[0] = None
                act(lrow[:, :], ps[lb][0:1, :], AF.Ln, [PK[lb], "lrow"], ["lrow"])
                act(lrec[:, :], lrow[:, :], AF.Exp, ["lrow", "coef_row"], ["lrec"],
                    scale=-1.0)
                act(lhi[:, :], lrec[:, :], AF.Copy, ["lrec", "lhi"], ["lhi"])
                vec(lambda e: e.tensor_sub(lrow[:, :], lrec[:, :], lhi[:, :]),
                    ["lrec", "lhi", "lrow"], ["llo_f"])
                vec(lambda e: e.tensor_copy(llo[:, :], lrow[:, :]), ["llo_f"], ["llo"])
                deferred[0] = (lambda c=c, ub=ub: fin_mms(c, ub))
            if deferred[0] is not None:
                deferred[0]()
                deferred[0] = None
            dma("sync", Ua[h * 128:(h + 1) * 128, :], Uh_sb[:, :], ["Uh"], [f"UaH{h}"],
                tag=f"spill{h}")
        ua_ready = [f"UaH{h}" for h in range(H)]
        wsteps = wprep_steps()
        wi = 0
        outk = []
        for Q in range(4):
            for e_blk in range(NB):
                wkey = akeys(e_blk * 1024, 1024)
                dma("gpsimd", A(e_blk * 1024, 1024),
                    wo_a[e_blk * 128:(e_blk + 1) * 128, Q * 1024:(Q + 1) * 1024],
                    wkey, wkey, tag=f"wo{Q}_{e_blk}")
            for tp in range(NT // 2):
                if wi < len(wsteps):
                    wsteps[wi]()
                    wi += 1
                if wi < len(wsteps):
                    wsteps[wi]()
                    wi += 1
                usl = tp % 2
                ubase = 32768 + usl * 8192
                uk = akeys(ubase, 8192)
                dma("sync", A(ubase, 8192),
                    Ua[:, tp * 256:(tp + 1) * 256].rearrange("(eb p) q -> p eb q", p=128),
                    ua_ready + uk, uk, tag=f"uat{Q}_{tp}")
                for tj in range(2):
                    t = tp * 2 + tj
                    for oc in range(2):
                        rsl = oc
                        dma("sync", rbuf[rsl][:, :],
                            x0[t * 128:(t + 1) * 128,
                               Q * 1024 + oc * 512: Q * 1024 + (oc + 1) * 512],
                            [f"rbuf{rsl}"], [f"rbuf{rsl}"])
                        bank = ps[oc]
                        for e_blk in range(NB):
                            mm(bank[:, :], A(ubase + e_blk * 256 + tj * 128, 128),
                               A(e_blk * 1024 + oc * 512, 512),
                               uk + akeys(e_blk * 1024 + oc * 512, 512), [PK[oc]],
                               start=(e_blk == 0), stop=(e_blk == NB - 1))
                        vec(lambda e, bank=bank, rsl=rsl: e.tensor_add(
                            nbuf[rsl][:, :], bank[:, :], rbuf[rsl][:, :]),
                            [PK[oc], f"rbuf{rsl}"], [f"nbuf{rsl}"], tag=f"woev{Q}_{t}_{oc}")
                        o8 = Q * 2 + oc
                        outk.append(f"x1r{t}_{o8}")
                        dma("sync", x1[t * 128:(t + 1) * 128, o8 * 512:(o8 + 1) * 512],
                            nbuf[rsl][:, :], [f"nbuf{rsl}"], [f"x1r{t}_{o8}"],
                            tag=f"x1st{t}_{o8}")
        while wi < len(wsteps):
            wsteps[wi]()
            wi += 1
        return outk

    def phase_f(xin_ap, xout_ap, xname, xin_name, inst):
        ack = bkeys(ACT_, 4096)
        dma("sync", Bv(ACT_, 4096), AcH[:, :], ["AcHk"] + ack, ack, tag=f"f{inst}Ac")
        dma("gpsimd", row_big[:, :], ffn_bias, ["row_big"], ["row_big"], tag=f"f{inst}fb")
        wt_keys = [f"WTr{i}" for i in range(NB)]
        outk = []
        for ch in range(2):
            for tt in range(8):
                t = ch * 8 + tt
                layernorm_block(
                    xin_ap, t,
                    dst_base=lambda db, tt=tt: db * 1024 + tt * 128,
                    dst_stride=1024,
                    dst_keys_fn=lambda g4: [f"A{(g4 * 4) // 2}", f"A{(g4 * 4 + 3) // 2}"],
                    xin_keys=[f"{xin_name}r{t}_{o}" for o in range(8)],
                    tag=f"f{inst}ln")
            for half in range(2):
                bank = ps[4 + half]
                for db in range(NB):
                    mm(bank[:, :], Bv(ACT_ + db * 128, 128),
                       A(db * 1024 + half * 512, 512),
                       ack + [f"A{(db * 1024 + half * 512) // AG}"], [PK[4 + half]],
                       start=(db == 0), stop=(db == NB - 1))
                act(hbf[:, half * 512:(half + 1) * 512], bank[:, :], AF.Copy,
                    [PK[4 + half], "coef_col"], ["hbf"], scale=coef_col[:, 0:1])
            for oc in range(8):
                wsl = oc % 2
                wck = akeys(32768 + wsl * 16384, 16384)
                dma("sync", A(32768 + wsl * 16384, 16384),
                    WT[:, oc * 512:(oc + 1) * 512].rearrange("(b p) o -> p b o", p=128),
                    wt_keys + wck, wck, tag=f"f{inst}wt{ch}_{oc}")
                dma("gpsimd", bct_sb[:, 0:512], BtH[:, oc * 512:(oc + 1) * 512],
                    bt_keys + ["Kt"], ["Kt"], tag=f"f{inst}bt{ch}_{oc}")
                for tt in range(8):
                    t = ch * 8 + tt
                    bank = ps[tt % 2]
                    for db in range(NB):
                        mm(bank[:, :], A(db * 1024 + tt * 128, 128),
                           A(32768 + wsl * 16384 + db * 512, 512),
                           [f"A{(db * 1024 + tt * 128) // AG}",
                            f"A{(32768 + wsl * 16384 + db * 512) // AG}"],
                           [PK[tt % 2]], start=(db == 0), stop=False)
                    mm(bank[:, :], hbf[:, tt * 128:(tt + 1) * 128], bct_sb[:, 0:512],
                       ["hbf", "Kt"], [PK[tt % 2]], start=False, stop=False)
                    mm(bank[:, :], ones_row[0:1, 0:128],
                       row_big[0:1, oc * 512:(oc + 1) * 512],
                       ["ones_row", "row_big"], [PK[tt % 2]], start=False, stop=True)
                    rsl = tt % 2
                    dma("scalar", rbuf[rsl][:, :],
                        xin_ap[t * 128:(t + 1) * 128, oc * 512:(oc + 1) * 512],
                        [f"{xin_name}r{t}_{oc}", f"rbuf{rsl}"], [f"rbuf{rsl}"])
                    vec(lambda e, bank=bank, rsl=rsl: e.tensor_add(
                        nbuf[rsl][:, :], bank[:, :], rbuf[rsl][:, :]),
                        [PK[tt % 2], f"rbuf{rsl}"], [f"nbuf{rsl}"])
                    k = f"{xname}r{t}_{oc}"
                    outk.append(k)
                    dma("scalar", xout_ap[t * 128:(t + 1) * 128, oc * 512:(oc + 1) * 512],
                        nbuf[rsl][:, :], [f"nbuf{rsl}"], [k])
        return outk

    def phase_b(xin_ap, xout_ap, xin_name):
        for ch in range(4):
            for tt in range(4):
                t = ch * 4 + tt
                layernorm_block(
                    xin_ap, t,
                    dst_base=lambda db, tt=tt: db * 512 + tt * 128,
                    dst_stride=512,
                    dst_keys_fn=lambda g4: [f"A{g4}"],
                    xin_keys=[f"{xin_name}r{t}_{o}" for o in range(8)],
                    tag="bln")
            bq, bkv = ps[2], ps[3]
            for db in range(NB):
                mm(bq[0:QL, :], wqd_sb[:, db * QL:(db + 1) * QL], A(db * 512, 512),
                   ["wqd_sb", f"A{db // 4}"], [PK[2]], start=(db == 0), stop=False)
            mm(bq[0:QL, :], bqd_row[0:1, :], ones_row[0:1, :], ["bqd_row", "ones_row"],
               [PK[2]], start=False, stop=True)
            for db in range(NB):
                mm(bkv[0:KV, :], wkvd_sb[:, db * KV:(db + 1) * KV], A(db * 512, 512),
                   ["wkvd_sb", f"A{db // 4}"], [PK[3]], start=(db == 0), stop=False)
            mm(bkv[0:KV, :], bkvd_row[0:1, :], ones_row[0:1, :], ["bkvd_row", "ones_row"],
               [PK[3]], start=False, stop=True)
            act(Cq_sb[0:QL, ch * 512:(ch + 1) * 512], bq[0:QL, :], AF.Copy, [PK[2]], ["Uh"])
            act(Ckv_sb[0:KV, ch * 512:(ch + 1) * 512], bkv[0:KV, :], AF.Copy, [PK[3]], ["Kt"])
        vec(lambda e: e.memset(Ckvs_sb[:, :], 0.0), ["Ckvs"], ["CkvsM"])
        for kb in range(NT):
            pkc = psb[0:128, (kb % 4) * 128:(kb % 4) * 128 + 4]
            tr(pkc, Ckv_sb[0:KV, kb * 128:(kb + 1) * 128], KV, ["Kt"], ["psb"])
            vec(lambda e, pkc=pkc, kb=kb: e.tensor_copy(
                Ckvs_sb[:, kb * 40: kb * 40 + 4], pkc),
                ["psb", "CkvsM"], ["CkvsM"])
        vec(lambda e: e.memset(bass.AP(Ckvs_sb, 32, [[NT * 40, 128], [40, NT], [1, 1]]),
                               1.0),
            ["CkvsM"], ["Ckvs"])
        dma("gpsimd", row_big[:, :], bv_u, ["row_big"], ["row_big"], tag="b_bvu")
        for h in range(H):
            sl = h % 2
            dma("gpsimd", wvu_h[:, :], Wv_u[:, h * 128:(h + 1) * 128], ["wvu_h"],
                ["wvu_h"], tag=f"wvu{h}")
            qhk = bkeys(QH, 2048)
            for c in range(4):
                bqh = ps[4]
                mm(bqh[0:KV, :], wqk_sb[:, h * KV:(h + 1) * KV],
                   Cq_sb[0:QL, c * 512:(c + 1) * 512], ["wqk_sb", "Uh"], [PK[4]],
                   start=True, stop=False)
                mm(bqh[0:KV, :], bqk_row[0:1, h * KV:(h + 1) * KV], ones_row[0:1, :],
                   ["bqk_row", "ones_row"], [PK[4]], start=False, stop=True)
                act(BIGB[0:KV, QH + c * 512: QH + (c + 1) * 512], bqh[0:KV, :], AF.Copy,
                    [PK[4]] + qhk, qhk)
            BROT = (0, 1, 4, 5)

            def b_score(kb, c, i):
                bS = ps[BROT[i % 4]]
                skey = PK[BROT[i % 4]]
                ekey = f"expS{i % 4}"
                mm(bS[:, :], Ckv_sb[0:KV, kb * 128:(kb + 1) * 128],
                   BIGB[0:KV, QH + c * 512: QH + (c + 1) * 512], ["Kt"] + qhk,
                   [skey], start=True, stop=True, tag=f"bsc{h}_{c}_{kb}")
                act(expS[i % 4][:, :], bS[:, :], AF.Exp, [skey], [ekey], scale=SCALE_B)

            def b_u(kb, c, i):
                mm(ps[2 + c % 2][0:33, :], Ckvs_sb[:, kb * 40: kb * 40 + 33],
                   expS[i % 4][:, :],
                   ["Ckvs", f"expS{i % 4}"], [PK[2 + c % 2]], start=(kb == 0),
                   stop=(kb == NT - 1))

            def b_chain(c):
                bU = ps[2 + c % 2]
                act(lrow[:, :], bU[32:33, :], AF.Ln, [PK[2 + c % 2], "lrow"], ["lrow"])
                act(lrec[:, :], lrow[:, :], AF.Exp, ["lrow", "coef_row"], ["lrec"],
                    scale=-1.0)
                act(lhi[:, :], lrec[:, :], AF.Copy, ["lrec", "lhi"], ["lhi"])
                vec(lambda e: e.tensor_sub(lrow[:, :], lrec[:, :], lhi[:, :]),
                    ["lrec", "lhi", "lrow"], ["llo_f"])
                vec(lambda e: e.tensor_copy(llo[:, :], lrow[:, :]), ["llo_f"], ["llo"])

            def b_final(c):
                bU = ps[2 + c % 2]
                vec(lambda e, bU=bU: e.tensor_copy(u5_sb[0:4, :], bU[0:4, :]),
                    [PK[2 + c % 2], "u5"], ["u5"])
                bb_i = BROT[c % 4]
                bB = ps[bb_i]
                mm(bB[:, :], ones_row[0:1, 0:128], lhi[0:1, :], ["ones_row", "lhi"],
                   [PK[bb_i]], start=True, stop=False)
                mm(bB[:, :], ones_row[0:1, 0:128], llo[0:1, :], ["ones_row", "llo"],
                   [PK[bb_i]], start=False, stop=True)
                vec(lambda e, bB=bB: e.tensor_copy(bcast_sb[:, :], bB[:, :]),
                    [PK[bb_i], "bcast"], ["bcast"])
                vec(lambda e: e.tensor_mul(ctx_bf[0:KV, 0:512], u5_sb[0:KV, :],
                                           bcast_sb[0:KV, :]),
                    ["u5", "bcast"], ["Qt"])
                bt_i = BROT[(c + 1) % 4]
                bT = ps[bt_i]
                mm(bT[:, :], wvu_h[0:KV, :], ctx_bf[0:KV, 0:512],
                   ["wvu_h", "Qt"], [PK[bt_i]], start=True, stop=False)
                mm(bT[:, :], row_big[0:1, h * 128:(h + 1) * 128], ones_row[0:1, :],
                   ["row_big", "ones_row"], [PK[bt_i]], start=False, stop=True)
                ahk = bkeys(AH[sl], 2048)
                act(Bv(AH[sl] + c * 512, 512), bT[:, :], AF.Copy,
                    [PK[bt_i]] + ahk, ahk)

            for cpass in range(2):
                cA, cB = 2 * cpass, 2 * cpass + 1
                flat = [(kb, (cA, cB)[j], 2 * kb + j) for kb in range(NT)
                        for j in range(2)]
                for idx, (kb, c, i) in enumerate(flat):
                    if idx >= 4:
                        b_u(*flat[idx - 4])
                    b_score(kb, c, i)
                for idx in range(len(flat) - 4, len(flat)):
                    b_u(*flat[idx])
                    if idx == len(flat) - 2:
                        b_chain(cA)
                b_final(cA)
                b_chain(cB)
                b_final(cB)
            dma("sync", Abh[h * 128:(h + 1) * 128, :], Bv(AH[sl], 2048),
                bkeys(AH[sl], 2048), [f"AbH{h}"])
        ab_ready = [f"AbH{h}" for h in range(H)]
        dma("gpsimd", row_big[:, :], bo_b, ["row_big"], ["row_big"], tag="b_bo")
        outk = []
        for Q in range(4):
            for e_blk in range(NB):
                wkey = akeys(e_blk * 1024, 1024)
                dma("gpsimd", A(e_blk * 1024, 1024),
                    Wo_b[e_blk * 128:(e_blk + 1) * 128, Q * 1024:(Q + 1) * 1024],
                    wkey, wkey, tag=f"bwo{Q}_{e_blk}")
            for tp in range(NT // 2):
                usl = tp % 2
                ubase = 32768 + usl * 8192
                uk = akeys(ubase, 8192)
                dma("sync", A(ubase, 8192),
                    Abh[:, tp * 256:(tp + 1) * 256].rearrange("(eb p) q -> p eb q", p=128),
                    ab_ready + uk, uk)
                for tj in range(2):
                    t = tp * 2 + tj
                    for oc in range(2):
                        rsl = oc
                        o8 = Q * 2 + oc
                        dma("sync", rbuf[rsl][:, :],
                            xin_ap[t * 128:(t + 1) * 128, o8 * 512:(o8 + 1) * 512],
                            [f"{xin_name}r{t}_{o8}", f"rbuf{rsl}"], [f"rbuf{rsl}"])
                        bank = ps[oc]
                        for e_blk in range(NB):
                            mm(bank[:, :], A(ubase + e_blk * 256 + tj * 128, 128),
                               A(e_blk * 1024 + oc * 512, 512),
                               uk + akeys(e_blk * 1024 + oc * 512, 512), [PK[oc]],
                               start=(e_blk == 0), stop=False)
                        mm(bank[:, :], ones_row[0:1, 0:128],
                           row_big[0:1, o8 * 512:(o8 + 1) * 512],
                           ["ones_row", "row_big"], [PK[oc]], start=False, stop=True)
                        vec(lambda e, bank=bank, rsl=rsl: e.tensor_add(
                            nbuf[rsl][:, :], bank[:, :], rbuf[rsl][:, :]),
                            [PK[oc], f"rbuf{rsl}"], [f"nbuf{rsl}"])
                        outk.append(f"x3r{t}_{o8}")
                        dma("sync", xout_ap[t * 128:(t + 1) * 128, o8 * 512:(o8 + 1) * 512],
                            nbuf[rsl][:, :], [f"nbuf{rsl}"], [f"x3r{t}_{o8}"])
        return outk

    if upto == "b0":
        keys = phase_a()
        k2 = phase_f(x1, x2, "x2", "x1", 1)
        k3 = phase_b(x2, x3, "x2")
        final = []
        final = list(k3)
        for i in range(32):
            sl = i % 2
            kk = bkeys(UAT[sl], 4096)
            dma("sync", Bv(UAT[sl], 2048), Abh[i * 128:(i + 1) * 128, :],
                k3 + kk, kk)
            dma("sync", dbg[:, i * 2048:(i + 1) * 2048], Bv(UAT[sl], 2048), kk,
                [f"dg{i}"])
            final.append(f"dg{i}")
        dma("sync", y[0:128, 0:512], rbuf[0][:, :], ["rbuf0"], ["yk"])
        final.append("yk")
    elif upto == "wt":
        wsteps = wprep_steps()
        for st_ in wsteps:
            st_()
        final = []
        for i in range(8):
            sl = i % 2
            kk = bkeys(UAT[sl], 4096)
            dma("sync", Bv(UAT[sl], 4096), WT[i * 128:(i + 1) * 128, :],
                [f"WTr{i}"] + kk, kk)
            dma("sync", dbg[:, i * 4096:(i + 1) * 4096], Bv(UAT[sl], 4096), kk,
                [f"dw{i}"])
            final.append(f"dw{i}")
        ack = bkeys(ACT_, 4096)
        dma("sync", Bv(ACT_, 4096), AcH[:, :], ["AcHk"] + ack, ack)
        dma("sync", dbg[:, 8 * 4096: 8 * 4096 + 4096], Bv(ACT_, 4096), ack, ["dA"])
        final.append("dA")
        bck = bkeys(XBF[0], 4096)
        dma("sync", Bv(XBF[0], 4096), BtH[:, :], bt_keys + bck, bck)
        dma("sync", dbg[:, 9 * 4096: 9 * 4096 + 4096], Bv(XBF[0], 4096), bck, ["dB"])
        final.append("dB")
        vec(lambda e: e.tensor_copy(expS[0][:, 0:1], coef_col[:, 0:1]),
            ["coef_col", "expS0"], ["expS0"])
        vec(lambda e: e.memset(expS[0][:, 1:512], 0.0), ["expS0"], ["expS0"])
        dma("sync", dbg[:, 10 * 4096: 10 * 4096 + 512], expS[0][:, 0:512], ["expS0"], ["dC"])
        final.append("dC")
        dma("sync", y[0:128, 0:512], rbuf[0][:, :], ["rbuf0"], ["yk"])
        final.append("yk")
    elif upto == "ua":
        keys = phase_a()
        final = []
        for i in range(32):
            sl = i % 2
            kk = bkeys(UAT[sl], 4096)
            dma("sync", Bv(UAT[sl], 2048), Ua[i * 128:(i + 1) * 128, :],
                keys + kk, kk)
            dma("sync", dbg[:, i * 2048:(i + 1) * 2048], Bv(UAT[sl], 2048), kk,
                [f"dg{i}"])
            final.append(f"dg{i}")
        dma("sync", y[0:128, 0:512], rbuf[0][:, :], ["rbuf0"], ["yk"])
        final.append("yk")
    elif upto == "h0":
        for t in range(NT):
            layernorm_block(
                x0, t,
                dst_base=lambda db, t=t: db * AG + t * 128,
                dst_stride=AG,
                dst_keys_fn=lambda g4: [f"A{g4 * 4 + j}" for j in range(4)],
                xin_keys=[], tag="a1")
        h = 0
        qwk = bkeys(QW, 4096)
        kwk = bkeys(KW, 4096)
        vwk = bkeys(VW2, 4096)
        dma("gpsimd", Bv(QW, 4096),
            wq[:, h * 192: h * 192 + 128].rearrange("(b p) e -> p b e", p=128),
            qwk, qwk)
        dma("gpsimd", Bv(KW, 4096),
            wkv[:, h * 128: (h + 1) * 128].rearrange("(b p) e -> p b e", p=128),
            kwk, kwk)
        dma("gpsimd", Bv(VW2, 4096),
            wkv[:, 4096 + h * 128: 4096 + (h + 1) * 128].rearrange(
                "(b p) e -> p b e", p=128),
            vwk, vwk)
        vtk = bkeys(VT, 2048)
        vsk = bkeys(VS, 2048)
        for c in range(4):
            bv_ = ps[2 + c % 2]
            for db in range(NB):
                mm(bv_[:, :], Bv(VW2 + db * 128, 128), A(db * AG + c * 512, 512),
                   vwk + [f"A{db}"], [PK[2 + c % 2]], start=(db == 0),
                   stop=(db == NB - 1))
            act(Bv(VT + c * 512, 512), bv_[:, :], AF.Copy, [PK[2 + c % 2]] + vtk, vtk)
        for g4 in range(4):
            for j in range(4):
                kb = g4 * 4 + j
                tr(psb[:, j * 128:(j + 1) * 128], Bv(VT + kb * 128, 128), 128,
                   vtk, ["psb"])
            B.emit("scalar",
                   lambda e, g4=g4: e.copy(Bv(VS + g4 * 512, 512), psb[:, :]),
                   ["psb"] + vsk, vsk)
        for c in range(4):
            bq, bk = ps[4], ps[5]
            for db in range(NB):
                mm(bq[:, :], Bv(QW + db * 128, 128), A(db * AG + c * 512, 512),
                   qwk + [f"A{db}"], [PK[4]], start=(db == 0), stop=(db == NB - 1))
            for db in range(NB):
                mm(bk[:, :], Bv(KW + db * 128, 128), A(db * AG + c * 512, 512),
                   kwk + [f"A{db}"], [PK[5]], start=(db == 0), stop=(db == NB - 1))
            act(Qt_sb[:, c * 512:(c + 1) * 512], bq[:, :], AF.Copy, [PK[4]], ["Qt"])
            act(Kt_sb[:, c * 512:(c + 1) * 512], bk[:, :], AF.Copy, [PK[5]], ["Kt"])
        for c in range(4):
            nkb = 4 * c + 4
            bU, bL = ps[2], ps[3]
            for kb in range(nkb):
                bS = ps[kb % 2]
                skey = PK[kb % 2]
                ekey = f"expS{kb % 2}"
                mm(bS[:, :], Kt_sb[:, kb * 128:(kb + 1) * 128],
                   Qt_sb[:, c * 512:(c + 1) * 512], ["Kt", "Qt"], [skey],
                   start=True, stop=True)
                act(expS[kb % 2][:, :], bS[:, :], AF.Exp, [skey], [ekey],
                    scale=SCALE_A)
                if kb >= 4 * c:
                    m = kb - 4 * c
                    gp(lambda e, kb=kb, m=m: e.affine_select(
                        expS[kb % 2][:, :], expS[kb % 2][:, :], [[1, 512]],
                        ALU.is_ge, 0.0, base=-128 * m, channel_multiplier=-1),
                        [ekey], [ekey])
                mm(bU[:, :], Bv(VS + kb * 128, 128), expS[kb % 2][:, :],
                   vsk + [ekey], [PK[2]], start=(kb == 0), stop=(kb == nkb - 1))
                mm(bL[0:1, :], ones_col[:, :], expS[kb % 2][:, :],
                   ["ones_col", ekey], [PK[3]],
                   start=(kb == 0), stop=(kb == nkb - 1))
            vec(lambda e: e.tensor_copy(lrow[:, :], ps[3][0:1, :]), [PK[3]], ["lrow"])
            vec(lambda e: e.reciprocal(lrec[:, :], lrow[:, :]), ["lrow", "coef_row"],
                ["lrec"])
            vec(lambda e: e.tensor_copy(lhi[:, :], lrec[:, :]), ["lrec"], ["lhi"])
            vec(lambda e: e.tensor_sub(lrow[:, :], lrec[:, :], lhi[:, :]),
                ["lrec", "lhi", "lrow"], ["llo_f"])
            vec(lambda e: e.tensor_copy(llo[:, :], lrow[:, :]), ["llo_f"], ["llo"])
            bB = ps[4]
            mm(bB[:, :], ones_row[0:1, 0:128], lhi[0:1, :], ["ones_row", "lhi"],
               [PK[4]], start=True, stop=False)
            mm(bB[:, :], ones_row[0:1, 0:128], llo[0:1, :], ["ones_row", "llo"],
               [PK[4]], start=False, stop=True)
            vec(lambda e: e.tensor_copy(bcast_sb[:, :], ps[4][:, :]), [PK[4]],
                ["bcast"])
            vec(lambda e, c=c: e.tensor_mul(Uh_sb[:, c * 512:(c + 1) * 512],
                                            ps[2][:, :], bcast_sb[:, :]),
                [PK[2], "bcast"], ["Uh"])
        dma("sync", dbg[:, 0:2048], Qt_sb[:, :], ["Qt"], ["d0"])
        dma("sync", dbg[:, 2048:4096], Kt_sb[:, :], ["Kt"], ["d1"])
        dma("sync", dbg[:, 4096:6144], Bv(VS, 2048), vsk, ["d2"])
        dma("sync", dbg[:, 6144:8192], Uh_sb[:, :], ["Uh"], ["d3"])
        dma("sync", y[0:128, 0:512], rbuf[0][:, :], ["rbuf0"], ["yk"])
        final = ["d0", "d1", "d2", "d3", "yk"]
    elif upto == "ln":
        for t in range(NT):
            layernorm_block(
                x0, t,
                dst_base=lambda db, t=t: db * AG + t * 128,
                dst_stride=AG,
                dst_keys_fn=lambda g4: [f"A{g4 * 4 + j}" for j in range(4)],
                xin_keys=[], tag="a1")
        dma("sync", dbg, BIGA[:, :], [f"A{g}" for g in range(32)], ["dbgk"])
        dma("sync", y[0:128, 0:512], rbuf[0][:, :], ["rbuf0"], ["yk"])
        final = ["dbgk", "yk"]
    elif upto == "a":
        keys = phase_a()
        for t in range(NT):
            for o in range(8):
                sl = o % 2
                dma("sync", rbuf[sl][:, :], x1[t * 128:(t + 1) * 128, o * 512:(o + 1) * 512],
                    keys + [f"rbuf{sl}"], [f"rbuf{sl}"])
                dma("sync", y[t * 128:(t + 1) * 128, o * 512:(o + 1) * 512], rbuf[sl][:, :],
                    [f"rbuf{sl}"], [f"yk{t}_{o}"])
        final = [f"yk{t}_{o}" for t in range(NT) for o in range(8)]
    elif upto == "f1":
        keys = phase_a()
        final = phase_f(x1, y, "y", "x1", 1)
    elif upto == "b":
        keys = phase_a()
        k2 = phase_f(x1, x2, "x2", "x1", 1)
        final = phase_b(x2, y, "x2")
    else:
        keys = phase_a()
        k2 = phase_f(x1, x2, "x2", "x1", 1)
        k3 = phase_b(x2, x3, "x2")
        final = phase_f(x3, y, "y", "x3", 2)
    B.emit("sync", lambda e: e.nop(), reads=final, writes=[])
    B.finalize_and_emit(stack)
    stack.close()
    return nc


_NC_CACHE = {}


def _get_nc(upto="full"):
    if upto not in _NC_CACHE:
        _NC_CACHE[upto] = build_program(upto)
    return _NC_CACHE[upto]


def _in_maps(inputs):
    c = np.ascontiguousarray
    hs = inputs["L0_attn_a_hidden_states"]
    base = {
        "wq": c(inputs["L0_attn_a_wq"]),
        "wkv": c(inputs["L0_attn_a_wkv"]),
        "wo_a": c(inputs["L0_attn_a_wo"]),
        "Wq_d": c(inputs["L0_attn_b_Wq_d"]),
        "bq_d": c(inputs["L0_attn_b_bq_d"]).reshape(1, QL),
        "W_qk": c(inputs["L0_attn_b_W_qk"]),
        "b_qk": c(inputs["L0_attn_b_b_qk"]).reshape(1, 128),
        "Wkv_d": c(inputs["L0_attn_b_Wkv_d"]),
        "bkv_d": c(inputs["L0_attn_b_bkv_d"]).reshape(1, KV),
        "Wv_u": c(inputs["L0_attn_b_Wv_u"]),
        "bv_u": c(inputs["L0_attn_b_bv_u"]).reshape(1, D),
        "Wo_b": c(inputs["L0_attn_b_Wo"]),
        "bo_b": c(inputs["L0_attn_b_bo"]).reshape(1, D),
        "lora_A": c(inputs["L0_ffn_lora_A"]).reshape(128, D),
        "lora_B": c(inputs["L0_ffn_lora_B"]).reshape(8 * D, 16),
        "ffn_W": c(inputs["L0_ffn_W"]),
        "ffn_bias": c(inputs["L0_ffn_bias"]).reshape(1, D),
        "lora_w": c(inputs["L0_ffn_lora_weights"]).reshape(1, 8),
    }
    maps = []
    for b in range(8):
        m = dict(base)
        m["x0"] = c(np.asarray(hs[b]))
        maps.append(m)
    return maps


def kernel(**inputs):
    nc = _get_nc("full")
    maps = _in_maps(inputs)
    res = run_bass_kernel_spmd(nc, maps, core_ids=list(range(8)))
    out = np.stack([np.asarray(r["y"]).astype(np.float32) for r in res.results], axis=0)
    return out


# version 09
# speedup vs baseline: 1.1875x; 1.0055x over previous
import numpy as np
import ml_dtypes
import concourse.bass as bass
import concourse.mybir as mybir
from concourse.bass_utils import run_bass_kernel_spmd

F32 = mybir.dt.float32
BF16 = mybir.dt.bfloat16
AF = mybir.ActivationFunctionType
ALU = mybir.AluOpType

S = 2048
D = 4096
NB = D // 128
NT = S // 128
H = 32
SCALE_A = float(192.0 ** -0.5)
QL, KV = 12, 4
SCALE_B = 0.5
EPS = 1e-5
AG = 2048


class Op:
    __slots__ = ("engine", "fn", "dma", "deps", "needed", "rank", "waits",
                 "inc_sem", "inc_amt", "tag")

    def __init__(self, engine, fn, dma, tag):
        self.engine = engine
        self.fn = fn
        self.dma = dma
        self.deps = []
        self.needed = False
        self.rank = 0
        self.waits = []
        self.inc_sem = None
        self.inc_amt = 0
        self.tag = tag


class Builder:
    ENGINES = ("tensor", "vector", "scalar", "gpsimd", "sync")

    def __init__(self, nc):
        self.nc = nc
        self.ops = {e: [] for e in self.ENGINES}
        self.last_write = {}
        self.readers = {}

    def emit(self, engine, fn, reads=(), writes=(), dma=False, tag=""):
        op = Op(engine, fn, dma, tag)
        deps = {}
        for k in reads:
            w = self.last_write.get(k)
            if w is not None:
                deps[id(w)] = w
        for k in writes:
            w = self.last_write.get(k)
            if w is not None:
                deps[id(w)] = w
            for r in self.readers.get(k, ()):
                deps[id(r)] = r
        for d in deps.values():
            if d is op:
                continue
            if d.engine == "tensor" and engine == "tensor" and not d.dma and not dma:
                continue
            op.deps.append(d)
        for k in reads:
            self.readers.setdefault(k, []).append(op)
        for k in writes:
            self.last_write[k] = op
            self.readers[k] = []
        self.ops[engine].append(op)
        return op

    LANES = 8

    def finalize_and_emit(self, stack):
        nc = self.nc
        csems = {e: stack.enter_context(nc.semaphore(f"s_{e}"))
                 for e in ("tensor", "vector", "scalar", "sync")}
        lsems = {}
        for q in ("sync", "gpsimd", "scalar"):
            lsems[("d", q)] = [stack.enter_context(nc.semaphore(f"d_{q}{i}"))
                               for i in range(self.LANES)]
        lsems[("c", "gpsimd")] = [stack.enter_context(nc.semaphore(f"g_{i}"))
                                  for i in range(self.LANES)]
        for e in self.ENGINES:
            for op in self.ops[e]:
                for d in op.deps:
                    d.needed = True
        counters = {}
        for e in self.ENGINES:
            for op in self.ops[e]:
                if op.dma:
                    key = ("d", e)
                    i = counters.get(key, 0)
                    counters[key] = i + 1
                    op.rank = i
                    op.inc_sem = lsems[key][i % self.LANES]
                    op.inc_amt = 16
                    if i >= self.LANES:
                        op.waits.append((op.inc_sem, 16 * (i // self.LANES)))
                elif e == "gpsimd":
                    key = ("c", "gpsimd")
                    i = counters.get(key, 0)
                    counters[key] = i + 1
                    op.rank = i
                    op.inc_sem = lsems[key][i % self.LANES]
                    op.inc_amt = 1
                    if i >= self.LANES:
                        op.waits.append((op.inc_sem, i // self.LANES))
                elif op.needed:
                    key = ("c", e)
                    counters[key] = counters.get(key, 0) + 1
                    op.rank = counters[key]
                    op.inc_sem = csems[e]
                    op.inc_amt = 1
        for e in self.ENGINES:
            known = {}
            for op in self.ops[e]:
                for d in op.deps:
                    if d.dma:
                        sem = lsems[("d", d.engine)][d.rank % self.LANES]
                        val = 16 * (d.rank // self.LANES + 1)
                    elif d.engine == "gpsimd":
                        sem = lsems[("c", "gpsimd")][d.rank % self.LANES]
                        val = d.rank // self.LANES + 1
                    else:
                        sem, val = csems[d.engine], d.rank
                    kid = id(sem)
                    if known.get(kid, -1) < val:
                        op.waits.append((sem, val))
                        known[kid] = val
        blk = stack.enter_context(nc.Block())
        for ename in self.ENGINES:
            ops = self.ops[ename]

            def section(eng, ops=ops):
                for op in ops:
                    for (sem, val) in op.waits:
                        eng.wait_ge(sem, val)
                    inst = op.fn(eng)
                    if op.inc_sem is not None:
                        inst.then_inc(op.inc_sem, op.inc_amt)

            getattr(blk, ename)(section)


def build_program(upto="full"):
    nc = bass.Bass("TRN2", target_bir_lowering=False)
    x0 = nc.dram_tensor("x0", [S, D], F32, kind="ExternalInput").ap()
    wq = nc.dram_tensor("wq", [D, 6144], F32, kind="ExternalInput").ap()
    wkv = nc.dram_tensor("wkv", [D, 8256], F32, kind="ExternalInput").ap()
    wo_a = nc.dram_tensor("wo_a", [D, D], F32, kind="ExternalInput").ap()
    Wq_d = nc.dram_tensor("Wq_d", [D, QL], F32, kind="ExternalInput").ap()
    bq_d = nc.dram_tensor("bq_d", [1, QL], F32, kind="ExternalInput").ap()
    W_qk = nc.dram_tensor("W_qk", [QL, 128], F32, kind="ExternalInput").ap()
    b_qk = nc.dram_tensor("b_qk", [1, 128], F32, kind="ExternalInput").ap()
    Wkv_d = nc.dram_tensor("Wkv_d", [D, KV], F32, kind="ExternalInput").ap()
    bkv_d = nc.dram_tensor("bkv_d", [1, KV], F32, kind="ExternalInput").ap()
    Wv_u = nc.dram_tensor("Wv_u", [KV, D], F32, kind="ExternalInput").ap()
    bv_u = nc.dram_tensor("bv_u", [1, D], F32, kind="ExternalInput").ap()
    Wo_b = nc.dram_tensor("Wo_b", [D, D], F32, kind="ExternalInput").ap()
    bo_b = nc.dram_tensor("bo_b", [1, D], F32, kind="ExternalInput").ap()
    lora_A = nc.dram_tensor("lora_A", [128, D], F32, kind="ExternalInput").ap()
    lora_B = nc.dram_tensor("lora_B", [8 * D, 16], F32, kind="ExternalInput").ap()
    ffn_W = nc.dram_tensor("ffn_W", [D, D], F32, kind="ExternalInput").ap()
    ffn_bias = nc.dram_tensor("ffn_bias", [1, D], F32, kind="ExternalInput").ap()
    lora_w = nc.dram_tensor("lora_w", [1, 8], F32, kind="ExternalInput").ap()
    y = nc.dram_tensor("y", [S, D], F32, kind="ExternalOutput").ap()
    x1 = nc.dram_tensor("x1", [S, D], F32).ap()
    x2 = nc.dram_tensor("x2", [S, D], F32).ap()
    x3 = nc.dram_tensor("x3", [S, D], F32).ap()
    Ua = nc.dram_tensor("Ua", [D, S], BF16).ap()
    Abh = nc.dram_tensor("Abh", [D, S], BF16).ap()
    WT = nc.dram_tensor("WTs", [D, D], BF16).ap()
    AcH = nc.dram_tensor("AcH", [128, D], BF16).ap()
    BtH = nc.dram_tensor("BtH", [128, D], BF16).ap()
    cfH = nc.dram_tensor("cfH", [1, 128], F32).ap()
    dbg = nc.dram_tensor("dbg", [128, 32 * AG], BF16, kind="ExternalOutput").ap() \
        if upto in ("ln", "h0", "ua", "wt", "b0") else None

    from contextlib import ExitStack
    stack = ExitStack()
    sb = lambda name, shape, dt: stack.enter_context(nc.sbuf_tensor(name, shape, dt))
    psa = lambda name, shape, dt: stack.enter_context(nc.psum_tensor(name, shape, dt))
    B = Builder(nc)

    BIGA = sb("BIGA", [128, 32 * AG], BF16)
    BIGB = sb("BIGB", [128, 8 * AG], BF16)
    ones_row = sb("ones_row", [1, 512], BF16)
    ones_col = sb("ones_col", [128, 1], BF16)
    ident = sb("ident", [128, 128], BF16)
    masks = sb("masks", [128, 896], BF16)
    lacc = sb("lacc", [128, 512], F32)
    oncf = sb("oncf", [128, 1], F32)
    wqd_sb = sb("wqd_sb", [128, NB * QL], BF16)
    wkvd_sb = sb("wkvd_sb", [128, NB * KV], BF16)
    bqd_row = sb("bqd_row", [1, QL], BF16)
    bkvd_row = sb("bkvd_row", [1, KV], BF16)
    wqk_sb = sb("wqk_sb", [QL, 128], BF16)
    bqk_row = sb("bqk_row", [1, 128], BF16)
    wvu_h = sb("wvu_h", [KV, 128], BF16)
    bvu_h = sb("bvu_h", [1, 128], BF16)
    row_big = sb("row_big", [1, D], BF16)
    lw_sb = sb("lw_sb", [1, 8], F32)
    coef_col = sb("coef_col", [128, 1], F32)
    stat = sb("stat", [128, 16], F32)
    eps_col = sb("eps_col", [128, 1], F32)
    diag_bf = [sb(f"diag{i}", [128, 128], BF16) for i in range(2)]
    bcol_bf = [sb(f"bcolb{i}", [128, 1], BF16) for i in range(2)]
    brow_bf = [sb(f"browb{i}", [1, 128], BF16) for i in range(2)]
    Qt_sb = sb("Qt_sb", [128, S], BF16)
    Kt_sb = sb("Kt_sb", [128, S], BF16)
    expS = [sb(f"expS{i}", [128, 512], BF16) for i in range(4)]
    lhi = sb("lhi", [1, 512], BF16)
    llo = sb("llo", [1, 512], BF16)
    bcast_sb = sb("bcast_sb", [128, 512], F32)
    Uh_sb = sb("Uh_sb", [128, S], BF16)
    rbuf = [sb(f"rbuf{i}", [128, 512], F32) for i in range(2)]
    nbuf = [sb(f"nbuf{i}", [128, 512], F32) for i in range(2)]
    lrow = nbuf[0][0:1, :]
    lrec = nbuf[1][0:1, :]
    hbf = sb("hbf", [128, 1024], BF16)
    Ckvs_sb = sb("Ckvs_sb", [128, NT * 40], BF16)
    u5_sb = lacc
    Cq_sb = Uh_sb
    Ckv_sb = Kt_sb
    ctx_bf = Qt_sb
    bct_sb = Kt_sb

    ps = [psa(f"ps{i}", [128, 512], F32) for i in range(6)]
    psb = psa("psb", [128, 512], BF16)
    psb2 = psa("psb2", [128, 512], BF16)
    PK = [f"ps{i}" for i in range(6)]

    def A(c0, w):
        return BIGA[:, c0:c0 + w]

    def Bv(c0, w):
        return BIGB[:, c0:c0 + w]

    def akeys(c0, w):
        return [f"A{g}" for g in range(c0 // AG, (c0 + w - 1) // AG + 1)]

    def bkeys(c0, w):
        return [f"B{g}" for g in range(c0 // AG, (c0 + w - 1) // AG + 1)]

    XBF = [0, 4096]
    SQS = 8192
    QW, KW, VW2 = 0, 4096, 8192
    VT, VS = 12288, 14336
    ACT_ = 12288
    AH = [12288, 14336]
    UAT = [8192, 12288]
    WCOL = [0, 0]
    WSTG = 4096
    STGB = 0
    STG16 = 4096
    QH = 0

    def dma(queue, out_ap, in_ap, reads, writes, tag=""):
        return B.emit(queue, lambda e: e.dma_start(out_ap, in_ap), reads, writes,
                      dma=True, tag=tag)

    def mm(out_ap, lhsT_ap, rhs_ap, reads, writes, start, stop, tag=""):
        return B.emit("tensor",
                      lambda e: e.matmul(out_ap, lhsT_ap, rhs_ap, start=start, stop=stop),
                      reads, writes, tag=tag)

    def tr(out_ap, in_ap, p, reads, writes, tag=""):
        return B.emit("tensor",
                      lambda e: e.transpose(out_ap, in_ap, ident[0:p, 0:p]),
                      list(reads) + ["ident"], writes, tag=tag)

    def act(out_ap, in_ap, func, reads, writes, bias=0.0, scale=1.0, accum=None, tag=""):
        return B.emit("scalar",
                      lambda e: e.activation(out_ap, in_ap, func, bias=bias, scale=scale,
                                             accum_out=accum),
                      reads, writes, tag=tag)

    def vec(fn, reads, writes, tag=""):
        return B.emit("vector", fn, reads, writes, tag=tag)

    def gp(fn, reads, writes, tag=""):
        return B.emit("gpsimd", fn, reads, writes, tag=tag)

    gp(lambda e: e.memset(ones_row[:, :], 1.0), [], ["ones_row"])
    gp(lambda e: e.memset(ones_col[:, :], 1.0), [], ["ones_col"])
    gp(lambda e: e.memset(eps_col[:, :], EPS), [], ["eps_col"])
    gp(lambda e: e.memset(oncf[:, :], 1.0), [], ["oncf"])
    gp(lambda e: e.memset(masks[:, :], 1.0), ["mkM"], ["mkM"])
    gp(lambda e: e.affine_select(masks[:, :], masks[:, :], [[1, 896]], ALU.is_ge, 0.0,
                                 base=-384, channel_multiplier=-1), ["mkM"], ["masks"])
    gp(lambda e: e.memset(ident[:, :], 1.0), ["identM"], ["identM"])
    gp(lambda e: e.affine_select(ident[:, :], ident[:, :], [[1, 128]], ALU.is_equal, 0.0,
                                 base=0, channel_multiplier=-1), ["identM"], ["ident"])
    dma("gpsimd", wqd_sb[:, :], Wq_d.rearrange("(b p) l -> p b l", p=128), [], ["wqd_sb"])
    dma("gpsimd", wkvd_sb[:, :], Wkv_d.rearrange("(b p) l -> p b l", p=128), [], ["wkvd_sb"])
    dma("gpsimd", bqd_row[:, :], bq_d, [], ["bqd_row"])
    dma("gpsimd", bkvd_row[:, :], bkv_d, [], ["bkvd_row"])
    dma("gpsimd", wqk_sb[:, :], W_qk, [], ["wqk_sb"])
    dma("gpsimd", bqk_row[:, :], b_qk, [], ["bqk_row"])
    dma("sync", lw_sb[:, :], lora_w, [], ["lw_sb"])
    vec(lambda e: e.tensor_scalar_mul(
        lrec[0:1, 0:128], bass.AP(lw_sb, 0, [[8, 1], [1, 8], [0, 16]]), 2.0),
        ["lw_sb", "lrec"], ["coef_row"])
    dma("sync", cfH[:, :], lrec[0:1, 0:128], ["coef_row"], ["cfHk"])
    dma("sync", coef_col[:, :], cfH.rearrange("a (p o) -> p (a o)", p=128),
        ["cfHk"], ["coef_col"])
    dma("gpsimd", Bv(STGB, 4096), lora_A, bkeys(STGB, 4096),
        bkeys(STGB, 4096) + ["stgA"])
    for g4 in range(8):
        for j in range(4):
            db = g4 * 4 + j
            tr(psb[:, j * 128:(j + 1) * 128], Bv(STGB + db * 128, 128), 128,
               ["stgA"] + bkeys(STGB + db * 128, 128), ["psb"])
        B.emit("scalar",
               lambda e, g4=g4: e.copy(Bv(WSTG + g4 * 512, 512), psb[:, :]),
               ["psb"], bkeys(WSTG + g4 * 512, 512) + ["stgAc"])
    dma("sync", AcH[:, :], Bv(WSTG, 4096), ["stgAc"] + bkeys(WSTG, 4096), ["AcHk"])
    for n in range(8):
        dma("gpsimd", Bv(STGB, 512),
            lora_B[n * D:(n + 1) * D, :].rearrange("(bb p) r -> p bb r", p=128),
            ["stgA", "stgBn"] + bkeys(STGB, 512), bkeys(STGB, 512) + ["stgBn"])
        for b4 in range(8):
            for j in range(4):
                bb = b4 * 4 + j
                tr(psb[0:16, j * 128:(j + 1) * 128], Bv(STGB + bb * 16, 16), 128,
                   ["stgBn"] + bkeys(STGB, 512), ["psb"])
            B.emit("vector",
                   lambda e, b4=b4: e.tensor_copy(
                       BIGB[0:16, STG16 + b4 * 512: STG16 + (b4 + 1) * 512],
                       psb[0:16, 0:512]),
                   ["psb"] + bkeys(STG16, 4096), bkeys(STG16, 4096) + ["stg16"])
        dma("sync", BtH[n * 16:(n + 1) * 16, :], BIGB[0:16, STG16:STG16 + 4096],
            ["stg16"] + bkeys(STG16, 4096), [f"BtHk{n}"])
    bt_keys = [f"BtHk{n}" for n in range(8)]

    def layernorm_block(xin_ap, t, dst_base, dst_stride, dst_keys_fn, xin_keys, tag):
        sl = t % 2
        xb = Bv(XBF[sl], 4096)
        kx = bkeys(XBF[sl], 4096)
        dma("gpsimd", xb, xin_ap[t * 128:(t + 1) * 128, :], kx + xin_keys, kx,
            tag=f"{tag}x{t}")
        junk = Bv(SQS, 4096)
        kj = bkeys(SQS, 4096)
        act(junk, xb, AF.Copy, kx + kj, kj + [f"sum{sl}"],
            accum=stat[:, sl * 8:sl * 8 + 1], tag=f"{tag}cp{t}")
        act(junk, xb, AF.Square, kx + kj, kj + [f"sumsq{sl}"],
            accum=stat[:, sl * 8 + 1:sl * 8 + 2], tag=f"{tag}sq{t}")
        st = stat[:, sl * 8:sl * 8 + 8]
        vec(lambda e, st=st: e.tensor_scalar_mul(st[:, 2:3], st[:, 0:1], 1.0 / D),
            [f"sum{sl}"], [f"mean{sl}"])
        vec(lambda e, st=st: e.tensor_scalar_mul(st[:, 3:4], st[:, 1:2], 1.0 / D),
            [f"sumsq{sl}"], [f"msq{sl}"])
        vec(lambda e, st=st: e.tensor_mul(st[:, 4:5], st[:, 2:3], st[:, 2:3]),
            [f"mean{sl}"], [f"m2{sl}"])
        vec(lambda e, st=st: e.tensor_sub(st[:, 3:4], st[:, 3:4], st[:, 4:5]),
            [f"msq{sl}", f"m2{sl}"], [f"var{sl}"])
        act(st[:, 4:5], st[:, 3:4], AF.Ln, [f"var{sl}", f"m2{sl}", "eps_col"],
            [f"lnv{sl}"], bias=eps_col[:, 0:1])
        act(st[:, 5:6], st[:, 4:5], AF.Exp, [f"lnv{sl}"], [f"rstd0{sl}"], scale=-0.5)
        vec(lambda e, st=st: e.tensor_mul(st[:, 6:7], st[:, 5:6], st[:, 5:6]),
            [f"rstd0{sl}"], [f"r2{sl}"])
        vec(lambda e, st=st: e.tensor_scalar_add(st[:, 4:5], st[:, 3:4], EPS),
            [f"var{sl}", f"lnv{sl}"], [f"vpe{sl}"])
        vec(lambda e, st=st: e.tensor_mul(st[:, 6:7], st[:, 6:7], st[:, 4:5]),
            [f"r2{sl}", f"vpe{sl}"], [f"r2{sl}"])
        vec(lambda e, st=st: e.tensor_scalar(st[:, 6:7], st[:, 6:7], -0.5, 1.5,
                                             ALU.mult, ALU.add),
            [f"r2{sl}"], [f"r2{sl}"])
        vec(lambda e, st=st: e.tensor_mul(st[:, 5:6], st[:, 5:6], st[:, 6:7]),
            [f"rstd0{sl}", f"r2{sl}"], [f"rstd{sl}"])
        vec(lambda e, st=st: e.tensor_mul(st[:, 7:8], st[:, 2:3], st[:, 5:6]),
            [f"mean{sl}", f"rstd{sl}"], [f"mr{sl}"])
        vec(lambda e, st=st, sl=sl: e.tensor_scalar_mul(diag_bf[sl][:, :], ident[:, :],
                                                        st[:, 5:6]),
            ["ident", f"rstd{sl}"], [f"diag{sl}"])
        vec(lambda e, st=st, sl=sl: e.tensor_copy(bcol_bf[sl][:, :], st[:, 7:8]),
            [f"mr{sl}"], [f"bcol{sl}"])
        pbr = psb[0:1, sl * 128:sl * 128 + 128]
        tr(pbr, bcol_bf[sl][:, :], 128, [f"bcol{sl}"], ["psb"])
        act(brow_bf[sl][:, :], pbr, AF.Copy, ["psb"], [f"brow{sl}"], scale=-1.0)
        for g4 in range(8):
            bank = ps[g4 % 2]
            bkey = PK[g4 % 2]
            for j in range(4):
                db = g4 * 4 + j
                mm(bank[:, j * 128:(j + 1) * 128], xb[:, db * 128:(db + 1) * 128],
                   diag_bf[sl][:, :], kx + [f"diag{sl}"], [bkey], start=True, stop=False)
                mm(bank[:, j * 128:(j + 1) * 128], ones_row[0:1, 0:128],
                   brow_bf[sl][0:1, :], ["ones_row", f"brow{sl}"], [bkey],
                   start=False, stop=True)
            off = dst_base(g4 * 4) + 0
            dst_ap = bass.AP(BIGA, off, [[32 * AG, 128], [dst_stride, 4], [1, 128]])
            src_ap = bass.AP(bank, 0, [[512, 128], [128, 4], [1, 128]])
            vec(lambda e, dst_ap=dst_ap, src_ap=src_ap: e.tensor_copy(dst_ap, src_ap),
                [bkey], dst_keys_fn(g4), tag=f"{tag}ev{t}_{g4}")

    def wprep_steps():
        steps = []
        for i in range(NB):
            sl = i % 2

            def step(i=i, sl=sl):
                dma("gpsimd", Bv(WCOL[sl], 4096),
                    ffn_W[:, i * 128:(i + 1) * 128].rearrange("(b p) d -> p b d", p=128),
                    bkeys(WCOL[sl], 4096), bkeys(WCOL[sl], 4096), tag=f"wp{i}")
                for b4 in range(8):
                    for j in range(4):
                        b = b4 * 4 + j
                        tr(psb[:, j * 128:(j + 1) * 128], Bv(WCOL[sl] + b * 128, 128),
                           128, bkeys(WCOL[sl], 4096), ["psb"])
                    B.emit("scalar",
                           lambda e, b4=b4: e.copy(Bv(WSTG + b4 * 512, 512), psb[:, :]),
                           ["psb"], bkeys(WSTG + b4 * 512, 512) + ["wstg"])
                dma("sync", WT[i * 128:(i + 1) * 128, :], Bv(WSTG, 4096),
                    ["wstg"], [f"WTr{i}"])

            steps.append(step)
        return steps

    def phase_a():
        for t in range(NT):
            layernorm_block(
                x0, t,
                dst_base=lambda db, t=t: db * AG + t * 128,
                dst_stride=AG,
                dst_keys_fn=lambda g4: [f"A{g4 * 4 + j}" for j in range(4)],
                xin_keys=[], tag="a1")
        for h in range(H):
            qwk = bkeys(QW, 4096)
            kwk = bkeys(KW, 4096)
            vwk = bkeys(VW2, 4096)
            dma("gpsimd", Bv(QW, 4096),
                wq[:, h * 192: h * 192 + 128].rearrange("(b p) e -> p b e", p=128),
                qwk, qwk, tag=f"qw{h}")
            dma("gpsimd", Bv(KW, 4096),
                wkv[:, h * 128: (h + 1) * 128].rearrange("(b p) e -> p b e", p=128),
                kwk, kwk, tag=f"kw{h}")
            dma("gpsimd", Bv(VW2, 4096),
                wkv[:, 4096 + h * 128: 4096 + (h + 1) * 128].rearrange(
                    "(b p) e -> p b e", p=128),
                vwk, vwk, tag=f"vw{h}")
            vtk = bkeys(VT, 2048)
            vsk = bkeys(VS, 2048)
            for c in range(4):
                bv_ = ps[2 + c % 2]
                for db in range(NB):
                    mm(bv_[:, :], Bv(VW2 + db * 128, 128), A(db * AG + c * 512, 512),
                       vwk + [f"A{db}"], [PK[2 + c % 2]], start=(db == 0),
                       stop=(db == NB - 1))
                act(Bv(VT + c * 512, 512), bv_[:, :], AF.Copy,
                    [PK[2 + c % 2]] + (vtk if c == 3 else []), [f"VTc{c}"])

            def v_transpose(g4):
                pb = psb if g4 % 2 == 0 else psb2
                pbk = "psb" if g4 % 2 == 0 else "psb2"
                for j in range(4):
                    kb = g4 * 4 + j
                    tr(pb[:, j * 128:(j + 1) * 128], Bv(VT + kb * 128, 128), 128,
                       [f"VTc{g4}"] + (vtk if g4 == 3 and j == 3 else []), [pbk])
                act(Bv(VS + g4 * 512, 512), pb[:, :], AF.Copy,
                    [pbk] + (vsk if g4 == 3 else []), [f"VSc{g4}"] + (vsk if g4 == 3 else []))

            for c in range(4):
                bq, bk = ps[4], ps[5]
                for db in range(NB):
                    mm(bq[:, :], Bv(QW + db * 128, 128), A(db * AG + c * 512, 512),
                       qwk + [f"A{db}"], [PK[4]], start=(db == 0), stop=(db == NB - 1))
                v_transpose(c)
                for db in range(NB):
                    mm(bk[:, :], Bv(KW + db * 128, 128), A(db * AG + c * 512, 512),
                       kwk + [f"A{db}"], [PK[5]], start=(db == 0), stop=(db == NB - 1))
                act(Qt_sb[:, c * 512:(c + 1) * 512], bq[:, :], AF.Copy, [PK[4]], ["Qt"])
                act(Kt_sb[:, c * 512:(c + 1) * 512], bk[:, :], AF.Copy, [PK[5]], ["Kt"])
            deferred = [None]

            def fin_mms(c, ub):
                bb_i = (0, 1, 4, 5)[(4 * c + 5) % 4]
                bB = ps[bb_i]
                mm(bB[:, :], ones_row[0:1, 0:128], lhi[0:1, :], ["ones_row", "lhi"],
                   [PK[bb_i]], start=True, stop=False)
                mm(bB[:, :], ones_row[0:1, 0:128], llo[0:1, :], ["ones_row", "llo"],
                   [PK[bb_i]], start=False, stop=True)
                vec(lambda e, bB=bB: e.tensor_copy(bcast_sb[:, :], bB[:, :]), [PK[bb_i]],
                    ["bcast"])
                vec(lambda e, c=c, ub=ub: e.tensor_mul(Uh_sb[:, c * 512:(c + 1) * 512],
                                                       ps[ub][:, :], bcast_sb[:, :]),
                    [PK[ub], "bcast"], ["Uh"], tag=f"Un{h}_{c}")

            upend = []

            def u_flush(n):
                while upend and len(upend) > n:
                    upend.pop(0)()

            for c in range(4):
                nkb = 4 * c + 4
                ub = 2 + c % 2
                ROT = (0, 1, 4, 5)
                for kb in range(nkb):
                    u_flush(2)
                    bS = ps[ROT[kb % 4]]
                    skey = PK[ROT[kb % 4]]
                    ekey = f"expS{kb % 4}"
                    mm(bS[:, :], Kt_sb[:, kb * 128:(kb + 1) * 128],
                       Qt_sb[:, c * 512:(c + 1) * 512], ["Kt", "Qt"], [skey],
                       start=True, stop=True, tag=f"sc{h}_{c}_{kb}")
                    act(expS[kb % 4][:, :], bS[:, :], AF.Exp, [skey], [ekey],
                        scale=SCALE_A, tag=f"exp{h}_{c}_{kb}")
                    if kb >= 4 * c:
                        m = kb - 4 * c
                        vec(lambda e, kb=kb, m=m: e.tensor_mul(
                            expS[kb % 4][:, :], expS[kb % 4][:, :],
                            masks[:, 384 - 128 * m: 896 - 128 * m]),
                            [ekey, "masks"], [ekey], tag=f"msk{h}_{c}_{kb}")
                    def u_mm(kb=kb, c=c, ub=ub, nkb=nkb, ekey=ekey):
                        mm(ps[ub][:, :], Bv(VS + kb * 128, 128), expS[kb % 4][:, :],
                           [f"VSc{kb // 4}", ekey], [PK[ub]], start=(kb == 0),
                           stop=(kb == nkb - 1))
                    upend.append(u_mm)
                    if kb == 0:
                        vec(lambda e, kb=kb: e.tensor_copy(lacc[:, :], expS[kb % 4][:, :]),
                            [ekey, "lacc"], ["lacc"])
                    else:
                        vec(lambda e, kb=kb: e.tensor_add(lacc[:, :], lacc[:, :],
                                                          expS[kb % 4][:, :]),
                            [ekey, "lacc"], ["lacc"])
                    if kb == 5 and deferred[0] is not None:
                        deferred[0]()
                        deferred[0] = None
                u_flush(0)
                lb = ROT[nkb % 4]
                mm(ps[lb][0:1, :], oncf[:, :], lacc[:, :], ["oncf", "lacc"], [PK[lb]],
                   start=True, stop=True)
                if deferred[0] is not None:
                    deferred[0]()
                    deferred[0] = None
                act(lrow[:, :], ps[lb][0:1, :], AF.Ln, [PK[lb], "lrow"], ["lrow"])
                act(lrec[:, :], lrow[:, :], AF.Exp, ["lrow", "coef_row"], ["lrec"],
                    scale=-1.0)
                act(lhi[:, :], lrec[:, :], AF.Copy, ["lrec", "lhi"], ["lhi"])
                vec(lambda e: e.tensor_sub(lrow[:, :], lrec[:, :], lhi[:, :]),
                    ["lrec", "lhi", "lrow"], ["llo_f"])
                vec(lambda e: e.tensor_copy(llo[:, :], lrow[:, :]), ["llo_f"], ["llo"])
                deferred[0] = (lambda c=c, ub=ub: fin_mms(c, ub))
            if deferred[0] is not None:
                deferred[0]()
                deferred[0] = None
            dma("sync", Ua[h * 128:(h + 1) * 128, :], Uh_sb[:, :], ["Uh"], [f"UaH{h}"],
                tag=f"spill{h}")
        ua_ready = [f"UaH{h}" for h in range(H)]
        wsteps = wprep_steps()
        wi = 0
        outk = []
        for Q in range(4):
            for e_blk in range(NB):
                wkey = akeys(e_blk * 1024, 1024)
                dma("gpsimd", A(e_blk * 1024, 1024),
                    wo_a[e_blk * 128:(e_blk + 1) * 128, Q * 1024:(Q + 1) * 1024],
                    wkey, wkey, tag=f"wo{Q}_{e_blk}")
            for tp in range(NT // 4):
                for _ in range(4):
                    if wi < len(wsteps):
                        wsteps[wi]()
                        wi += 1
                usl = tp % 2
                ubase = 32768 + usl * 16384
                uk = akeys(ubase, 16384)
                dma("sync", A(ubase, 16384),
                    Ua[:, tp * 512:(tp + 1) * 512].rearrange("(eb p) q -> p eb q", p=128),
                    ua_ready + uk, uk, tag=f"uat{Q}_{tp}")
                for tj in range(4):
                    t = tp * 4 + tj
                    for oc in range(2):
                        rsl = oc
                        dma("sync", rbuf[rsl][:, :],
                            x0[t * 128:(t + 1) * 128,
                               Q * 1024 + oc * 512: Q * 1024 + (oc + 1) * 512],
                            [f"rbuf{rsl}"], [f"rbuf{rsl}"])
                        bank = ps[oc]
                        for e_blk in range(NB):
                            mm(bank[:, :], A(ubase + e_blk * 512 + tj * 128, 128),
                               A(e_blk * 1024 + oc * 512, 512),
                               uk + akeys(e_blk * 1024 + oc * 512, 512), [PK[oc]],
                               start=(e_blk == 0), stop=(e_blk == NB - 1))
                        vec(lambda e, bank=bank, rsl=rsl: e.tensor_add(
                            nbuf[rsl][:, :], bank[:, :], rbuf[rsl][:, :]),
                            [PK[oc], f"rbuf{rsl}"], [f"nbuf{rsl}"], tag=f"woev{Q}_{t}_{oc}")
                        o8 = Q * 2 + oc
                        outk.append(f"x1r{t}_{o8}")
                        dma("sync", x1[t * 128:(t + 1) * 128, o8 * 512:(o8 + 1) * 512],
                            nbuf[rsl][:, :], [f"nbuf{rsl}"], [f"x1r{t}_{o8}"],
                            tag=f"x1st{t}_{o8}")
        while wi < len(wsteps):
            wsteps[wi]()
            wi += 1
        return outk

    def phase_f(xin_ap, xout_ap, xname, xin_name, inst):
        ack = bkeys(ACT_, 4096)
        dma("sync", Bv(ACT_, 4096), AcH[:, :], ["AcHk"] + ack, ack, tag=f"f{inst}Ac")
        dma("gpsimd", row_big[:, :], ffn_bias, ["row_big"], ["row_big"], tag=f"f{inst}fb")
        wt_keys = [f"WTr{i}" for i in range(NB)]
        outk = []
        for ch in range(2):
            for tt in range(8):
                t = ch * 8 + tt
                layernorm_block(
                    xin_ap, t,
                    dst_base=lambda db, tt=tt: db * 1024 + tt * 128,
                    dst_stride=1024,
                    dst_keys_fn=lambda g4: [f"A{(g4 * 4) // 2}", f"A{(g4 * 4 + 3) // 2}"],
                    xin_keys=[f"{xin_name}r{t}_{o}" for o in range(8)],
                    tag=f"f{inst}ln")
            for half in range(2):
                bank = ps[4 + half]
                for db in range(NB):
                    mm(bank[:, :], Bv(ACT_ + db * 128, 128),
                       A(db * 1024 + half * 512, 512),
                       ack + [f"A{(db * 1024 + half * 512) // AG}"], [PK[4 + half]],
                       start=(db == 0), stop=(db == NB - 1))
                act(hbf[:, half * 512:(half + 1) * 512], bank[:, :], AF.Copy,
                    [PK[4 + half], "coef_col"], ["hbf"], scale=coef_col[:, 0:1])
            for oc in range(8):
                wsl = oc % 2
                wck = akeys(32768 + wsl * 16384, 16384)
                dma("sync", A(32768 + wsl * 16384, 16384),
                    WT[:, oc * 512:(oc + 1) * 512].rearrange("(b p) o -> p b o", p=128),
                    wt_keys + wck, wck, tag=f"f{inst}wt{ch}_{oc}")
                dma("gpsimd", bct_sb[:, 0:512], BtH[:, oc * 512:(oc + 1) * 512],
                    bt_keys + ["Kt"], ["Kt"], tag=f"f{inst}bt{ch}_{oc}")
                for tt in range(8):
                    t = ch * 8 + tt
                    bank = ps[tt % 2]
                    for db in range(NB):
                        mm(bank[:, :], A(db * 1024 + tt * 128, 128),
                           A(32768 + wsl * 16384 + db * 512, 512),
                           [f"A{(db * 1024 + tt * 128) // AG}",
                            f"A{(32768 + wsl * 16384 + db * 512) // AG}"],
                           [PK[tt % 2]], start=(db == 0), stop=False)
                    mm(bank[:, :], hbf[:, tt * 128:(tt + 1) * 128], bct_sb[:, 0:512],
                       ["hbf", "Kt"], [PK[tt % 2]], start=False, stop=False)
                    mm(bank[:, :], ones_row[0:1, 0:128],
                       row_big[0:1, oc * 512:(oc + 1) * 512],
                       ["ones_row", "row_big"], [PK[tt % 2]], start=False, stop=True)
                    rsl = tt % 2
                    dma("scalar", rbuf[rsl][:, :],
                        xin_ap[t * 128:(t + 1) * 128, oc * 512:(oc + 1) * 512],
                        [f"{xin_name}r{t}_{oc}", f"rbuf{rsl}"], [f"rbuf{rsl}"])
                    vec(lambda e, bank=bank, rsl=rsl: e.tensor_add(
                        nbuf[rsl][:, :], bank[:, :], rbuf[rsl][:, :]),
                        [PK[tt % 2], f"rbuf{rsl}"], [f"nbuf{rsl}"])
                    k = f"{xname}r{t}_{oc}"
                    outk.append(k)
                    dma("scalar", xout_ap[t * 128:(t + 1) * 128, oc * 512:(oc + 1) * 512],
                        nbuf[rsl][:, :], [f"nbuf{rsl}"], [k])
        return outk

    def phase_b(xin_ap, xout_ap, xin_name):
        for ch in range(4):
            for tt in range(4):
                t = ch * 4 + tt
                layernorm_block(
                    xin_ap, t,
                    dst_base=lambda db, tt=tt: db * 512 + tt * 128,
                    dst_stride=512,
                    dst_keys_fn=lambda g4: [f"A{g4}"],
                    xin_keys=[f"{xin_name}r{t}_{o}" for o in range(8)],
                    tag="bln")
            bq, bkv = ps[2], ps[3]
            for db in range(NB):
                mm(bq[0:QL, :], wqd_sb[:, db * QL:(db + 1) * QL], A(db * 512, 512),
                   ["wqd_sb", f"A{db // 4}"], [PK[2]], start=(db == 0), stop=False)
            mm(bq[0:QL, :], bqd_row[0:1, :], ones_row[0:1, :], ["bqd_row", "ones_row"],
               [PK[2]], start=False, stop=True)
            for db in range(NB):
                mm(bkv[0:KV, :], wkvd_sb[:, db * KV:(db + 1) * KV], A(db * 512, 512),
                   ["wkvd_sb", f"A{db // 4}"], [PK[3]], start=(db == 0), stop=False)
            mm(bkv[0:KV, :], bkvd_row[0:1, :], ones_row[0:1, :], ["bkvd_row", "ones_row"],
               [PK[3]], start=False, stop=True)
            act(Cq_sb[0:QL, ch * 512:(ch + 1) * 512], bq[0:QL, :], AF.Copy, [PK[2]], ["Uh"])
            act(Ckv_sb[0:KV, ch * 512:(ch + 1) * 512], bkv[0:KV, :], AF.Copy, [PK[3]], ["Kt"])
        vec(lambda e: e.memset(Ckvs_sb[:, :], 0.0), ["Ckvs"], ["CkvsM"])
        for kb in range(NT):
            pkc = psb[0:128, (kb % 4) * 128:(kb % 4) * 128 + 4]
            tr(pkc, Ckv_sb[0:KV, kb * 128:(kb + 1) * 128], KV, ["Kt"], ["psb"])
            vec(lambda e, pkc=pkc, kb=kb: e.tensor_copy(
                Ckvs_sb[:, kb * 40: kb * 40 + 4], pkc),
                ["psb", "CkvsM"], ["CkvsM"])
        vec(lambda e: e.memset(bass.AP(Ckvs_sb, 32, [[NT * 40, 128], [40, NT], [1, 1]]),
                               1.0),
            ["CkvsM"], ["Ckvs"])
        dma("gpsimd", row_big[:, :], bv_u, ["row_big"], ["row_big"], tag="b_bvu")
        for h in range(H):
            sl = h % 2
            dma("gpsimd", wvu_h[:, :], Wv_u[:, h * 128:(h + 1) * 128], ["wvu_h"],
                ["wvu_h"], tag=f"wvu{h}")
            qhk = bkeys(QH, 2048)
            for c in range(4):
                bqh = ps[4]
                mm(bqh[0:KV, :], wqk_sb[:, h * KV:(h + 1) * KV],
                   Cq_sb[0:QL, c * 512:(c + 1) * 512], ["wqk_sb", "Uh"], [PK[4]],
                   start=True, stop=False)
                mm(bqh[0:KV, :], bqk_row[0:1, h * KV:(h + 1) * KV], ones_row[0:1, :],
                   ["bqk_row", "ones_row"], [PK[4]], start=False, stop=True)
                act(BIGB[0:KV, QH + c * 512: QH + (c + 1) * 512], bqh[0:KV, :], AF.Copy,
                    [PK[4]] + qhk, qhk)
            BROT = (0, 1, 4, 5)

            def b_score(kb, c, i):
                bS = ps[BROT[i % 4]]
                skey = PK[BROT[i % 4]]
                ekey = f"expS{i % 4}"
                mm(bS[:, :], Ckv_sb[0:KV, kb * 128:(kb + 1) * 128],
                   BIGB[0:KV, QH + c * 512: QH + (c + 1) * 512], ["Kt"] + qhk,
                   [skey], start=True, stop=True, tag=f"bsc{h}_{c}_{kb}")
                act(expS[i % 4][:, :], bS[:, :], AF.Exp, [skey], [ekey], scale=SCALE_B)

            def b_u(kb, c, i):
                mm(ps[2 + c % 2][0:33, :], Ckvs_sb[:, kb * 40: kb * 40 + 33],
                   expS[i % 4][:, :],
                   ["Ckvs", f"expS{i % 4}"], [PK[2 + c % 2]], start=(kb == 0),
                   stop=(kb == NT - 1))

            def b_chain(c):
                bU = ps[2 + c % 2]
                act(lrow[:, :], bU[32:33, :], AF.Ln, [PK[2 + c % 2], "lrow"], ["lrow"])
                act(lrec[:, :], lrow[:, :], AF.Exp, ["lrow", "coef_row"], ["lrec"],
                    scale=-1.0)
                act(lhi[:, :], lrec[:, :], AF.Copy, ["lrec", "lhi"], ["lhi"])
                vec(lambda e: e.tensor_sub(lrow[:, :], lrec[:, :], lhi[:, :]),
                    ["lrec", "lhi", "lrow"], ["llo_f"])
                vec(lambda e: e.tensor_copy(llo[:, :], lrow[:, :]), ["llo_f"], ["llo"])

            def b_final(c):
                bU = ps[2 + c % 2]
                vec(lambda e, bU=bU: e.tensor_copy(u5_sb[0:4, :], bU[0:4, :]),
                    [PK[2 + c % 2], "u5"], ["u5"])
                bb_i = BROT[c % 4]
                bB = ps[bb_i]
                mm(bB[:, :], ones_row[0:1, 0:128], lhi[0:1, :], ["ones_row", "lhi"],
                   [PK[bb_i]], start=True, stop=False)
                mm(bB[:, :], ones_row[0:1, 0:128], llo[0:1, :], ["ones_row", "llo"],
                   [PK[bb_i]], start=False, stop=True)
                vec(lambda e, bB=bB: e.tensor_copy(bcast_sb[:, :], bB[:, :]),
                    [PK[bb_i], "bcast"], ["bcast"])
                vec(lambda e: e.tensor_mul(ctx_bf[0:KV, 0:512], u5_sb[0:KV, :],
                                           bcast_sb[0:KV, :]),
                    ["u5", "bcast"], ["Qt"])
                bt_i = BROT[(c + 1) % 4]
                bT = ps[bt_i]
                mm(bT[:, :], wvu_h[0:KV, :], ctx_bf[0:KV, 0:512],
                   ["wvu_h", "Qt"], [PK[bt_i]], start=True, stop=False)
                mm(bT[:, :], row_big[0:1, h * 128:(h + 1) * 128], ones_row[0:1, :],
                   ["row_big", "ones_row"], [PK[bt_i]], start=False, stop=True)
                ahk = bkeys(AH[sl], 2048)
                act(Bv(AH[sl] + c * 512, 512), bT[:, :], AF.Copy,
                    [PK[bt_i]] + ahk, ahk)

            for cpass in range(2):
                cA, cB = 2 * cpass, 2 * cpass + 1
                flat = [(kb, (cA, cB)[j], 2 * kb + j) for kb in range(NT)
                        for j in range(2)]
                for idx, (kb, c, i) in enumerate(flat):
                    if idx >= 4:
                        b_u(*flat[idx - 4])
                    b_score(kb, c, i)
                for idx in range(len(flat) - 4, len(flat)):
                    b_u(*flat[idx])
                    if idx == len(flat) - 2:
                        b_chain(cA)
                b_final(cA)
                b_chain(cB)
                b_final(cB)
            dma("sync", Abh[h * 128:(h + 1) * 128, :], Bv(AH[sl], 2048),
                bkeys(AH[sl], 2048), [f"AbH{h}"])
        ab_ready = [f"AbH{h}" for h in range(H)]
        dma("gpsimd", row_big[:, :], bo_b, ["row_big"], ["row_big"], tag="b_bo")
        outk = []
        for Q in range(4):
            for e_blk in range(NB):
                wkey = akeys(e_blk * 1024, 1024)
                dma("gpsimd", A(e_blk * 1024, 1024),
                    Wo_b[e_blk * 128:(e_blk + 1) * 128, Q * 1024:(Q + 1) * 1024],
                    wkey, wkey, tag=f"bwo{Q}_{e_blk}")
            for tp in range(NT // 4):
                usl = tp % 2
                ubase = 32768 + usl * 16384
                uk = akeys(ubase, 16384)
                dma("sync", A(ubase, 16384),
                    Abh[:, tp * 512:(tp + 1) * 512].rearrange("(eb p) q -> p eb q", p=128),
                    ab_ready + uk, uk)
                for tj in range(4):
                    t = tp * 4 + tj
                    for oc in range(2):
                        rsl = oc
                        o8 = Q * 2 + oc
                        dma("sync", rbuf[rsl][:, :],
                            xin_ap[t * 128:(t + 1) * 128, o8 * 512:(o8 + 1) * 512],
                            [f"{xin_name}r{t}_{o8}", f"rbuf{rsl}"], [f"rbuf{rsl}"])
                        bank = ps[oc]
                        for e_blk in range(NB):
                            mm(bank[:, :], A(ubase + e_blk * 512 + tj * 128, 128),
                               A(e_blk * 1024 + oc * 512, 512),
                               uk + akeys(e_blk * 1024 + oc * 512, 512), [PK[oc]],
                               start=(e_blk == 0), stop=False)
                        mm(bank[:, :], ones_row[0:1, 0:128],
                           row_big[0:1, o8 * 512:(o8 + 1) * 512],
                           ["ones_row", "row_big"], [PK[oc]], start=False, stop=True)
                        vec(lambda e, bank=bank, rsl=rsl: e.tensor_add(
                            nbuf[rsl][:, :], bank[:, :], rbuf[rsl][:, :]),
                            [PK[oc], f"rbuf{rsl}"], [f"nbuf{rsl}"])
                        outk.append(f"x3r{t}_{o8}")
                        dma("sync", xout_ap[t * 128:(t + 1) * 128, o8 * 512:(o8 + 1) * 512],
                            nbuf[rsl][:, :], [f"nbuf{rsl}"], [f"x3r{t}_{o8}"])
        return outk

    if upto == "b0":
        keys = phase_a()
        k2 = phase_f(x1, x2, "x2", "x1", 1)
        k3 = phase_b(x2, x3, "x2")
        final = []
        final = list(k3)
        for i in range(32):
            sl = i % 2
            kk = bkeys(UAT[sl], 4096)
            dma("sync", Bv(UAT[sl], 2048), Abh[i * 128:(i + 1) * 128, :],
                k3 + kk, kk)
            dma("sync", dbg[:, i * 2048:(i + 1) * 2048], Bv(UAT[sl], 2048), kk,
                [f"dg{i}"])
            final.append(f"dg{i}")
        dma("sync", y[0:128, 0:512], rbuf[0][:, :], ["rbuf0"], ["yk"])
        final.append("yk")
    elif upto == "wt":
        wsteps = wprep_steps()
        for st_ in wsteps:
            st_()
        final = []
        for i in range(8):
            sl = i % 2
            kk = bkeys(UAT[sl], 4096)
            dma("sync", Bv(UAT[sl], 4096), WT[i * 128:(i + 1) * 128, :],
                [f"WTr{i}"] + kk, kk)
            dma("sync", dbg[:, i * 4096:(i + 1) * 4096], Bv(UAT[sl], 4096), kk,
                [f"dw{i}"])
            final.append(f"dw{i}")
        ack = bkeys(ACT_, 4096)
        dma("sync", Bv(ACT_, 4096), AcH[:, :], ["AcHk"] + ack, ack)
        dma("sync", dbg[:, 8 * 4096: 8 * 4096 + 4096], Bv(ACT_, 4096), ack, ["dA"])
        final.append("dA")
        bck = bkeys(XBF[0], 4096)
        dma("sync", Bv(XBF[0], 4096), BtH[:, :], bt_keys + bck, bck)
        dma("sync", dbg[:, 9 * 4096: 9 * 4096 + 4096], Bv(XBF[0], 4096), bck, ["dB"])
        final.append("dB")
        vec(lambda e: e.tensor_copy(expS[0][:, 0:1], coef_col[:, 0:1]),
            ["coef_col", "expS0"], ["expS0"])
        vec(lambda e: e.memset(expS[0][:, 1:512], 0.0), ["expS0"], ["expS0"])
        dma("sync", dbg[:, 10 * 4096: 10 * 4096 + 512], expS[0][:, 0:512], ["expS0"], ["dC"])
        final.append("dC")
        dma("sync", y[0:128, 0:512], rbuf[0][:, :], ["rbuf0"], ["yk"])
        final.append("yk")
    elif upto == "ua":
        keys = phase_a()
        final = []
        for i in range(32):
            sl = i % 2
            kk = bkeys(UAT[sl], 4096)
            dma("sync", Bv(UAT[sl], 2048), Ua[i * 128:(i + 1) * 128, :],
                keys + kk, kk)
            dma("sync", dbg[:, i * 2048:(i + 1) * 2048], Bv(UAT[sl], 2048), kk,
                [f"dg{i}"])
            final.append(f"dg{i}")
        dma("sync", y[0:128, 0:512], rbuf[0][:, :], ["rbuf0"], ["yk"])
        final.append("yk")
    elif upto == "h0":
        for t in range(NT):
            layernorm_block(
                x0, t,
                dst_base=lambda db, t=t: db * AG + t * 128,
                dst_stride=AG,
                dst_keys_fn=lambda g4: [f"A{g4 * 4 + j}" for j in range(4)],
                xin_keys=[], tag="a1")
        h = 0
        qwk = bkeys(QW, 4096)
        kwk = bkeys(KW, 4096)
        vwk = bkeys(VW2, 4096)
        dma("gpsimd", Bv(QW, 4096),
            wq[:, h * 192: h * 192 + 128].rearrange("(b p) e -> p b e", p=128),
            qwk, qwk)
        dma("gpsimd", Bv(KW, 4096),
            wkv[:, h * 128: (h + 1) * 128].rearrange("(b p) e -> p b e", p=128),
            kwk, kwk)
        dma("gpsimd", Bv(VW2, 4096),
            wkv[:, 4096 + h * 128: 4096 + (h + 1) * 128].rearrange(
                "(b p) e -> p b e", p=128),
            vwk, vwk)
        vtk = bkeys(VT, 2048)
        vsk = bkeys(VS, 2048)
        for c in range(4):
            bv_ = ps[2 + c % 2]
            for db in range(NB):
                mm(bv_[:, :], Bv(VW2 + db * 128, 128), A(db * AG + c * 512, 512),
                   vwk + [f"A{db}"], [PK[2 + c % 2]], start=(db == 0),
                   stop=(db == NB - 1))
            act(Bv(VT + c * 512, 512), bv_[:, :], AF.Copy, [PK[2 + c % 2]] + vtk, vtk)
        for g4 in range(4):
            for j in range(4):
                kb = g4 * 4 + j
                tr(psb[:, j * 128:(j + 1) * 128], Bv(VT + kb * 128, 128), 128,
                   vtk, ["psb"])
            B.emit("scalar",
                   lambda e, g4=g4: e.copy(Bv(VS + g4 * 512, 512), psb[:, :]),
                   ["psb"] + vsk, vsk)
        for c in range(4):
            bq, bk = ps[4], ps[5]
            for db in range(NB):
                mm(bq[:, :], Bv(QW + db * 128, 128), A(db * AG + c * 512, 512),
                   qwk + [f"A{db}"], [PK[4]], start=(db == 0), stop=(db == NB - 1))
            for db in range(NB):
                mm(bk[:, :], Bv(KW + db * 128, 128), A(db * AG + c * 512, 512),
                   kwk + [f"A{db}"], [PK[5]], start=(db == 0), stop=(db == NB - 1))
            act(Qt_sb[:, c * 512:(c + 1) * 512], bq[:, :], AF.Copy, [PK[4]], ["Qt"])
            act(Kt_sb[:, c * 512:(c + 1) * 512], bk[:, :], AF.Copy, [PK[5]], ["Kt"])
        for c in range(4):
            nkb = 4 * c + 4
            bU, bL = ps[2], ps[3]
            for kb in range(nkb):
                bS = ps[kb % 2]
                skey = PK[kb % 2]
                ekey = f"expS{kb % 2}"
                mm(bS[:, :], Kt_sb[:, kb * 128:(kb + 1) * 128],
                   Qt_sb[:, c * 512:(c + 1) * 512], ["Kt", "Qt"], [skey],
                   start=True, stop=True)
                act(expS[kb % 2][:, :], bS[:, :], AF.Exp, [skey], [ekey],
                    scale=SCALE_A)
                if kb >= 4 * c:
                    m = kb - 4 * c
                    gp(lambda e, kb=kb, m=m: e.affine_select(
                        expS[kb % 2][:, :], expS[kb % 2][:, :], [[1, 512]],
                        ALU.is_ge, 0.0, base=-128 * m, channel_multiplier=-1),
                        [ekey], [ekey])
                mm(bU[:, :], Bv(VS + kb * 128, 128), expS[kb % 2][:, :],
                   vsk + [ekey], [PK[2]], start=(kb == 0), stop=(kb == nkb - 1))
                mm(bL[0:1, :], ones_col[:, :], expS[kb % 2][:, :],
                   ["ones_col", ekey], [PK[3]],
                   start=(kb == 0), stop=(kb == nkb - 1))
            vec(lambda e: e.tensor_copy(lrow[:, :], ps[3][0:1, :]), [PK[3]], ["lrow"])
            vec(lambda e: e.reciprocal(lrec[:, :], lrow[:, :]), ["lrow", "coef_row"],
                ["lrec"])
            vec(lambda e: e.tensor_copy(lhi[:, :], lrec[:, :]), ["lrec"], ["lhi"])
            vec(lambda e: e.tensor_sub(lrow[:, :], lrec[:, :], lhi[:, :]),
                ["lrec", "lhi", "lrow"], ["llo_f"])
            vec(lambda e: e.tensor_copy(llo[:, :], lrow[:, :]), ["llo_f"], ["llo"])
            bB = ps[4]
            mm(bB[:, :], ones_row[0:1, 0:128], lhi[0:1, :], ["ones_row", "lhi"],
               [PK[4]], start=True, stop=False)
            mm(bB[:, :], ones_row[0:1, 0:128], llo[0:1, :], ["ones_row", "llo"],
               [PK[4]], start=False, stop=True)
            vec(lambda e: e.tensor_copy(bcast_sb[:, :], ps[4][:, :]), [PK[4]],
                ["bcast"])
            vec(lambda e, c=c: e.tensor_mul(Uh_sb[:, c * 512:(c + 1) * 512],
                                            ps[2][:, :], bcast_sb[:, :]),
                [PK[2], "bcast"], ["Uh"])
        dma("sync", dbg[:, 0:2048], Qt_sb[:, :], ["Qt"], ["d0"])
        dma("sync", dbg[:, 2048:4096], Kt_sb[:, :], ["Kt"], ["d1"])
        dma("sync", dbg[:, 4096:6144], Bv(VS, 2048), vsk, ["d2"])
        dma("sync", dbg[:, 6144:8192], Uh_sb[:, :], ["Uh"], ["d3"])
        dma("sync", y[0:128, 0:512], rbuf[0][:, :], ["rbuf0"], ["yk"])
        final = ["d0", "d1", "d2", "d3", "yk"]
    elif upto == "ln":
        for t in range(NT):
            layernorm_block(
                x0, t,
                dst_base=lambda db, t=t: db * AG + t * 128,
                dst_stride=AG,
                dst_keys_fn=lambda g4: [f"A{g4 * 4 + j}" for j in range(4)],
                xin_keys=[], tag="a1")
        dma("sync", dbg, BIGA[:, :], [f"A{g}" for g in range(32)], ["dbgk"])
        dma("sync", y[0:128, 0:512], rbuf[0][:, :], ["rbuf0"], ["yk"])
        final = ["dbgk", "yk"]
    elif upto == "a":
        keys = phase_a()
        for t in range(NT):
            for o in range(8):
                sl = o % 2
                dma("sync", rbuf[sl][:, :], x1[t * 128:(t + 1) * 128, o * 512:(o + 1) * 512],
                    keys + [f"rbuf{sl}"], [f"rbuf{sl}"])
                dma("sync", y[t * 128:(t + 1) * 128, o * 512:(o + 1) * 512], rbuf[sl][:, :],
                    [f"rbuf{sl}"], [f"yk{t}_{o}"])
        final = [f"yk{t}_{o}" for t in range(NT) for o in range(8)]
    elif upto == "f1":
        keys = phase_a()
        final = phase_f(x1, y, "y", "x1", 1)
    elif upto == "b":
        keys = phase_a()
        k2 = phase_f(x1, x2, "x2", "x1", 1)
        final = phase_b(x2, y, "x2")
    else:
        keys = phase_a()
        k2 = phase_f(x1, x2, "x2", "x1", 1)
        k3 = phase_b(x2, x3, "x2")
        final = phase_f(x3, y, "y", "x3", 2)
    B.emit("sync", lambda e: e.nop(), reads=final, writes=[])
    B.finalize_and_emit(stack)
    stack.close()
    return nc


_NC_CACHE = {}


def _get_nc(upto="full"):
    if upto not in _NC_CACHE:
        _NC_CACHE[upto] = build_program(upto)
    return _NC_CACHE[upto]


def _in_maps(inputs):
    c = np.ascontiguousarray
    hs = inputs["L0_attn_a_hidden_states"]
    base = {
        "wq": c(inputs["L0_attn_a_wq"]),
        "wkv": c(inputs["L0_attn_a_wkv"]),
        "wo_a": c(inputs["L0_attn_a_wo"]),
        "Wq_d": c(inputs["L0_attn_b_Wq_d"]),
        "bq_d": c(inputs["L0_attn_b_bq_d"]).reshape(1, QL),
        "W_qk": c(inputs["L0_attn_b_W_qk"]),
        "b_qk": c(inputs["L0_attn_b_b_qk"]).reshape(1, 128),
        "Wkv_d": c(inputs["L0_attn_b_Wkv_d"]),
        "bkv_d": c(inputs["L0_attn_b_bkv_d"]).reshape(1, KV),
        "Wv_u": c(inputs["L0_attn_b_Wv_u"]),
        "bv_u": c(inputs["L0_attn_b_bv_u"]).reshape(1, D),
        "Wo_b": c(inputs["L0_attn_b_Wo"]),
        "bo_b": c(inputs["L0_attn_b_bo"]).reshape(1, D),
        "lora_A": c(inputs["L0_ffn_lora_A"]).reshape(128, D),
        "lora_B": c(inputs["L0_ffn_lora_B"]).reshape(8 * D, 16),
        "ffn_W": c(inputs["L0_ffn_W"]),
        "ffn_bias": c(inputs["L0_ffn_bias"]).reshape(1, D),
        "lora_w": c(inputs["L0_ffn_lora_weights"]).reshape(1, 8),
    }
    maps = []
    for b in range(8):
        m = dict(base)
        m["x0"] = c(np.asarray(hs[b]))
        maps.append(m)
    return maps


def kernel(**inputs):
    nc = _get_nc("full")
    maps = _in_maps(inputs)
    res = run_bass_kernel_spmd(nc, maps, core_ids=list(range(8)))
    out = np.stack([np.asarray(r["y"]).astype(np.float32) for r in res.results], axis=0)
    return out


# version 10
# speedup vs baseline: 1.2174x; 1.0252x over previous
import numpy as np
import ml_dtypes
import concourse.bass as bass
import concourse.mybir as mybir
from concourse.bass_utils import run_bass_kernel_spmd

F32 = mybir.dt.float32
BF16 = mybir.dt.bfloat16
AF = mybir.ActivationFunctionType
ALU = mybir.AluOpType

S = 2048
D = 4096
NB = D // 128
NT = S // 128
H = 32
SCALE_A = float(192.0 ** -0.5)
QL, KV = 12, 4
SCALE_B = 0.5
EPS = 1e-5
AG = 2048


class Op:
    __slots__ = ("engine", "fn", "dma", "deps", "needed", "rank", "waits",
                 "inc_sem", "inc_amt", "tag")

    def __init__(self, engine, fn, dma, tag):
        self.engine = engine
        self.fn = fn
        self.dma = dma
        self.deps = []
        self.needed = False
        self.rank = 0
        self.waits = []
        self.inc_sem = None
        self.inc_amt = 0
        self.tag = tag


class Builder:
    ENGINES = ("tensor", "vector", "scalar", "gpsimd", "sync")

    def __init__(self, nc):
        self.nc = nc
        self.ops = {e: [] for e in self.ENGINES}
        self.last_write = {}
        self.readers = {}

    def emit(self, engine, fn, reads=(), writes=(), dma=False, tag=""):
        op = Op(engine, fn, dma, tag)
        deps = {}
        for k in reads:
            w = self.last_write.get(k)
            if w is not None:
                deps[id(w)] = w
        for k in writes:
            w = self.last_write.get(k)
            if w is not None:
                deps[id(w)] = w
            for r in self.readers.get(k, ()):
                deps[id(r)] = r
        for d in deps.values():
            if d is op:
                continue
            if d.engine == "tensor" and engine == "tensor" and not d.dma and not dma:
                continue
            op.deps.append(d)
        for k in reads:
            self.readers.setdefault(k, []).append(op)
        for k in writes:
            self.last_write[k] = op
            self.readers[k] = []
        self.ops[engine].append(op)
        return op

    LANES = 8

    def finalize_and_emit(self, stack):
        nc = self.nc
        csems = {e: stack.enter_context(nc.semaphore(f"s_{e}"))
                 for e in ("tensor", "vector", "scalar", "sync")}
        lsems = {}
        for q in ("sync", "gpsimd", "scalar"):
            lsems[("d", q)] = [stack.enter_context(nc.semaphore(f"d_{q}{i}"))
                               for i in range(self.LANES)]
        lsems[("c", "gpsimd")] = [stack.enter_context(nc.semaphore(f"g_{i}"))
                                  for i in range(self.LANES)]
        for e in self.ENGINES:
            for op in self.ops[e]:
                for d in op.deps:
                    d.needed = True
        counters = {}
        for e in self.ENGINES:
            for op in self.ops[e]:
                if op.dma:
                    key = ("d", e)
                    i = counters.get(key, 0)
                    counters[key] = i + 1
                    op.rank = i
                    op.inc_sem = lsems[key][i % self.LANES]
                    op.inc_amt = 16
                    if i >= self.LANES:
                        op.waits.append((op.inc_sem, 16 * (i // self.LANES)))
                elif e == "gpsimd":
                    key = ("c", "gpsimd")
                    i = counters.get(key, 0)
                    counters[key] = i + 1
                    op.rank = i
                    op.inc_sem = lsems[key][i % self.LANES]
                    op.inc_amt = 1
                    if i >= self.LANES:
                        op.waits.append((op.inc_sem, i // self.LANES))
                elif op.needed:
                    key = ("c", e)
                    counters[key] = counters.get(key, 0) + 1
                    op.rank = counters[key]
                    op.inc_sem = csems[e]
                    op.inc_amt = 1
        for e in self.ENGINES:
            known = {}
            for op in self.ops[e]:
                for d in op.deps:
                    if d.dma:
                        sem = lsems[("d", d.engine)][d.rank % self.LANES]
                        val = 16 * (d.rank // self.LANES + 1)
                    elif d.engine == "gpsimd":
                        sem = lsems[("c", "gpsimd")][d.rank % self.LANES]
                        val = d.rank // self.LANES + 1
                    else:
                        sem, val = csems[d.engine], d.rank
                    kid = id(sem)
                    if known.get(kid, -1) < val:
                        op.waits.append((sem, val))
                        known[kid] = val
        blk = stack.enter_context(nc.Block())
        for ename in self.ENGINES:
            ops = self.ops[ename]

            def section(eng, ops=ops):
                for op in ops:
                    for (sem, val) in op.waits:
                        eng.wait_ge(sem, val)
                    inst = op.fn(eng)
                    if op.inc_sem is not None:
                        inst.then_inc(op.inc_sem, op.inc_amt)

            getattr(blk, ename)(section)


def build_program(upto="full"):
    nc = bass.Bass("TRN2", target_bir_lowering=False)
    x0 = nc.dram_tensor("x0", [S, D], F32, kind="ExternalInput").ap()
    wq = nc.dram_tensor("wq", [D, 6144], F32, kind="ExternalInput").ap()
    wkv = nc.dram_tensor("wkv", [D, 8256], F32, kind="ExternalInput").ap()
    wo_a = nc.dram_tensor("wo_a", [D, D], F32, kind="ExternalInput").ap()
    Wq_d = nc.dram_tensor("Wq_d", [D, QL], F32, kind="ExternalInput").ap()
    bq_d = nc.dram_tensor("bq_d", [1, QL], F32, kind="ExternalInput").ap()
    W_qk = nc.dram_tensor("W_qk", [QL, 128], F32, kind="ExternalInput").ap()
    b_qk = nc.dram_tensor("b_qk", [1, 128], F32, kind="ExternalInput").ap()
    Wkv_d = nc.dram_tensor("Wkv_d", [D, KV], F32, kind="ExternalInput").ap()
    bkv_d = nc.dram_tensor("bkv_d", [1, KV], F32, kind="ExternalInput").ap()
    Wv_u = nc.dram_tensor("Wv_u", [KV, D], F32, kind="ExternalInput").ap()
    bv_u = nc.dram_tensor("bv_u", [1, D], F32, kind="ExternalInput").ap()
    Wo_b = nc.dram_tensor("Wo_b", [D, D], F32, kind="ExternalInput").ap()
    bo_b = nc.dram_tensor("bo_b", [1, D], F32, kind="ExternalInput").ap()
    lora_A = nc.dram_tensor("lora_A", [128, D], F32, kind="ExternalInput").ap()
    lora_B = nc.dram_tensor("lora_B", [8 * D, 16], F32, kind="ExternalInput").ap()
    ffn_W = nc.dram_tensor("ffn_W", [D, D], F32, kind="ExternalInput").ap()
    ffn_bias = nc.dram_tensor("ffn_bias", [1, D], F32, kind="ExternalInput").ap()
    lora_w = nc.dram_tensor("lora_w", [1, 8], F32, kind="ExternalInput").ap()
    y = nc.dram_tensor("y", [S, D], F32, kind="ExternalOutput").ap()
    x1 = nc.dram_tensor("x1", [S, D], F32).ap()
    x2 = nc.dram_tensor("x2", [S, D], F32).ap()
    x3 = nc.dram_tensor("x3", [S, D], F32).ap()
    Ua = nc.dram_tensor("Ua", [D, S], BF16).ap()
    Abh = nc.dram_tensor("Abh", [D, S], BF16).ap()
    WT = nc.dram_tensor("WTs", [D, D], BF16).ap()
    AcH = nc.dram_tensor("AcH", [128, D], BF16).ap()
    BtH = nc.dram_tensor("BtH", [128, D], BF16).ap()
    cfH = nc.dram_tensor("cfH", [1, 128], F32).ap()
    dbg = nc.dram_tensor("dbg", [128, 32 * AG], BF16, kind="ExternalOutput").ap() \
        if upto in ("ln", "h0", "ua", "wt", "b0") else None

    from contextlib import ExitStack
    stack = ExitStack()
    sb = lambda name, shape, dt: stack.enter_context(nc.sbuf_tensor(name, shape, dt))
    psa = lambda name, shape, dt: stack.enter_context(nc.psum_tensor(name, shape, dt))
    B = Builder(nc)

    BIGA = sb("BIGA", [128, 32 * AG], BF16)
    BIGB = sb("BIGB", [128, 8 * AG], BF16)
    ones_row = sb("ones_row", [1, 512], BF16)
    ones_col = sb("ones_col", [128, 1], BF16)
    ident = sb("ident", [128, 128], BF16)
    masks = sb("masks", [128, 896], BF16)
    lacc = sb("lacc", [128, 512], F32)
    oncf = sb("oncf", [128, 1], F32)
    wqd_sb = sb("wqd_sb", [128, NB * QL], BF16)
    wkvd_sb = sb("wkvd_sb", [128, NB * KV], BF16)
    bqd_row = sb("bqd_row", [1, QL], BF16)
    bkvd_row = sb("bkvd_row", [1, KV], BF16)
    wqk_sb = sb("wqk_sb", [QL, 128], BF16)
    bqk_row = sb("bqk_row", [1, 128], BF16)
    wvu_h = sb("wvu_h", [KV, 128], BF16)
    bvu_h = sb("bvu_h", [1, 128], BF16)
    row_big = sb("row_big", [1, D], BF16)
    lw_sb = sb("lw_sb", [1, 8], F32)
    coef_col = sb("coef_col", [128, 1], F32)
    stat = sb("stat", [128, 16], F32)
    eps_col = sb("eps_col", [128, 1], F32)
    diag_bf = [sb(f"diag{i}", [128, 128], BF16) for i in range(2)]
    bcol_bf = [sb(f"bcolb{i}", [128, 1], BF16) for i in range(2)]
    brow_bf = [sb(f"browb{i}", [1, 128], BF16) for i in range(2)]
    Qt_sb = sb("Qt_sb", [128, S], BF16)
    Kt_sb = sb("Kt_sb", [128, S], BF16)
    expS = [sb(f"expS{i}", [128, 512], BF16) for i in range(4)]
    lhi = sb("lhi", [1, 512], BF16)
    llo = sb("llo", [1, 512], BF16)
    bcast_sb = sb("bcast_sb", [128, 512], F32)
    Uh_sb = sb("Uh_sb", [128, S], BF16)
    rbuf = [sb(f"rbuf{i}", [128, 512], F32) for i in range(2)]
    nbuf = [sb(f"nbuf{i}", [128, 512], F32) for i in range(2)]
    lrow = nbuf[0][0:1, :]
    lrec = nbuf[1][0:1, :]
    hbf = sb("hbf", [128, 1024], BF16)
    Ckvs_sb = sb("Ckvs_sb", [128, NT * 40], BF16)
    u5_sb = lacc
    Cq_sb = Uh_sb
    Ckv_sb = Kt_sb
    ctx_bf = Qt_sb
    bct_sb = Kt_sb

    ps = [psa(f"ps{i}", [128, 512], F32) for i in range(6)]
    psb = psa("psb", [128, 512], BF16)
    psb2 = psa("psb2", [128, 512], BF16)
    PK = [f"ps{i}" for i in range(6)]

    def A(c0, w):
        return BIGA[:, c0:c0 + w]

    def Bv(c0, w):
        return BIGB[:, c0:c0 + w]

    def akeys(c0, w):
        return [f"A{g}" for g in range(c0 // AG, (c0 + w - 1) // AG + 1)]

    def bkeys(c0, w):
        return [f"B{g}" for g in range(c0 // AG, (c0 + w - 1) // AG + 1)]

    XBF = [0, 4096]
    SQS = 8192
    QW, KW, VW2 = 0, 4096, 8192
    VT, VS = 12288, 14336
    ACT_ = 12288
    AH = [12288, 14336]
    UAT = [8192, 12288]
    WCOL = [0, 0]
    WSTG = 4096
    STGB = 0
    STG16 = 4096
    QH = 0

    def dma(queue, out_ap, in_ap, reads, writes, tag=""):
        return B.emit(queue, lambda e: e.dma_start(out_ap, in_ap), reads, writes,
                      dma=True, tag=tag)

    def mm(out_ap, lhsT_ap, rhs_ap, reads, writes, start, stop, tag=""):
        return B.emit("tensor",
                      lambda e: e.matmul(out_ap, lhsT_ap, rhs_ap, start=start, stop=stop),
                      reads, writes, tag=tag)

    def tr(out_ap, in_ap, p, reads, writes, tag=""):
        return B.emit("tensor",
                      lambda e: e.transpose(out_ap, in_ap, ident[0:p, 0:p]),
                      list(reads) + ["ident"], writes, tag=tag)

    def act(out_ap, in_ap, func, reads, writes, bias=0.0, scale=1.0, accum=None, tag=""):
        return B.emit("scalar",
                      lambda e: e.activation(out_ap, in_ap, func, bias=bias, scale=scale,
                                             accum_out=accum),
                      reads, writes, tag=tag)

    def vec(fn, reads, writes, tag=""):
        return B.emit("vector", fn, reads, writes, tag=tag)

    def gp(fn, reads, writes, tag=""):
        return B.emit("gpsimd", fn, reads, writes, tag=tag)

    gp(lambda e: e.memset(ones_row[:, :], 1.0), [], ["ones_row"])
    gp(lambda e: e.memset(ones_col[:, :], 1.0), [], ["ones_col"])
    gp(lambda e: e.memset(eps_col[:, :], EPS), [], ["eps_col"])
    gp(lambda e: e.memset(oncf[:, :], 1.0), [], ["oncf"])
    gp(lambda e: e.memset(masks[:, :], 1.0), ["mkM"], ["mkM"])
    gp(lambda e: e.affine_select(masks[:, :], masks[:, :], [[1, 896]], ALU.is_ge, 0.0,
                                 base=-384, channel_multiplier=-1), ["mkM"], ["masks"])
    gp(lambda e: e.memset(ident[:, :], 1.0), ["identM"], ["identM"])
    gp(lambda e: e.affine_select(ident[:, :], ident[:, :], [[1, 128]], ALU.is_equal, 0.0,
                                 base=0, channel_multiplier=-1), ["identM"], ["ident"])
    dma("gpsimd", wqd_sb[:, :], Wq_d.rearrange("(b p) l -> p b l", p=128), [], ["wqd_sb"])
    dma("gpsimd", wkvd_sb[:, :], Wkv_d.rearrange("(b p) l -> p b l", p=128), [], ["wkvd_sb"])
    dma("gpsimd", bqd_row[:, :], bq_d, [], ["bqd_row"])
    dma("gpsimd", bkvd_row[:, :], bkv_d, [], ["bkvd_row"])
    dma("gpsimd", wqk_sb[:, :], W_qk, [], ["wqk_sb"])
    dma("gpsimd", bqk_row[:, :], b_qk, [], ["bqk_row"])
    dma("sync", lw_sb[:, :], lora_w, [], ["lw_sb"])
    vec(lambda e: e.tensor_scalar_mul(
        lrec[0:1, 0:128], bass.AP(lw_sb, 0, [[8, 1], [1, 8], [0, 16]]), 2.0),
        ["lw_sb", "lrec"], ["coef_row"])
    dma("sync", cfH[:, :], lrec[0:1, 0:128], ["coef_row"], ["cfHk"])
    dma("sync", coef_col[:, :], cfH.rearrange("a (p o) -> p (a o)", p=128),
        ["cfHk"], ["coef_col"])
    dma("gpsimd", Bv(STGB, 4096), lora_A, bkeys(STGB, 4096),
        bkeys(STGB, 4096) + ["stgA"])
    for g4 in range(8):
        for j in range(4):
            db = g4 * 4 + j
            tr(psb[:, j * 128:(j + 1) * 128], Bv(STGB + db * 128, 128), 128,
               ["stgA"] + bkeys(STGB + db * 128, 128), ["psb"])
        B.emit("scalar",
               lambda e, g4=g4: e.copy(Bv(WSTG + g4 * 512, 512), psb[:, :]),
               ["psb"], bkeys(WSTG + g4 * 512, 512) + ["stgAc"])
    dma("sync", AcH[:, :], Bv(WSTG, 4096), ["stgAc"] + bkeys(WSTG, 4096), ["AcHk"])
    for n in range(8):
        dma("gpsimd", Bv(STGB, 512),
            lora_B[n * D:(n + 1) * D, :].rearrange("(bb p) r -> p bb r", p=128),
            ["stgA", "stgBn"] + bkeys(STGB, 512), bkeys(STGB, 512) + ["stgBn"])
        for b4 in range(8):
            for j in range(4):
                bb = b4 * 4 + j
                tr(psb[0:16, j * 128:(j + 1) * 128], Bv(STGB + bb * 16, 16), 128,
                   ["stgBn"] + bkeys(STGB, 512), ["psb"])
            B.emit("vector",
                   lambda e, b4=b4: e.tensor_copy(
                       BIGB[0:16, STG16 + b4 * 512: STG16 + (b4 + 1) * 512],
                       psb[0:16, 0:512]),
                   ["psb"] + bkeys(STG16, 4096), bkeys(STG16, 4096) + ["stg16"])
        dma("sync", BtH[n * 16:(n + 1) * 16, :], BIGB[0:16, STG16:STG16 + 4096],
            ["stg16"] + bkeys(STG16, 4096), [f"BtHk{n}"])
    bt_keys = [f"BtHk{n}" for n in range(8)]

    def layernorm_block(xin_ap, t, dst_base, dst_stride, dst_keys_fn, xin_keys, tag):
        sl = t % 2
        xb = Bv(XBF[sl], 4096)
        kx = bkeys(XBF[sl], 4096)
        dma("gpsimd", xb, xin_ap[t * 128:(t + 1) * 128, :], kx + xin_keys, kx,
            tag=f"{tag}x{t}")
        junk = Bv(SQS, 4096)
        kj = bkeys(SQS, 4096)
        act(junk, xb, AF.Copy, kx + kj, kj + [f"sum{sl}"],
            accum=stat[:, sl * 8:sl * 8 + 1], tag=f"{tag}cp{t}")
        act(junk, xb, AF.Square, kx + kj, kj + [f"sumsq{sl}"],
            accum=stat[:, sl * 8 + 1:sl * 8 + 2], tag=f"{tag}sq{t}")
        st = stat[:, sl * 8:sl * 8 + 8]
        vec(lambda e, st=st: e.tensor_scalar_mul(st[:, 2:3], st[:, 0:1], 1.0 / D),
            [f"sum{sl}"], [f"mean{sl}"])
        vec(lambda e, st=st: e.tensor_scalar_mul(st[:, 3:4], st[:, 1:2], 1.0 / D),
            [f"sumsq{sl}"], [f"msq{sl}"])
        vec(lambda e, st=st: e.tensor_mul(st[:, 4:5], st[:, 2:3], st[:, 2:3]),
            [f"mean{sl}"], [f"m2{sl}"])
        vec(lambda e, st=st: e.tensor_sub(st[:, 3:4], st[:, 3:4], st[:, 4:5]),
            [f"msq{sl}", f"m2{sl}"], [f"var{sl}"])
        act(st[:, 4:5], st[:, 3:4], AF.Ln, [f"var{sl}", f"m2{sl}", "eps_col"],
            [f"lnv{sl}"], bias=eps_col[:, 0:1])
        act(st[:, 5:6], st[:, 4:5], AF.Exp, [f"lnv{sl}"], [f"rstd0{sl}"], scale=-0.5)
        vec(lambda e, st=st: e.tensor_mul(st[:, 6:7], st[:, 5:6], st[:, 5:6]),
            [f"rstd0{sl}"], [f"r2{sl}"])
        vec(lambda e, st=st: e.tensor_scalar_add(st[:, 4:5], st[:, 3:4], EPS),
            [f"var{sl}", f"lnv{sl}"], [f"vpe{sl}"])
        vec(lambda e, st=st: e.tensor_mul(st[:, 6:7], st[:, 6:7], st[:, 4:5]),
            [f"r2{sl}", f"vpe{sl}"], [f"r2{sl}"])
        vec(lambda e, st=st: e.tensor_scalar(st[:, 6:7], st[:, 6:7], -0.5, 1.5,
                                             ALU.mult, ALU.add),
            [f"r2{sl}"], [f"r2{sl}"])
        vec(lambda e, st=st: e.tensor_mul(st[:, 5:6], st[:, 5:6], st[:, 6:7]),
            [f"rstd0{sl}", f"r2{sl}"], [f"rstd{sl}"])
        vec(lambda e, st=st: e.tensor_mul(st[:, 7:8], st[:, 2:3], st[:, 5:6]),
            [f"mean{sl}", f"rstd{sl}"], [f"mr{sl}"])
        vec(lambda e, st=st, sl=sl: e.tensor_scalar_mul(diag_bf[sl][:, :], ident[:, :],
                                                        st[:, 5:6]),
            ["ident", f"rstd{sl}"], [f"diag{sl}"])
        vec(lambda e, st=st, sl=sl: e.tensor_copy(bcol_bf[sl][:, :], st[:, 7:8]),
            [f"mr{sl}"], [f"bcol{sl}"])
        pbr = psb[0:1, sl * 128:sl * 128 + 128]
        tr(pbr, bcol_bf[sl][:, :], 128, [f"bcol{sl}"], ["psb"])
        act(brow_bf[sl][:, :], pbr, AF.Copy, ["psb"], [f"brow{sl}"], scale=-1.0)
        for g4 in range(8):
            bank = ps[g4 % 2]
            bkey = PK[g4 % 2]
            for j in range(4):
                db = g4 * 4 + j
                mm(bank[:, j * 128:(j + 1) * 128], xb[:, db * 128:(db + 1) * 128],
                   diag_bf[sl][:, :], kx + [f"diag{sl}"], [bkey], start=True, stop=False)
                mm(bank[:, j * 128:(j + 1) * 128], ones_row[0:1, 0:128],
                   brow_bf[sl][0:1, :], ["ones_row", f"brow{sl}"], [bkey],
                   start=False, stop=True)
            off = dst_base(g4 * 4) + 0
            dst_ap = bass.AP(BIGA, off, [[32 * AG, 128], [dst_stride, 4], [1, 128]])
            src_ap = bass.AP(bank, 0, [[512, 128], [128, 4], [1, 128]])
            vec(lambda e, dst_ap=dst_ap, src_ap=src_ap: e.tensor_copy(dst_ap, src_ap),
                [bkey], dst_keys_fn(g4), tag=f"{tag}ev{t}_{g4}")

    def wprep_steps():
        steps = []
        for i in range(NB):
            sl = i % 2

            def step(i=i, sl=sl):
                dma("gpsimd", Bv(WCOL[sl], 4096),
                    ffn_W[:, i * 128:(i + 1) * 128].rearrange("(b p) d -> p b d", p=128),
                    bkeys(WCOL[sl], 4096), bkeys(WCOL[sl], 4096), tag=f"wp{i}")
                for b4 in range(8):
                    for j in range(4):
                        b = b4 * 4 + j
                        tr(psb[:, j * 128:(j + 1) * 128], Bv(WCOL[sl] + b * 128, 128),
                           128, bkeys(WCOL[sl], 4096), ["psb"])
                    B.emit("scalar",
                           lambda e, b4=b4: e.copy(Bv(WSTG + b4 * 512, 512), psb[:, :]),
                           ["psb"], bkeys(WSTG + b4 * 512, 512) + ["wstg"])
                dma("sync", WT[i * 128:(i + 1) * 128, :], Bv(WSTG, 4096),
                    ["wstg"], [f"WTr{i}"])

            steps.append(step)
        return steps

    def phase_a():
        for t in range(NT):
            layernorm_block(
                x0, t,
                dst_base=lambda db, t=t: db * AG + t * 128,
                dst_stride=AG,
                dst_keys_fn=lambda g4: [f"A{g4 * 4 + j}" for j in range(4)],
                xin_keys=[], tag="a1")
        for h in range(H):
            qwk = bkeys(QW, 4096)
            kwk = bkeys(KW, 4096)
            vwk = bkeys(VW2, 4096)
            dma("gpsimd", Bv(QW, 4096),
                wq[:, h * 192: h * 192 + 128].rearrange("(b p) e -> p b e", p=128),
                qwk, qwk, tag=f"qw{h}")
            dma("gpsimd", Bv(KW, 4096),
                wkv[:, h * 128: (h + 1) * 128].rearrange("(b p) e -> p b e", p=128),
                kwk, kwk, tag=f"kw{h}")
            dma("gpsimd", Bv(VW2, 4096),
                wkv[:, 4096 + h * 128: 4096 + (h + 1) * 128].rearrange(
                    "(b p) e -> p b e", p=128),
                vwk, vwk, tag=f"vw{h}")
            vtk = bkeys(VT, 2048)
            vsk = bkeys(VS, 2048)
            for c in range(4):
                bv_ = ps[2 + c % 2]
                for db in range(NB):
                    mm(bv_[:, :], Bv(VW2 + db * 128, 128), A(db * AG + c * 512, 512),
                       vwk + [f"A{db}"], [PK[2 + c % 2]], start=(db == 0),
                       stop=(db == NB - 1))
                act(Bv(VT + c * 512, 512), bv_[:, :], AF.Copy,
                    [PK[2 + c % 2]] + (vtk if c == 3 else []), [f"VTc{c}"])

            def v_transpose(g4):
                pb = psb if g4 % 2 == 0 else psb2
                pbk = "psb" if g4 % 2 == 0 else "psb2"
                for j in range(4):
                    kb = g4 * 4 + j
                    tr(pb[:, j * 128:(j + 1) * 128], Bv(VT + kb * 128, 128), 128,
                       [f"VTc{g4}"] + (vtk if g4 == 3 and j == 3 else []), [pbk])
                act(Bv(VS + g4 * 512, 512), pb[:, :], AF.Copy,
                    [pbk] + (vsk if g4 == 3 else []), [f"VSc{g4}"] + (vsk if g4 == 3 else []))

            for c in range(4):
                bq, bk = ps[4], ps[5]
                for db in range(NB):
                    mm(bq[:, :], Bv(QW + db * 128, 128), A(db * AG + c * 512, 512),
                       qwk + [f"A{db}"], [PK[4]], start=(db == 0), stop=(db == NB - 1))
                for db in range(NB):
                    mm(bk[:, :], Bv(KW + db * 128, 128), A(db * AG + c * 512, 512),
                       kwk + [f"A{db}"], [PK[5]], start=(db == 0), stop=(db == NB - 1))
                act(Qt_sb[:, c * 512:(c + 1) * 512], bq[:, :], AF.Copy, [PK[4]], ["Qt"])
                act(Kt_sb[:, c * 512:(c + 1) * 512], bk[:, :], AF.Copy, [PK[5]], ["Kt"])
            deferred = [None]

            def fin_mms(c, ub):
                bb_i = (0, 1, 4, 5)[(4 * c + 5) % 4]
                bB = ps[bb_i]
                mm(bB[:, :], ones_row[0:1, 0:128], lhi[0:1, :], ["ones_row", "lhi"],
                   [PK[bb_i]], start=True, stop=False)
                mm(bB[:, :], ones_row[0:1, 0:128], llo[0:1, :], ["ones_row", "llo"],
                   [PK[bb_i]], start=False, stop=True)
                vec(lambda e, bB=bB: e.tensor_copy(bcast_sb[:, :], bB[:, :]), [PK[bb_i]],
                    ["bcast"])
                vec(lambda e, c=c, ub=ub: e.tensor_mul(Uh_sb[:, c * 512:(c + 1) * 512],
                                                       ps[ub][:, :], bcast_sb[:, :]),
                    [PK[ub], "bcast"], ["Uh"], tag=f"Un{h}_{c}")

            upend = []

            def u_flush(n):
                while upend and len(upend) > n:
                    upend.pop(0)()

            v_transpose(0)
            for c in range(4):
                nkb = 4 * c + 4
                ub = 2 + c % 2
                ROT = (0, 1, 4, 5)
                for kb in range(nkb):
                    u_flush(2)
                    if kb == 1 and c < 3:
                        v_transpose(c + 1)
                    bS = ps[ROT[kb % 4]]
                    skey = PK[ROT[kb % 4]]
                    ekey = f"expS{kb % 4}"
                    mm(bS[:, :], Kt_sb[:, kb * 128:(kb + 1) * 128],
                       Qt_sb[:, c * 512:(c + 1) * 512], ["Kt", "Qt"], [skey],
                       start=True, stop=True, tag=f"sc{h}_{c}_{kb}")
                    act(expS[kb % 4][:, :], bS[:, :], AF.Exp, [skey], [ekey],
                        scale=SCALE_A, tag=f"exp{h}_{c}_{kb}")
                    if kb >= 4 * c:
                        m = kb - 4 * c
                        vec(lambda e, kb=kb, m=m: e.tensor_mul(
                            expS[kb % 4][:, :], expS[kb % 4][:, :],
                            masks[:, 384 - 128 * m: 896 - 128 * m]),
                            [ekey, "masks"], [ekey], tag=f"msk{h}_{c}_{kb}")
                    def u_mm(kb=kb, c=c, ub=ub, nkb=nkb, ekey=ekey):
                        mm(ps[ub][:, :], Bv(VS + kb * 128, 128), expS[kb % 4][:, :],
                           [f"VSc{kb // 4}", ekey], [PK[ub]], start=(kb == 0),
                           stop=(kb == nkb - 1))
                    upend.append(u_mm)
                    if kb == 0:
                        vec(lambda e, kb=kb: e.tensor_copy(lacc[:, :], expS[kb % 4][:, :]),
                            [ekey, "lacc"], ["lacc"])
                    else:
                        vec(lambda e, kb=kb: e.tensor_add(lacc[:, :], lacc[:, :],
                                                          expS[kb % 4][:, :]),
                            [ekey, "lacc"], ["lacc"])
                    if kb == 5 and deferred[0] is not None:
                        deferred[0]()
                        deferred[0] = None
                u_flush(0)
                lb = ROT[nkb % 4]
                mm(ps[lb][0:1, :], oncf[:, :], lacc[:, :], ["oncf", "lacc"], [PK[lb]],
                   start=True, stop=True)
                if deferred[0] is not None:
                    deferred[0]()
                    deferred[0] = None
                act(lrow[:, :], ps[lb][0:1, :], AF.Ln, [PK[lb], "lrow"], ["lrow"])
                act(lrec[:, :], lrow[:, :], AF.Exp, ["lrow", "coef_row"], ["lrec"],
                    scale=-1.0)
                act(lhi[:, :], lrec[:, :], AF.Copy, ["lrec", "lhi"], ["lhi"])
                vec(lambda e: e.tensor_sub(lrow[:, :], lrec[:, :], lhi[:, :]),
                    ["lrec", "lhi", "lrow"], ["llo_f"])
                vec(lambda e: e.tensor_copy(llo[:, :], lrow[:, :]), ["llo_f"], ["llo"])
                deferred[0] = (lambda c=c, ub=ub: fin_mms(c, ub))
            if deferred[0] is not None:
                deferred[0]()
                deferred[0] = None
            dma("sync", Ua[h * 128:(h + 1) * 128, :], Uh_sb[:, :], ["Uh"], [f"UaH{h}"],
                tag=f"spill{h}")
        ua_ready = [f"UaH{h}" for h in range(H)]
        wsteps = wprep_steps()
        wi = 0
        outk = []
        for Q in range(4):
            for e_blk in range(NB):
                wkey = akeys(e_blk * 1024, 1024)
                dma("gpsimd", A(e_blk * 1024, 1024),
                    wo_a[e_blk * 128:(e_blk + 1) * 128, Q * 1024:(Q + 1) * 1024],
                    wkey, wkey, tag=f"wo{Q}_{e_blk}")
            for tp in range(NT // 4):
                for _ in range(4):
                    if wi < len(wsteps):
                        wsteps[wi]()
                        wi += 1
                usl = tp % 2
                ubase = 32768 + usl * 16384
                uk = akeys(ubase, 16384)
                dma("sync", A(ubase, 16384),
                    Ua[:, tp * 512:(tp + 1) * 512].rearrange("(eb p) q -> p eb q", p=128),
                    ua_ready + uk, uk, tag=f"uat{Q}_{tp}")
                for tj in range(4):
                    t = tp * 4 + tj
                    for oc in range(2):
                        rsl = oc
                        dma("scalar", rbuf[rsl][:, :],
                            x0[t * 128:(t + 1) * 128,
                               Q * 1024 + oc * 512: Q * 1024 + (oc + 1) * 512],
                            [f"rbuf{rsl}"], [f"rbuf{rsl}"])
                        bi = oc + 2 * (tj % 2)
                        bank = ps[bi]
                        for e_blk in range(NB):
                            mm(bank[:, :], A(ubase + e_blk * 512 + tj * 128, 128),
                               A(e_blk * 1024 + oc * 512, 512),
                               uk + akeys(e_blk * 1024 + oc * 512, 512), [PK[bi]],
                               start=(e_blk == 0), stop=(e_blk == NB - 1))
                        vec(lambda e, bank=bank, rsl=rsl: e.tensor_add(
                            nbuf[rsl][:, :], bank[:, :], rbuf[rsl][:, :]),
                            [PK[bi], f"rbuf{rsl}"], [f"nbuf{rsl}"], tag=f"woev{Q}_{t}_{oc}")
                        o8 = Q * 2 + oc
                        outk.append(f"x1r{t}_{o8}")
                        dma("scalar", x1[t * 128:(t + 1) * 128, o8 * 512:(o8 + 1) * 512],
                            nbuf[rsl][:, :], [f"nbuf{rsl}"], [f"x1r{t}_{o8}"],
                            tag=f"x1st{t}_{o8}")
        while wi < len(wsteps):
            wsteps[wi]()
            wi += 1
        return outk

    def phase_f(xin_ap, xout_ap, xname, xin_name, inst):
        ack = bkeys(ACT_, 4096)
        dma("sync", Bv(ACT_, 4096), AcH[:, :], ["AcHk"] + ack, ack, tag=f"f{inst}Ac")
        dma("gpsimd", row_big[:, :], ffn_bias, ["row_big"], ["row_big"], tag=f"f{inst}fb")
        wt_keys = [f"WTr{i}" for i in range(NB)]
        outk = []
        for ch in range(2):
            for tt in range(8):
                t = ch * 8 + tt
                layernorm_block(
                    xin_ap, t,
                    dst_base=lambda db, tt=tt: db * 1024 + tt * 128,
                    dst_stride=1024,
                    dst_keys_fn=lambda g4: [f"A{(g4 * 4) // 2}", f"A{(g4 * 4 + 3) // 2}"],
                    xin_keys=[f"{xin_name}r{t}_{o}" for o in range(8)],
                    tag=f"f{inst}ln")
            for half in range(2):
                bank = ps[4 + half]
                for db in range(NB):
                    mm(bank[:, :], Bv(ACT_ + db * 128, 128),
                       A(db * 1024 + half * 512, 512),
                       ack + [f"A{(db * 1024 + half * 512) // AG}"], [PK[4 + half]],
                       start=(db == 0), stop=(db == NB - 1))
                act(hbf[:, half * 512:(half + 1) * 512], bank[:, :], AF.Copy,
                    [PK[4 + half], "coef_col"], ["hbf"], scale=coef_col[:, 0:1])
            for oc in range(8):
                wsl = oc % 2
                wck = akeys(32768 + wsl * 16384, 16384)
                dma("sync", A(32768 + wsl * 16384, 16384),
                    WT[:, oc * 512:(oc + 1) * 512].rearrange("(b p) o -> p b o", p=128),
                    wt_keys + wck, wck, tag=f"f{inst}wt{ch}_{oc}")
                dma("gpsimd", bct_sb[:, 0:512], BtH[:, oc * 512:(oc + 1) * 512],
                    bt_keys + ["Kt"], ["Kt"], tag=f"f{inst}bt{ch}_{oc}")
                for tt in range(8):
                    t = ch * 8 + tt
                    bank = ps[tt % 2]
                    for db in range(NB):
                        mm(bank[:, :], A(db * 1024 + tt * 128, 128),
                           A(32768 + wsl * 16384 + db * 512, 512),
                           [f"A{(db * 1024 + tt * 128) // AG}",
                            f"A{(32768 + wsl * 16384 + db * 512) // AG}"],
                           [PK[tt % 2]], start=(db == 0), stop=False)
                    mm(bank[:, :], hbf[:, tt * 128:(tt + 1) * 128], bct_sb[:, 0:512],
                       ["hbf", "Kt"], [PK[tt % 2]], start=False, stop=False)
                    mm(bank[:, :], ones_row[0:1, 0:128],
                       row_big[0:1, oc * 512:(oc + 1) * 512],
                       ["ones_row", "row_big"], [PK[tt % 2]], start=False, stop=True)
                    rsl = tt % 2
                    dma("scalar", rbuf[rsl][:, :],
                        xin_ap[t * 128:(t + 1) * 128, oc * 512:(oc + 1) * 512],
                        [f"{xin_name}r{t}_{oc}", f"rbuf{rsl}"], [f"rbuf{rsl}"])
                    vec(lambda e, bank=bank, rsl=rsl: e.tensor_add(
                        nbuf[rsl][:, :], bank[:, :], rbuf[rsl][:, :]),
                        [PK[tt % 2], f"rbuf{rsl}"], [f"nbuf{rsl}"])
                    k = f"{xname}r{t}_{oc}"
                    outk.append(k)
                    dma("scalar", xout_ap[t * 128:(t + 1) * 128, oc * 512:(oc + 1) * 512],
                        nbuf[rsl][:, :], [f"nbuf{rsl}"], [k])
        return outk

    def phase_b(xin_ap, xout_ap, xin_name):
        for ch in range(4):
            for tt in range(4):
                t = ch * 4 + tt
                layernorm_block(
                    xin_ap, t,
                    dst_base=lambda db, tt=tt: db * 512 + tt * 128,
                    dst_stride=512,
                    dst_keys_fn=lambda g4: [f"A{g4}"],
                    xin_keys=[f"{xin_name}r{t}_{o}" for o in range(8)],
                    tag="bln")
            bq, bkv = ps[2], ps[3]
            for db in range(NB):
                mm(bq[0:QL, :], wqd_sb[:, db * QL:(db + 1) * QL], A(db * 512, 512),
                   ["wqd_sb", f"A{db // 4}"], [PK[2]], start=(db == 0), stop=False)
            mm(bq[0:QL, :], bqd_row[0:1, :], ones_row[0:1, :], ["bqd_row", "ones_row"],
               [PK[2]], start=False, stop=True)
            for db in range(NB):
                mm(bkv[0:KV, :], wkvd_sb[:, db * KV:(db + 1) * KV], A(db * 512, 512),
                   ["wkvd_sb", f"A{db // 4}"], [PK[3]], start=(db == 0), stop=False)
            mm(bkv[0:KV, :], bkvd_row[0:1, :], ones_row[0:1, :], ["bkvd_row", "ones_row"],
               [PK[3]], start=False, stop=True)
            act(Cq_sb[0:QL, ch * 512:(ch + 1) * 512], bq[0:QL, :], AF.Copy, [PK[2]], ["Uh"])
            act(Ckv_sb[0:KV, ch * 512:(ch + 1) * 512], bkv[0:KV, :], AF.Copy, [PK[3]], ["Kt"])
        vec(lambda e: e.memset(Ckvs_sb[:, :], 0.0), ["Ckvs"], ["CkvsM"])
        for kb in range(NT):
            pkc = psb[0:128, (kb % 4) * 128:(kb % 4) * 128 + 4]
            tr(pkc, Ckv_sb[0:KV, kb * 128:(kb + 1) * 128], KV, ["Kt"], ["psb"])
            vec(lambda e, pkc=pkc, kb=kb: e.tensor_copy(
                Ckvs_sb[:, kb * 40: kb * 40 + 4], pkc),
                ["psb", "CkvsM"], ["CkvsM"])
        vec(lambda e: e.memset(bass.AP(Ckvs_sb, 32, [[NT * 40, 128], [40, NT], [1, 1]]),
                               1.0),
            ["CkvsM"], ["Ckvs"])
        dma("gpsimd", row_big[:, :], bv_u, ["row_big"], ["row_big"], tag="b_bvu")
        for h in range(H):
            sl = h % 2
            dma("gpsimd", wvu_h[:, :], Wv_u[:, h * 128:(h + 1) * 128], ["wvu_h"],
                ["wvu_h"], tag=f"wvu{h}")
            qhk = bkeys(QH, 2048)
            for c in range(4):
                bqh = ps[4]
                mm(bqh[0:KV, :], wqk_sb[:, h * KV:(h + 1) * KV],
                   Cq_sb[0:QL, c * 512:(c + 1) * 512], ["wqk_sb", "Uh"], [PK[4]],
                   start=True, stop=False)
                mm(bqh[0:KV, :], bqk_row[0:1, h * KV:(h + 1) * KV], ones_row[0:1, :],
                   ["bqk_row", "ones_row"], [PK[4]], start=False, stop=True)
                act(BIGB[0:KV, QH + c * 512: QH + (c + 1) * 512], bqh[0:KV, :], AF.Copy,
                    [PK[4]] + qhk, qhk)
            BROT = (0, 1, 4, 5)

            def b_score(kb, c, i):
                bS = ps[BROT[i % 4]]
                skey = PK[BROT[i % 4]]
                ekey = f"expS{i % 4}"
                mm(bS[:, :], Ckv_sb[0:KV, kb * 128:(kb + 1) * 128],
                   BIGB[0:KV, QH + c * 512: QH + (c + 1) * 512], ["Kt"] + qhk,
                   [skey], start=True, stop=True, tag=f"bsc{h}_{c}_{kb}")
                act(expS[i % 4][:, :], bS[:, :], AF.Exp, [skey], [ekey], scale=SCALE_B)

            def b_u(kb, c, i):
                mm(ps[2 + c % 2][0:33, :], Ckvs_sb[:, kb * 40: kb * 40 + 33],
                   expS[i % 4][:, :],
                   ["Ckvs", f"expS{i % 4}"], [PK[2 + c % 2]], start=(kb == 0),
                   stop=(kb == NT - 1))

            def b_chain(c):
                bU = ps[2 + c % 2]
                act(lrow[:, :], bU[32:33, :], AF.Ln, [PK[2 + c % 2], "lrow"], ["lrow"])
                act(lrec[:, :], lrow[:, :], AF.Exp, ["lrow", "coef_row"], ["lrec"],
                    scale=-1.0)
                act(lhi[:, :], lrec[:, :], AF.Copy, ["lrec", "lhi"], ["lhi"])
                vec(lambda e: e.tensor_sub(lrow[:, :], lrec[:, :], lhi[:, :]),
                    ["lrec", "lhi", "lrow"], ["llo_f"])
                vec(lambda e: e.tensor_copy(llo[:, :], lrow[:, :]), ["llo_f"], ["llo"])

            def b_final(c):
                bU = ps[2 + c % 2]
                vec(lambda e, bU=bU: e.tensor_copy(u5_sb[0:4, :], bU[0:4, :]),
                    [PK[2 + c % 2], "u5"], ["u5"])
                bb_i = BROT[c % 4]
                bB = ps[bb_i]
                mm(bB[:, :], ones_row[0:1, 0:128], lhi[0:1, :], ["ones_row", "lhi"],
                   [PK[bb_i]], start=True, stop=False)
                mm(bB[:, :], ones_row[0:1, 0:128], llo[0:1, :], ["ones_row", "llo"],
                   [PK[bb_i]], start=False, stop=True)
                vec(lambda e, bB=bB: e.tensor_copy(bcast_sb[:, :], bB[:, :]),
                    [PK[bb_i], "bcast"], ["bcast"])
                vec(lambda e: e.tensor_mul(ctx_bf[0:KV, 0:512], u5_sb[0:KV, :],
                                           bcast_sb[0:KV, :]),
                    ["u5", "bcast"], ["Qt"])
                bt_i = BROT[(c + 1) % 4]
                bT = ps[bt_i]
                mm(bT[:, :], wvu_h[0:KV, :], ctx_bf[0:KV, 0:512],
                   ["wvu_h", "Qt"], [PK[bt_i]], start=True, stop=False)
                mm(bT[:, :], row_big[0:1, h * 128:(h + 1) * 128], ones_row[0:1, :],
                   ["row_big", "ones_row"], [PK[bt_i]], start=False, stop=True)
                ahk = bkeys(AH[sl], 2048)
                act(Bv(AH[sl] + c * 512, 512), bT[:, :], AF.Copy,
                    [PK[bt_i]] + ahk, ahk)

            for cpass in range(2):
                cA, cB = 2 * cpass, 2 * cpass + 1
                flat = [(kb, (cA, cB)[j], 2 * kb + j) for kb in range(NT)
                        for j in range(2)]
                for idx, (kb, c, i) in enumerate(flat):
                    if idx >= 4:
                        b_u(*flat[idx - 4])
                    b_score(kb, c, i)
                for idx in range(len(flat) - 4, len(flat)):
                    b_u(*flat[idx])
                    if idx == len(flat) - 2:
                        b_chain(cA)
                b_final(cA)
                b_chain(cB)
                b_final(cB)
            dma("sync", Abh[h * 128:(h + 1) * 128, :], Bv(AH[sl], 2048),
                bkeys(AH[sl], 2048), [f"AbH{h}"])
        ab_ready = [f"AbH{h}" for h in range(H)]
        dma("gpsimd", row_big[:, :], bo_b, ["row_big"], ["row_big"], tag="b_bo")
        outk = []
        for Q in range(4):
            for e_blk in range(NB):
                wkey = akeys(e_blk * 1024, 1024)
                dma("gpsimd", A(e_blk * 1024, 1024),
                    Wo_b[e_blk * 128:(e_blk + 1) * 128, Q * 1024:(Q + 1) * 1024],
                    wkey, wkey, tag=f"bwo{Q}_{e_blk}")
            for tp in range(NT // 4):
                usl = tp % 2
                ubase = 32768 + usl * 16384
                uk = akeys(ubase, 16384)
                dma("sync", A(ubase, 16384),
                    Abh[:, tp * 512:(tp + 1) * 512].rearrange("(eb p) q -> p eb q", p=128),
                    ab_ready + uk, uk)
                for tj in range(4):
                    t = tp * 4 + tj
                    for oc in range(2):
                        rsl = oc
                        o8 = Q * 2 + oc
                        dma("scalar", rbuf[rsl][:, :],
                            xin_ap[t * 128:(t + 1) * 128, o8 * 512:(o8 + 1) * 512],
                            [f"{xin_name}r{t}_{o8}", f"rbuf{rsl}"], [f"rbuf{rsl}"])
                        bi = oc + 2 * (tj % 2)
                        bank = ps[bi]
                        for e_blk in range(NB):
                            mm(bank[:, :], A(ubase + e_blk * 512 + tj * 128, 128),
                               A(e_blk * 1024 + oc * 512, 512),
                               uk + akeys(e_blk * 1024 + oc * 512, 512), [PK[bi]],
                               start=(e_blk == 0), stop=False)
                        mm(bank[:, :], ones_row[0:1, 0:128],
                           row_big[0:1, o8 * 512:(o8 + 1) * 512],
                           ["ones_row", "row_big"], [PK[bi]], start=False, stop=True)
                        vec(lambda e, bank=bank, rsl=rsl: e.tensor_add(
                            nbuf[rsl][:, :], bank[:, :], rbuf[rsl][:, :]),
                            [PK[bi], f"rbuf{rsl}"], [f"nbuf{rsl}"])
                        outk.append(f"x3r{t}_{o8}")
                        dma("scalar", xout_ap[t * 128:(t + 1) * 128, o8 * 512:(o8 + 1) * 512],
                            nbuf[rsl][:, :], [f"nbuf{rsl}"], [f"x3r{t}_{o8}"])
        return outk

    if upto == "b0":
        keys = phase_a()
        k2 = phase_f(x1, x2, "x2", "x1", 1)
        k3 = phase_b(x2, x3, "x2")
        final = []
        final = list(k3)
        for i in range(32):
            sl = i % 2
            kk = bkeys(UAT[sl], 4096)
            dma("sync", Bv(UAT[sl], 2048), Abh[i * 128:(i + 1) * 128, :],
                k3 + kk, kk)
            dma("sync", dbg[:, i * 2048:(i + 1) * 2048], Bv(UAT[sl], 2048), kk,
                [f"dg{i}"])
            final.append(f"dg{i}")
        dma("sync", y[0:128, 0:512], rbuf[0][:, :], ["rbuf0"], ["yk"])
        final.append("yk")
    elif upto == "wt":
        wsteps = wprep_steps()
        for st_ in wsteps:
            st_()
        final = []
        for i in range(8):
            sl = i % 2
            kk = bkeys(UAT[sl], 4096)
            dma("sync", Bv(UAT[sl], 4096), WT[i * 128:(i + 1) * 128, :],
                [f"WTr{i}"] + kk, kk)
            dma("sync", dbg[:, i * 4096:(i + 1) * 4096], Bv(UAT[sl], 4096), kk,
                [f"dw{i}"])
            final.append(f"dw{i}")
        ack = bkeys(ACT_, 4096)
        dma("sync", Bv(ACT_, 4096), AcH[:, :], ["AcHk"] + ack, ack)
        dma("sync", dbg[:, 8 * 4096: 8 * 4096 + 4096], Bv(ACT_, 4096), ack, ["dA"])
        final.append("dA")
        bck = bkeys(XBF[0], 4096)
        dma("sync", Bv(XBF[0], 4096), BtH[:, :], bt_keys + bck, bck)
        dma("sync", dbg[:, 9 * 4096: 9 * 4096 + 4096], Bv(XBF[0], 4096), bck, ["dB"])
        final.append("dB")
        vec(lambda e: e.tensor_copy(expS[0][:, 0:1], coef_col[:, 0:1]),
            ["coef_col", "expS0"], ["expS0"])
        vec(lambda e: e.memset(expS[0][:, 1:512], 0.0), ["expS0"], ["expS0"])
        dma("sync", dbg[:, 10 * 4096: 10 * 4096 + 512], expS[0][:, 0:512], ["expS0"], ["dC"])
        final.append("dC")
        dma("sync", y[0:128, 0:512], rbuf[0][:, :], ["rbuf0"], ["yk"])
        final.append("yk")
    elif upto == "ua":
        keys = phase_a()
        final = []
        for i in range(32):
            sl = i % 2
            kk = bkeys(UAT[sl], 4096)
            dma("sync", Bv(UAT[sl], 2048), Ua[i * 128:(i + 1) * 128, :],
                keys + kk, kk)
            dma("sync", dbg[:, i * 2048:(i + 1) * 2048], Bv(UAT[sl], 2048), kk,
                [f"dg{i}"])
            final.append(f"dg{i}")
        dma("sync", y[0:128, 0:512], rbuf[0][:, :], ["rbuf0"], ["yk"])
        final.append("yk")
    elif upto == "h0":
        for t in range(NT):
            layernorm_block(
                x0, t,
                dst_base=lambda db, t=t: db * AG + t * 128,
                dst_stride=AG,
                dst_keys_fn=lambda g4: [f"A{g4 * 4 + j}" for j in range(4)],
                xin_keys=[], tag="a1")
        h = 0
        qwk = bkeys(QW, 4096)
        kwk = bkeys(KW, 4096)
        vwk = bkeys(VW2, 4096)
        dma("gpsimd", Bv(QW, 4096),
            wq[:, h * 192: h * 192 + 128].rearrange("(b p) e -> p b e", p=128),
            qwk, qwk)
        dma("gpsimd", Bv(KW, 4096),
            wkv[:, h * 128: (h + 1) * 128].rearrange("(b p) e -> p b e", p=128),
            kwk, kwk)
        dma("gpsimd", Bv(VW2, 4096),
            wkv[:, 4096 + h * 128: 4096 + (h + 1) * 128].rearrange(
                "(b p) e -> p b e", p=128),
            vwk, vwk)
        vtk = bkeys(VT, 2048)
        vsk = bkeys(VS, 2048)
        for c in range(4):
            bv_ = ps[2 + c % 2]
            for db in range(NB):
                mm(bv_[:, :], Bv(VW2 + db * 128, 128), A(db * AG + c * 512, 512),
                   vwk + [f"A{db}"], [PK[2 + c % 2]], start=(db == 0),
                   stop=(db == NB - 1))
            act(Bv(VT + c * 512, 512), bv_[:, :], AF.Copy, [PK[2 + c % 2]] + vtk, vtk)
        for g4 in range(4):
            for j in range(4):
                kb = g4 * 4 + j
                tr(psb[:, j * 128:(j + 1) * 128], Bv(VT + kb * 128, 128), 128,
                   vtk, ["psb"])
            B.emit("scalar",
                   lambda e, g4=g4: e.copy(Bv(VS + g4 * 512, 512), psb[:, :]),
                   ["psb"] + vsk, vsk)
        for c in range(4):
            bq, bk = ps[4], ps[5]
            for db in range(NB):
                mm(bq[:, :], Bv(QW + db * 128, 128), A(db * AG + c * 512, 512),
                   qwk + [f"A{db}"], [PK[4]], start=(db == 0), stop=(db == NB - 1))
            for db in range(NB):
                mm(bk[:, :], Bv(KW + db * 128, 128), A(db * AG + c * 512, 512),
                   kwk + [f"A{db}"], [PK[5]], start=(db == 0), stop=(db == NB - 1))
            act(Qt_sb[:, c * 512:(c + 1) * 512], bq[:, :], AF.Copy, [PK[4]], ["Qt"])
            act(Kt_sb[:, c * 512:(c + 1) * 512], bk[:, :], AF.Copy, [PK[5]], ["Kt"])
        for c in range(4):
            nkb = 4 * c + 4
            bU, bL = ps[2], ps[3]
            for kb in range(nkb):
                bS = ps[kb % 2]
                skey = PK[kb % 2]
                ekey = f"expS{kb % 2}"
                mm(bS[:, :], Kt_sb[:, kb * 128:(kb + 1) * 128],
                   Qt_sb[:, c * 512:(c + 1) * 512], ["Kt", "Qt"], [skey],
                   start=True, stop=True)
                act(expS[kb % 2][:, :], bS[:, :], AF.Exp, [skey], [ekey],
                    scale=SCALE_A)
                if kb >= 4 * c:
                    m = kb - 4 * c
                    gp(lambda e, kb=kb, m=m: e.affine_select(
                        expS[kb % 2][:, :], expS[kb % 2][:, :], [[1, 512]],
                        ALU.is_ge, 0.0, base=-128 * m, channel_multiplier=-1),
                        [ekey], [ekey])
                mm(bU[:, :], Bv(VS + kb * 128, 128), expS[kb % 2][:, :],
                   vsk + [ekey], [PK[2]], start=(kb == 0), stop=(kb == nkb - 1))
                mm(bL[0:1, :], ones_col[:, :], expS[kb % 2][:, :],
                   ["ones_col", ekey], [PK[3]],
                   start=(kb == 0), stop=(kb == nkb - 1))
            vec(lambda e: e.tensor_copy(lrow[:, :], ps[3][0:1, :]), [PK[3]], ["lrow"])
            vec(lambda e: e.reciprocal(lrec[:, :], lrow[:, :]), ["lrow", "coef_row"],
                ["lrec"])
            vec(lambda e: e.tensor_copy(lhi[:, :], lrec[:, :]), ["lrec"], ["lhi"])
            vec(lambda e: e.tensor_sub(lrow[:, :], lrec[:, :], lhi[:, :]),
                ["lrec", "lhi", "lrow"], ["llo_f"])
            vec(lambda e: e.tensor_copy(llo[:, :], lrow[:, :]), ["llo_f"], ["llo"])
            bB = ps[4]
            mm(bB[:, :], ones_row[0:1, 0:128], lhi[0:1, :], ["ones_row", "lhi"],
               [PK[4]], start=True, stop=False)
            mm(bB[:, :], ones_row[0:1, 0:128], llo[0:1, :], ["ones_row", "llo"],
               [PK[4]], start=False, stop=True)
            vec(lambda e: e.tensor_copy(bcast_sb[:, :], ps[4][:, :]), [PK[4]],
                ["bcast"])
            vec(lambda e, c=c: e.tensor_mul(Uh_sb[:, c * 512:(c + 1) * 512],
                                            ps[2][:, :], bcast_sb[:, :]),
                [PK[2], "bcast"], ["Uh"])
        dma("sync", dbg[:, 0:2048], Qt_sb[:, :], ["Qt"], ["d0"])
        dma("sync", dbg[:, 2048:4096], Kt_sb[:, :], ["Kt"], ["d1"])
        dma("sync", dbg[:, 4096:6144], Bv(VS, 2048), vsk, ["d2"])
        dma("sync", dbg[:, 6144:8192], Uh_sb[:, :], ["Uh"], ["d3"])
        dma("sync", y[0:128, 0:512], rbuf[0][:, :], ["rbuf0"], ["yk"])
        final = ["d0", "d1", "d2", "d3", "yk"]
    elif upto == "ln":
        for t in range(NT):
            layernorm_block(
                x0, t,
                dst_base=lambda db, t=t: db * AG + t * 128,
                dst_stride=AG,
                dst_keys_fn=lambda g4: [f"A{g4 * 4 + j}" for j in range(4)],
                xin_keys=[], tag="a1")
        dma("sync", dbg, BIGA[:, :], [f"A{g}" for g in range(32)], ["dbgk"])
        dma("sync", y[0:128, 0:512], rbuf[0][:, :], ["rbuf0"], ["yk"])
        final = ["dbgk", "yk"]
    elif upto == "a":
        keys = phase_a()
        for t in range(NT):
            for o in range(8):
                sl = o % 2
                dma("sync", rbuf[sl][:, :], x1[t * 128:(t + 1) * 128, o * 512:(o + 1) * 512],
                    keys + [f"rbuf{sl}"], [f"rbuf{sl}"])
                dma("sync", y[t * 128:(t + 1) * 128, o * 512:(o + 1) * 512], rbuf[sl][:, :],
                    [f"rbuf{sl}"], [f"yk{t}_{o}"])
        final = [f"yk{t}_{o}" for t in range(NT) for o in range(8)]
    elif upto == "f1":
        keys = phase_a()
        final = phase_f(x1, y, "y", "x1", 1)
    elif upto == "b":
        keys = phase_a()
        k2 = phase_f(x1, x2, "x2", "x1", 1)
        final = phase_b(x2, y, "x2")
    else:
        keys = phase_a()
        k2 = phase_f(x1, x2, "x2", "x1", 1)
        k3 = phase_b(x2, x3, "x2")
        final = phase_f(x3, y, "y", "x3", 2)
    B.emit("sync", lambda e: e.nop(), reads=final, writes=[])
    B.finalize_and_emit(stack)
    stack.close()
    return nc


_NC_CACHE = {}


def _get_nc(upto="full"):
    if upto not in _NC_CACHE:
        _NC_CACHE[upto] = build_program(upto)
    return _NC_CACHE[upto]


def _in_maps(inputs):
    c = np.ascontiguousarray
    hs = inputs["L0_attn_a_hidden_states"]
    base = {
        "wq": c(inputs["L0_attn_a_wq"]),
        "wkv": c(inputs["L0_attn_a_wkv"]),
        "wo_a": c(inputs["L0_attn_a_wo"]),
        "Wq_d": c(inputs["L0_attn_b_Wq_d"]),
        "bq_d": c(inputs["L0_attn_b_bq_d"]).reshape(1, QL),
        "W_qk": c(inputs["L0_attn_b_W_qk"]),
        "b_qk": c(inputs["L0_attn_b_b_qk"]).reshape(1, 128),
        "Wkv_d": c(inputs["L0_attn_b_Wkv_d"]),
        "bkv_d": c(inputs["L0_attn_b_bkv_d"]).reshape(1, KV),
        "Wv_u": c(inputs["L0_attn_b_Wv_u"]),
        "bv_u": c(inputs["L0_attn_b_bv_u"]).reshape(1, D),
        "Wo_b": c(inputs["L0_attn_b_Wo"]),
        "bo_b": c(inputs["L0_attn_b_bo"]).reshape(1, D),
        "lora_A": c(inputs["L0_ffn_lora_A"]).reshape(128, D),
        "lora_B": c(inputs["L0_ffn_lora_B"]).reshape(8 * D, 16),
        "ffn_W": c(inputs["L0_ffn_W"]),
        "ffn_bias": c(inputs["L0_ffn_bias"]).reshape(1, D),
        "lora_w": c(inputs["L0_ffn_lora_weights"]).reshape(1, 8),
    }
    maps = []
    for b in range(8):
        m = dict(base)
        m["x0"] = c(np.asarray(hs[b]))
        maps.append(m)
    return maps


def kernel(**inputs):
    nc = _get_nc("full")
    maps = _in_maps(inputs)
    res = run_bass_kernel_spmd(nc, maps, core_ids=list(range(8)))
    out = np.stack([np.asarray(r["y"]).astype(np.float32) for r in res.results], axis=0)
    return out


# version 11
# speedup vs baseline: 1.2423x; 1.0204x over previous
import numpy as np
import ml_dtypes
import concourse.bass as bass
import concourse.mybir as mybir
from concourse.bass_utils import run_bass_kernel_spmd

F32 = mybir.dt.float32
BF16 = mybir.dt.bfloat16
AF = mybir.ActivationFunctionType
ALU = mybir.AluOpType

S = 2048
D = 4096
NB = D // 128
NT = S // 128
H = 32
SCALE_A = float(192.0 ** -0.5)
QL, KV = 12, 4
SCALE_B = 0.5
EPS = 1e-5
AG = 2048


class Op:
    __slots__ = ("engine", "fn", "dma", "deps", "needed", "rank", "waits",
                 "inc_sem", "inc_amt", "tag")

    def __init__(self, engine, fn, dma, tag):
        self.engine = engine
        self.fn = fn
        self.dma = dma
        self.deps = []
        self.needed = False
        self.rank = 0
        self.waits = []
        self.inc_sem = None
        self.inc_amt = 0
        self.tag = tag


class Builder:
    ENGINES = ("tensor", "vector", "scalar", "gpsimd", "sync")

    def __init__(self, nc):
        self.nc = nc
        self.ops = {e: [] for e in self.ENGINES}
        self.last_write = {}
        self.readers = {}

    def emit(self, engine, fn, reads=(), writes=(), dma=False, tag=""):
        op = Op(engine, fn, dma, tag)
        deps = {}
        for k in reads:
            w = self.last_write.get(k)
            if w is not None:
                deps[id(w)] = w
        for k in writes:
            w = self.last_write.get(k)
            if w is not None:
                deps[id(w)] = w
            for r in self.readers.get(k, ()):
                deps[id(r)] = r
        for d in deps.values():
            if d is op:
                continue
            if d.engine == "tensor" and engine == "tensor" and not d.dma and not dma:
                continue
            op.deps.append(d)
        for k in reads:
            self.readers.setdefault(k, []).append(op)
        for k in writes:
            self.last_write[k] = op
            self.readers[k] = []
        self.ops[engine].append(op)
        return op

    LANES = 8

    def finalize_and_emit(self, stack):
        nc = self.nc
        csems = {e: stack.enter_context(nc.semaphore(f"s_{e}"))
                 for e in ("tensor", "vector", "scalar", "sync")}
        lsems = {}
        for q in ("sync", "gpsimd", "scalar"):
            lsems[("d", q)] = [stack.enter_context(nc.semaphore(f"d_{q}{i}"))
                               for i in range(self.LANES)]
        lsems[("c", "gpsimd")] = [stack.enter_context(nc.semaphore(f"g_{i}"))
                                  for i in range(self.LANES)]
        for e in self.ENGINES:
            for op in self.ops[e]:
                for d in op.deps:
                    d.needed = True
        counters = {}
        for e in self.ENGINES:
            for op in self.ops[e]:
                if op.dma:
                    key = ("d", e)
                    i = counters.get(key, 0)
                    counters[key] = i + 1
                    op.rank = i
                    op.inc_sem = lsems[key][i % self.LANES]
                    op.inc_amt = 16
                    if i >= self.LANES:
                        op.waits.append((op.inc_sem, 16 * (i // self.LANES)))
                elif e == "gpsimd":
                    key = ("c", "gpsimd")
                    i = counters.get(key, 0)
                    counters[key] = i + 1
                    op.rank = i
                    op.inc_sem = lsems[key][i % self.LANES]
                    op.inc_amt = 1
                    if i >= self.LANES:
                        op.waits.append((op.inc_sem, i // self.LANES))
                elif op.needed:
                    key = ("c", e)
                    counters[key] = counters.get(key, 0) + 1
                    op.rank = counters[key]
                    op.inc_sem = csems[e]
                    op.inc_amt = 1
        for e in self.ENGINES:
            known = {}
            for op in self.ops[e]:
                for d in op.deps:
                    if d.dma:
                        sem = lsems[("d", d.engine)][d.rank % self.LANES]
                        val = 16 * (d.rank // self.LANES + 1)
                    elif d.engine == "gpsimd":
                        sem = lsems[("c", "gpsimd")][d.rank % self.LANES]
                        val = d.rank // self.LANES + 1
                    else:
                        sem, val = csems[d.engine], d.rank
                    kid = id(sem)
                    if known.get(kid, -1) < val:
                        op.waits.append((sem, val))
                        known[kid] = val
        blk = stack.enter_context(nc.Block())
        for ename in self.ENGINES:
            ops = self.ops[ename]

            def section(eng, ops=ops):
                for op in ops:
                    for (sem, val) in op.waits:
                        eng.wait_ge(sem, val)
                    inst = op.fn(eng)
                    if op.inc_sem is not None:
                        inst.then_inc(op.inc_sem, op.inc_amt)

            getattr(blk, ename)(section)


def build_program(upto="full"):
    nc = bass.Bass("TRN2", target_bir_lowering=False)
    x0 = nc.dram_tensor("x0", [S, D], F32, kind="ExternalInput").ap()
    wq = nc.dram_tensor("wq", [D, 6144], F32, kind="ExternalInput").ap()
    wkv = nc.dram_tensor("wkv", [D, 8256], F32, kind="ExternalInput").ap()
    wo_a = nc.dram_tensor("wo_a", [D, D], F32, kind="ExternalInput").ap()
    Wq_d = nc.dram_tensor("Wq_d", [D, QL], F32, kind="ExternalInput").ap()
    bq_d = nc.dram_tensor("bq_d", [1, QL], F32, kind="ExternalInput").ap()
    W_qk = nc.dram_tensor("W_qk", [QL, 128], F32, kind="ExternalInput").ap()
    b_qk = nc.dram_tensor("b_qk", [1, 128], F32, kind="ExternalInput").ap()
    Wkv_d = nc.dram_tensor("Wkv_d", [D, KV], F32, kind="ExternalInput").ap()
    bkv_d = nc.dram_tensor("bkv_d", [1, KV], F32, kind="ExternalInput").ap()
    Wv_u = nc.dram_tensor("Wv_u", [KV, D], F32, kind="ExternalInput").ap()
    bv_u = nc.dram_tensor("bv_u", [1, D], F32, kind="ExternalInput").ap()
    Wo_b = nc.dram_tensor("Wo_b", [D, D], F32, kind="ExternalInput").ap()
    bo_b = nc.dram_tensor("bo_b", [1, D], F32, kind="ExternalInput").ap()
    lora_A = nc.dram_tensor("lora_A", [128, D], F32, kind="ExternalInput").ap()
    lora_B = nc.dram_tensor("lora_B", [8 * D, 16], F32, kind="ExternalInput").ap()
    ffn_W = nc.dram_tensor("ffn_W", [D, D], F32, kind="ExternalInput").ap()
    ffn_bias = nc.dram_tensor("ffn_bias", [1, D], F32, kind="ExternalInput").ap()
    lora_w = nc.dram_tensor("lora_w", [1, 8], F32, kind="ExternalInput").ap()
    y = nc.dram_tensor("y", [S, D], F32, kind="ExternalOutput").ap()
    x1 = nc.dram_tensor("x1", [S, D], F32).ap()
    x2 = nc.dram_tensor("x2", [S, D], F32).ap()
    x3 = nc.dram_tensor("x3", [S, D], F32).ap()
    Ua = nc.dram_tensor("Ua", [D, S], BF16).ap()
    Abh = nc.dram_tensor("Abh", [D, S], BF16).ap()
    WT = nc.dram_tensor("WTs", [D, D], BF16).ap()
    AcH = nc.dram_tensor("AcH", [128, D], BF16).ap()
    BtH = nc.dram_tensor("BtH", [128, D], BF16).ap()
    cfH = nc.dram_tensor("cfH", [1, 128], F32).ap()
    dbg = nc.dram_tensor("dbg", [128, 32 * AG], BF16, kind="ExternalOutput").ap() \
        if upto in ("ln", "h0", "ua", "wt", "b0") else None

    from contextlib import ExitStack
    stack = ExitStack()
    sb = lambda name, shape, dt: stack.enter_context(nc.sbuf_tensor(name, shape, dt))
    psa = lambda name, shape, dt: stack.enter_context(nc.psum_tensor(name, shape, dt))
    B = Builder(nc)

    BIGA = sb("BIGA", [128, 32 * AG], BF16)
    BIGB = sb("BIGB", [128, 8 * AG], BF16)
    ones_row = sb("ones_row", [1, 512], BF16)
    ones_col = sb("ones_col", [128, 1], BF16)
    ident = sb("ident", [128, 128], BF16)
    masks = sb("masks", [128, 896], BF16)
    lacc = sb("lacc", [128, 512], F32)
    oncf = sb("oncf", [128, 1], F32)
    wqd_sb = sb("wqd_sb", [128, NB * QL], BF16)
    wkvd_sb = sb("wkvd_sb", [128, NB * KV], BF16)
    bqd_row = sb("bqd_row", [1, QL], BF16)
    bkvd_row = sb("bkvd_row", [1, KV], BF16)
    wqk_sb = sb("wqk_sb", [QL, 128], BF16)
    bqk_row = sb("bqk_row", [1, 128], BF16)
    wvu_h = sb("wvu_h", [KV, 128], BF16)
    bvu_h = sb("bvu_h", [1, 128], BF16)
    row_big = sb("row_big", [1, D], BF16)
    lw_sb = sb("lw_sb", [1, 8], F32)
    coef_col = sb("coef_col", [128, 1], F32)
    stat = sb("stat", [128, 16], F32)
    eps_col = sb("eps_col", [128, 1], F32)
    diag_bf = [sb(f"diag{i}", [128, 128], BF16) for i in range(2)]
    bcol_bf = [sb(f"bcolb{i}", [128, 1], BF16) for i in range(2)]
    brow_bf = [sb(f"browb{i}", [1, 128], BF16) for i in range(2)]
    Qt_sb = sb("Qt_sb", [128, S], BF16)
    Kt_sb = sb("Kt_sb", [128, S], BF16)
    expS = [sb(f"expS{i}", [128, 512], BF16) for i in range(4)]
    lhi = sb("lhi", [1, 512], BF16)
    llo = sb("llo", [1, 512], BF16)
    bcast_sb = sb("bcast_sb", [128, 512], F32)
    Uh_sb = sb("Uh_sb", [128, S], BF16)
    rbuf = [sb(f"rbuf{i}", [128, 512], F32) for i in range(2)]
    nbuf = [sb(f"nbuf{i}", [128, 512], F32) for i in range(2)]
    lrow = nbuf[0][0:1, :]
    lrec = nbuf[1][0:1, :]
    hbf = sb("hbf", [128, 1024], BF16)
    Ckvs_sb = sb("Ckvs_sb", [128, NT * 40], BF16)
    u5_sb = lacc
    Cq_sb = Uh_sb
    Ckv_sb = Kt_sb
    ctx_bf = Qt_sb
    bct_sb = Kt_sb

    ps = [psa(f"ps{i}", [128, 512], F32) for i in range(6)]
    psb = psa("psb", [128, 512], BF16)
    psb2 = psa("psb2", [128, 512], BF16)
    PK = [f"ps{i}" for i in range(6)]

    def A(c0, w):
        return BIGA[:, c0:c0 + w]

    def Bv(c0, w):
        return BIGB[:, c0:c0 + w]

    def akeys(c0, w):
        return [f"A{g}" for g in range(c0 // AG, (c0 + w - 1) // AG + 1)]

    def bkeys(c0, w):
        return [f"B{g}" for g in range(c0 // AG, (c0 + w - 1) // AG + 1)]

    XBF = [0, 4096]
    SQS = 8192
    QW, KW, VW2 = 0, 4096, 8192
    VT, VS = 12288, 14336
    ACT_ = 12288
    AH = [12288, 14336]
    UAT = [8192, 12288]
    WCOL = [0, 0]
    WSTG = 4096
    STGB = 0
    STG16 = 4096
    QH = 0

    def dma(queue, out_ap, in_ap, reads, writes, tag=""):
        return B.emit(queue, lambda e: e.dma_start(out_ap, in_ap), reads, writes,
                      dma=True, tag=tag)

    def mm(out_ap, lhsT_ap, rhs_ap, reads, writes, start, stop, tag=""):
        return B.emit("tensor",
                      lambda e: e.matmul(out_ap, lhsT_ap, rhs_ap, start=start, stop=stop),
                      reads, writes, tag=tag)

    def tr(out_ap, in_ap, p, reads, writes, tag=""):
        return B.emit("tensor",
                      lambda e: e.transpose(out_ap, in_ap, ident[0:p, 0:p]),
                      list(reads) + ["ident"], writes, tag=tag)

    def act(out_ap, in_ap, func, reads, writes, bias=0.0, scale=1.0, accum=None, tag=""):
        return B.emit("scalar",
                      lambda e: e.activation(out_ap, in_ap, func, bias=bias, scale=scale,
                                             accum_out=accum),
                      reads, writes, tag=tag)

    def vec(fn, reads, writes, tag=""):
        return B.emit("vector", fn, reads, writes, tag=tag)

    def gp(fn, reads, writes, tag=""):
        return B.emit("gpsimd", fn, reads, writes, tag=tag)

    gp(lambda e: e.memset(ones_row[:, :], 1.0), [], ["ones_row"])
    gp(lambda e: e.memset(ones_col[:, :], 1.0), [], ["ones_col"])
    gp(lambda e: e.memset(eps_col[:, :], EPS), [], ["eps_col"])
    gp(lambda e: e.memset(oncf[:, :], 1.0), [], ["oncf"])
    gp(lambda e: e.memset(masks[:, :], 1.0), ["mkM"], ["mkM"])
    gp(lambda e: e.affine_select(masks[:, :], masks[:, :], [[1, 896]], ALU.is_ge, 0.0,
                                 base=-384, channel_multiplier=-1), ["mkM"], ["masks"])
    gp(lambda e: e.memset(ident[:, :], 1.0), ["identM"], ["identM"])
    gp(lambda e: e.affine_select(ident[:, :], ident[:, :], [[1, 128]], ALU.is_equal, 0.0,
                                 base=0, channel_multiplier=-1), ["identM"], ["ident"])
    dma("gpsimd", wqd_sb[:, :], Wq_d.rearrange("(b p) l -> p b l", p=128), [], ["wqd_sb"])
    dma("gpsimd", wkvd_sb[:, :], Wkv_d.rearrange("(b p) l -> p b l", p=128), [], ["wkvd_sb"])
    dma("gpsimd", bqd_row[:, :], bq_d, [], ["bqd_row"])
    dma("gpsimd", bkvd_row[:, :], bkv_d, [], ["bkvd_row"])
    dma("gpsimd", wqk_sb[:, :], W_qk, [], ["wqk_sb"])
    dma("gpsimd", bqk_row[:, :], b_qk, [], ["bqk_row"])
    dma("sync", lw_sb[:, :], lora_w, [], ["lw_sb"])
    vec(lambda e: e.tensor_scalar_mul(
        lrec[0:1, 0:128], bass.AP(lw_sb, 0, [[8, 1], [1, 8], [0, 16]]), 2.0),
        ["lw_sb", "lrec"], ["coef_row"])
    dma("sync", cfH[:, :], lrec[0:1, 0:128], ["coef_row"], ["cfHk"])
    dma("sync", coef_col[:, :], cfH.rearrange("a (p o) -> p (a o)", p=128),
        ["cfHk"], ["coef_col"])
    dma("gpsimd", Bv(STGB, 4096), lora_A, bkeys(STGB, 4096),
        bkeys(STGB, 4096) + ["stgA"])
    for g4 in range(8):
        for j in range(4):
            db = g4 * 4 + j
            tr(psb[:, j * 128:(j + 1) * 128], Bv(STGB + db * 128, 128), 128,
               ["stgA"] + bkeys(STGB + db * 128, 128), ["psb"])
        B.emit("scalar",
               lambda e, g4=g4: e.copy(Bv(WSTG + g4 * 512, 512), psb[:, :]),
               ["psb"], bkeys(WSTG + g4 * 512, 512) + ["stgAc"])
    dma("sync", AcH[:, :], Bv(WSTG, 4096), ["stgAc"] + bkeys(WSTG, 4096), ["AcHk"])
    for n in range(8):
        dma("gpsimd", Bv(STGB, 512),
            lora_B[n * D:(n + 1) * D, :].rearrange("(bb p) r -> p bb r", p=128),
            ["stgA", "stgBn"] + bkeys(STGB, 512), bkeys(STGB, 512) + ["stgBn"])
        for b4 in range(8):
            for j in range(4):
                bb = b4 * 4 + j
                tr(psb[0:16, j * 128:(j + 1) * 128], Bv(STGB + bb * 16, 16), 128,
                   ["stgBn"] + bkeys(STGB, 512), ["psb"])
            B.emit("vector",
                   lambda e, b4=b4: e.tensor_copy(
                       BIGB[0:16, STG16 + b4 * 512: STG16 + (b4 + 1) * 512],
                       psb[0:16, 0:512]),
                   ["psb"] + bkeys(STG16, 4096), bkeys(STG16, 4096) + ["stg16"])
        dma("sync", BtH[n * 16:(n + 1) * 16, :], BIGB[0:16, STG16:STG16 + 4096],
            ["stg16"] + bkeys(STG16, 4096), [f"BtHk{n}"])
    bt_keys = [f"BtHk{n}" for n in range(8)]

    def layernorm_block(xin_ap, t, dst_base, dst_stride, dst_keys_fn, xin_keys, tag):
        sl = t % 2
        xb = Bv(XBF[sl], 4096)
        kx = bkeys(XBF[sl], 4096)
        dma("gpsimd", xb, xin_ap[t * 128:(t + 1) * 128, :], kx + xin_keys, kx,
            tag=f"{tag}x{t}")
        junk = Bv(SQS, 4096)
        kj = bkeys(SQS, 4096)
        act(junk, xb, AF.Copy, kx + kj, kj + [f"sum{sl}"],
            accum=stat[:, sl * 8:sl * 8 + 1], tag=f"{tag}cp{t}")
        act(junk, xb, AF.Square, kx + kj, kj + [f"sumsq{sl}"],
            accum=stat[:, sl * 8 + 1:sl * 8 + 2], tag=f"{tag}sq{t}")
        st = stat[:, sl * 8:sl * 8 + 8]
        vec(lambda e, st=st: e.tensor_scalar_mul(st[:, 2:3], st[:, 0:1], 1.0 / D),
            [f"sum{sl}"], [f"mean{sl}"])
        vec(lambda e, st=st: e.tensor_scalar_mul(st[:, 3:4], st[:, 1:2], 1.0 / D),
            [f"sumsq{sl}"], [f"msq{sl}"])
        vec(lambda e, st=st: e.tensor_mul(st[:, 4:5], st[:, 2:3], st[:, 2:3]),
            [f"mean{sl}"], [f"m2{sl}"])
        vec(lambda e, st=st: e.tensor_sub(st[:, 3:4], st[:, 3:4], st[:, 4:5]),
            [f"msq{sl}", f"m2{sl}"], [f"var{sl}"])
        act(st[:, 4:5], st[:, 3:4], AF.Ln, [f"var{sl}", f"m2{sl}", "eps_col"],
            [f"lnv{sl}"], bias=eps_col[:, 0:1])
        act(st[:, 5:6], st[:, 4:5], AF.Exp, [f"lnv{sl}"], [f"rstd0{sl}"], scale=-0.5)
        vec(lambda e, st=st: e.tensor_mul(st[:, 6:7], st[:, 5:6], st[:, 5:6]),
            [f"rstd0{sl}"], [f"r2{sl}"])
        vec(lambda e, st=st: e.tensor_scalar_add(st[:, 4:5], st[:, 3:4], EPS),
            [f"var{sl}", f"lnv{sl}"], [f"vpe{sl}"])
        vec(lambda e, st=st: e.tensor_mul(st[:, 6:7], st[:, 6:7], st[:, 4:5]),
            [f"r2{sl}", f"vpe{sl}"], [f"r2{sl}"])
        vec(lambda e, st=st: e.tensor_scalar(st[:, 6:7], st[:, 6:7], -0.5, 1.5,
                                             ALU.mult, ALU.add),
            [f"r2{sl}"], [f"r2{sl}"])
        vec(lambda e, st=st: e.tensor_mul(st[:, 5:6], st[:, 5:6], st[:, 6:7]),
            [f"rstd0{sl}", f"r2{sl}"], [f"rstd{sl}"])
        vec(lambda e, st=st: e.tensor_mul(st[:, 7:8], st[:, 2:3], st[:, 5:6]),
            [f"mean{sl}", f"rstd{sl}"], [f"mr{sl}"])
        vec(lambda e, st=st, sl=sl: e.tensor_scalar_mul(diag_bf[sl][:, :], ident[:, :],
                                                        st[:, 5:6]),
            ["ident", f"rstd{sl}"], [f"diag{sl}"])
        vec(lambda e, st=st, sl=sl: e.tensor_copy(bcol_bf[sl][:, :], st[:, 7:8]),
            [f"mr{sl}"], [f"bcol{sl}"])
        pbr = psb[0:1, sl * 128:sl * 128 + 128]
        tr(pbr, bcol_bf[sl][:, :], 128, [f"bcol{sl}"], ["psb"])
        act(brow_bf[sl][:, :], pbr, AF.Copy, ["psb"], [f"brow{sl}"], scale=-1.0)
        for g4 in range(8):
            bank = ps[g4 % 2]
            bkey = PK[g4 % 2]
            for j in range(4):
                db = g4 * 4 + j
                mm(bank[:, j * 128:(j + 1) * 128], xb[:, db * 128:(db + 1) * 128],
                   diag_bf[sl][:, :], kx + [f"diag{sl}"], [bkey], start=True, stop=False)
                mm(bank[:, j * 128:(j + 1) * 128], ones_row[0:1, 0:128],
                   brow_bf[sl][0:1, :], ["ones_row", f"brow{sl}"], [bkey],
                   start=False, stop=True)
            off = dst_base(g4 * 4) + 0
            dst_ap = bass.AP(BIGA, off, [[32 * AG, 128], [dst_stride, 4], [1, 128]])
            src_ap = bass.AP(bank, 0, [[512, 128], [128, 4], [1, 128]])
            vec(lambda e, dst_ap=dst_ap, src_ap=src_ap: e.tensor_copy(dst_ap, src_ap),
                [bkey], dst_keys_fn(g4), tag=f"{tag}ev{t}_{g4}")

    def wprep_steps():
        steps = []
        for i in range(NB):
            sl = i % 2

            def step(i=i, sl=sl):
                dma("gpsimd", Bv(WCOL[sl], 4096),
                    ffn_W[:, i * 128:(i + 1) * 128].rearrange("(b p) d -> p b d", p=128),
                    bkeys(WCOL[sl], 4096), bkeys(WCOL[sl], 4096), tag=f"wp{i}")
                for b4 in range(8):
                    for j in range(4):
                        b = b4 * 4 + j
                        tr(psb[:, j * 128:(j + 1) * 128], Bv(WCOL[sl] + b * 128, 128),
                           128, bkeys(WCOL[sl], 4096), ["psb"])
                    B.emit("scalar",
                           lambda e, b4=b4: e.copy(Bv(WSTG + b4 * 512, 512), psb[:, :]),
                           ["psb"], bkeys(WSTG + b4 * 512, 512) + ["wstg"])
                dma("sync", WT[i * 128:(i + 1) * 128, :], Bv(WSTG, 4096),
                    ["wstg"] + bkeys(WSTG, 4096), [f"WTr{i}"])

            steps.append(step)
        return steps

    def phase_a():
        for t in range(NT):
            layernorm_block(
                x0, t,
                dst_base=lambda db, t=t: db * AG + t * 128,
                dst_stride=AG,
                dst_keys_fn=lambda g4: [f"A{g4 * 4 + j}" for j in range(4)],
                xin_keys=[], tag="a1")
        for h in range(H):
            qwk = bkeys(QW, 4096)
            kwk = bkeys(KW, 4096)
            vwk = bkeys(VW2, 4096)
            dma("gpsimd", Bv(QW, 4096),
                wq[:, h * 192: h * 192 + 128].rearrange("(b p) e -> p b e", p=128),
                qwk, qwk, tag=f"qw{h}")
            dma("gpsimd", Bv(KW, 4096),
                wkv[:, h * 128: (h + 1) * 128].rearrange("(b p) e -> p b e", p=128),
                kwk, kwk, tag=f"kw{h}")
            dma("gpsimd", Bv(VW2, 4096),
                wkv[:, 4096 + h * 128: 4096 + (h + 1) * 128].rearrange(
                    "(b p) e -> p b e", p=128),
                vwk, vwk, tag=f"vw{h}")
            vtk = bkeys(VT, 2048)
            vsk = bkeys(VS, 2048)
            for c in range(4):
                bv_ = ps[2 + c % 2]
                for db in range(NB):
                    mm(bv_[:, :], Bv(VW2 + db * 128, 128), A(db * AG + c * 512, 512),
                       vwk + [f"A{db}"], [PK[2 + c % 2]], start=(db == 0),
                       stop=(db == NB - 1))
                act(Bv(VT + c * 512, 512), bv_[:, :], AF.Copy,
                    [PK[2 + c % 2]] + (vtk if c == 3 else []), [f"VTc{c}"])

            def v_transpose(g4):
                pb = psb if g4 % 2 == 0 else psb2
                pbk = "psb" if g4 % 2 == 0 else "psb2"
                for j in range(4):
                    kb = g4 * 4 + j
                    tr(pb[:, j * 128:(j + 1) * 128], Bv(VT + kb * 128, 128), 128,
                       [f"VTc{g4}"] + (vtk if g4 == 3 and j == 3 else []), [pbk])
                act(Bv(VS + g4 * 512, 512), pb[:, :], AF.Copy,
                    [pbk] + (vsk if g4 == 3 else []), [f"VSc{g4}"] + (vsk if g4 == 3 else []))

            for c in range(4):
                bq, bk = ps[4], ps[5]
                for db in range(NB):
                    mm(bq[:, :], Bv(QW + db * 128, 128), A(db * AG + c * 512, 512),
                       qwk + [f"A{db}"], [PK[4]], start=(db == 0), stop=(db == NB - 1))
                for db in range(NB):
                    mm(bk[:, :], Bv(KW + db * 128, 128), A(db * AG + c * 512, 512),
                       kwk + [f"A{db}"], [PK[5]], start=(db == 0), stop=(db == NB - 1))
                act(Qt_sb[:, c * 512:(c + 1) * 512], bq[:, :], AF.Copy, [PK[4]], ["Qt"])
                act(Kt_sb[:, c * 512:(c + 1) * 512], bk[:, :], AF.Copy, [PK[5]], ["Kt"])
            deferred = [None]

            def fin_mms(c, ub):
                bb_i = (0, 1, 4, 5)[(4 * c + 5) % 4]
                bB = ps[bb_i]
                mm(bB[:, :], ones_row[0:1, 0:128], lhi[0:1, :], ["ones_row", "lhi"],
                   [PK[bb_i]], start=True, stop=False)
                mm(bB[:, :], ones_row[0:1, 0:128], llo[0:1, :], ["ones_row", "llo"],
                   [PK[bb_i]], start=False, stop=True)
                vec(lambda e, bB=bB: e.tensor_copy(bcast_sb[:, :], bB[:, :]), [PK[bb_i]],
                    ["bcast"])
                vec(lambda e, c=c, ub=ub: e.tensor_mul(Uh_sb[:, c * 512:(c + 1) * 512],
                                                       ps[ub][:, :], bcast_sb[:, :]),
                    [PK[ub], "bcast"], ["Uh"], tag=f"Un{h}_{c}")

            upend = []

            def u_flush(n):
                while upend and len(upend) > n:
                    upend.pop(0)()

            v_transpose(0)
            for c in range(4):
                nkb = 4 * c + 4
                ub = 2 + c % 2
                ROT = (0, 1, 4, 5)
                for kb in range(nkb):
                    u_flush(2)
                    if kb == 1 and c < 3:
                        v_transpose(c + 1)
                    bS = ps[ROT[kb % 4]]
                    skey = PK[ROT[kb % 4]]
                    ekey = f"expS{kb % 4}"
                    mm(bS[:, :], Kt_sb[:, kb * 128:(kb + 1) * 128],
                       Qt_sb[:, c * 512:(c + 1) * 512], ["Kt", "Qt"], [skey],
                       start=True, stop=True, tag=f"sc{h}_{c}_{kb}")
                    act(expS[kb % 4][:, :], bS[:, :], AF.Exp, [skey], [ekey],
                        scale=SCALE_A, tag=f"exp{h}_{c}_{kb}")
                    if kb >= 4 * c:
                        m = kb - 4 * c
                        vec(lambda e, kb=kb, m=m: e.tensor_mul(
                            expS[kb % 4][:, :], expS[kb % 4][:, :],
                            masks[:, 384 - 128 * m: 896 - 128 * m]),
                            [ekey, "masks"], [ekey], tag=f"msk{h}_{c}_{kb}")
                    def u_mm(kb=kb, c=c, ub=ub, nkb=nkb, ekey=ekey):
                        mm(ps[ub][:, :], Bv(VS + kb * 128, 128), expS[kb % 4][:, :],
                           [f"VSc{kb // 4}", ekey], [PK[ub]], start=(kb == 0),
                           stop=(kb == nkb - 1))
                    upend.append(u_mm)
                    if kb == 0:
                        vec(lambda e, kb=kb: e.tensor_copy(lacc[:, :], expS[kb % 4][:, :]),
                            [ekey, "lacc"], ["lacc"])
                    else:
                        vec(lambda e, kb=kb: e.tensor_add(lacc[:, :], lacc[:, :],
                                                          expS[kb % 4][:, :]),
                            [ekey, "lacc"], ["lacc"])
                    if kb == 5 and deferred[0] is not None:
                        deferred[0]()
                        deferred[0] = None
                u_flush(0)
                lb = ROT[nkb % 4]
                mm(ps[lb][0:1, :], oncf[:, :], lacc[:, :], ["oncf", "lacc"], [PK[lb]],
                   start=True, stop=True)
                if deferred[0] is not None:
                    deferred[0]()
                    deferred[0] = None
                act(lrow[:, :], ps[lb][0:1, :], AF.Ln, [PK[lb], "lrow"], ["lrow"])
                act(lrec[:, :], lrow[:, :], AF.Exp, ["lrow", "coef_row"], ["lrec"],
                    scale=-1.0)
                act(lhi[:, :], lrec[:, :], AF.Copy, ["lrec", "lhi"], ["lhi"])
                vec(lambda e: e.tensor_sub(lrow[:, :], lrec[:, :], lhi[:, :]),
                    ["lrec", "lhi", "lrow"], ["llo_f"])
                vec(lambda e: e.tensor_copy(llo[:, :], lrow[:, :]), ["llo_f"], ["llo"])
                deferred[0] = (lambda c=c, ub=ub: fin_mms(c, ub))
            if deferred[0] is not None:
                deferred[0]()
                deferred[0] = None
            dma("sync", Ua[h * 128:(h + 1) * 128, :], Uh_sb[:, :], ["Uh"], [f"UaH{h}"],
                tag=f"spill{h}")
        ua_ready = [f"UaH{h}" for h in range(H)]
        wsteps = wprep_steps()
        wi = 0
        outk = []
        for Q in range(4):
            for e_blk in range(NB):
                wkey = akeys(e_blk * 1024, 1024)
                dma("gpsimd", A(e_blk * 1024, 1024),
                    wo_a[e_blk * 128:(e_blk + 1) * 128, Q * 1024:(Q + 1) * 1024],
                    wkey, wkey, tag=f"wo{Q}_{e_blk}")
            for tp in range(NT // 4):
                for _ in range(4):
                    if wi < len(wsteps):
                        wsteps[wi]()
                        wi += 1
                usl = tp % 2
                ubase = 32768 + usl * 16384
                uk = akeys(ubase, 16384)
                dma("sync", A(ubase, 16384),
                    Ua[:, tp * 512:(tp + 1) * 512].rearrange("(eb p) q -> p eb q", p=128),
                    ua_ready + uk, uk, tag=f"uat{Q}_{tp}")
                for tj in range(4):
                    t = tp * 4 + tj
                    for oc in range(2):
                        rsl = oc
                        dma("scalar", rbuf[rsl][:, :],
                            x0[t * 128:(t + 1) * 128,
                               Q * 1024 + oc * 512: Q * 1024 + (oc + 1) * 512],
                            [f"rbuf{rsl}"], [f"rbuf{rsl}"])
                        bi = oc + 2 * (tj % 2)
                        bank = ps[bi]
                        for e_blk in range(NB):
                            mm(bank[:, :], A(ubase + e_blk * 512 + tj * 128, 128),
                               A(e_blk * 1024 + oc * 512, 512),
                               uk + akeys(e_blk * 1024 + oc * 512, 512), [PK[bi]],
                               start=(e_blk == 0), stop=(e_blk == NB - 1))
                        vec(lambda e, bank=bank, rsl=rsl: e.tensor_add(
                            nbuf[rsl][:, :], bank[:, :], rbuf[rsl][:, :]),
                            [PK[bi], f"rbuf{rsl}"], [f"nbuf{rsl}"], tag=f"woev{Q}_{t}_{oc}")
                        o8 = Q * 2 + oc
                        outk.append(f"x1r{t}_{o8}")
                        dma("scalar", x1[t * 128:(t + 1) * 128, o8 * 512:(o8 + 1) * 512],
                            nbuf[rsl][:, :], [f"nbuf{rsl}"], [f"x1r{t}_{o8}"],
                            tag=f"x1st{t}_{o8}")
        while wi < len(wsteps):
            wsteps[wi]()
            wi += 1
        return outk

    def phase_f(xin_ap, xout_ap, xname, xin_name, inst):
        ack = bkeys(ACT_, 4096)
        dma("sync", Bv(ACT_, 4096), AcH[:, :], ["AcHk"] + ack, ack, tag=f"f{inst}Ac")
        dma("gpsimd", row_big[:, :], ffn_bias, ["row_big"], ["row_big"], tag=f"f{inst}fb")
        wt_keys = [f"WTr{i}" for i in range(NB)]
        outk = []
        for ch in range(2):
            for tt in range(8):
                t = ch * 8 + tt
                layernorm_block(
                    xin_ap, t,
                    dst_base=lambda db, tt=tt: db * 1024 + tt * 128,
                    dst_stride=1024,
                    dst_keys_fn=lambda g4: [f"A{(g4 * 4) // 2}", f"A{(g4 * 4 + 3) // 2}"],
                    xin_keys=[f"{xin_name}r{t}_{o}" for o in range(8)],
                    tag=f"f{inst}ln")
            for half in range(2):
                bank = ps[4 + half]
                for db in range(NB):
                    mm(bank[:, :], Bv(ACT_ + db * 128, 128),
                       A(db * 1024 + half * 512, 512),
                       ack + [f"A{(db * 1024 + half * 512) // AG}"], [PK[4 + half]],
                       start=(db == 0), stop=(db == NB - 1))
                act(hbf[:, half * 512:(half + 1) * 512], bank[:, :], AF.Copy,
                    [PK[4 + half], "coef_col"], ["hbf"], scale=coef_col[:, 0:1])
            for oc in range(8):
                wsl = oc % 2
                wck = akeys(32768 + wsl * 16384, 16384)
                dma("sync", A(32768 + wsl * 16384, 16384),
                    WT[:, oc * 512:(oc + 1) * 512].rearrange("(b p) o -> p b o", p=128),
                    wt_keys + wck, wck, tag=f"f{inst}wt{ch}_{oc}")
                dma("gpsimd", bct_sb[:, 0:512], BtH[:, oc * 512:(oc + 1) * 512],
                    bt_keys + ["Kt"], ["Kt"], tag=f"f{inst}bt{ch}_{oc}")
                for tt in range(8):
                    t = ch * 8 + tt
                    bank = ps[tt % 2]
                    for db in range(NB):
                        mm(bank[:, :], A(db * 1024 + tt * 128, 128),
                           A(32768 + wsl * 16384 + db * 512, 512),
                           [f"A{(db * 1024 + tt * 128) // AG}",
                            f"A{(32768 + wsl * 16384 + db * 512) // AG}"],
                           [PK[tt % 2]], start=(db == 0), stop=False)
                    mm(bank[:, :], hbf[:, tt * 128:(tt + 1) * 128], bct_sb[:, 0:512],
                       ["hbf", "Kt"], [PK[tt % 2]], start=False, stop=False)
                    mm(bank[:, :], ones_row[0:1, 0:128],
                       row_big[0:1, oc * 512:(oc + 1) * 512],
                       ["ones_row", "row_big"], [PK[tt % 2]], start=False, stop=True)
                    rsl = tt % 2
                    dma("scalar", rbuf[rsl][:, :],
                        xin_ap[t * 128:(t + 1) * 128, oc * 512:(oc + 1) * 512],
                        [f"{xin_name}r{t}_{oc}", f"rbuf{rsl}"], [f"rbuf{rsl}"])
                    vec(lambda e, bank=bank, rsl=rsl: e.tensor_add(
                        nbuf[rsl][:, :], bank[:, :], rbuf[rsl][:, :]),
                        [PK[tt % 2], f"rbuf{rsl}"], [f"nbuf{rsl}"])
                    k = f"{xname}r{t}_{oc}"
                    outk.append(k)
                    dma("scalar", xout_ap[t * 128:(t + 1) * 128, oc * 512:(oc + 1) * 512],
                        nbuf[rsl][:, :], [f"nbuf{rsl}"], [k])
        return outk

    def phase_b(xin_ap, xout_ap, xin_name):
        for ch in range(4):
            for tt in range(4):
                t = ch * 4 + tt
                layernorm_block(
                    xin_ap, t,
                    dst_base=lambda db, tt=tt: db * 512 + tt * 128,
                    dst_stride=512,
                    dst_keys_fn=lambda g4: [f"A{g4}"],
                    xin_keys=[f"{xin_name}r{t}_{o}" for o in range(8)],
                    tag="bln")
            bq, bkv = ps[2], ps[3]
            for db in range(NB):
                mm(bq[0:QL, :], wqd_sb[:, db * QL:(db + 1) * QL], A(db * 512, 512),
                   ["wqd_sb", f"A{db // 4}"], [PK[2]], start=(db == 0), stop=False)
            mm(bq[0:QL, :], bqd_row[0:1, :], ones_row[0:1, :], ["bqd_row", "ones_row"],
               [PK[2]], start=False, stop=True)
            for db in range(NB):
                mm(bkv[0:KV, :], wkvd_sb[:, db * KV:(db + 1) * KV], A(db * 512, 512),
                   ["wkvd_sb", f"A{db // 4}"], [PK[3]], start=(db == 0), stop=False)
            mm(bkv[0:KV, :], bkvd_row[0:1, :], ones_row[0:1, :], ["bkvd_row", "ones_row"],
               [PK[3]], start=False, stop=True)
            act(Cq_sb[0:QL, ch * 512:(ch + 1) * 512], bq[0:QL, :], AF.Copy, [PK[2]], ["Uh"])
            act(Ckv_sb[0:KV, ch * 512:(ch + 1) * 512], bkv[0:KV, :], AF.Copy, [PK[3]], ["Kt"])
        vec(lambda e: e.memset(Ckvs_sb[:, :], 0.0), ["Ckvs"], ["CkvsM"])
        for kb in range(NT):
            pkc = psb[0:128, (kb % 4) * 128:(kb % 4) * 128 + 4]
            tr(pkc, Ckv_sb[0:KV, kb * 128:(kb + 1) * 128], KV, ["Kt"], ["psb"])
            vec(lambda e, pkc=pkc, kb=kb: e.tensor_copy(
                Ckvs_sb[:, kb * 40: kb * 40 + 4], pkc),
                ["psb", "CkvsM"], ["CkvsM"])
        vec(lambda e: e.memset(bass.AP(Ckvs_sb, 32, [[NT * 40, 128], [40, NT], [1, 1]]),
                               1.0),
            ["CkvsM"], ["Ckvs"])
        dma("gpsimd", row_big[:, :], bv_u, ["row_big"], ["row_big"], tag="b_bvu")
        for h in range(H):
            sl = h % 2
            dma("gpsimd", wvu_h[:, :], Wv_u[:, h * 128:(h + 1) * 128], ["wvu_h"],
                ["wvu_h"], tag=f"wvu{h}")
            qhk = bkeys(QH, 2048)
            for c in range(4):
                bqh = ps[4]
                mm(bqh[0:KV, :], wqk_sb[:, h * KV:(h + 1) * KV],
                   Cq_sb[0:QL, c * 512:(c + 1) * 512], ["wqk_sb", "Uh"], [PK[4]],
                   start=True, stop=False)
                mm(bqh[0:KV, :], bqk_row[0:1, h * KV:(h + 1) * KV], ones_row[0:1, :],
                   ["bqk_row", "ones_row"], [PK[4]], start=False, stop=True)
                act(BIGB[0:KV, QH + c * 512: QH + (c + 1) * 512], bqh[0:KV, :], AF.Copy,
                    [PK[4]] + qhk, qhk)
            BROT = (0, 1, 4, 5)

            def b_score(kb, c, i):
                bS = ps[BROT[i % 4]]
                skey = PK[BROT[i % 4]]
                ekey = f"expS{i % 4}"
                mm(bS[:, :], Ckv_sb[0:KV, kb * 128:(kb + 1) * 128],
                   BIGB[0:KV, QH + c * 512: QH + (c + 1) * 512], ["Kt"] + qhk,
                   [skey], start=True, stop=True, tag=f"bsc{h}_{c}_{kb}")
                act(expS[i % 4][:, :], bS[:, :], AF.Exp, [skey], [ekey], scale=SCALE_B)

            def b_u(kb, c, i):
                mm(ps[2 + c % 2][0:33, :], Ckvs_sb[:, kb * 40: kb * 40 + 33],
                   expS[i % 4][:, :],
                   ["Ckvs", f"expS{i % 4}"], [PK[2 + c % 2]], start=(kb == 0),
                   stop=(kb == NT - 1))

            def b_chain(c):
                bU = ps[2 + c % 2]
                act(lrow[:, :], bU[32:33, :], AF.Ln, [PK[2 + c % 2], "lrow"], ["lrow"])
                act(lrec[:, :], lrow[:, :], AF.Exp, ["lrow", "coef_row"], ["lrec"],
                    scale=-1.0)
                act(lhi[:, :], lrec[:, :], AF.Copy, ["lrec", "lhi"], ["lhi"])
                vec(lambda e: e.tensor_sub(lrow[:, :], lrec[:, :], lhi[:, :]),
                    ["lrec", "lhi", "lrow"], ["llo_f"])
                vec(lambda e: e.tensor_copy(llo[:, :], lrow[:, :]), ["llo_f"], ["llo"])

            def b_final(c):
                bU = ps[2 + c % 2]
                vec(lambda e, bU=bU: e.tensor_copy(u5_sb[0:4, :], bU[0:4, :]),
                    [PK[2 + c % 2], "u5"], ["u5"])
                bb_i = BROT[c % 4]
                bB = ps[bb_i]
                mm(bB[:, :], ones_row[0:1, 0:128], lhi[0:1, :], ["ones_row", "lhi"],
                   [PK[bb_i]], start=True, stop=False)
                mm(bB[:, :], ones_row[0:1, 0:128], llo[0:1, :], ["ones_row", "llo"],
                   [PK[bb_i]], start=False, stop=True)
                vec(lambda e, bB=bB: e.tensor_copy(bcast_sb[:, :], bB[:, :]),
                    [PK[bb_i], "bcast"], ["bcast"])
                vec(lambda e: e.tensor_mul(ctx_bf[0:KV, 0:512], u5_sb[0:KV, :],
                                           bcast_sb[0:KV, :]),
                    ["u5", "bcast"], ["Qt"])
                bt_i = BROT[(c + 1) % 4]
                bT = ps[bt_i]
                mm(bT[:, :], wvu_h[0:KV, :], ctx_bf[0:KV, 0:512],
                   ["wvu_h", "Qt"], [PK[bt_i]], start=True, stop=False)
                mm(bT[:, :], row_big[0:1, h * 128:(h + 1) * 128], ones_row[0:1, :],
                   ["row_big", "ones_row"], [PK[bt_i]], start=False, stop=True)
                ahk = bkeys(AH[sl], 2048)
                act(Bv(AH[sl] + c * 512, 512), bT[:, :], AF.Copy,
                    [PK[bt_i]] + ahk, ahk)

            for cpass in range(2):
                cA, cB = 2 * cpass, 2 * cpass + 1
                flat = [(kb, (cA, cB)[j], 2 * kb + j) for kb in range(NT)
                        for j in range(2)]
                for idx, (kb, c, i) in enumerate(flat):
                    if idx >= 4:
                        b_u(*flat[idx - 4])
                    b_score(kb, c, i)
                for idx in range(len(flat) - 4, len(flat)):
                    b_u(*flat[idx])
                    if idx == len(flat) - 2:
                        b_chain(cA)
                b_final(cA)
                b_chain(cB)
                b_final(cB)
            dma("sync", Abh[h * 128:(h + 1) * 128, :], Bv(AH[sl], 2048),
                bkeys(AH[sl], 2048), [f"AbH{h}"])
        ab_ready = [f"AbH{h}" for h in range(H)]
        dma("gpsimd", row_big[:, :], bo_b, ["row_big"], ["row_big"], tag="b_bo")
        outk = []
        for Q in range(4):
            for e_blk in range(NB):
                wkey = akeys(e_blk * 1024, 1024)
                dma("gpsimd", A(e_blk * 1024, 1024),
                    Wo_b[e_blk * 128:(e_blk + 1) * 128, Q * 1024:(Q + 1) * 1024],
                    wkey, wkey, tag=f"bwo{Q}_{e_blk}")
            for tp in range(NT // 4):
                usl = tp % 2
                ubase = 32768 + usl * 16384
                uk = akeys(ubase, 16384)
                dma("sync", A(ubase, 16384),
                    Abh[:, tp * 512:(tp + 1) * 512].rearrange("(eb p) q -> p eb q", p=128),
                    ab_ready + uk, uk)
                for tj in range(4):
                    t = tp * 4 + tj
                    for oc in range(2):
                        rsl = oc
                        o8 = Q * 2 + oc
                        dma("scalar", rbuf[rsl][:, :],
                            xin_ap[t * 128:(t + 1) * 128, o8 * 512:(o8 + 1) * 512],
                            [f"{xin_name}r{t}_{o8}", f"rbuf{rsl}"], [f"rbuf{rsl}"])
                        bi = oc + 2 * (tj % 2)
                        bank = ps[bi]
                        for e_blk in range(NB):
                            mm(bank[:, :], A(ubase + e_blk * 512 + tj * 128, 128),
                               A(e_blk * 1024 + oc * 512, 512),
                               uk + akeys(e_blk * 1024 + oc * 512, 512), [PK[bi]],
                               start=(e_blk == 0), stop=False)
                        mm(bank[:, :], ones_row[0:1, 0:128],
                           row_big[0:1, o8 * 512:(o8 + 1) * 512],
                           ["ones_row", "row_big"], [PK[bi]], start=False, stop=True)
                        vec(lambda e, bank=bank, rsl=rsl: e.tensor_add(
                            nbuf[rsl][:, :], bank[:, :], rbuf[rsl][:, :]),
                            [PK[bi], f"rbuf{rsl}"], [f"nbuf{rsl}"])
                        outk.append(f"x3r{t}_{o8}")
                        dma("scalar", xout_ap[t * 128:(t + 1) * 128, o8 * 512:(o8 + 1) * 512],
                            nbuf[rsl][:, :], [f"nbuf{rsl}"], [f"x3r{t}_{o8}"])
        return outk

    if upto == "b0":
        keys = phase_a()
        k2 = phase_f(x1, x2, "x2", "x1", 1)
        k3 = phase_b(x2, x3, "x2")
        final = []
        final = list(k3)
        for i in range(32):
            sl = i % 2
            kk = bkeys(UAT[sl], 4096)
            dma("sync", Bv(UAT[sl], 2048), Abh[i * 128:(i + 1) * 128, :],
                k3 + kk, kk)
            dma("sync", dbg[:, i * 2048:(i + 1) * 2048], Bv(UAT[sl], 2048), kk,
                [f"dg{i}"])
            final.append(f"dg{i}")
        dma("sync", y[0:128, 0:512], rbuf[0][:, :], ["rbuf0"], ["yk"])
        final.append("yk")
    elif upto == "wt":
        wsteps = wprep_steps()
        for st_ in wsteps:
            st_()
        final = []
        for i in range(8):
            sl = i % 2
            kk = bkeys(UAT[sl], 4096)
            dma("sync", Bv(UAT[sl], 4096), WT[i * 128:(i + 1) * 128, :],
                [f"WTr{i}"] + kk, kk)
            dma("sync", dbg[:, i * 4096:(i + 1) * 4096], Bv(UAT[sl], 4096), kk,
                [f"dw{i}"])
            final.append(f"dw{i}")
        ack = bkeys(ACT_, 4096)
        dma("sync", Bv(ACT_, 4096), AcH[:, :], ["AcHk"] + ack, ack)
        dma("sync", dbg[:, 8 * 4096: 8 * 4096 + 4096], Bv(ACT_, 4096), ack, ["dA"])
        final.append("dA")
        bck = bkeys(XBF[0], 4096)
        dma("sync", Bv(XBF[0], 4096), BtH[:, :], bt_keys + bck, bck)
        dma("sync", dbg[:, 9 * 4096: 9 * 4096 + 4096], Bv(XBF[0], 4096), bck, ["dB"])
        final.append("dB")
        vec(lambda e: e.tensor_copy(expS[0][:, 0:1], coef_col[:, 0:1]),
            ["coef_col", "expS0"], ["expS0"])
        vec(lambda e: e.memset(expS[0][:, 1:512], 0.0), ["expS0"], ["expS0"])
        dma("sync", dbg[:, 10 * 4096: 10 * 4096 + 512], expS[0][:, 0:512], ["expS0"], ["dC"])
        final.append("dC")
        dma("sync", y[0:128, 0:512], rbuf[0][:, :], ["rbuf0"], ["yk"])
        final.append("yk")
    elif upto == "ua":
        keys = phase_a()
        final = []
        for i in range(32):
            sl = i % 2
            kk = bkeys(UAT[sl], 4096)
            dma("sync", Bv(UAT[sl], 2048), Ua[i * 128:(i + 1) * 128, :],
                keys + kk, kk)
            dma("sync", dbg[:, i * 2048:(i + 1) * 2048], Bv(UAT[sl], 2048), kk,
                [f"dg{i}"])
            final.append(f"dg{i}")
        dma("sync", y[0:128, 0:512], rbuf[0][:, :], ["rbuf0"], ["yk"])
        final.append("yk")
    elif upto == "h0":
        for t in range(NT):
            layernorm_block(
                x0, t,
                dst_base=lambda db, t=t: db * AG + t * 128,
                dst_stride=AG,
                dst_keys_fn=lambda g4: [f"A{g4 * 4 + j}" for j in range(4)],
                xin_keys=[], tag="a1")
        h = 0
        qwk = bkeys(QW, 4096)
        kwk = bkeys(KW, 4096)
        vwk = bkeys(VW2, 4096)
        dma("gpsimd", Bv(QW, 4096),
            wq[:, h * 192: h * 192 + 128].rearrange("(b p) e -> p b e", p=128),
            qwk, qwk)
        dma("gpsimd", Bv(KW, 4096),
            wkv[:, h * 128: (h + 1) * 128].rearrange("(b p) e -> p b e", p=128),
            kwk, kwk)
        dma("gpsimd", Bv(VW2, 4096),
            wkv[:, 4096 + h * 128: 4096 + (h + 1) * 128].rearrange(
                "(b p) e -> p b e", p=128),
            vwk, vwk)
        vtk = bkeys(VT, 2048)
        vsk = bkeys(VS, 2048)
        for c in range(4):
            bv_ = ps[2 + c % 2]
            for db in range(NB):
                mm(bv_[:, :], Bv(VW2 + db * 128, 128), A(db * AG + c * 512, 512),
                   vwk + [f"A{db}"], [PK[2 + c % 2]], start=(db == 0),
                   stop=(db == NB - 1))
            act(Bv(VT + c * 512, 512), bv_[:, :], AF.Copy, [PK[2 + c % 2]] + vtk, vtk)
        for g4 in range(4):
            for j in range(4):
                kb = g4 * 4 + j
                tr(psb[:, j * 128:(j + 1) * 128], Bv(VT + kb * 128, 128), 128,
                   vtk, ["psb"])
            B.emit("scalar",
                   lambda e, g4=g4: e.copy(Bv(VS + g4 * 512, 512), psb[:, :]),
                   ["psb"] + vsk, vsk)
        for c in range(4):
            bq, bk = ps[4], ps[5]
            for db in range(NB):
                mm(bq[:, :], Bv(QW + db * 128, 128), A(db * AG + c * 512, 512),
                   qwk + [f"A{db}"], [PK[4]], start=(db == 0), stop=(db == NB - 1))
            for db in range(NB):
                mm(bk[:, :], Bv(KW + db * 128, 128), A(db * AG + c * 512, 512),
                   kwk + [f"A{db}"], [PK[5]], start=(db == 0), stop=(db == NB - 1))
            act(Qt_sb[:, c * 512:(c + 1) * 512], bq[:, :], AF.Copy, [PK[4]], ["Qt"])
            act(Kt_sb[:, c * 512:(c + 1) * 512], bk[:, :], AF.Copy, [PK[5]], ["Kt"])
        for c in range(4):
            nkb = 4 * c + 4
            bU, bL = ps[2], ps[3]
            for kb in range(nkb):
                bS = ps[kb % 2]
                skey = PK[kb % 2]
                ekey = f"expS{kb % 2}"
                mm(bS[:, :], Kt_sb[:, kb * 128:(kb + 1) * 128],
                   Qt_sb[:, c * 512:(c + 1) * 512], ["Kt", "Qt"], [skey],
                   start=True, stop=True)
                act(expS[kb % 2][:, :], bS[:, :], AF.Exp, [skey], [ekey],
                    scale=SCALE_A)
                if kb >= 4 * c:
                    m = kb - 4 * c
                    gp(lambda e, kb=kb, m=m: e.affine_select(
                        expS[kb % 2][:, :], expS[kb % 2][:, :], [[1, 512]],
                        ALU.is_ge, 0.0, base=-128 * m, channel_multiplier=-1),
                        [ekey], [ekey])
                mm(bU[:, :], Bv(VS + kb * 128, 128), expS[kb % 2][:, :],
                   vsk + [ekey], [PK[2]], start=(kb == 0), stop=(kb == nkb - 1))
                mm(bL[0:1, :], ones_col[:, :], expS[kb % 2][:, :],
                   ["ones_col", ekey], [PK[3]],
                   start=(kb == 0), stop=(kb == nkb - 1))
            vec(lambda e: e.tensor_copy(lrow[:, :], ps[3][0:1, :]), [PK[3]], ["lrow"])
            vec(lambda e: e.reciprocal(lrec[:, :], lrow[:, :]), ["lrow", "coef_row"],
                ["lrec"])
            vec(lambda e: e.tensor_copy(lhi[:, :], lrec[:, :]), ["lrec"], ["lhi"])
            vec(lambda e: e.tensor_sub(lrow[:, :], lrec[:, :], lhi[:, :]),
                ["lrec", "lhi", "lrow"], ["llo_f"])
            vec(lambda e: e.tensor_copy(llo[:, :], lrow[:, :]), ["llo_f"], ["llo"])
            bB = ps[4]
            mm(bB[:, :], ones_row[0:1, 0:128], lhi[0:1, :], ["ones_row", "lhi"],
               [PK[4]], start=True, stop=False)
            mm(bB[:, :], ones_row[0:1, 0:128], llo[0:1, :], ["ones_row", "llo"],
               [PK[4]], start=False, stop=True)
            vec(lambda e: e.tensor_copy(bcast_sb[:, :], ps[4][:, :]), [PK[4]],
                ["bcast"])
            vec(lambda e, c=c: e.tensor_mul(Uh_sb[:, c * 512:(c + 1) * 512],
                                            ps[2][:, :], bcast_sb[:, :]),
                [PK[2], "bcast"], ["Uh"])
        dma("sync", dbg[:, 0:2048], Qt_sb[:, :], ["Qt"], ["d0"])
        dma("sync", dbg[:, 2048:4096], Kt_sb[:, :], ["Kt"], ["d1"])
        dma("sync", dbg[:, 4096:6144], Bv(VS, 2048), vsk, ["d2"])
        dma("sync", dbg[:, 6144:8192], Uh_sb[:, :], ["Uh"], ["d3"])
        dma("sync", y[0:128, 0:512], rbuf[0][:, :], ["rbuf0"], ["yk"])
        final = ["d0", "d1", "d2", "d3", "yk"]
    elif upto == "ln":
        for t in range(NT):
            layernorm_block(
                x0, t,
                dst_base=lambda db, t=t: db * AG + t * 128,
                dst_stride=AG,
                dst_keys_fn=lambda g4: [f"A{g4 * 4 + j}" for j in range(4)],
                xin_keys=[], tag="a1")
        dma("sync", dbg, BIGA[:, :], [f"A{g}" for g in range(32)], ["dbgk"])
        dma("sync", y[0:128, 0:512], rbuf[0][:, :], ["rbuf0"], ["yk"])
        final = ["dbgk", "yk"]
    elif upto == "a":
        keys = phase_a()
        for t in range(NT):
            for o in range(8):
                sl = o % 2
                dma("sync", rbuf[sl][:, :], x1[t * 128:(t + 1) * 128, o * 512:(o + 1) * 512],
                    keys + [f"rbuf{sl}"], [f"rbuf{sl}"])
                dma("sync", y[t * 128:(t + 1) * 128, o * 512:(o + 1) * 512], rbuf[sl][:, :],
                    [f"rbuf{sl}"], [f"yk{t}_{o}"])
        final = [f"yk{t}_{o}" for t in range(NT) for o in range(8)]
    elif upto == "f1":
        keys = phase_a()
        final = phase_f(x1, y, "y", "x1", 1)
    elif upto == "b":
        keys = phase_a()
        k2 = phase_f(x1, x2, "x2", "x1", 1)
        final = phase_b(x2, y, "x2")
    else:
        keys = phase_a()
        k2 = phase_f(x1, x2, "x2", "x1", 1)
        k3 = phase_b(x2, x3, "x2")
        final = phase_f(x3, y, "y", "x3", 2)
    B.emit("sync", lambda e: e.nop(), reads=final, writes=[])
    B.finalize_and_emit(stack)
    stack.close()
    return nc


_NC_CACHE = {}


def _get_nc(upto="full"):
    if upto not in _NC_CACHE:
        _NC_CACHE[upto] = build_program(upto)
    return _NC_CACHE[upto]


def _in_maps(inputs):
    c = np.ascontiguousarray
    hs = inputs["L0_attn_a_hidden_states"]
    base = {
        "wq": c(inputs["L0_attn_a_wq"]),
        "wkv": c(inputs["L0_attn_a_wkv"]),
        "wo_a": c(inputs["L0_attn_a_wo"]),
        "Wq_d": c(inputs["L0_attn_b_Wq_d"]),
        "bq_d": c(inputs["L0_attn_b_bq_d"]).reshape(1, QL),
        "W_qk": c(inputs["L0_attn_b_W_qk"]),
        "b_qk": c(inputs["L0_attn_b_b_qk"]).reshape(1, 128),
        "Wkv_d": c(inputs["L0_attn_b_Wkv_d"]),
        "bkv_d": c(inputs["L0_attn_b_bkv_d"]).reshape(1, KV),
        "Wv_u": c(inputs["L0_attn_b_Wv_u"]),
        "bv_u": c(inputs["L0_attn_b_bv_u"]).reshape(1, D),
        "Wo_b": c(inputs["L0_attn_b_Wo"]),
        "bo_b": c(inputs["L0_attn_b_bo"]).reshape(1, D),
        "lora_A": c(inputs["L0_ffn_lora_A"]).reshape(128, D),
        "lora_B": c(inputs["L0_ffn_lora_B"]).reshape(8 * D, 16),
        "ffn_W": c(inputs["L0_ffn_W"]),
        "ffn_bias": c(inputs["L0_ffn_bias"]).reshape(1, D),
        "lora_w": c(inputs["L0_ffn_lora_weights"]).reshape(1, 8),
    }
    maps = []
    for b in range(8):
        m = dict(base)
        m["x0"] = c(np.asarray(hs[b]))
        maps.append(m)
    return maps


def kernel(**inputs):
    nc = _get_nc("full")
    maps = _in_maps(inputs)
    res = run_bass_kernel_spmd(nc, maps, core_ids=list(range(8)))
    out = np.stack([np.asarray(r["y"]).astype(np.float32) for r in res.results], axis=0)
    return out


# version 12 (final)
# speedup vs baseline: 1.2529x; 1.0086x over previous
import numpy as np
import ml_dtypes
import concourse.bass as bass
import concourse.mybir as mybir
from concourse.bass_utils import run_bass_kernel_spmd

F32 = mybir.dt.float32
BF16 = mybir.dt.bfloat16
AF = mybir.ActivationFunctionType
ALU = mybir.AluOpType

S = 2048
D = 4096
NB = D // 128
NT = S // 128
H = 32
SCALE_A = float(192.0 ** -0.5)
QL, KV = 12, 4
SCALE_B = 0.5
EPS = 1e-5
AG = 2048


class Op:
    __slots__ = ("engine", "fn", "dma", "deps", "needed", "rank", "waits",
                 "inc_sem", "inc_amt", "tag")

    def __init__(self, engine, fn, dma, tag):
        self.engine = engine
        self.fn = fn
        self.dma = dma
        self.deps = []
        self.needed = False
        self.rank = 0
        self.waits = []
        self.inc_sem = None
        self.inc_amt = 0
        self.tag = tag


class Builder:
    ENGINES = ("tensor", "vector", "scalar", "gpsimd", "sync")

    def __init__(self, nc):
        self.nc = nc
        self.ops = {e: [] for e in self.ENGINES}
        self.last_write = {}
        self.readers = {}

    def emit(self, engine, fn, reads=(), writes=(), dma=False, tag=""):
        op = Op(engine, fn, dma, tag)
        deps = {}
        for k in reads:
            w = self.last_write.get(k)
            if w is not None:
                deps[id(w)] = w
        for k in writes:
            w = self.last_write.get(k)
            if w is not None:
                deps[id(w)] = w
            for r in self.readers.get(k, ()):
                deps[id(r)] = r
        for d in deps.values():
            if d is op:
                continue
            if d.engine == "tensor" and engine == "tensor" and not d.dma and not dma:
                continue
            op.deps.append(d)
        for k in reads:
            self.readers.setdefault(k, []).append(op)
        for k in writes:
            self.last_write[k] = op
            self.readers[k] = []
        self.ops[engine].append(op)
        return op

    LANES = 8

    def finalize_and_emit(self, stack):
        nc = self.nc
        csems = {e: stack.enter_context(nc.semaphore(f"s_{e}"))
                 for e in ("tensor", "vector", "scalar", "sync")}
        lsems = {}
        for q in ("sync", "gpsimd", "scalar"):
            lsems[("d", q)] = [stack.enter_context(nc.semaphore(f"d_{q}{i}"))
                               for i in range(self.LANES)]
        lsems[("c", "gpsimd")] = [stack.enter_context(nc.semaphore(f"g_{i}"))
                                  for i in range(self.LANES)]
        for e in self.ENGINES:
            for op in self.ops[e]:
                for d in op.deps:
                    d.needed = True
        counters = {}
        for e in self.ENGINES:
            for op in self.ops[e]:
                if op.dma:
                    key = ("d", e)
                    i = counters.get(key, 0)
                    counters[key] = i + 1
                    op.rank = i
                    op.inc_sem = lsems[key][i % self.LANES]
                    op.inc_amt = 16
                    if i >= self.LANES:
                        op.waits.append((op.inc_sem, 16 * (i // self.LANES)))
                elif e == "gpsimd":
                    key = ("c", "gpsimd")
                    i = counters.get(key, 0)
                    counters[key] = i + 1
                    op.rank = i
                    op.inc_sem = lsems[key][i % self.LANES]
                    op.inc_amt = 1
                    if i >= self.LANES:
                        op.waits.append((op.inc_sem, i // self.LANES))
                elif op.needed:
                    key = ("c", e)
                    counters[key] = counters.get(key, 0) + 1
                    op.rank = counters[key]
                    op.inc_sem = csems[e]
                    op.inc_amt = 1
        for e in self.ENGINES:
            known = {}
            for op in self.ops[e]:
                for d in op.deps:
                    if d.dma:
                        sem = lsems[("d", d.engine)][d.rank % self.LANES]
                        val = 16 * (d.rank // self.LANES + 1)
                    elif d.engine == "gpsimd":
                        sem = lsems[("c", "gpsimd")][d.rank % self.LANES]
                        val = d.rank // self.LANES + 1
                    else:
                        sem, val = csems[d.engine], d.rank
                    kid = id(sem)
                    if known.get(kid, -1) < val:
                        op.waits.append((sem, val))
                        known[kid] = val
        blk = stack.enter_context(nc.Block())
        for ename in self.ENGINES:
            ops = self.ops[ename]

            def section(eng, ops=ops):
                for op in ops:
                    for (sem, val) in op.waits:
                        eng.wait_ge(sem, val)
                    inst = op.fn(eng)
                    if op.inc_sem is not None:
                        inst.then_inc(op.inc_sem, op.inc_amt)

            getattr(blk, ename)(section)


def build_program(upto="full"):
    nc = bass.Bass("TRN2", target_bir_lowering=False)
    x0 = nc.dram_tensor("x0", [S, D], F32, kind="ExternalInput").ap()
    wq = nc.dram_tensor("wq", [D, 6144], F32, kind="ExternalInput").ap()
    wkv = nc.dram_tensor("wkv", [D, 8256], F32, kind="ExternalInput").ap()
    wo_a = nc.dram_tensor("wo_a", [D, D], F32, kind="ExternalInput").ap()
    Wq_d = nc.dram_tensor("Wq_d", [D, QL], F32, kind="ExternalInput").ap()
    bq_d = nc.dram_tensor("bq_d", [1, QL], F32, kind="ExternalInput").ap()
    W_qk = nc.dram_tensor("W_qk", [QL, 128], F32, kind="ExternalInput").ap()
    b_qk = nc.dram_tensor("b_qk", [1, 128], F32, kind="ExternalInput").ap()
    Wkv_d = nc.dram_tensor("Wkv_d", [D, KV], F32, kind="ExternalInput").ap()
    bkv_d = nc.dram_tensor("bkv_d", [1, KV], F32, kind="ExternalInput").ap()
    Wv_u = nc.dram_tensor("Wv_u", [KV, D], F32, kind="ExternalInput").ap()
    bv_u = nc.dram_tensor("bv_u", [1, D], F32, kind="ExternalInput").ap()
    Wo_b = nc.dram_tensor("Wo_b", [D, D], F32, kind="ExternalInput").ap()
    bo_b = nc.dram_tensor("bo_b", [1, D], F32, kind="ExternalInput").ap()
    lora_A = nc.dram_tensor("lora_A", [128, D], F32, kind="ExternalInput").ap()
    lora_B = nc.dram_tensor("lora_B", [8 * D, 16], F32, kind="ExternalInput").ap()
    ffn_W = nc.dram_tensor("ffn_W", [D, D], F32, kind="ExternalInput").ap()
    ffn_bias = nc.dram_tensor("ffn_bias", [1, D], F32, kind="ExternalInput").ap()
    lora_w = nc.dram_tensor("lora_w", [1, 8], F32, kind="ExternalInput").ap()
    y = nc.dram_tensor("y", [S, D], F32, kind="ExternalOutput").ap()
    x1 = nc.dram_tensor("x1", [S, D], F32).ap()
    x2 = nc.dram_tensor("x2", [S, D], F32).ap()
    x3 = nc.dram_tensor("x3", [S, D], F32).ap()
    Ua = nc.dram_tensor("Ua", [D, S], BF16).ap()
    Abh = nc.dram_tensor("Abh", [D, S], BF16).ap()
    WT = nc.dram_tensor("WTs", [D, D], BF16).ap()
    AcH = nc.dram_tensor("AcH", [128, D], BF16).ap()
    BtH = nc.dram_tensor("BtH", [128, D], BF16).ap()
    cfH = nc.dram_tensor("cfH", [1, 128], F32).ap()
    dbg = nc.dram_tensor("dbg", [128, 32 * AG], BF16, kind="ExternalOutput").ap() \
        if upto in ("ln", "h0", "ua", "wt", "b0") else None

    from contextlib import ExitStack
    stack = ExitStack()
    sb = lambda name, shape, dt: stack.enter_context(nc.sbuf_tensor(name, shape, dt))
    psa = lambda name, shape, dt: stack.enter_context(nc.psum_tensor(name, shape, dt))
    B = Builder(nc)

    BIGA = sb("BIGA", [128, 32 * AG], BF16)
    BIGB = sb("BIGB", [128, 8 * AG], BF16)
    ones_row = sb("ones_row", [1, 512], BF16)
    ones_col = sb("ones_col", [128, 1], BF16)
    ident = sb("ident", [128, 128], BF16)
    masks = sb("masks", [128, 896], BF16)
    lacc = sb("lacc", [128, 512], F32)
    oncf = sb("oncf", [128, 1], F32)
    wqd_sb = sb("wqd_sb", [128, NB * QL], BF16)
    wkvd_sb = sb("wkvd_sb", [128, NB * KV], BF16)
    bqd_row = sb("bqd_row", [1, QL], BF16)
    bkvd_row = sb("bkvd_row", [1, KV], BF16)
    wqk_sb = sb("wqk_sb", [QL, 128], BF16)
    bqk_row = sb("bqk_row", [1, 128], BF16)
    wvu_h = sb("wvu_h", [KV, 128], BF16)
    bvu_h = sb("bvu_h", [1, 128], BF16)
    row_big = sb("row_big", [1, D], BF16)
    lw_sb = sb("lw_sb", [1, 8], F32)
    coef_col = sb("coef_col", [128, 1], F32)
    stat = sb("stat", [128, 16], F32)
    eps_col = sb("eps_col", [128, 1], F32)
    diag_bf = [sb(f"diag{i}", [128, 128], BF16) for i in range(2)]
    bcol_bf = [sb(f"bcolb{i}", [128, 1], BF16) for i in range(2)]
    brow_bf = [sb(f"browb{i}", [1, 128], BF16) for i in range(2)]
    Qt_sb = sb("Qt_sb", [128, S], BF16)
    Kt_sb = sb("Kt_sb", [128, S], BF16)
    expS = [sb(f"expS{i}", [128, 512], BF16) for i in range(4)]
    lhi = sb("lhi", [1, 512], BF16)
    llo = sb("llo", [1, 512], BF16)
    bcast_sb = sb("bcast_sb", [128, 512], F32)
    Uh_sb = sb("Uh_sb", [128, S], BF16)
    rbuf = [sb(f"rbuf{i}", [128, 512], F32) for i in range(2)]
    nbuf = [sb(f"nbuf{i}", [128, 512], F32) for i in range(2)]
    lrow = nbuf[0][0:1, :]
    lrec = nbuf[1][0:1, :]
    hbf = sb("hbf", [128, 1024], BF16)
    Ckvs_sb = sb("Ckvs_sb", [128, NT * 40], BF16)
    u5_sb = lacc
    Cq_sb = Uh_sb
    Ckv_sb = Kt_sb
    ctx_bf = Qt_sb
    bct_sb = Kt_sb

    ps = [psa(f"ps{i}", [128, 512], F32) for i in range(6)]
    psb = psa("psb", [128, 512], BF16)
    psb2 = psa("psb2", [128, 512], BF16)
    PK = [f"ps{i}" for i in range(6)]

    def A(c0, w):
        return BIGA[:, c0:c0 + w]

    def Bv(c0, w):
        return BIGB[:, c0:c0 + w]

    def akeys(c0, w):
        return [f"A{g}" for g in range(c0 // AG, (c0 + w - 1) // AG + 1)]

    def bkeys(c0, w):
        return [f"B{g}" for g in range(c0 // AG, (c0 + w - 1) // AG + 1)]

    XBF = [0, 4096]
    SQS = 8192
    QW, KW, VW2 = 0, 4096, 8192
    VT, VS = 12288, 14336
    ACT_ = 12288
    AH = [12288, 14336]
    UAT = [8192, 12288]
    WCOL = [0, 0]
    WSTG = 4096
    STGB = 0
    STG16 = 4096
    QH = 0

    def dma(queue, out_ap, in_ap, reads, writes, tag=""):
        return B.emit(queue, lambda e: e.dma_start(out_ap, in_ap), reads, writes,
                      dma=True, tag=tag)

    def mm(out_ap, lhsT_ap, rhs_ap, reads, writes, start, stop, tag=""):
        return B.emit("tensor",
                      lambda e: e.matmul(out_ap, lhsT_ap, rhs_ap, start=start, stop=stop),
                      reads, writes, tag=tag)

    def tr(out_ap, in_ap, p, reads, writes, tag=""):
        return B.emit("tensor",
                      lambda e: e.transpose(out_ap, in_ap, ident[0:p, 0:p]),
                      list(reads) + ["ident"], writes, tag=tag)

    def act(out_ap, in_ap, func, reads, writes, bias=0.0, scale=1.0, accum=None, tag=""):
        return B.emit("scalar",
                      lambda e: e.activation(out_ap, in_ap, func, bias=bias, scale=scale,
                                             accum_out=accum),
                      reads, writes, tag=tag)

    def vec(fn, reads, writes, tag=""):
        return B.emit("vector", fn, reads, writes, tag=tag)

    def gp(fn, reads, writes, tag=""):
        return B.emit("gpsimd", fn, reads, writes, tag=tag)

    gp(lambda e: e.memset(ones_row[:, :], 1.0), [], ["ones_row"])
    gp(lambda e: e.memset(ones_col[:, :], 1.0), [], ["ones_col"])
    gp(lambda e: e.memset(eps_col[:, :], EPS), [], ["eps_col"])
    gp(lambda e: e.memset(oncf[:, :], 1.0), [], ["oncf"])
    gp(lambda e: e.memset(masks[:, :], 1.0), ["mkM"], ["mkM"])
    gp(lambda e: e.affine_select(masks[:, :], masks[:, :], [[1, 896]], ALU.is_ge, 0.0,
                                 base=-384, channel_multiplier=-1), ["mkM"], ["masks"])
    gp(lambda e: e.memset(ident[:, :], 1.0), ["identM"], ["identM"])
    gp(lambda e: e.affine_select(ident[:, :], ident[:, :], [[1, 128]], ALU.is_equal, 0.0,
                                 base=0, channel_multiplier=-1), ["identM"], ["ident"])
    dma("gpsimd", wqd_sb[:, :], Wq_d.rearrange("(b p) l -> p b l", p=128), [], ["wqd_sb"])
    dma("gpsimd", wkvd_sb[:, :], Wkv_d.rearrange("(b p) l -> p b l", p=128), [], ["wkvd_sb"])
    dma("gpsimd", bqd_row[:, :], bq_d, [], ["bqd_row"])
    dma("gpsimd", bkvd_row[:, :], bkv_d, [], ["bkvd_row"])
    dma("gpsimd", wqk_sb[:, :], W_qk, [], ["wqk_sb"])
    dma("gpsimd", bqk_row[:, :], b_qk, [], ["bqk_row"])
    dma("sync", lw_sb[:, :], lora_w, [], ["lw_sb"])
    vec(lambda e: e.tensor_scalar_mul(
        lrec[0:1, 0:128], bass.AP(lw_sb, 0, [[8, 1], [1, 8], [0, 16]]), 2.0),
        ["lw_sb", "lrec"], ["coef_row"])
    dma("sync", cfH[:, :], lrec[0:1, 0:128], ["coef_row"], ["cfHk"])
    dma("sync", coef_col[:, :], cfH.rearrange("a (p o) -> p (a o)", p=128),
        ["cfHk"], ["coef_col"])
    dma("gpsimd", Bv(STGB, 4096), lora_A, bkeys(STGB, 4096),
        bkeys(STGB, 4096) + ["stgA"])
    for g4 in range(8):
        for j in range(4):
            db = g4 * 4 + j
            tr(psb[:, j * 128:(j + 1) * 128], Bv(STGB + db * 128, 128), 128,
               ["stgA"] + bkeys(STGB + db * 128, 128), ["psb"])
        B.emit("scalar",
               lambda e, g4=g4: e.copy(Bv(WSTG + g4 * 512, 512), psb[:, :]),
               ["psb"], bkeys(WSTG + g4 * 512, 512) + ["stgAc"])
    dma("sync", AcH[:, :], Bv(WSTG, 4096), ["stgAc"] + bkeys(WSTG, 4096), ["AcHk"])
    for n in range(8):
        dma("gpsimd", Bv(STGB, 512),
            lora_B[n * D:(n + 1) * D, :].rearrange("(bb p) r -> p bb r", p=128),
            ["stgA", "stgBn"] + bkeys(STGB, 512), bkeys(STGB, 512) + ["stgBn"])
        for b4 in range(8):
            for j in range(4):
                bb = b4 * 4 + j
                tr(psb[0:16, j * 128:(j + 1) * 128], Bv(STGB + bb * 16, 16), 128,
                   ["stgBn"] + bkeys(STGB, 512), ["psb"])
            B.emit("vector",
                   lambda e, b4=b4: e.tensor_copy(
                       BIGB[0:16, STG16 + b4 * 512: STG16 + (b4 + 1) * 512],
                       psb[0:16, 0:512]),
                   ["psb"] + bkeys(STG16, 4096), bkeys(STG16, 4096) + ["stg16"])
        dma("sync", BtH[n * 16:(n + 1) * 16, :], BIGB[0:16, STG16:STG16 + 4096],
            ["stg16"] + bkeys(STG16, 4096), [f"BtHk{n}"])
    bt_keys = [f"BtHk{n}" for n in range(8)]

    def layernorm_block(xin_ap, t, dst_base, dst_stride, dst_keys_fn, xin_keys, tag):
        sl = t % 2
        xb = Bv(XBF[sl], 4096)
        kx = bkeys(XBF[sl], 4096)
        dma("gpsimd", xb, xin_ap[t * 128:(t + 1) * 128, :], kx + xin_keys, kx,
            tag=f"{tag}x{t}")
        junk = Bv(SQS, 4096)
        kj = bkeys(SQS, 4096)
        act(junk, xb, AF.Copy, kx + kj, kj + [f"sum{sl}"],
            accum=stat[:, sl * 8:sl * 8 + 1], tag=f"{tag}cp{t}")
        act(junk, xb, AF.Square, kx + kj, kj + [f"sumsq{sl}"],
            accum=stat[:, sl * 8 + 1:sl * 8 + 2], tag=f"{tag}sq{t}")
        st = stat[:, sl * 8:sl * 8 + 8]
        vec(lambda e, st=st: e.tensor_scalar_mul(st[:, 2:3], st[:, 0:1], 1.0 / D),
            [f"sum{sl}"], [f"mean{sl}"])
        vec(lambda e, st=st: e.tensor_scalar_mul(st[:, 3:4], st[:, 1:2], 1.0 / D),
            [f"sumsq{sl}"], [f"msq{sl}"])
        vec(lambda e, st=st: e.tensor_mul(st[:, 4:5], st[:, 2:3], st[:, 2:3]),
            [f"mean{sl}"], [f"m2{sl}"])
        vec(lambda e, st=st: e.tensor_sub(st[:, 3:4], st[:, 3:4], st[:, 4:5]),
            [f"msq{sl}", f"m2{sl}"], [f"var{sl}"])
        act(st[:, 4:5], st[:, 3:4], AF.Ln, [f"var{sl}", f"m2{sl}", "eps_col"],
            [f"lnv{sl}"], bias=eps_col[:, 0:1])
        act(st[:, 5:6], st[:, 4:5], AF.Exp, [f"lnv{sl}"], [f"rstd0{sl}"], scale=-0.5)
        vec(lambda e, st=st: e.tensor_mul(st[:, 6:7], st[:, 5:6], st[:, 5:6]),
            [f"rstd0{sl}"], [f"r2{sl}"])
        vec(lambda e, st=st: e.tensor_scalar_add(st[:, 4:5], st[:, 3:4], EPS),
            [f"var{sl}", f"lnv{sl}"], [f"vpe{sl}"])
        vec(lambda e, st=st: e.tensor_mul(st[:, 6:7], st[:, 6:7], st[:, 4:5]),
            [f"r2{sl}", f"vpe{sl}"], [f"r2{sl}"])
        vec(lambda e, st=st: e.tensor_scalar(st[:, 6:7], st[:, 6:7], -0.5, 1.5,
                                             ALU.mult, ALU.add),
            [f"r2{sl}"], [f"r2{sl}"])
        vec(lambda e, st=st: e.tensor_mul(st[:, 5:6], st[:, 5:6], st[:, 6:7]),
            [f"rstd0{sl}", f"r2{sl}"], [f"rstd{sl}"])
        vec(lambda e, st=st: e.tensor_mul(st[:, 7:8], st[:, 2:3], st[:, 5:6]),
            [f"mean{sl}", f"rstd{sl}"], [f"mr{sl}"])
        vec(lambda e, st=st, sl=sl: e.tensor_scalar_mul(diag_bf[sl][:, :], ident[:, :],
                                                        st[:, 5:6]),
            ["ident", f"rstd{sl}"], [f"diag{sl}"])
        vec(lambda e, st=st, sl=sl: e.tensor_copy(bcol_bf[sl][:, :], st[:, 7:8]),
            [f"mr{sl}"], [f"bcol{sl}"])
        pbr = psb[0:1, sl * 128:sl * 128 + 128]
        tr(pbr, bcol_bf[sl][:, :], 128, [f"bcol{sl}"], ["psb"])
        act(brow_bf[sl][:, :], pbr, AF.Copy, ["psb"], [f"brow{sl}"], scale=-1.0)
        for g4 in range(8):
            bank = ps[g4 % 2]
            bkey = PK[g4 % 2]
            for j in range(4):
                db = g4 * 4 + j
                mm(bank[:, j * 128:(j + 1) * 128], xb[:, db * 128:(db + 1) * 128],
                   diag_bf[sl][:, :], kx + [f"diag{sl}"], [bkey], start=True, stop=False)
                mm(bank[:, j * 128:(j + 1) * 128], ones_row[0:1, 0:128],
                   brow_bf[sl][0:1, :], ["ones_row", f"brow{sl}"], [bkey],
                   start=False, stop=True)
            off = dst_base(g4 * 4) + 0
            dst_ap = bass.AP(BIGA, off, [[32 * AG, 128], [dst_stride, 4], [1, 128]])
            src_ap = bass.AP(bank, 0, [[512, 128], [128, 4], [1, 128]])
            vec(lambda e, dst_ap=dst_ap, src_ap=src_ap: e.tensor_copy(dst_ap, src_ap),
                [bkey], dst_keys_fn(g4), tag=f"{tag}ev{t}_{g4}")

    def wprep_steps():
        steps = []
        for i in range(NB):
            sl = i % 2

            def step(i=i, sl=sl):
                wst = WSTG + (i % 2) * 4096
                dma("gpsimd", Bv(WCOL[sl], 4096),
                    ffn_W[:, i * 128:(i + 1) * 128].rearrange("(b p) d -> p b d", p=128),
                    bkeys(WCOL[sl], 4096), bkeys(WCOL[sl], 4096), tag=f"wp{i}")
                for b4 in range(8):
                    for j in range(4):
                        b = b4 * 4 + j
                        tr(psb[:, j * 128:(j + 1) * 128], Bv(WCOL[sl] + b * 128, 128),
                           128, bkeys(WCOL[sl], 4096), ["psb"])
                    B.emit("scalar",
                           lambda e, b4=b4, wst=wst: e.copy(Bv(wst + b4 * 512, 512),
                                                            psb[:, :]),
                           ["psb"], bkeys(wst + b4 * 512, 512) + [f"wstg{i % 2}"])
                dma("sync", WT[i * 128:(i + 1) * 128, :], Bv(wst, 4096),
                    [f"wstg{i % 2}"] + bkeys(wst, 4096), [f"WTr{i}"])

            steps.append(step)
        return steps

    def phase_a():
        for t in range(NT):
            layernorm_block(
                x0, t,
                dst_base=lambda db, t=t: db * AG + t * 128,
                dst_stride=AG,
                dst_keys_fn=lambda g4: [f"A{g4 * 4 + j}" for j in range(4)],
                xin_keys=[], tag="a1")
        for h in range(H):
            qwk = bkeys(QW, 4096)
            kwk = bkeys(KW, 4096)
            vwk = bkeys(VW2, 4096)
            dma("gpsimd", Bv(QW, 4096),
                wq[:, h * 192: h * 192 + 128].rearrange("(b p) e -> p b e", p=128),
                qwk, qwk, tag=f"qw{h}")
            dma("gpsimd", Bv(KW, 4096),
                wkv[:, h * 128: (h + 1) * 128].rearrange("(b p) e -> p b e", p=128),
                kwk, kwk, tag=f"kw{h}")
            dma("gpsimd", Bv(VW2, 4096),
                wkv[:, 4096 + h * 128: 4096 + (h + 1) * 128].rearrange(
                    "(b p) e -> p b e", p=128),
                vwk, vwk, tag=f"vw{h}")
            vtk = bkeys(VT, 2048)
            vsk = bkeys(VS, 2048)
            for c in range(4):
                bv_ = ps[2 + c % 2]
                for db in range(NB):
                    mm(bv_[:, :], Bv(VW2 + db * 128, 128), A(db * AG + c * 512, 512),
                       vwk + [f"A{db}"], [PK[2 + c % 2]], start=(db == 0),
                       stop=(db == NB - 1))
                act(Bv(VT + c * 512, 512), bv_[:, :], AF.Copy,
                    [PK[2 + c % 2]] + (vtk if c == 3 else []), [f"VTc{c}"])

            def v_transpose(g4):
                pb = psb if g4 % 2 == 0 else psb2
                pbk = "psb" if g4 % 2 == 0 else "psb2"
                for j in range(4):
                    kb = g4 * 4 + j
                    tr(pb[:, j * 128:(j + 1) * 128], Bv(VT + kb * 128, 128), 128,
                       [f"VTc{g4}"] + (vtk if g4 == 3 and j == 3 else []), [pbk])
                act(Bv(VS + g4 * 512, 512), pb[:, :], AF.Copy,
                    [pbk] + (vsk if g4 == 3 else []), [f"VSc{g4}"] + (vsk if g4 == 3 else []))

            for c in range(4):
                bq, bk = ps[4], ps[5]
                for db in range(NB):
                    mm(bq[:, :], Bv(QW + db * 128, 128), A(db * AG + c * 512, 512),
                       qwk + [f"A{db}"], [PK[4]], start=(db == 0), stop=(db == NB - 1))
                for db in range(NB):
                    mm(bk[:, :], Bv(KW + db * 128, 128), A(db * AG + c * 512, 512),
                       kwk + [f"A{db}"], [PK[5]], start=(db == 0), stop=(db == NB - 1))
                act(Qt_sb[:, c * 512:(c + 1) * 512], bq[:, :], AF.Copy, [PK[4]], ["Qt"])
                act(Kt_sb[:, c * 512:(c + 1) * 512], bk[:, :], AF.Copy, [PK[5]], ["Kt"])
            deferred = [None]

            def fin_mms(c, ub):
                bb_i = (0, 1, 4, 5)[(4 * c + 5) % 4]
                bB = ps[bb_i]
                mm(bB[:, :], ones_row[0:1, 0:128], lhi[0:1, :], ["ones_row", "lhi"],
                   [PK[bb_i]], start=True, stop=False)
                mm(bB[:, :], ones_row[0:1, 0:128], llo[0:1, :], ["ones_row", "llo"],
                   [PK[bb_i]], start=False, stop=True)
                vec(lambda e, bB=bB: e.tensor_copy(bcast_sb[:, :], bB[:, :]), [PK[bb_i]],
                    ["bcast"])
                vec(lambda e, c=c, ub=ub: e.tensor_mul(Uh_sb[:, c * 512:(c + 1) * 512],
                                                       ps[ub][:, :], bcast_sb[:, :]),
                    [PK[ub], "bcast"], ["Uh"], tag=f"Un{h}_{c}")

            upend = []

            def u_flush(n):
                while upend and len(upend) > n:
                    upend.pop(0)()

            v_transpose(0)
            for c in range(4):
                nkb = 4 * c + 4
                ub = 2 + c % 2
                ROT = (0, 1, 4, 5)
                for kb in range(nkb):
                    u_flush(2)
                    if kb == 1 and c < 3:
                        v_transpose(c + 1)
                    bS = ps[ROT[kb % 4]]
                    skey = PK[ROT[kb % 4]]
                    ekey = f"expS{kb % 4}"
                    mm(bS[:, :], Kt_sb[:, kb * 128:(kb + 1) * 128],
                       Qt_sb[:, c * 512:(c + 1) * 512], ["Kt", "Qt"], [skey],
                       start=True, stop=True, tag=f"sc{h}_{c}_{kb}")
                    act(expS[kb % 4][:, :], bS[:, :], AF.Exp, [skey], [ekey],
                        scale=SCALE_A, tag=f"exp{h}_{c}_{kb}")
                    if kb >= 4 * c:
                        m = kb - 4 * c
                        vec(lambda e, kb=kb, m=m: e.tensor_mul(
                            expS[kb % 4][:, :], expS[kb % 4][:, :],
                            masks[:, 384 - 128 * m: 896 - 128 * m]),
                            [ekey, "masks"], [ekey], tag=f"msk{h}_{c}_{kb}")
                    def u_mm(kb=kb, c=c, ub=ub, nkb=nkb, ekey=ekey):
                        mm(ps[ub][:, :], Bv(VS + kb * 128, 128), expS[kb % 4][:, :],
                           [f"VSc{kb // 4}", ekey], [PK[ub]], start=(kb == 0),
                           stop=(kb == nkb - 1))
                    upend.append(u_mm)
                    if kb == 0:
                        vec(lambda e, kb=kb: e.tensor_copy(lacc[:, :], expS[kb % 4][:, :]),
                            [ekey, "lacc"], ["lacc"])
                    else:
                        vec(lambda e, kb=kb: e.tensor_add(lacc[:, :], lacc[:, :],
                                                          expS[kb % 4][:, :]),
                            [ekey, "lacc"], ["lacc"])
                    if kb == 5 and deferred[0] is not None:
                        deferred[0]()
                        deferred[0] = None
                u_flush(0)
                lb = ROT[nkb % 4]
                mm(ps[lb][0:1, :], oncf[:, :], lacc[:, :], ["oncf", "lacc"], [PK[lb]],
                   start=True, stop=True)
                if deferred[0] is not None:
                    deferred[0]()
                    deferred[0] = None
                act(lrow[:, :], ps[lb][0:1, :], AF.Ln, [PK[lb], "lrow"], ["lrow"])
                act(lrec[:, :], lrow[:, :], AF.Exp, ["lrow", "coef_row"], ["lrec"],
                    scale=-1.0)
                act(lhi[:, :], lrec[:, :], AF.Copy, ["lrec", "lhi"], ["lhi"])
                vec(lambda e: e.tensor_sub(lrow[:, :], lrec[:, :], lhi[:, :]),
                    ["lrec", "lhi", "lrow"], ["llo_f"])
                vec(lambda e: e.tensor_copy(llo[:, :], lrow[:, :]), ["llo_f"], ["llo"])
                deferred[0] = (lambda c=c, ub=ub: fin_mms(c, ub))
            if deferred[0] is not None:
                deferred[0]()
                deferred[0] = None
            dma("sync", Ua[h * 128:(h + 1) * 128, :], Uh_sb[:, :], ["Uh"], [f"UaH{h}"],
                tag=f"spill{h}")
        ua_ready = [f"UaH{h}" for h in range(H)]
        wsteps = wprep_steps()
        wi = 0
        outk = []
        for Q in range(4):
            for e_blk in range(NB):
                wkey = akeys(e_blk * 1024, 1024)
                dma("gpsimd", A(e_blk * 1024, 1024),
                    wo_a[e_blk * 128:(e_blk + 1) * 128, Q * 1024:(Q + 1) * 1024],
                    wkey, wkey, tag=f"wo{Q}_{e_blk}")
            for tp in range(NT // 4):
                for _ in range(4):
                    if wi < len(wsteps):
                        wsteps[wi]()
                        wi += 1
                usl = tp % 2
                ubase = 32768 + usl * 16384
                uk = akeys(ubase, 16384)
                dma("sync", A(ubase, 16384),
                    Ua[:, tp * 512:(tp + 1) * 512].rearrange("(eb p) q -> p eb q", p=128),
                    ua_ready + uk, uk, tag=f"uat{Q}_{tp}")
                for tj in range(4):
                    t = tp * 4 + tj
                    for oc in range(2):
                        rsl = oc
                        dma("scalar", rbuf[rsl][:, :],
                            x0[t * 128:(t + 1) * 128,
                               Q * 1024 + oc * 512: Q * 1024 + (oc + 1) * 512],
                            [f"rbuf{rsl}"], [f"rbuf{rsl}"])
                        bi = oc + 2 * (tj % 2)
                        bank = ps[bi]
                        for e_blk in range(NB):
                            mm(bank[:, :], A(ubase + e_blk * 512 + tj * 128, 128),
                               A(e_blk * 1024 + oc * 512, 512),
                               uk + akeys(e_blk * 1024 + oc * 512, 512), [PK[bi]],
                               start=(e_blk == 0), stop=(e_blk == NB - 1))
                        vec(lambda e, bank=bank, rsl=rsl: e.tensor_add(
                            nbuf[rsl][:, :], bank[:, :], rbuf[rsl][:, :]),
                            [PK[bi], f"rbuf{rsl}"], [f"nbuf{rsl}"], tag=f"woev{Q}_{t}_{oc}")
                        o8 = Q * 2 + oc
                        outk.append(f"x1r{t}_{o8}")
                        dma("scalar", x1[t * 128:(t + 1) * 128, o8 * 512:(o8 + 1) * 512],
                            nbuf[rsl][:, :], [f"nbuf{rsl}"], [f"x1r{t}_{o8}"],
                            tag=f"x1st{t}_{o8}")
        while wi < len(wsteps):
            wsteps[wi]()
            wi += 1
        return outk

    def phase_f(xin_ap, xout_ap, xname, xin_name, inst):
        ack = bkeys(ACT_, 4096)
        dma("sync", Bv(ACT_, 4096), AcH[:, :], ["AcHk"] + ack, ack, tag=f"f{inst}Ac")
        dma("gpsimd", row_big[:, :], ffn_bias, ["row_big"], ["row_big"], tag=f"f{inst}fb")
        wt_keys = [f"WTr{i}" for i in range(NB)]
        outk = []
        for ch in range(2):
            for tt in range(8):
                t = ch * 8 + tt
                layernorm_block(
                    xin_ap, t,
                    dst_base=lambda db, tt=tt: db * 1024 + tt * 128,
                    dst_stride=1024,
                    dst_keys_fn=lambda g4: [f"A{(g4 * 4) // 2}", f"A{(g4 * 4 + 3) // 2}"],
                    xin_keys=[f"{xin_name}r{t}_{o}" for o in range(8)],
                    tag=f"f{inst}ln")
            for half in range(2):
                bank = ps[4 + half]
                for db in range(NB):
                    mm(bank[:, :], Bv(ACT_ + db * 128, 128),
                       A(db * 1024 + half * 512, 512),
                       ack + [f"A{(db * 1024 + half * 512) // AG}"], [PK[4 + half]],
                       start=(db == 0), stop=(db == NB - 1))
                act(hbf[:, half * 512:(half + 1) * 512], bank[:, :], AF.Copy,
                    [PK[4 + half], "coef_col"], ["hbf"], scale=coef_col[:, 0:1])
            for oc in range(8):
                wsl = oc % 2
                wck = akeys(32768 + wsl * 16384, 16384)
                dma("sync", A(32768 + wsl * 16384, 16384),
                    WT[:, oc * 512:(oc + 1) * 512].rearrange("(b p) o -> p b o", p=128),
                    wt_keys + wck, wck, tag=f"f{inst}wt{ch}_{oc}")
                dma("gpsimd", bct_sb[:, 0:512], BtH[:, oc * 512:(oc + 1) * 512],
                    bt_keys + ["Kt"], ["Kt"], tag=f"f{inst}bt{ch}_{oc}")
                for tt in range(8):
                    t = ch * 8 + tt
                    bank = ps[tt % 2]
                    for db in range(NB):
                        mm(bank[:, :], A(db * 1024 + tt * 128, 128),
                           A(32768 + wsl * 16384 + db * 512, 512),
                           [f"A{(db * 1024 + tt * 128) // AG}",
                            f"A{(32768 + wsl * 16384 + db * 512) // AG}"],
                           [PK[tt % 2]], start=(db == 0), stop=False)
                    mm(bank[:, :], hbf[:, tt * 128:(tt + 1) * 128], bct_sb[:, 0:512],
                       ["hbf", "Kt"], [PK[tt % 2]], start=False, stop=False)
                    mm(bank[:, :], ones_row[0:1, 0:128],
                       row_big[0:1, oc * 512:(oc + 1) * 512],
                       ["ones_row", "row_big"], [PK[tt % 2]], start=False, stop=True)
                    rsl = tt % 2
                    dma("scalar", rbuf[rsl][:, :],
                        xin_ap[t * 128:(t + 1) * 128, oc * 512:(oc + 1) * 512],
                        [f"{xin_name}r{t}_{oc}", f"rbuf{rsl}"], [f"rbuf{rsl}"])
                    vec(lambda e, bank=bank, rsl=rsl: e.tensor_add(
                        nbuf[rsl][:, :], bank[:, :], rbuf[rsl][:, :]),
                        [PK[tt % 2], f"rbuf{rsl}"], [f"nbuf{rsl}"])
                    k = f"{xname}r{t}_{oc}"
                    outk.append(k)
                    dma("scalar", xout_ap[t * 128:(t + 1) * 128, oc * 512:(oc + 1) * 512],
                        nbuf[rsl][:, :], [f"nbuf{rsl}"], [k])
        return outk

    def phase_b(xin_ap, xout_ap, xin_name):
        for ch in range(4):
            for tt in range(4):
                t = ch * 4 + tt
                layernorm_block(
                    xin_ap, t,
                    dst_base=lambda db, tt=tt: db * 512 + tt * 128,
                    dst_stride=512,
                    dst_keys_fn=lambda g4: [f"A{g4}"],
                    xin_keys=[f"{xin_name}r{t}_{o}" for o in range(8)],
                    tag="bln")
            bq, bkv = ps[2], ps[3]
            for db in range(NB):
                mm(bq[0:QL, :], wqd_sb[:, db * QL:(db + 1) * QL], A(db * 512, 512),
                   ["wqd_sb", f"A{db // 4}"], [PK[2]], start=(db == 0), stop=False)
            mm(bq[0:QL, :], bqd_row[0:1, :], ones_row[0:1, :], ["bqd_row", "ones_row"],
               [PK[2]], start=False, stop=True)
            for db in range(NB):
                mm(bkv[0:KV, :], wkvd_sb[:, db * KV:(db + 1) * KV], A(db * 512, 512),
                   ["wkvd_sb", f"A{db // 4}"], [PK[3]], start=(db == 0), stop=False)
            mm(bkv[0:KV, :], bkvd_row[0:1, :], ones_row[0:1, :], ["bkvd_row", "ones_row"],
               [PK[3]], start=False, stop=True)
            act(Cq_sb[0:QL, ch * 512:(ch + 1) * 512], bq[0:QL, :], AF.Copy, [PK[2]], ["Uh"])
            act(Ckv_sb[0:KV, ch * 512:(ch + 1) * 512], bkv[0:KV, :], AF.Copy, [PK[3]], ["Kt"])
        vec(lambda e: e.memset(Ckvs_sb[:, :], 0.0), ["Ckvs"], ["CkvsM"])
        for kb in range(NT):
            pkc = psb[0:128, (kb % 4) * 128:(kb % 4) * 128 + 4]
            tr(pkc, Ckv_sb[0:KV, kb * 128:(kb + 1) * 128], KV, ["Kt"], ["psb"])
            vec(lambda e, pkc=pkc, kb=kb: e.tensor_copy(
                Ckvs_sb[:, kb * 40: kb * 40 + 4], pkc),
                ["psb", "CkvsM"], ["CkvsM"])
        vec(lambda e: e.memset(bass.AP(Ckvs_sb, 32, [[NT * 40, 128], [40, NT], [1, 1]]),
                               1.0),
            ["CkvsM"], ["Ckvs"])
        dma("gpsimd", row_big[:, :], bv_u, ["row_big"], ["row_big"], tag="b_bvu")
        for h in range(H):
            sl = h % 2
            dma("gpsimd", wvu_h[:, :], Wv_u[:, h * 128:(h + 1) * 128], ["wvu_h"],
                ["wvu_h"], tag=f"wvu{h}")
            qhk = bkeys(QH, 2048)
            for c in range(4):
                bqh = ps[4]
                mm(bqh[0:KV, :], wqk_sb[:, h * KV:(h + 1) * KV],
                   Cq_sb[0:QL, c * 512:(c + 1) * 512], ["wqk_sb", "Uh"], [PK[4]],
                   start=True, stop=False)
                mm(bqh[0:KV, :], bqk_row[0:1, h * KV:(h + 1) * KV], ones_row[0:1, :],
                   ["bqk_row", "ones_row"], [PK[4]], start=False, stop=True)
                act(BIGB[0:KV, QH + c * 512: QH + (c + 1) * 512], bqh[0:KV, :], AF.Copy,
                    [PK[4]] + qhk, qhk)
            BROT = (0, 1, 4, 5)

            def b_score(kb, c, i):
                bS = ps[BROT[i % 4]]
                skey = PK[BROT[i % 4]]
                ekey = f"expS{i % 4}"
                mm(bS[:, :], Ckv_sb[0:KV, kb * 128:(kb + 1) * 128],
                   BIGB[0:KV, QH + c * 512: QH + (c + 1) * 512], ["Kt"] + qhk,
                   [skey], start=True, stop=True, tag=f"bsc{h}_{c}_{kb}")
                act(expS[i % 4][:, :], bS[:, :], AF.Exp, [skey], [ekey], scale=SCALE_B)

            def b_u(kb, c, i):
                mm(ps[2 + c % 2][0:33, :], Ckvs_sb[:, kb * 40: kb * 40 + 33],
                   expS[i % 4][:, :],
                   ["Ckvs", f"expS{i % 4}"], [PK[2 + c % 2]], start=(kb == 0),
                   stop=(kb == NT - 1))

            def b_chain(c):
                bU = ps[2 + c % 2]
                act(lrow[:, :], bU[32:33, :], AF.Ln, [PK[2 + c % 2], "lrow"], ["lrow"])
                act(lrec[:, :], lrow[:, :], AF.Exp, ["lrow", "coef_row"], ["lrec"],
                    scale=-1.0)
                act(lhi[:, :], lrec[:, :], AF.Copy, ["lrec", "lhi"], ["lhi"])
                vec(lambda e: e.tensor_sub(lrow[:, :], lrec[:, :], lhi[:, :]),
                    ["lrec", "lhi", "lrow"], ["llo_f"])
                vec(lambda e: e.tensor_copy(llo[:, :], lrow[:, :]), ["llo_f"], ["llo"])

            def b_final(c):
                bU = ps[2 + c % 2]
                vec(lambda e, bU=bU: e.tensor_copy(u5_sb[0:4, :], bU[0:4, :]),
                    [PK[2 + c % 2], "u5"], ["u5"])
                bb_i = BROT[c % 4]
                bB = ps[bb_i]
                mm(bB[:, :], ones_row[0:1, 0:128], lhi[0:1, :], ["ones_row", "lhi"],
                   [PK[bb_i]], start=True, stop=False)
                mm(bB[:, :], ones_row[0:1, 0:128], llo[0:1, :], ["ones_row", "llo"],
                   [PK[bb_i]], start=False, stop=True)
                vec(lambda e, bB=bB: e.tensor_copy(bcast_sb[:, :], bB[:, :]),
                    [PK[bb_i], "bcast"], ["bcast"])
                vec(lambda e: e.tensor_mul(ctx_bf[0:KV, 0:512], u5_sb[0:KV, :],
                                           bcast_sb[0:KV, :]),
                    ["u5", "bcast"], ["Qt"])
                bt_i = BROT[(c + 1) % 4]
                bT = ps[bt_i]
                mm(bT[:, :], wvu_h[0:KV, :], ctx_bf[0:KV, 0:512],
                   ["wvu_h", "Qt"], [PK[bt_i]], start=True, stop=False)
                mm(bT[:, :], row_big[0:1, h * 128:(h + 1) * 128], ones_row[0:1, :],
                   ["row_big", "ones_row"], [PK[bt_i]], start=False, stop=True)
                ahk = bkeys(AH[sl], 2048)
                act(Bv(AH[sl] + c * 512, 512), bT[:, :], AF.Copy,
                    [PK[bt_i]] + ahk, ahk)

            for cpass in range(2):
                cA, cB = 2 * cpass, 2 * cpass + 1
                flat = [(kb, (cA, cB)[j], 2 * kb + j) for kb in range(NT)
                        for j in range(2)]
                for idx, (kb, c, i) in enumerate(flat):
                    if idx >= 4:
                        b_u(*flat[idx - 4])
                    b_score(kb, c, i)
                for idx in range(len(flat) - 4, len(flat)):
                    b_u(*flat[idx])
                    if idx == len(flat) - 2:
                        b_chain(cA)
                b_final(cA)
                b_chain(cB)
                b_final(cB)
            dma("sync", Abh[h * 128:(h + 1) * 128, :], Bv(AH[sl], 2048),
                bkeys(AH[sl], 2048), [f"AbH{h}"])
        ab_ready = [f"AbH{h}" for h in range(H)]
        dma("gpsimd", row_big[:, :], bo_b, ["row_big"], ["row_big"], tag="b_bo")
        outk = []
        for Q in range(4):
            for e_blk in range(NB):
                wkey = akeys(e_blk * 1024, 1024)
                dma("gpsimd", A(e_blk * 1024, 1024),
                    Wo_b[e_blk * 128:(e_blk + 1) * 128, Q * 1024:(Q + 1) * 1024],
                    wkey, wkey, tag=f"bwo{Q}_{e_blk}")
            for tp in range(NT // 4):
                usl = tp % 2
                ubase = 32768 + usl * 16384
                uk = akeys(ubase, 16384)
                dma("sync", A(ubase, 16384),
                    Abh[:, tp * 512:(tp + 1) * 512].rearrange("(eb p) q -> p eb q", p=128),
                    ab_ready + uk, uk)
                for tj in range(4):
                    t = tp * 4 + tj
                    for oc in range(2):
                        rsl = oc
                        o8 = Q * 2 + oc
                        dma("scalar", rbuf[rsl][:, :],
                            xin_ap[t * 128:(t + 1) * 128, o8 * 512:(o8 + 1) * 512],
                            [f"{xin_name}r{t}_{o8}", f"rbuf{rsl}"], [f"rbuf{rsl}"])
                        bi = oc + 2 * (tj % 2)
                        bank = ps[bi]
                        for e_blk in range(NB):
                            mm(bank[:, :], A(ubase + e_blk * 512 + tj * 128, 128),
                               A(e_blk * 1024 + oc * 512, 512),
                               uk + akeys(e_blk * 1024 + oc * 512, 512), [PK[bi]],
                               start=(e_blk == 0), stop=False)
                        mm(bank[:, :], ones_row[0:1, 0:128],
                           row_big[0:1, o8 * 512:(o8 + 1) * 512],
                           ["ones_row", "row_big"], [PK[bi]], start=False, stop=True)
                        vec(lambda e, bank=bank, rsl=rsl: e.tensor_add(
                            nbuf[rsl][:, :], bank[:, :], rbuf[rsl][:, :]),
                            [PK[bi], f"rbuf{rsl}"], [f"nbuf{rsl}"])
                        outk.append(f"x3r{t}_{o8}")
                        dma("scalar", xout_ap[t * 128:(t + 1) * 128, o8 * 512:(o8 + 1) * 512],
                            nbuf[rsl][:, :], [f"nbuf{rsl}"], [f"x3r{t}_{o8}"])
        return outk

    if upto == "b0":
        keys = phase_a()
        k2 = phase_f(x1, x2, "x2", "x1", 1)
        k3 = phase_b(x2, x3, "x2")
        final = []
        final = list(k3)
        for i in range(32):
            sl = i % 2
            kk = bkeys(UAT[sl], 4096)
            dma("sync", Bv(UAT[sl], 2048), Abh[i * 128:(i + 1) * 128, :],
                k3 + kk, kk)
            dma("sync", dbg[:, i * 2048:(i + 1) * 2048], Bv(UAT[sl], 2048), kk,
                [f"dg{i}"])
            final.append(f"dg{i}")
        dma("sync", y[0:128, 0:512], rbuf[0][:, :], ["rbuf0"], ["yk"])
        final.append("yk")
    elif upto == "wt":
        wsteps = wprep_steps()
        for st_ in wsteps:
            st_()
        final = []
        for i in range(8):
            sl = i % 2
            kk = bkeys(UAT[sl], 4096)
            dma("sync", Bv(UAT[sl], 4096), WT[i * 128:(i + 1) * 128, :],
                [f"WTr{i}"] + kk, kk)
            dma("sync", dbg[:, i * 4096:(i + 1) * 4096], Bv(UAT[sl], 4096), kk,
                [f"dw{i}"])
            final.append(f"dw{i}")
        ack = bkeys(ACT_, 4096)
        dma("sync", Bv(ACT_, 4096), AcH[:, :], ["AcHk"] + ack, ack)
        dma("sync", dbg[:, 8 * 4096: 8 * 4096 + 4096], Bv(ACT_, 4096), ack, ["dA"])
        final.append("dA")
        bck = bkeys(XBF[0], 4096)
        dma("sync", Bv(XBF[0], 4096), BtH[:, :], bt_keys + bck, bck)
        dma("sync", dbg[:, 9 * 4096: 9 * 4096 + 4096], Bv(XBF[0], 4096), bck, ["dB"])
        final.append("dB")
        vec(lambda e: e.tensor_copy(expS[0][:, 0:1], coef_col[:, 0:1]),
            ["coef_col", "expS0"], ["expS0"])
        vec(lambda e: e.memset(expS[0][:, 1:512], 0.0), ["expS0"], ["expS0"])
        dma("sync", dbg[:, 10 * 4096: 10 * 4096 + 512], expS[0][:, 0:512], ["expS0"], ["dC"])
        final.append("dC")
        dma("sync", y[0:128, 0:512], rbuf[0][:, :], ["rbuf0"], ["yk"])
        final.append("yk")
    elif upto == "ua":
        keys = phase_a()
        final = []
        for i in range(32):
            sl = i % 2
            kk = bkeys(UAT[sl], 4096)
            dma("sync", Bv(UAT[sl], 2048), Ua[i * 128:(i + 1) * 128, :],
                keys + kk, kk)
            dma("sync", dbg[:, i * 2048:(i + 1) * 2048], Bv(UAT[sl], 2048), kk,
                [f"dg{i}"])
            final.append(f"dg{i}")
        dma("sync", y[0:128, 0:512], rbuf[0][:, :], ["rbuf0"], ["yk"])
        final.append("yk")
    elif upto == "h0":
        for t in range(NT):
            layernorm_block(
                x0, t,
                dst_base=lambda db, t=t: db * AG + t * 128,
                dst_stride=AG,
                dst_keys_fn=lambda g4: [f"A{g4 * 4 + j}" for j in range(4)],
                xin_keys=[], tag="a1")
        h = 0
        qwk = bkeys(QW, 4096)
        kwk = bkeys(KW, 4096)
        vwk = bkeys(VW2, 4096)
        dma("gpsimd", Bv(QW, 4096),
            wq[:, h * 192: h * 192 + 128].rearrange("(b p) e -> p b e", p=128),
            qwk, qwk)
        dma("gpsimd", Bv(KW, 4096),
            wkv[:, h * 128: (h + 1) * 128].rearrange("(b p) e -> p b e", p=128),
            kwk, kwk)
        dma("gpsimd", Bv(VW2, 4096),
            wkv[:, 4096 + h * 128: 4096 + (h + 1) * 128].rearrange(
                "(b p) e -> p b e", p=128),
            vwk, vwk)
        vtk = bkeys(VT, 2048)
        vsk = bkeys(VS, 2048)
        for c in range(4):
            bv_ = ps[2 + c % 2]
            for db in range(NB):
                mm(bv_[:, :], Bv(VW2 + db * 128, 128), A(db * AG + c * 512, 512),
                   vwk + [f"A{db}"], [PK[2 + c % 2]], start=(db == 0),
                   stop=(db == NB - 1))
            act(Bv(VT + c * 512, 512), bv_[:, :], AF.Copy, [PK[2 + c % 2]] + vtk, vtk)
        for g4 in range(4):
            for j in range(4):
                kb = g4 * 4 + j
                tr(psb[:, j * 128:(j + 1) * 128], Bv(VT + kb * 128, 128), 128,
                   vtk, ["psb"])
            B.emit("scalar",
                   lambda e, g4=g4: e.copy(Bv(VS + g4 * 512, 512), psb[:, :]),
                   ["psb"] + vsk, vsk)
        for c in range(4):
            bq, bk = ps[4], ps[5]
            for db in range(NB):
                mm(bq[:, :], Bv(QW + db * 128, 128), A(db * AG + c * 512, 512),
                   qwk + [f"A{db}"], [PK[4]], start=(db == 0), stop=(db == NB - 1))
            for db in range(NB):
                mm(bk[:, :], Bv(KW + db * 128, 128), A(db * AG + c * 512, 512),
                   kwk + [f"A{db}"], [PK[5]], start=(db == 0), stop=(db == NB - 1))
            act(Qt_sb[:, c * 512:(c + 1) * 512], bq[:, :], AF.Copy, [PK[4]], ["Qt"])
            act(Kt_sb[:, c * 512:(c + 1) * 512], bk[:, :], AF.Copy, [PK[5]], ["Kt"])
        for c in range(4):
            nkb = 4 * c + 4
            bU, bL = ps[2], ps[3]
            for kb in range(nkb):
                bS = ps[kb % 2]
                skey = PK[kb % 2]
                ekey = f"expS{kb % 2}"
                mm(bS[:, :], Kt_sb[:, kb * 128:(kb + 1) * 128],
                   Qt_sb[:, c * 512:(c + 1) * 512], ["Kt", "Qt"], [skey],
                   start=True, stop=True)
                act(expS[kb % 2][:, :], bS[:, :], AF.Exp, [skey], [ekey],
                    scale=SCALE_A)
                if kb >= 4 * c:
                    m = kb - 4 * c
                    gp(lambda e, kb=kb, m=m: e.affine_select(
                        expS[kb % 2][:, :], expS[kb % 2][:, :], [[1, 512]],
                        ALU.is_ge, 0.0, base=-128 * m, channel_multiplier=-1),
                        [ekey], [ekey])
                mm(bU[:, :], Bv(VS + kb * 128, 128), expS[kb % 2][:, :],
                   vsk + [ekey], [PK[2]], start=(kb == 0), stop=(kb == nkb - 1))
                mm(bL[0:1, :], ones_col[:, :], expS[kb % 2][:, :],
                   ["ones_col", ekey], [PK[3]],
                   start=(kb == 0), stop=(kb == nkb - 1))
            vec(lambda e: e.tensor_copy(lrow[:, :], ps[3][0:1, :]), [PK[3]], ["lrow"])
            vec(lambda e: e.reciprocal(lrec[:, :], lrow[:, :]), ["lrow", "coef_row"],
                ["lrec"])
            vec(lambda e: e.tensor_copy(lhi[:, :], lrec[:, :]), ["lrec"], ["lhi"])
            vec(lambda e: e.tensor_sub(lrow[:, :], lrec[:, :], lhi[:, :]),
                ["lrec", "lhi", "lrow"], ["llo_f"])
            vec(lambda e: e.tensor_copy(llo[:, :], lrow[:, :]), ["llo_f"], ["llo"])
            bB = ps[4]
            mm(bB[:, :], ones_row[0:1, 0:128], lhi[0:1, :], ["ones_row", "lhi"],
               [PK[4]], start=True, stop=False)
            mm(bB[:, :], ones_row[0:1, 0:128], llo[0:1, :], ["ones_row", "llo"],
               [PK[4]], start=False, stop=True)
            vec(lambda e: e.tensor_copy(bcast_sb[:, :], ps[4][:, :]), [PK[4]],
                ["bcast"])
            vec(lambda e, c=c: e.tensor_mul(Uh_sb[:, c * 512:(c + 1) * 512],
                                            ps[2][:, :], bcast_sb[:, :]),
                [PK[2], "bcast"], ["Uh"])
        dma("sync", dbg[:, 0:2048], Qt_sb[:, :], ["Qt"], ["d0"])
        dma("sync", dbg[:, 2048:4096], Kt_sb[:, :], ["Kt"], ["d1"])
        dma("sync", dbg[:, 4096:6144], Bv(VS, 2048), vsk, ["d2"])
        dma("sync", dbg[:, 6144:8192], Uh_sb[:, :], ["Uh"], ["d3"])
        dma("sync", y[0:128, 0:512], rbuf[0][:, :], ["rbuf0"], ["yk"])
        final = ["d0", "d1", "d2", "d3", "yk"]
    elif upto == "ln":
        for t in range(NT):
            layernorm_block(
                x0, t,
                dst_base=lambda db, t=t: db * AG + t * 128,
                dst_stride=AG,
                dst_keys_fn=lambda g4: [f"A{g4 * 4 + j}" for j in range(4)],
                xin_keys=[], tag="a1")
        dma("sync", dbg, BIGA[:, :], [f"A{g}" for g in range(32)], ["dbgk"])
        dma("sync", y[0:128, 0:512], rbuf[0][:, :], ["rbuf0"], ["yk"])
        final = ["dbgk", "yk"]
    elif upto == "a":
        keys = phase_a()
        for t in range(NT):
            for o in range(8):
                sl = o % 2
                dma("sync", rbuf[sl][:, :], x1[t * 128:(t + 1) * 128, o * 512:(o + 1) * 512],
                    keys + [f"rbuf{sl}"], [f"rbuf{sl}"])
                dma("sync", y[t * 128:(t + 1) * 128, o * 512:(o + 1) * 512], rbuf[sl][:, :],
                    [f"rbuf{sl}"], [f"yk{t}_{o}"])
        final = [f"yk{t}_{o}" for t in range(NT) for o in range(8)]
    elif upto == "f1":
        keys = phase_a()
        final = phase_f(x1, y, "y", "x1", 1)
    elif upto == "b":
        keys = phase_a()
        k2 = phase_f(x1, x2, "x2", "x1", 1)
        final = phase_b(x2, y, "x2")
    else:
        keys = phase_a()
        k2 = phase_f(x1, x2, "x2", "x1", 1)
        k3 = phase_b(x2, x3, "x2")
        final = phase_f(x3, y, "y", "x3", 2)
    B.emit("sync", lambda e: e.nop(), reads=final, writes=[])
    B.finalize_and_emit(stack)
    stack.close()
    return nc


_NC_CACHE = {}


def _get_nc(upto="full"):
    if upto not in _NC_CACHE:
        _NC_CACHE[upto] = build_program(upto)
    return _NC_CACHE[upto]


def _in_maps(inputs):
    c = np.ascontiguousarray
    hs = inputs["L0_attn_a_hidden_states"]
    base = {
        "wq": c(inputs["L0_attn_a_wq"]),
        "wkv": c(inputs["L0_attn_a_wkv"]),
        "wo_a": c(inputs["L0_attn_a_wo"]),
        "Wq_d": c(inputs["L0_attn_b_Wq_d"]),
        "bq_d": c(inputs["L0_attn_b_bq_d"]).reshape(1, QL),
        "W_qk": c(inputs["L0_attn_b_W_qk"]),
        "b_qk": c(inputs["L0_attn_b_b_qk"]).reshape(1, 128),
        "Wkv_d": c(inputs["L0_attn_b_Wkv_d"]),
        "bkv_d": c(inputs["L0_attn_b_bkv_d"]).reshape(1, KV),
        "Wv_u": c(inputs["L0_attn_b_Wv_u"]),
        "bv_u": c(inputs["L0_attn_b_bv_u"]).reshape(1, D),
        "Wo_b": c(inputs["L0_attn_b_Wo"]),
        "bo_b": c(inputs["L0_attn_b_bo"]).reshape(1, D),
        "lora_A": c(inputs["L0_ffn_lora_A"]).reshape(128, D),
        "lora_B": c(inputs["L0_ffn_lora_B"]).reshape(8 * D, 16),
        "ffn_W": c(inputs["L0_ffn_W"]),
        "ffn_bias": c(inputs["L0_ffn_bias"]).reshape(1, D),
        "lora_w": c(inputs["L0_ffn_lora_weights"]).reshape(1, 8),
    }
    maps = []
    for b in range(8):
        m = dict(base)
        m["x0"] = c(np.asarray(hs[b]))
        maps.append(m)
    return maps


def kernel(**inputs):
    nc = _get_nc("full")
    maps = _in_maps(inputs)
    res = run_bass_kernel_spmd(nc, maps, core_ids=list(range(8)))
    out = np.stack([np.asarray(r["y"]).astype(np.float32) for r in res.results], axis=0)
    return out
